# version 07
# baseline (speedup 1.0000x reference)
import numpy as np
import concourse.bass as bass
import concourse.mybir as mybir
from concourse.bass_utils import run_bass_kernel_spmd
from contextlib import ExitStack

F32 = mybir.dt.float32
BF16 = mybir.dt.bfloat16
AF = mybir.ActivationFunctionType
ALU = mybir.AluOpType
AX = mybir.AxisListType.X

C = 1024
NCH = 8
EPS = 1e-5
FF = 16384


class _DebugStop(Exception):
    pass


class KB:
    def __init__(self, T):
        self.T = T
        self.NT = T // 128
        self.NG = T // 512
        nc = bass.Bass("TRN2", target_bir_lowering=False)
        self.nc = nc
        self.pe, self.dv, self.ac, self.po, self.sy = nc.tensor, nc.vector, nc.scalar, nc.gpsimd, nc.sync
        self.st = ExitStack()
        self.cnt = {}
        self.sem = {}
        NT = self.NT

        dram = lambda n, shp, k: nc.dram_tensor(n, shp, F32, kind=k).ap()
        self.d_x = dram("x", [2, T, C], "ExternalInput")
        self.d_wqkv = [dram("wqkv0", [C, 3 * C], "ExternalInput"),
                       dram("wqkv1", [C, 3 * C], "ExternalInput")]
        self.d_wp = [dram("wp0", [C, C], "ExternalInput"),
                     dram("wp1", [C, C], "ExternalInput")]
        self.d_ln = dram("lnw", [2, 8, 128], "ExternalInput")
        self.d_rwT = dram("rwT", [8, C], "ExternalInput")
        self.d_w1 = dram("w1", [C, FF], "ExternalInput")
        self.d_w2 = dram("w2", [FF, C], "ExternalInput")
        self.d_id = dram("ident", [128, 128], "ExternalInput")
        self.d_mk = dram("masks", [4, 128, 512], "ExternalInput")
        self.d_ones = dram("ones", [128, 128], "ExternalInput")
        self.d_out = dram("out", [2, T, C], "ExternalOutput")

        sb = lambda n, shp, dt: self.st.enter_context(nc.sbuf_tensor(n, shp, dt))
        self.XR = sb("XR", [128, NT * C], BF16)
        self.HF = sb("HF", [128, NCH * T], BF16)
        self.BIGB = sb("BIGB", [128, 4 * T], F32)
        self.KVW = sb("KVW", [128, max(4 * T + (T // 128) * 4 * 130, 16384)], BF16)
        self.AR1 = sb("AR1", [128, max(8 * T, 4 * T + 8192)], BF16)
        self.AST = sb("AST", [128, 4096], BF16)
        self.HPQ = sb("HPQ", [128, 8192], BF16)
        self.ATM = sb("ATM", [128, 2048], BF16)
        self.SQS = sb("SQS", [128, C], BF16)
        self.MK = sb("MK", [128, 4 * 512], BF16)
        self.IDB = sb("IDB", [128, 128], BF16)
        self.IDF = sb("IDF", [128, 128], F32)
        self.ONES = sb("ONESB", [128, 128], BF16)
        self.LNW = sb("LNW", [128, 16], F32)
        self.RW = sb("RW", [128, 64], BF16)
        self.WTOK = sb("WTOK", [128, NT * 8], F32)
        self.WSQ = sb("WSQ", [128, 4 * 1024], BF16)
        self.LGT = sb("LGT", [128, 16], F32)
        self.S1 = sb("S1", [128, 2], F32)
        self.S2 = sb("S2", [128, 2], F32)
        self.MUc = sb("MUc", [128, 2], F32)
        self.VPc = sb("VPc", [128, 2], F32)
        self.TVA = sb("TVA", [128, 2], F32)
        self.RSTc = sb("RSTc", [128, 2], F32)
        self.MUCACHE = sb("MUCACHE", [128, NT], F32)
        self.RSTCACHE = sb("RSTCACHE", [128, NT], F32)
        self.NM = sb("NM", [128, 8], F32)
        self.KN2A = sb("KN2A", [128, 8], F32)
        self.QN2A = sb("QN2A", [128, 8], F32)
        self.OL4 = sb("OL4", [128, 4], F32)
        self.BND = sb("BND", [128, 8], F32)
        self.NKND = sb("NKND", [128, 8], F32)
        self.QN2 = sb("QN2", [128, 8], F32)
        self.NMH = sb("NMH", [128, 8], F32)
        self.NMHB = sb("NMHB", [128, 8], BF16)
        self.LL = sb("LL", [128, 8], F32)
        self.NGM = sb("NGM", [128, 4], F32)
        self.BB = sb("BB", [128, 8], F32)
        self.BL = sb("BL", [128, 8], F32)
        self.LT = sb("LT", [128, 4], F32)
        self.LIV = sb("LIV", [128, 4], F32)
        self.RM1 = sb("RM1", [128, 1], F32)
        self.RM1N = sb("RM1N", [128, 1], F32)
        self.RM2 = sb("RM2", [128, 1], F32)
        self.RE2 = sb("RE2", [128, 1], F32)
        self.RDEN = sb("RDEN", [128, 1], F32)
        self.RW1 = sb("RW1", [128, 1], F32)
        self.RW2 = sb("RW2", [128, 1], F32)
        self.RMK1 = sb("RMK1", [128, 8], F32)
        self.RL2 = sb("RL2", [128, 8], F32)
        self.RMK2 = sb("RMK2", [128, 8], F32)

        self.PS = self.st.enter_context(nc.psum_tensor("PS", [128, 4096], F32))

        for s in ["pe", "dv", "ac", "po", "qld", "qx", "qdbg", "qo0", "qo1",
                  "wa0", "wa1", "wa2", "wa3", "wq", "w1a", "w1b", "w2o"]:
            self.sem[s] = self.st.enter_context(nc.semaphore("sem_" + s))
            self.cnt[s] = 0

        self.ln_free = [0, 0]
        self.bank_free = [0, 0, 0, 0]
        self.slot_free = [0, 0]
        self.nm_free = [0, 0]
        self.o_evac = [0, 0]
        self.ptb_free = {0: 0, 4: 0, 7: 0}
        self.pts_free = [0, 0]
        self.liv_free = [0, 0, 0, 0]
        self.ll_free = [0, 0]
        self.stb_free = [0, 0, 0, 0]
        self.ptslot_free = [0, 0, 0, 0]
        self.obank_free = [0, 0, 0, 0]
        self.rt_osg_free = 0
        self.rt_otb_free = 0
        self.itc = 0

    def inc(self, s, instr, n=1):
        instr.then_inc(self.sem[s], n)
        self.cnt[s] += n
        return self.cnt[s]

    def w(self, eng, s, tok):
        if tok and tok > 0:
            eng.wait_ge(self.sem[s], tok)

    def barrier(self):
        eng_of = {"pe": self.pe, "dv": self.dv, "ac": self.ac, "po": self.po}
        self.dv.drain()
        self.ac.drain()
        self.po.drain()
        for a in eng_of:
            for b in eng_of:
                if a != b and self.cnt[b] > 0:
                    eng_of[a].wait_ge(self.sem[b], self.cnt[b])
            for s in ("qo0", "qo1"):
                if self.cnt[s] > 0:
                    eng_of[a].wait_ge(self.sem[s], self.cnt[s])

    def psb(self, bank, w=512):
        return self.PS[:, bank * 512: bank * 512 + w]

    def pt_bf16(self, bank):
        return self.PS[:, bank * 512:(bank + 1) * 512].bitcast(BF16)

    @property
    def L8(self):
        return self.HPQ[:].bitcast(F32)

    @property
    def LG(self):
        return self.HPQ[:].bitcast(F32)

    @property
    def UU(self):
        return self.AR1
    def PPslot(self, slot):
        return self.AR1[:, slot * 2 * self.T:(slot + 1) * 2 * self.T]
    def QH(self):
        return self.AR1[:, 0: 4 * self.T]
    def PTb(self, slot):
        o = 4 * self.T + slot * 512
        return self.AR1[:, o: o + 512]
    def OSG(self):
        o4 = (4 * self.T + 2048) // 2
        return self.AR1[:].bitcast(F32)[:, o4: o4 + 512]
    def OTM(self):
        o4 = (4 * self.T + 2048) // 2 + 512
        return self.AR1[:].bitcast(F32)[:, o4: o4 + 272]
    def OTB(self):
        o = 4 * self.T + 2048 + 1024 + 544
        return self.AR1[:, o: o + 256]

    def KF(self):
        return self.KVW[:, 0: 4 * self.T]
    def VT(self):
        return self.KVW[:, 4 * self.T: 4 * self.T + (self.T // 128) * 4 * 130]
    def W1Q(self, slot):
        return self.KVW[:, slot * 4096:(slot + 1) * 4096]
    @property
    def W2O(self):
        return self.KVW[:, 8192: 16384]

    @property
    def OF(self):
        return self.BIGB[:].bitcast(BF16)

    def load_consts(self):
        po, pe, dv = self.po, self.pe, self.dv
        self.inc("qld", po.dma_start(self.IDB[:], self.d_id), 16)
        self.inc("qld", po.dma_start(self.IDF[:], self.d_id), 16)
        self.inc("qld", po.dma_start(self.ONES[:], self.d_ones), 16)
        mk = self.d_mk.rearrange("r p k -> p r k")
        self.inc("qld", po.dma_start(self.MK[:].rearrange("p (r k) -> p r k", r=4), mk), 16)
        self.inc("qld", po.dma_start(self.L8[0:8, 0:128], self.d_ln[0]), 16)
        self.inc("qld", po.dma_start(self.L8[0:8, 128:256], self.d_ln[1]), 16)
        t = self.inc("qld", po.dma_start(self.L8[0:8, 256:1280], self.d_rwT), 16)
        self.w(pe, "qld", t)
        pe.transpose(self.PS[:, 0:8], self.L8[0:8, 0:128], self.IDF[0:8, 0:8])
        pe.transpose(self.PS[:, 8:16], self.L8[0:8, 128:256], self.IDF[0:8, 0:8])
        for j in range(8):
            mm = pe.transpose(self.PS[:, 512 + j * 8: 512 + (j + 1) * 8],
                              self.L8[0:8, 256 + j * 128: 256 + (j + 1) * 128],
                              self.IDF[0:8, 0:8])
        ptok = self.inc("pe", mm)
        self.w(dv, "pe", ptok)
        dv.tensor_copy(self.LNW[:, 0:16], self.PS[:, 0:16])
        ins = dv.tensor_copy(self.RW[:], self.PS[:, 512: 512 + 64])
        self.const_tok = self.inc("dv", ins)

    def load_x(self, seq):
        src = self.d_x[seq].rearrange("(n p) c -> p n c", p=128)
        dst = self.XR[:].rearrange("p (n c) -> p n c", c=C)
        return self.inc("qx", self.po.dma_start(dst, src), 16)

    def ln_stats(self, src_ap, par, src_dv_tok=0):
        dv, ac, po = self.dv, self.ac, self.po
        c1 = 1.0 / C
        if src_dv_tok:
            dv.drain()
        t_dv = self.inc("dv", dv.reduce_sum(self.S1[:, par:par + 1], src_ap, axis=AX))
        self.w(ac, "dv", max(self.ln_free[par], src_dv_tok))
        t_ac = self.inc("ac", ac.activation(self.SQS[:], src_ap, AF.Square,
                                            accum_out=self.S2[:, par:par + 1]))
        self.w(dv, "ac", t_ac)
        dv.drain()
        dv.tensor_scalar(self.MUc[:, par:par + 1], self.S1[:, par:par + 1], c1, None, op0=ALU.mult)
        dv.drain()
        dv.tensor_tensor(self.TVA[:, par:par + 1], self.MUc[:, par:par + 1],
                         self.MUc[:, par:par + 1], op=ALU.mult)
        dv.drain()
        dv.tensor_scalar(self.TVA[:, par:par + 1], self.TVA[:, par:par + 1], EPS, None,
                         op0=ALU.subtract)
        dv.tensor_scalar(self.VPc[:, par:par + 1], self.S2[:, par:par + 1], c1, None, op0=ALU.mult)
        dv.drain()
        ins = dv.tensor_tensor(self.VPc[:, par:par + 1], self.VPc[:, par:par + 1],
                               self.TVA[:, par:par + 1], op=ALU.subtract)
        t_po = self.inc("dv", ins)
        self.w(ac, "dv", t_po)
        ac.activation(self.RSTc[:, par:par + 1], self.VPc[:, par:par + 1], AF.Ln)
        ac.drain()
        ins = ac.activation(self.RSTc[:, par:par + 1], self.RSTc[:, par:par + 1], AF.Exp, scale=-0.5)
        t_rst = self.inc("ac", ins)
        return (self.MUc[:, par:par + 1], self.RSTc[:, par:par + 1], t_rst)

    def h_build(self, src_buf, gain_idx=None, cache_stats=False):
        nc, pe, dv = self.nc, self.pe, self.dv
        T = self.T
        nx_cons = [0, 0]
        pt_free = self.ptb_free[4]
        for i in range(self.NT):
            par = i % 2
            src = src_buf[:, i * C:(i + 1) * C]
            mu, rstd, t_rst = self.ln_stats(src, par)
            self.w(dv, "ac", t_rst)
            self.w(dv, "pe", nx_cons[par])
            nx = self.AST[:, par * C:(par + 1) * C]
            ins = dv.tensor_scalar(nx, src, mu, rstd, op0=ALU.subtract, op1=ALU.mult)
            t_nx = self.inc("dv", ins)
            self.ln_free[par] = t_nx
            if cache_stats:
                dv.tensor_copy(self.MUCACHE[:, i:i + 1], mu)
                ins = dv.tensor_copy(self.RSTCACHE[:, i:i + 1], rstd)
                t_nx = self.inc("dv", ins)
                self.ln_free[par] = t_nx
            self.w(pe, "dv", t_nx)
            self.w(pe, "dv", pt_free)
            ptb = self.pt_bf16(4)
            for j in range(NCH):
                mm = pe.transpose(ptb[:, j * 128:(j + 1) * 128], nx[:, j * 128:(j + 1) * 128], self.IDB[:])
            t_tr = self.inc("pe", mm)
            nx_cons[par] = t_tr
            self.w(dv, "pe", t_tr)
            if gain_idx is None:
                out = self.HF[:].rearrange("p (j t) -> p j t", j=NCH)[:, :, i * 128:(i + 1) * 128]
                ins = dv.tensor_copy(out, ptb[:].rearrange("p (j q) -> p j q", j=NCH))
            else:
                for j in range(NCH):
                    ins = dv.tensor_scalar(self.HF[:, j * T + i * 128: j * T + (i + 1) * 128],
                                           ptb[:, j * 128:(j + 1) * 128],
                                           self.LNW[:, gain_idx * 8 + j: gain_idx * 8 + j + 1],
                                           None, op0=ALU.mult)
            pt_free = self.inc("dv", ins)
        self.ptb_free[4] = pt_free

    def kv_build(self, mha, half):
        pe, dv, po = self.pe, self.dv, self.po
        T, NG = self.T, self.NG
        wslot_free = [0, 0, 0, 0]
        wtok = [0, 0, 0, 0]
        kvbank_free = [0, 0, 0, 0]
        pt_free = self.ptb_free[4]
        vstage_cons = [0, 0]
        nblk = (T // 128) * 4
        ones_ap = self.VT().rearrange("p (b c) -> p b c", c=130)[:, :, 64:130:65]
        ins = dv.memset(ones_ap, 1.0)
        t_ones = self.inc("dv", ins)
        sidx = 0
        for p in range(4):
            pair = half * 4 + p
            streams = (("q", 0), ("k", C), ("v", 2 * C))
            sl_of = {}
            for which, cbase in streams:
                sl = sidx % 4
                sidx += 1
                sl_of[which] = sl
                col0 = cbase + pair * 128
                src = self.d_wqkv[mha][:, col0:col0 + 128].rearrange("(j p) f -> p j f", p=128)
                self.w(po, "pe", wslot_free[sl])
                dst = self.WSQ[:, sl * 1024:(sl + 1) * 1024].rearrange("p (j f) -> p j f", j=8)
                wtok[sl] = self.inc("wa%d" % sl, po.dma_start(dst, src), 16)
            for which, dstbuf, scale in (("k", self.KF(), None), ("q", self.QH(), 0.125)):
                sl = sl_of[which]
                self.w(pe, "wa%d" % sl, wtok[sl])
                for g in range(NG):
                    bank = g % 2
                    self.w(pe, "dv", kvbank_free[bank])
                    for j in range(NCH):
                        mm = pe.matmul(self.psb(bank),
                                       self.WSQ[:, sl * 1024 + j * 128: sl * 1024 + (j + 1) * 128],
                                       self.HF[:, j * T + g * 512: j * T + (g + 1) * 512],
                                       start=(j == 0), stop=(j == NCH - 1))
                    t_mm = self.inc("pe", mm)
                    if g == NG - 1:
                        wslot_free[sl] = t_mm
                    self.w(dv, "pe", t_mm)
                    dsl = dstbuf[:, p * T + g * 512: p * T + (g + 1) * 512]
                    if scale is None:
                        ins = dv.tensor_copy(dsl, self.psb(bank))
                    else:
                        ins = dv.tensor_scalar(dsl, self.psb(bank), scale, None, op0=ALU.mult)
                    kvbank_free[bank] = self.inc("dv", ins)
            sl = sl_of["v"]
            self.w(pe, "wa%d" % sl, wtok[sl])
            for g in range(NG):
                bank = 2 + g % 2
                self.w(pe, "dv", kvbank_free[bank])
                for j in range(NCH):
                    mm = pe.matmul(self.psb(bank),
                                   self.WSQ[:, sl * 1024 + j * 128: sl * 1024 + (j + 1) * 128],
                                   self.HF[:, j * T + g * 512: j * T + (g + 1) * 512],
                                   start=(j == 0), stop=(j == NCH - 1))
                t_mm = self.inc("pe", mm)
                if g == NG - 1:
                    wslot_free[sl] = t_mm
                vsl = g % 2
                self.w(dv, "pe", t_mm)
                self.w(dv, "pe", vstage_cons[vsl])
                vst = self.AST[:, vsl * 512:(vsl + 1) * 512]
                ins = dv.tensor_copy(vst, self.psb(bank))
                t_ev = self.inc("dv", ins)
                kvbank_free[bank] = t_ev
                self.w(pe, "dv", t_ev)
                self.w(pe, "dv", pt_free)
                self.w(pe, "dv", t_ones)
                ptb = self.pt_bf16(4)
                for kk in range(4):
                    mm = pe.transpose(ptb[:, kk * 128:(kk + 1) * 128], vst[:, kk * 128:(kk + 1) * 128],
                                      self.IDB[:])
                t_tr = self.inc("pe", mm)
                vstage_cons[vsl] = t_tr
                self.w(dv, "pe", t_tr)
                vtr = self.VT().rearrange("p (b c) -> p b c", c=130)
                out = vtr[:, g * 16 + p: g * 16 + 13 + p: 4, 0:129]
                out = out.rearrange("p b (h d) -> p b h d", d=64) if False else \
                    vtr[:, g * 16 + p: g * 16 + 13 + p: 4, :].rearrange(
                        "p b c -> p b c")[:, :, 0:130]
                base = (g * 16 + p) * 130
                out = bass.AP(self.VT().tensor, self.VT().offset + base,
                              [list(self.VT().ap[0]), [4 * 130, 4], [65, 2], [1, 64]])
                ins = dv.tensor_copy(out, ptb[:, 0:512].rearrange("p (k h d) -> p k h d", k=4, h=2))
                pt_free = self.inc("dv", ins)
        self.ptb_free[4] = pt_free

    def emit_qknorms(self, half):
        pe, dv, ac = self.pe, self.dv, self.ac
        T = self.T
        nck = T // 512
        t_red = 0
        t_mm = 0
        for src_i, buf, dst in ((0, self.KF(), self.KN2A), (1, self.QH(), self.QN2A)):
            for wave in range(2):
                sq = self.HPQ[:, 0: 2 * T]
                t_sq = 0
                for pw in range(2):
                    p = wave * 2 + pw
                    self.w(ac, "pe", t_mm)
                    ins = ac.activation(sq[:, pw * T:(pw + 1) * T],
                                        buf[:, p * T:(p + 1) * T], AF.Square)
                    t_sq = self.inc("ac", ins)
                self.w(pe, "ac", t_sq)
                self.w(pe, "dv", t_red)
                for pw in range(2):
                    for h in range(2):
                        r = h * 64
                        for cc in range(nck):
                            mm = pe.matmul(
                                self.PS[r:r + 1, pw * 2048 + cc * 512: pw * 2048 + (cc + 1) * 512],
                                self.ONES[r:r + 64, 0:1],
                                sq[r:r + 64, pw * T + cc * 512: pw * T + (cc + 1) * 512],
                                start=True, stop=True)
                        t_mm = self.inc("pe", mm)
                for pw in range(2):
                    p = wave * 2 + pw
                    for h in range(2):
                        r = h * 64
                        self.w(dv, "pe", t_mm)
                        dv.drain()
                        ins = dv.reduce_max(dst[r:r + 1, p:p + 1],
                                            self.PS[r:r + 1, pw * 2048: pw * 2048 + T], axis=AX)
                        t_red = self.inc("dv", ins)
        self.w(dv, "dv", t_red)
        dv.drain()
        for h in range(2):
            r = h * 64
            dv.tensor_scalar(self.NMH[r:r + 1, 0:4], self.QN2A[r:r + 1, 0:4], -4.08, None, op0=ALU.mult)
            dv.tensor_scalar(self.NMH[r:r + 1, 4:8], self.KN2A[r:r + 1, 0:4], -1.02 / 16.0,
                             None, op0=ALU.mult)
        dv.drain()
        for h in range(2):
            r = h * 64
            dv.tensor_tensor(self.NMH[r:r + 1, 0:4], self.NMH[r:r + 1, 0:4],
                             self.NMH[r:r + 1, 4:8], op=ALU.add)
        dv.drain()
        for h in range(2):
            r = h * 64
            ins = dv.tensor_copy(self.NMHB[r:r + 1, h * 4: h * 4 + 4], self.NMH[r:r + 1, 0:4])
        t_nb = self.inc("dv", ins)
        self.w(pe, "dv", t_nb)
        for h in range(2):
            r = h * 64
            mm = pe.matmul(self.PS[:, 7 * 512 + h * 4: 7 * 512 + h * 4 + 4],
                           self.ONES[r:r + 1, 0:128], self.NMHB[r:r + 1, h * 4: h * 4 + 4],
                           start=True, stop=True)
        t_mm2 = self.inc("pe", mm)
        self.w(dv, "pe", t_mm2)
        ins = dv.tensor_copy(self.BND[:, 0:8], self.PS[:, 7 * 512: 7 * 512 + 8])
        t_bc = self.inc("dv", ins)
        self.bnd_tok = t_bc
        self.kn_guard = t_bc

    def attention_half(self, mha, half):
        pe, dv, ac = self.pe, self.dv, self.ac
        T, NG = self.T, self.NG
        for qg in range(NG):
            gp = qg % 2
            for p in range(4):
                nj = 4 * qg + 4
                avtok = [0, 0]
                for j in range(nj):
                    cp = j % 2
                    stok = [0, 0]
                    for h in range(2):
                        bank = cp * 2 + h
                        self.w(pe, "ac", self.stb_free[bank])
                        if j == 0:
                            self.w(pe, "dv", self.obank_free[h * 2 + gp])
                            self.w(pe, "dv", self.kn_guard)
                        lhs = self.KF()[h * 64:(h + 1) * 64, p * T + j * 128: p * T + (j + 1) * 128]
                        rhs = self.QH()[h * 64:(h + 1) * 64, p * T + qg * 512: p * T + (qg + 1) * 512]
                        if j >= 4 * qg:
                            r = j - 4 * qg
                            pe.matmul(self.psb(bank), lhs, rhs, start=True, stop=False)
                            mm = pe.matmul(self.psb(bank), self.IDB[:],
                                           self.MK[:, r * 512:(r + 1) * 512], start=False, stop=True)
                        else:
                            mm = pe.matmul(self.psb(bank), lhs, rhs, start=True, stop=True)
                        stok[h] = self.inc("pe", mm)
                    for h in range(2):
                        bank = cp * 2 + h
                        self.w(ac, "pe", stok[h])
                        self.w(ac, "pe", self.ptslot_free[bank])
                        ins = ac.activation(self.PTb(bank), self.psb(bank), AF.Exp,
                                            bias=self.BND[:, h * 4 + p: h * 4 + p + 1])
                        etok = self.inc("ac", ins)
                        self.stb_free[bank] = etok
                    for h in range(2):
                        slot = cp * 2 + h
                        obank = 4 + h * 2 + gp
                        self.w(pe, "ac", self.stb_free[slot])
                        blk = (j * 4 + p) * 130
                        mm = pe.matmul(self.PS[0:65, obank * 512: obank * 512 + 512],
                                       self.VT()[:, blk + h * 65: blk + (h + 1) * 65],
                                       self.PTb(slot), start=(j == 0), stop=(j == nj - 1))
                        avtok[h] = self.ptslot_free[slot] = self.inc("pe", mm)
                for h in range(2):
                    self.o_roundtrip(half, p, qg, h, gp, avtok[h])

    def o_roundtrip(self, half, p, qg, h, gp, t_av):
        pe, dv = self.pe, self.dv
        T = self.T
        obank = 4 + h * 2 + gp
        pair = half * 4 + p
        self.w(dv, "pe", t_av)
        self.w(dv, "pe", self.rt_osg_free)
        ins = dv.tensor_copy(self.OSG()[0:65, 0:512], self.PS[0:65, obank * 512: obank * 512 + 512])
        t_ev = self.inc("dv", ins)
        self.w(pe, "dv", t_ev)
        for b in range(4):
            mm = pe.transpose(self.PS[:, obank * 512 + b * 65: obank * 512 + (b + 1) * 65],
                              self.OSG()[0:65, b * 128:(b + 1) * 128], self.IDF[0:65, 0:65])
        t_tr = self.inc("pe", mm)
        self.rt_osg_free = t_tr
        self.w(dv, "pe", t_tr)
        self.w(dv, "pe", self.rt_otb_free)
        dv.tensor_copy(self.OTM()[:, 0:260], self.PS[:, obank * 512: obank * 512 + 260])
        dv.drain()
        linv_src = bass.AP(self.OTM().tensor, self.OTM().offset + 64,
                           [list(self.OTM().ap[0]), [65, 4], [1, 1]])
        dv.reciprocal(self.OL4[:, 0:4], linv_src)
        dv.drain()
        for b in range(4):
            ins = dv.tensor_scalar(self.OTB()[:, b * 64:(b + 1) * 64],
                                   self.OTM()[:, b * 65: b * 65 + 64],
                                   self.OL4[:, b:b + 1], None, op0=ALU.mult)
        t_nb = self.inc("dv", ins)
        self.w(pe, "dv", t_nb)
        ptb = self.pt_bf16(obank)
        for b in range(4):
            mm = pe.transpose(ptb[0:64, b * 128:(b + 1) * 128],
                              self.OTB()[:, b * 64:(b + 1) * 64], self.IDB[:])
        t_tb = self.inc("pe", mm)
        self.rt_otb_free = t_tb
        self.w(dv, "pe", t_tb)
        ins = dv.tensor_copy(self.OF[h * 64:(h + 1) * 64, pair * T + qg * 512: pair * T + (qg + 1) * 512],
                             ptb[0:64, 0:512])
        self.obank_free[h * 2 + gp] = self.inc("dv", ins)

    def proj_phase(self, mha, is_inner):
        pe, dv, po = self.pe, self.dv, self.po
        T, NG = self.T, self.NG
        wslot_free = [0, 0]
        wtok = [0, 0]
        pbank_free = [0, 0]
        pt_free = self.ptb_free[4]
        afm_cons = 0
        atm_free = [0, 0]
        for g in range(NG):
            evtoks = []
            for ct in range(8):
                sl = ct % 2
                src = self.d_wp[mha][:, ct * 128:(ct + 1) * 128].rearrange("(j p) f -> p j f", p=128)
                self.w(po, "pe", wslot_free[sl])
                dst = self.WSQ[:, sl * 1024:(sl + 1) * 1024].rearrange("p (j f) -> p j f", j=8)
                wtok[sl] = self.inc("wa%d" % sl, po.dma_start(dst, src), 16)
                self.w(pe, "wa%d" % sl, wtok[sl])
                bank = ct % 2
                self.w(pe, "dv", pbank_free[bank])
                if ct == 0:
                    self.w(pe, "pe", afm_cons)
                for j in range(NCH):
                    mm = pe.matmul(self.psb(bank),
                                   self.WSQ[:, sl * 1024 + j * 128: sl * 1024 + (j + 1) * 128],
                                   self.OF[:, j * T + g * 512: j * T + (g + 1) * 512],
                                   start=(j == 0), stop=(j == NCH - 1))
                t_mm = self.inc("pe", mm)
                wslot_free[sl] = t_mm
                self.w(dv, "pe", t_mm)
                self.w(dv, "pe", afm_cons)
                ins = dv.tensor_copy(self.AST[:, ct * 512:(ct + 1) * 512], self.psb(bank))
                pbank_free[bank] = self.inc("dv", ins)
                evtoks.append(pbank_free[bank])
            afm_tok = max(evtoks)
            for tt in range(4):
                tslot = tt % 2
                self.w(pe, "dv", afm_tok)
                self.w(pe, "dv", pt_free)
                self.w(pe, "dv", atm_free[tslot])
                ptb = self.pt_bf16(4)
                for ct in range(8):
                    mm = pe.transpose(ptb[:, ct * 128:(ct + 1) * 128],
                                      self.AST[:, ct * 512 + tt * 128: ct * 512 + (tt + 1) * 128],
                                      self.IDB[:])
                t_tr = self.inc("pe", mm)
                if tt == 3:
                    afm_cons = t_tr
                self.w(dv, "pe", t_tr)
                atm = self.ATM[:, tslot * 1024:(tslot + 1) * 1024]
                ins = dv.tensor_copy(atm, ptb[:])
                pt_free = self.inc("dv", ins)
                ti = g * 4 + tt
                xr = self.XR[:, ti * C:(ti + 1) * C]
                if not is_inner:
                    par = tt % 2
                    mu, rstd, t_rst = self.ln_stats(atm, par, src_dv_tok=pt_free)
                    self.w(dv, "ac", t_rst)
                    dv.tensor_scalar(atm, atm, mu, rstd, op0=ALU.subtract, op1=ALU.mult)
                    dv.drain()
                    ins = dv.tensor_tensor(xr, xr, atm, op=ALU.add)
                    t_res = self.inc("dv", ins)
                    self.ln_free[par] = t_res
                    atm_free[tslot] = t_res
                else:
                    oslot = 1 - tslot
                    self.w(dv, "pe", 0)
                    self.w(dv, "dv", 0)
                    other = self.ATM[:, oslot * 1024:(oslot + 1) * 1024]
                    self.w(dv, "dv", atm_free[oslot]) if False else None
                    dv.tensor_scalar(other, xr, self.MUCACHE[:, ti:ti + 1],
                                     self.RSTCACHE[:, ti:ti + 1], op0=ALU.subtract, op1=ALU.mult)
                    dv.drain()
                    uu = self.UU[:, ti * C:(ti + 1) * C]
                    ins = dv.tensor_tensor(uu, other, atm, op=ALU.add)
                    t_res = self.inc("dv", ins)
                    atm_free[tslot] = t_res
                    atm_free[oslot] = t_res
        self.ptb_free[4] = pt_free

    def mha_slot(self, seq, mha):
        is_inner = (mha == 1)
        self.h_build(self.XR, gain_idx=(0 if is_inner else None), cache_stats=is_inner)
        self.barrier()
        if self.ck("h%d" % mha):
            self.dump("h", self.HF[:])
            raise _DebugStop()
        for half in range(2):
            self.kv_build(mha, half)
            self.barrier()
            self.emit_qknorms(half)
            if half == 0 and self.ck("kv%d" % mha):
                self.dump("kf", self.KF()[:])
                self.dump("vt", self.VT()[:])
                raise _DebugStop()
            self.attention_half(mha, half)
            self.barrier()
        if self.ck("of%d" % mha):
            self.dump("of", self.OF[:, 0:8 * self.T])
            raise _DebugStop()
        self.proj_phase(mha, is_inner)
        self.barrier()

    def router_phase(self):
        pe, dv, po, ac = self.pe, self.dv, self.po, self.ac
        T, NG = self.T, self.NG
        lg_free = 0
        lgt_free = [0, 0]
        ps5_free = 0
        wt_tok = 0
        for g in range(NG):
            for j in range(NCH):
                mm = pe.matmul(self.PS[0:8, 7 * 512: 7 * 512 + 512],
                               self.RW[:, j * 8:(j + 1) * 8],
                               self.HF[:, j * T + g * 512: j * T + (g + 1) * 512],
                               start=(j == 0), stop=(j == NCH - 1))
            t_mm = self.inc("pe", mm)
            self.w(dv, "pe", t_mm)
            self.w(dv, "pe", lg_free)
            ins = dv.tensor_copy(self.LG[0:8, 0:512], self.PS[0:8, 7 * 512: 7 * 512 + 512])
            t_lg = self.inc("dv", ins)
            for tt in range(4):
                par = tt % 2
                self.w(pe, "dv", t_lg)
                self.w(pe, "dv", max(lgt_free[par], ps5_free))
                mm = pe.transpose(self.PS[:, 5 * 512: 5 * 512 + 8],
                                  self.LG[0:8, tt * 128:(tt + 1) * 128], self.IDF[0:8, 0:8])
                t_tr = self.inc("pe", mm)
                if tt == 3:
                    lg_free = t_tr
                self.w(dv, "pe", t_tr)
                ins = dv.tensor_copy(self.LGT[:, par * 8:(par + 1) * 8],
                                     self.PS[:, 5 * 512: 5 * 512 + 8])
                t_lgt = self.inc("dv", ins)
                ps5_free = t_lgt
                lgt = self.LGT[:, par * 8:(par + 1) * 8]
                dv.drain()
                dv.reduce_max(self.RM1N[:], lgt, axis=AX, negate=True)
                dv.drain()
                dv.tensor_scalar(self.RM1[:], self.RM1N[:], -1.0, None, op0=ALU.mult)
                dv.drain()
                dv.tensor_scalar(self.RMK1[:], lgt, self.RM1[:], None, op0=ALU.is_ge)
                dv.drain()
                dv.scalar_tensor_tensor(self.RL2[:], self.RMK1[:], -1e30, lgt,
                                        op0=ALU.mult, op1=ALU.add)
                dv.drain()
                ins = dv.reduce_max(self.RM2[:], self.RL2[:], axis=AX)
                t_m2 = self.inc("dv", ins)
                self.w(ac, "dv", t_m2)
                ins = ac.activation(self.RE2[:], self.RM2[:], AF.Exp, bias=self.RM1N[:])
                t_e2 = self.inc("ac", ins)
                self.w(dv, "ac", t_e2)
                dv.tensor_scalar(self.RDEN[:], self.RE2[:], 1.0, None, op0=ALU.add)
                dv.drain()
                dv.tensor_scalar(self.RMK2[:], self.RL2[:], self.RM2[:], None, op0=ALU.is_ge)
                dv.drain()
                ti = g * 4 + tt
                dv.scalar_tensor_tensor(self.RMK2[:], self.RMK2[:], self.RE2[:], self.RMK1[:],
                                        op0=ALU.mult, op1=ALU.add)
                dv.reciprocal(self.RW1[:], self.RDEN[:])
                dv.drain()
                ins = dv.tensor_scalar(self.WTOK[:, ti * 8:(ti + 1) * 8], self.RMK2[:],
                                       self.RW1[:], None, op0=ALU.mult)
                wt_tok = self.inc("dv", ins)
                lgt_free[par] = wt_tok
        self.wtok_tok = wt_tok

    def moe_phase(self, seq):
        pe, dv, ac, po = self.pe, self.dv, self.ac, self.po
        T = self.T
        tgW = T // 2
        self.tgW = tgW
        self.WBv = self.AST[:, 0:tgW]
        nsub = max(1, tgW // 512)
        subW = min(512, tgW)
        nh = max(1, tgW // 512)
        for tg in range(2):
            tgbase = tg * tgW
            tgtile0 = tg * (tgW // 128)
            gel_tok = [0, 0]
            scl_tok = [0] * 8
            w1_tok = [0, 0]
            w1_free = [0, 0]
            w2_free = 0
            w2_tok_cur = 0
            w2_tok_prev = 0
            scl_prev = [0] * 8
            yevac_tok = [0] * (2 * nsub)
            y_last_evac = 0
            wb_tok = 0
            hpq_guard = 0

            def h_mms(fc, oct_):
                bank0 = (fc % 2) * 2
                sl = fc // 4
                self.w(pe, "w1a" if sl == 0 else "w1b", w1_tok[sl])
                self.w(pe, "ac", gel_tok[fc % 2])
                if fc == 0:
                    self.w(pe, "dv", wb_tok)
                for j in range(NCH):
                    for hh in range(nh):
                        mm = pe.matmul(self.PS[:, (bank0 + hh) * 512:(bank0 + hh) * 512 + subW],
                                       self.W1Q(sl)[:, j * 512 + (fc % 4) * 128: j * 512 + (fc % 4) * 128 + 128],
                                       self.HF[:, j * T + tgbase + hh * subW: j * T + tgbase + (hh + 1) * subW],
                                       start=(j == 0), stop=(j == NCH - 1))
                tok = self.inc("pe", mm)
                return tok

            def gelu_scale(fc, t_h):
                bank0 = (fc % 2) * 2
                self.w(ac, "pe", t_h)
                self.w(ac, "dv", scl_prev[fc])
                self.w(ac, "pe", hpq_guard)
                ins = ac.activation(self.HPQ[:, fc * tgW:(fc + 1) * tgW],
                                    self.PS[:, bank0 * 512: bank0 * 512 + tgW], AF.Gelu)
                t_g = self.inc("ac", ins)
                gel_tok[fc % 2] = t_g
                self.w(dv, "ac", t_g)
                ins = dv.tensor_tensor(self.HPQ[:, fc * tgW:(fc + 1) * tgW],
                                       self.HPQ[:, fc * tgW:(fc + 1) * tgW],
                                       self.WBv, op=ALU.mult)
                scl_tok[fc] = self.inc("dv", ins)

            def w2_combos(first):
                nonlocal w2_free, y_last_evac
                dv.drain()
                for ci, (ch, sub) in enumerate([(a, b) for a in range(2) for b in range(nsub)]):
                    self.w(pe, "w2o", w2_tok_prev)
                    self.w(pe, "dv", y_last_evac)
                    for fc in range(8):
                        self.w(pe, "dv", scl_prev[fc])
                        for cti in range(4):
                            cto = (ch * 4 + cti) * 128
                            mm = pe.matmul(self.PS[:, (4 + cti) * 512:(4 + cti) * 512 + subW],
                                           self.W2O[:, fc * 1024 + cto: fc * 1024 + cto + 128],
                                           self.HPQ[:, fc * tgW + sub * subW: fc * tgW + (sub + 1) * subW],
                                           start=(fc == 0), stop=(fc == 7))
                    t_c = self.inc("pe", mm)
                    w2_free = t_c
                    self.w(dv, "pe", t_c)
                    ysl = self.BIGB[:, 0:4 * T].rearrange("p (c t) -> p c t", t=tgW)[
                        :, ch * 4:(ch + 1) * 4, sub * subW:(sub + 1) * subW]
                    ps_y = self.PS[:, 2048:4096].rearrange("p (c t) -> p c t", t=512)[:, :, 0:subW]
                    if first:
                        ins = dv.tensor_scalar(ysl, ps_y, 1.0, None, op0=ALU.mult)
                    else:
                        ins = dv.scalar_tensor_tensor(ysl, ps_y, 1.0, ysl, op0=ALU.mult, op1=ALU.add)
                    yevac_tok[ci] = self.inc("dv", ins)
                    y_last_evac = yevac_tok[ci]

            for oct_ in range(16):
                if oct_ % 2 == 0:
                    wb_tok = self.wb_build(tgtile0, oct_ // 2, gel_tok, scl_tok)
                for q in range(2):
                    col0 = oct_ * 1024 + q * 512
                    src = self.d_w1[:, col0:col0 + 512].rearrange("(j p) f -> p j f", p=128)
                    self.w(po, "pe", w1_free[q])
                    dst = self.W1Q(q)[:].rearrange("p (j f) -> p j f", j=8)
                    w1_tok[q] = self.inc("w1a" if q == 0 else "w1b", po.dma_start(dst, src), 16)
                t_h0 = h_mms(0, oct_)
                t_h1 = h_mms(1, oct_)
                if oct_ > 0:
                    w2_combos(first=(oct_ == 1))
                hpq_guard = w2_free
                self.w(po, "pe", w2_free)
                src = self.d_w2[oct_ * 1024:(oct_ + 1) * 1024, :].rearrange("(j p) c -> p j c", p=128)
                dst = self.W2O[:].rearrange("p (j c) -> p j c", j=8)
                w2_tok_cur = self.inc("w2o", po.dma_start(dst, src), 16)
                gelu_scale(0, t_h0)
                gelu_scale(1, t_h1)
                for fc in range(2, 8):
                    t_h = h_mms(fc, oct_)
                    gelu_scale(fc, t_h)
                w1_free = [self.cnt["pe"], self.cnt["pe"]]
                w2_tok_prev = w2_tok_cur
                scl_prev = list(scl_tok)
            w2_combos(first=False)
            self.barrier()
            if self.ck("y") and tg == 0:
                self.dump("y", self.BIGB[:, 0:4 * self.T])
                self.dump("wtok", self.WTOK[:])
                self.dump("t2d", self.HF[:])
                raise _DebugStop()
            self.finale(seq, tg, tgW)
            self.barrier()

    def wb_build(self, tgtile0, expert, gel_tok, scl_tok):
        pe, dv = self.pe, self.dv
        ntile = self.tgW // 128
        dv.drain()
        self.w(dv, "pe", self.cnt["pe"])
        for tt in range(ntile):
            ti = tgtile0 + tt
            ins = dv.tensor_scalar(self.AST[:, 2048 + tt * 128: 2048 + (tt + 1) * 128],
                                   self.ONES[:], self.WTOK[:, ti * 8 + expert: ti * 8 + expert + 1],
                                   None, op0=ALU.mult)
        t_stage = self.inc("dv", ins)
        self.w(pe, "dv", t_stage)
        self.w(pe, "ac", gel_tok[0])
        ptb = self.pt_bf16(0)
        for tt in range(ntile):
            mm = pe.transpose(ptb[:, tt * 128:(tt + 1) * 128],
                              self.AST[:, 2048 + tt * 128: 2048 + (tt + 1) * 128], self.IDB[:])
        t_tr = self.inc("pe", mm)
        self.w(dv, "pe", t_tr)
        ins = dv.tensor_copy(self.WBv, ptb[:, 0:self.tgW])
        return self.inc("dv", ins)

    def finale(self, seq, tg, tgW):
        pe, dv, sy = self.pe, self.dv, self.sy
        ntile = tgW // 128
        astf = self.AST[:].bitcast(F32)
        atmf = self.ATM[:].bitcast(F32)
        outslot = [astf[:, 1024:2048], atmf[:, 0:1024]]
        out_tok = [0, 0]
        f_cons = 0
        tr_free = 0
        for tt in range(ntile):
            ti = tg * ntile + tt
            par = tt % 2
            self.w(pe, "dv", tr_free)
            for ct in range(8):
                bank = ct // 4
                mm = pe.transpose(self.PS[:, bank * 512 + (ct % 4) * 128: bank * 512 + (ct % 4) * 128 + 128],
                                  self.BIGB[:, ct * tgW + tt * 128: ct * tgW + (tt + 1) * 128],
                                  self.IDF[:])
            t_tr = self.inc("pe", mm)
            self.w(dv, "pe", t_tr)
            self.w(dv, "dv", 0)
            ftmp = astf[:, 0:1024]
            ins = dv.scalar_tensor_tensor(ftmp, self.PS[:, 0:1024], 1.0,
                                          self.UU[:, ti * C:(ti + 1) * C], op0=ALU.mult, op1=ALU.add)
            t_f = self.inc("dv", ins)
            tr_free = t_f
            mu, rstd, t_rst = self.ln_stats(ftmp, par, src_dv_tok=t_f)
            self.w(dv, "ac", t_rst)
            self.w(dv, "qo%d" % par, out_tok[par])
            nrm = outslot[par]
            dv.tensor_scalar(nrm, ftmp, mu, rstd, op0=ALU.subtract, op1=ALU.mult)
            dv.drain()
            ins = dv.tensor_tensor(nrm, nrm, self.XR[:, ti * C:(ti + 1) * C], op=ALU.add)
            t_o = self.inc("dv", ins)
            self.ln_free[par] = t_o
            self.w(sy, "dv", t_o)
            dst = self.d_out[seq, ti * 128:(ti + 1) * 128, :]
            out_tok[par] = self.inc("qo%d" % par, sy.dma_start(dst, nrm), 16)

    def dump(self, name, ap):
        w = ap.shape[-1]
        npart = ap.shape[0]
        d = self.nc.dram_tensor("dbg_" + name, [npart, w], F32, kind="ExternalOutput").ap()
        self.barrier()
        for s in self.sem:
            if s != "qdbg" and self.cnt[s]:
                self.po.wait_ge(self.sem[s], self.cnt[s])
        self.inc("qdbg", self.po.dma_start(d, ap), 16)

    def ck(self, phase):
        if self.debug_phase is None:
            return False
        return phase == self.debug_phase

    def build(self, debug_phase=None):
        self.debug_phase = debug_phase

        def fin():
            for s in ("qo0", "qo1", "qdbg"):
                if self.cnt[s]:
                    self.po.wait_ge(self.sem[s], self.cnt[s])
                    self.sy.wait_ge(self.sem[s], self.cnt[s])
            return self.nc

        self.load_consts()
        try:
            self._main()
        except _DebugStop:
            pass
        return fin()

    def _main(self):
        for seq in range(2):
            if seq == 1:
                self.barrier()
            xt = self.load_x(seq)
            for eng in (self.dv, self.pe, self.ac):
                self.w(eng, "qx", xt)
            self.w(self.dv, "dv", self.const_tok)
            if self.ck("x"):
                self.dump("x", self.XR[:])
                raise _DebugStop()
            self.mha_slot(seq, 0)
            if self.ck("mha0"):
                self.dump("x1", self.XR[:])
                raise _DebugStop()
            self.mha_slot(seq, 1)
            if self.ck("mha1"):
                self.dump("x1", self.XR[:])
                self.dump("u", self.UU[:, 0:self.NT * C])
                raise _DebugStop()
            self.h_build(self.UU, gain_idx=1, cache_stats=False)
            self.barrier()
            if self.ck("t2"):
                self.dump("t2", self.HF[:])
                raise _DebugStop()
            self.router_phase()
            self.barrier()
            if self.ck("router"):
                self.dump("wtok", self.WTOK[:])
                raise _DebugStop()
            self.moe_phase(seq)
            if self.ck("y"):
                raise _DebugStop()


def build_program(T=2048):
    return KB(T).build()


def make_masks():
    kk = np.arange(128)[:, None]
    qq = np.arange(512)[None, :]
    return np.stack([np.where(qq >= 128 * r + kk, 0.0, -1e30) for r in range(4)]).astype(np.float32)


def kernel(L0_attn_x, L0_attn_W_attn, L0_attn_W_proj, L0_ffn_ln1_w, L0_ffn_c_attn_w,
           L0_ffn_c_proj_w, L0_ffn_ln2_w, L0_ffn_router_w, L0_ffn_w1, L0_ffn_w2):
    x = np.ascontiguousarray(np.asarray(L0_attn_x, dtype=np.float32))
    B, T, _ = x.shape
    n_cores = 8
    per = B // n_cores
    nc = build_program(T)
    consts = {
        "wqkv0": np.ascontiguousarray(L0_attn_W_attn, dtype=np.float32),
        "wqkv1": np.ascontiguousarray(L0_ffn_c_attn_w, dtype=np.float32),
        "wp0": np.ascontiguousarray(L0_attn_W_proj, dtype=np.float32),
        "wp1": np.ascontiguousarray(L0_ffn_c_proj_w, dtype=np.float32),
        "lnw": np.ascontiguousarray(
            np.stack([np.asarray(L0_ffn_ln1_w, dtype=np.float32).reshape(8, 128),
                      np.asarray(L0_ffn_ln2_w, dtype=np.float32).reshape(8, 128)])),
        "rwT": np.ascontiguousarray(np.asarray(L0_ffn_router_w, dtype=np.float32).T),
        "w1": np.ascontiguousarray(L0_ffn_w1, dtype=np.float32),
        "w2": np.ascontiguousarray(L0_ffn_w2, dtype=np.float32),
        "ident": np.eye(128, dtype=np.float32),
        "masks": make_masks(),
        "ones": np.ones((128, 128), dtype=np.float32),
    }
    in_maps = []
    for c in range(n_cores):
        m = dict(consts)
        m["x"] = np.ascontiguousarray(x[c * per:(c + 1) * per])
        in_maps.append(m)
    res = run_bass_kernel_spmd(nc, in_maps, core_ids=list(range(n_cores)))
    return np.concatenate([np.asarray(r["out"], dtype=np.float32) for r in res.results], axis=0)


# version 08
# speedup vs baseline: 1.1268x; 1.1268x over previous
import numpy as np
import concourse.bass as bass
import concourse.mybir as mybir
from concourse.bass_utils import run_bass_kernel_spmd
from contextlib import ExitStack

F32 = mybir.dt.float32
BF16 = mybir.dt.bfloat16
AF = mybir.ActivationFunctionType
ALU = mybir.AluOpType
AX = mybir.AxisListType.X

C = 1024
NCH = 8
EPS = 1e-5
FF = 16384


class _DebugStop(Exception):
    pass


class KB:
    def __init__(self, T):
        self.T = T
        self.NT = T // 128
        self.NG = T // 512
        nc = bass.Bass("TRN2", target_bir_lowering=False)
        self.nc = nc
        self.pe, self.dv, self.ac, self.po, self.sy = nc.tensor, nc.vector, nc.scalar, nc.gpsimd, nc.sync
        self.st = ExitStack()
        self.cnt = {}
        self.sem = {}
        NT = self.NT

        dram = lambda n, shp, k: nc.dram_tensor(n, shp, F32, kind=k).ap()
        self.d_x = dram("x", [2, T, C], "ExternalInput")
        self.d_wqkv = [dram("wqkv0", [C, 3 * C], "ExternalInput"),
                       dram("wqkv1", [C, 3 * C], "ExternalInput")]
        self.d_wp = [dram("wp0", [C, C], "ExternalInput"),
                     dram("wp1", [C, C], "ExternalInput")]
        self.d_ln = dram("lnw", [2, 8, 128], "ExternalInput")
        self.d_rwT = dram("rwT", [8, C], "ExternalInput")
        self.d_w1 = dram("w1", [C, FF], "ExternalInput")
        self.d_w2 = dram("w2", [FF, C], "ExternalInput")
        self.d_id = dram("ident", [128, 128], "ExternalInput")
        self.d_mk = dram("masks", [4, 128, 512], "ExternalInput")
        self.d_ones = dram("ones", [128, 128], "ExternalInput")
        self.d_out = dram("out", [2, T, C], "ExternalOutput")

        sb = lambda n, shp, dt: self.st.enter_context(nc.sbuf_tensor(n, shp, dt))
        self.XR = sb("XR", [128, NT * C], BF16)
        self.HF = sb("HF", [128, NCH * T], BF16)
        self.BIGB = sb("BIGB", [128, 4 * T], F32)
        self.KVW = sb("KVW", [128, max(4 * T + (T // 128) * 4 * 130, 16384)], BF16)
        self.AR1 = sb("AR1", [128, max(8 * T, 4 * T + 8192)], BF16)
        self.AST = sb("AST", [128, 4096], BF16)
        self.HPQ = sb("HPQ", [128, 8192], BF16)
        self.ATM = sb("ATM", [128, 2048], BF16)
        self.SQS = sb("SQS", [128, C], BF16)
        self.MK = sb("MK", [128, 4 * 512], BF16)
        self.IDB = sb("IDB", [128, 128], BF16)
        self.IDF = sb("IDF", [128, 128], F32)
        self.ONES = sb("ONESB", [128, 128], BF16)
        self.LNW = sb("LNW", [128, 16], F32)
        self.RW = sb("RW", [128, 64], BF16)
        self.WTOK = sb("WTOK", [128, NT * 8], F32)
        self.WSQ = sb("WSQ", [128, 4 * 1024], BF16)
        self.LGT = sb("LGT", [128, 16], F32)
        self.S1 = sb("S1", [128, 2], F32)
        self.S2 = sb("S2", [128, 2], F32)
        self.MUc = sb("MUc", [128, 2], F32)
        self.VPc = sb("VPc", [128, 2], F32)
        self.TVA = sb("TVA", [128, 2], F32)
        self.RSTc = sb("RSTc", [128, 2], F32)
        self.MUCACHE = sb("MUCACHE", [128, NT], F32)
        self.RSTCACHE = sb("RSTCACHE", [128, NT], F32)
        self.NM = sb("NM", [128, 8], F32)
        self.KN2A = sb("KN2A", [128, 8], F32)
        self.QN2A = sb("QN2A", [128, 8], F32)
        self.OL4 = sb("OL4", [128, 4], F32)
        self.BND = sb("BND", [128, 8], F32)
        self.NKND = sb("NKND", [128, 8], F32)
        self.QN2 = sb("QN2", [128, 8], F32)
        self.NMH = sb("NMH", [128, 8], F32)
        self.NMHB = sb("NMHB", [128, 8], BF16)
        self.LL = sb("LL", [128, 8], F32)
        self.NGM = sb("NGM", [128, 4], F32)
        self.BB = sb("BB", [128, 8], F32)
        self.BL = sb("BL", [128, 8], F32)
        self.LT = sb("LT", [128, 4], F32)
        self.LIV = sb("LIV", [128, 4], F32)
        self.RM1 = sb("RM1", [128, 1], F32)
        self.RM1N = sb("RM1N", [128, 1], F32)
        self.RM2 = sb("RM2", [128, 1], F32)
        self.RE2 = sb("RE2", [128, 1], F32)
        self.RDEN = sb("RDEN", [128, 1], F32)
        self.RW1 = sb("RW1", [128, 1], F32)
        self.RW2 = sb("RW2", [128, 1], F32)
        self.RMK1 = sb("RMK1", [128, 8], F32)
        self.RL2 = sb("RL2", [128, 8], F32)
        self.RMK2 = sb("RMK2", [128, 8], F32)

        self.PS = self.st.enter_context(nc.psum_tensor("PS", [128, 4096], F32))

        for s in ["pe", "dv", "ac", "po", "qld", "qx", "qdbg", "qo0", "qo1",
                  "wa0", "wa1", "wa2", "wa3", "wq", "w1a", "w1b", "w2o"]:
            self.sem[s] = self.st.enter_context(nc.semaphore("sem_" + s))
            self.cnt[s] = 0

        self.ln_free = [0, 0]
        self.bank_free = [0, 0, 0, 0]
        self.slot_free = [0, 0]
        self.nm_free = [0, 0]
        self.o_evac = [0, 0]
        self.ptb_free = {0: 0, 4: 0, 7: 0}
        self.pts_free = [0, 0]
        self.liv_free = [0, 0, 0, 0]
        self.ll_free = [0, 0]
        self.stb_free = [0, 0, 0, 0]
        self.ptslot_free = [0, 0, 0, 0]
        self.obank_free = [0, 0, 0, 0]
        self.rt_osg_free = 0
        self.rt_otb_free = 0
        self.itc = 0

    def inc(self, s, instr, n=1):
        instr.then_inc(self.sem[s], n)
        self.cnt[s] += n
        return self.cnt[s]

    def w(self, eng, s, tok):
        if tok and tok > 0:
            eng.wait_ge(self.sem[s], tok)

    def barrier(self):
        eng_of = {"pe": self.pe, "dv": self.dv, "ac": self.ac, "po": self.po}
        self.dv.drain()
        self.ac.drain()
        self.po.drain()
        for a in eng_of:
            for b in eng_of:
                if a != b and self.cnt[b] > 0:
                    eng_of[a].wait_ge(self.sem[b], self.cnt[b])
            for s in ("qo0", "qo1"):
                if self.cnt[s] > 0:
                    eng_of[a].wait_ge(self.sem[s], self.cnt[s])

    def psb(self, bank, w=512):
        return self.PS[:, bank * 512: bank * 512 + w]

    def pt_bf16(self, bank):
        return self.PS[:, bank * 512:(bank + 1) * 512].bitcast(BF16)

    @property
    def L8(self):
        return self.HPQ[:].bitcast(F32)

    @property
    def LG(self):
        return self.HPQ[:].bitcast(F32)

    @property
    def UU(self):
        return self.AR1
    def PPslot(self, slot):
        return self.AR1[:, slot * 2 * self.T:(slot + 1) * 2 * self.T]
    def QH(self):
        return self.AR1[:, 0: 4 * self.T]
    def PTb(self, slot):
        o = 4 * self.T + slot * 1024
        return self.AR1[:, o: o + 1024]
    def OSG(self):
        o4 = (4 * self.T + 4096) // 2
        return self.AR1[:].bitcast(F32)[:, o4: o4 + 512]
    def OTM(self):
        o4 = (4 * self.T + 4096) // 2 + 512
        return self.AR1[:].bitcast(F32)[:, o4: o4 + 272]
    def OTB(self):
        o = 4 * self.T + 4096 + 1024 + 544
        return self.AR1[:, o: o + 256]

    def KF(self):
        return self.KVW[:, 0: 4 * self.T]
    def VT(self):
        return self.KVW[:, 4 * self.T: 4 * self.T + (self.T // 128) * 4 * 130]
    def W1Q(self, slot):
        return self.KVW[:, slot * 4096:(slot + 1) * 4096]
    @property
    def W2O(self):
        return self.KVW[:, 8192: 16384]

    @property
    def OF(self):
        return self.BIGB[:].bitcast(BF16)

    def load_consts(self):
        po, pe, dv = self.po, self.pe, self.dv
        self.inc("qld", po.dma_start(self.IDB[:], self.d_id), 16)
        self.inc("qld", po.dma_start(self.IDF[:], self.d_id), 16)
        self.inc("qld", po.dma_start(self.ONES[:], self.d_ones), 16)
        mk = self.d_mk.rearrange("r p k -> p r k")
        self.inc("qld", po.dma_start(self.MK[:].rearrange("p (r k) -> p r k", r=4), mk), 16)
        self.inc("qld", po.dma_start(self.L8[0:8, 0:128], self.d_ln[0]), 16)
        self.inc("qld", po.dma_start(self.L8[0:8, 128:256], self.d_ln[1]), 16)
        t = self.inc("qld", po.dma_start(self.L8[0:8, 256:1280], self.d_rwT), 16)
        self.w(pe, "qld", t)
        pe.transpose(self.PS[:, 0:8], self.L8[0:8, 0:128], self.IDF[0:8, 0:8])
        pe.transpose(self.PS[:, 8:16], self.L8[0:8, 128:256], self.IDF[0:8, 0:8])
        for j in range(8):
            mm = pe.transpose(self.PS[:, 512 + j * 8: 512 + (j + 1) * 8],
                              self.L8[0:8, 256 + j * 128: 256 + (j + 1) * 128],
                              self.IDF[0:8, 0:8])
        ptok = self.inc("pe", mm)
        self.w(dv, "pe", ptok)
        dv.tensor_copy(self.LNW[:, 0:16], self.PS[:, 0:16])
        ins = dv.tensor_copy(self.RW[:], self.PS[:, 512: 512 + 64])
        self.const_tok = self.inc("dv", ins)

    def load_x(self, seq):
        src = self.d_x[seq].rearrange("(n p) c -> p n c", p=128)
        dst = self.XR[:].rearrange("p (n c) -> p n c", c=C)
        return self.inc("qx", self.po.dma_start(dst, src), 16)

    def ln_stats(self, src_ap, par, src_dv_tok=0):
        dv, ac, po = self.dv, self.ac, self.po
        c1 = 1.0 / C
        if src_dv_tok:
            dv.drain()
        t_dv = self.inc("dv", dv.reduce_sum(self.S1[:, par:par + 1], src_ap, axis=AX))
        self.w(ac, "dv", max(self.ln_free[par], src_dv_tok))
        t_ac = self.inc("ac", ac.activation(self.SQS[:], src_ap, AF.Square,
                                            accum_out=self.S2[:, par:par + 1]))
        self.w(dv, "ac", t_ac)
        dv.drain()
        dv.tensor_scalar(self.MUc[:, par:par + 1], self.S1[:, par:par + 1], c1, None, op0=ALU.mult)
        dv.drain()
        dv.tensor_tensor(self.TVA[:, par:par + 1], self.MUc[:, par:par + 1],
                         self.MUc[:, par:par + 1], op=ALU.mult)
        dv.drain()
        dv.tensor_scalar(self.TVA[:, par:par + 1], self.TVA[:, par:par + 1], EPS, None,
                         op0=ALU.subtract)
        dv.tensor_scalar(self.VPc[:, par:par + 1], self.S2[:, par:par + 1], c1, None, op0=ALU.mult)
        dv.drain()
        ins = dv.tensor_tensor(self.VPc[:, par:par + 1], self.VPc[:, par:par + 1],
                               self.TVA[:, par:par + 1], op=ALU.subtract)
        t_po = self.inc("dv", ins)
        self.w(ac, "dv", t_po)
        ac.activation(self.RSTc[:, par:par + 1], self.VPc[:, par:par + 1], AF.Ln)
        ac.drain()
        ins = ac.activation(self.RSTc[:, par:par + 1], self.RSTc[:, par:par + 1], AF.Exp, scale=-0.5)
        t_rst = self.inc("ac", ins)
        return (self.MUc[:, par:par + 1], self.RSTc[:, par:par + 1], t_rst)

    def h_build(self, src_buf, gain_idx=None, cache_stats=False):
        nc, pe, dv = self.nc, self.pe, self.dv
        T = self.T
        nx_cons = [0, 0]
        pt_free = self.ptb_free[4]
        for i in range(self.NT):
            par = i % 2
            src = src_buf[:, i * C:(i + 1) * C]
            mu, rstd, t_rst = self.ln_stats(src, par)
            self.w(dv, "ac", t_rst)
            self.w(dv, "pe", nx_cons[par])
            nx = self.AST[:, par * C:(par + 1) * C]
            ins = dv.tensor_scalar(nx, src, mu, rstd, op0=ALU.subtract, op1=ALU.mult)
            t_nx = self.inc("dv", ins)
            self.ln_free[par] = t_nx
            if cache_stats:
                dv.tensor_copy(self.MUCACHE[:, i:i + 1], mu)
                ins = dv.tensor_copy(self.RSTCACHE[:, i:i + 1], rstd)
                t_nx = self.inc("dv", ins)
                self.ln_free[par] = t_nx
            self.w(pe, "dv", t_nx)
            self.w(pe, "dv", pt_free)
            ptb = self.pt_bf16(4)
            for j in range(NCH):
                mm = pe.transpose(ptb[:, j * 128:(j + 1) * 128], nx[:, j * 128:(j + 1) * 128], self.IDB[:])
            t_tr = self.inc("pe", mm)
            nx_cons[par] = t_tr
            self.w(dv, "pe", t_tr)
            if gain_idx is None:
                out = self.HF[:].rearrange("p (j t) -> p j t", j=NCH)[:, :, i * 128:(i + 1) * 128]
                ins = dv.tensor_copy(out, ptb[:].rearrange("p (j q) -> p j q", j=NCH))
            else:
                for j in range(NCH):
                    ins = dv.tensor_scalar(self.HF[:, j * T + i * 128: j * T + (i + 1) * 128],
                                           ptb[:, j * 128:(j + 1) * 128],
                                           self.LNW[:, gain_idx * 8 + j: gain_idx * 8 + j + 1],
                                           None, op0=ALU.mult)
            pt_free = self.inc("dv", ins)
        self.ptb_free[4] = pt_free

    def kv_build(self, mha, half):
        pe, dv, po = self.pe, self.dv, self.po
        T, NG = self.T, self.NG
        wslot_free = [0, 0, 0, 0]
        wtok = [0, 0, 0, 0]
        kvbank_free = [0, 0, 0, 0]
        pt_free = self.ptb_free[4]
        vstage_cons = [0, 0]
        nblk = (T // 128) * 4
        ones_ap = self.VT().rearrange("p (b c) -> p b c", c=130)[:, :, 64:130:65]
        ins = dv.memset(ones_ap, 1.0)
        t_ones = self.inc("dv", ins)
        sidx = 0
        for p in range(4):
            pair = half * 4 + p
            streams = (("q", 0), ("k", C), ("v", 2 * C))
            sl_of = {}
            for which, cbase in streams:
                sl = sidx % 4
                sidx += 1
                sl_of[which] = sl
                col0 = cbase + pair * 128
                src = self.d_wqkv[mha][:, col0:col0 + 128].rearrange("(j p) f -> p j f", p=128)
                self.w(po, "pe", wslot_free[sl])
                dst = self.WSQ[:, sl * 1024:(sl + 1) * 1024].rearrange("p (j f) -> p j f", j=8)
                wtok[sl] = self.inc("wa%d" % sl, po.dma_start(dst, src), 16)
            for which, dstbuf, scale in (("k", self.KF(), None), ("q", self.QH(), 0.125)):
                sl = sl_of[which]
                self.w(pe, "wa%d" % sl, wtok[sl])
                for g in range(NG):
                    bank = g % 2
                    self.w(pe, "dv", kvbank_free[bank])
                    for j in range(NCH):
                        mm = pe.matmul(self.psb(bank),
                                       self.WSQ[:, sl * 1024 + j * 128: sl * 1024 + (j + 1) * 128],
                                       self.HF[:, j * T + g * 512: j * T + (g + 1) * 512],
                                       start=(j == 0), stop=(j == NCH - 1))
                    t_mm = self.inc("pe", mm)
                    if g == NG - 1:
                        wslot_free[sl] = t_mm
                    self.w(dv, "pe", t_mm)
                    dsl = dstbuf[:, p * T + g * 512: p * T + (g + 1) * 512]
                    if scale is None:
                        ins = dv.tensor_copy(dsl, self.psb(bank))
                    else:
                        ins = dv.tensor_scalar(dsl, self.psb(bank), scale, None, op0=ALU.mult)
                    kvbank_free[bank] = self.inc("dv", ins)
            sl = sl_of["v"]
            self.w(pe, "wa%d" % sl, wtok[sl])
            for g in range(NG):
                bank = 2 + g % 2
                self.w(pe, "dv", kvbank_free[bank])
                for j in range(NCH):
                    mm = pe.matmul(self.psb(bank),
                                   self.WSQ[:, sl * 1024 + j * 128: sl * 1024 + (j + 1) * 128],
                                   self.HF[:, j * T + g * 512: j * T + (g + 1) * 512],
                                   start=(j == 0), stop=(j == NCH - 1))
                t_mm = self.inc("pe", mm)
                if g == NG - 1:
                    wslot_free[sl] = t_mm
                vsl = g % 2
                self.w(dv, "pe", t_mm)
                self.w(dv, "pe", vstage_cons[vsl])
                vst = self.AST[:, vsl * 512:(vsl + 1) * 512]
                ins = dv.tensor_copy(vst, self.psb(bank))
                t_ev = self.inc("dv", ins)
                kvbank_free[bank] = t_ev
                self.w(pe, "dv", t_ev)
                self.w(pe, "dv", pt_free)
                self.w(pe, "dv", t_ones)
                ptb = self.pt_bf16(4)
                for kk in range(4):
                    mm = pe.transpose(ptb[:, kk * 128:(kk + 1) * 128], vst[:, kk * 128:(kk + 1) * 128],
                                      self.IDB[:])
                t_tr = self.inc("pe", mm)
                vstage_cons[vsl] = t_tr
                self.w(dv, "pe", t_tr)
                vtr = self.VT().rearrange("p (b c) -> p b c", c=130)
                out = vtr[:, g * 16 + p: g * 16 + 13 + p: 4, 0:129]
                out = out.rearrange("p b (h d) -> p b h d", d=64) if False else \
                    vtr[:, g * 16 + p: g * 16 + 13 + p: 4, :].rearrange(
                        "p b c -> p b c")[:, :, 0:130]
                base = (g * 16 + p) * 130
                out = bass.AP(self.VT().tensor, self.VT().offset + base,
                              [list(self.VT().ap[0]), [4 * 130, 4], [65, 2], [1, 64]])
                ins = dv.tensor_copy(out, ptb[:, 0:512].rearrange("p (k h d) -> p k h d", k=4, h=2))
                pt_free = self.inc("dv", ins)
        self.ptb_free[4] = pt_free

    def emit_qknorms(self, half):
        pe, dv, ac = self.pe, self.dv, self.ac
        T = self.T
        nck = T // 512
        t_red = 0
        t_mm = 0
        for src_i, buf, dst in ((0, self.KF(), self.KN2A), (1, self.QH(), self.QN2A)):
            for wave in range(2):
                sq = self.HPQ[:, 0: 2 * T]
                t_sq = 0
                for pw in range(2):
                    p = wave * 2 + pw
                    self.w(ac, "pe", t_mm)
                    ins = ac.activation(sq[:, pw * T:(pw + 1) * T],
                                        buf[:, p * T:(p + 1) * T], AF.Square)
                    t_sq = self.inc("ac", ins)
                self.w(pe, "ac", t_sq)
                self.w(pe, "dv", t_red)
                for pw in range(2):
                    for h in range(2):
                        r = h * 64
                        for cc in range(nck):
                            mm = pe.matmul(
                                self.PS[r:r + 1, pw * 2048 + cc * 512: pw * 2048 + (cc + 1) * 512],
                                self.ONES[r:r + 64, 0:1],
                                sq[r:r + 64, pw * T + cc * 512: pw * T + (cc + 1) * 512],
                                start=True, stop=True)
                        t_mm = self.inc("pe", mm)
                for pw in range(2):
                    p = wave * 2 + pw
                    for h in range(2):
                        r = h * 64
                        self.w(dv, "pe", t_mm)
                        dv.drain()
                        ins = dv.reduce_max(dst[r:r + 1, p:p + 1],
                                            self.PS[r:r + 1, pw * 2048: pw * 2048 + T], axis=AX)
                        t_red = self.inc("dv", ins)
        self.w(dv, "dv", t_red)
        dv.drain()
        for h in range(2):
            r = h * 64
            dv.tensor_scalar(self.NMH[r:r + 1, 0:4], self.QN2A[r:r + 1, 0:4], -4.08, None, op0=ALU.mult)
            dv.tensor_scalar(self.NMH[r:r + 1, 4:8], self.KN2A[r:r + 1, 0:4], -1.02 / 16.0,
                             None, op0=ALU.mult)
        dv.drain()
        for h in range(2):
            r = h * 64
            dv.tensor_tensor(self.NMH[r:r + 1, 0:4], self.NMH[r:r + 1, 0:4],
                             self.NMH[r:r + 1, 4:8], op=ALU.add)
        dv.drain()
        for h in range(2):
            r = h * 64
            ins = dv.tensor_copy(self.NMHB[r:r + 1, h * 4: h * 4 + 4], self.NMH[r:r + 1, 0:4])
        t_nb = self.inc("dv", ins)
        self.w(pe, "dv", t_nb)
        for h in range(2):
            r = h * 64
            mm = pe.matmul(self.PS[:, 7 * 512 + h * 4: 7 * 512 + h * 4 + 4],
                           self.ONES[r:r + 1, 0:128], self.NMHB[r:r + 1, h * 4: h * 4 + 4],
                           start=True, stop=True)
        t_mm2 = self.inc("pe", mm)
        self.w(dv, "pe", t_mm2)
        ins = dv.tensor_copy(self.BND[:, 0:8], self.PS[:, 7 * 512: 7 * 512 + 8])
        t_bc = self.inc("dv", ins)
        self.bnd_tok = t_bc
        self.kn_guard = t_bc

    def attention_half(self, mha, half):
        pe, dv, ac = self.pe, self.dv, self.ac
        T, NG = self.T, self.NG
        for qg in range(NG):
            gp = qg % 2
            for p in range(4):
                nj = 4 * qg + 4
                npair = nj // 2
                avtok = [0, 0]
                exp_of = {}

                def emit_ST(jp):
                    toks = [0, 0]
                    for jj in range(2):
                        j = jp * 2 + jj
                        for h in range(2):
                            bank = h * 2 + jj
                            self.w(pe, "ac", self.stb_free[bank])
                            if j == 0:
                                self.w(pe, "dv", self.obank_free[h * 2 + gp])
                                self.w(pe, "dv", self.kn_guard)
                            lhs = self.KF()[h * 64:(h + 1) * 64,
                                            p * T + j * 128: p * T + (j + 1) * 128]
                            rhs = self.QH()[h * 64:(h + 1) * 64,
                                            p * T + qg * 512: p * T + (qg + 1) * 512]
                            if j >= 4 * qg:
                                r = j - 4 * qg
                                pe.matmul(self.psb(bank), lhs, rhs, start=True, stop=False)
                                mm = pe.matmul(self.psb(bank), self.IDB[:],
                                               self.MK[:, r * 512:(r + 1) * 512],
                                               start=False, stop=True)
                            else:
                                mm = pe.matmul(self.psb(bank), lhs, rhs, start=True, stop=True)
                            toks[h] = self.inc("pe", mm)
                    sp = jp % 2
                    for h in range(2):
                        slot = h * 2 + sp
                        self.w(ac, "pe", toks[h])
                        self.w(ac, "pe", self.ptslot_free[slot])
                        ins = ac.activation(self.PTb(slot), self.PS[:, (h * 2) * 512:(h * 2) * 512 + 1024],
                                            AF.Exp, bias=self.BND[:, h * 4 + p: h * 4 + p + 1])
                        etok = self.inc("ac", ins)
                        self.stb_free[h * 2] = etok
                        self.stb_free[h * 2 + 1] = etok
                        exp_of[(jp, h)] = etok

                def emit_AV(jp):
                    sp = jp % 2
                    for jj in range(2):
                        j = jp * 2 + jj
                        for h in range(2):
                            slot = h * 2 + sp
                            obank = 4 + h * 2 + gp
                            self.w(pe, "ac", exp_of[(jp, h)])
                            blk = (j * 4 + p) * 130
                            mm = pe.matmul(self.PS[0:65, obank * 512: obank * 512 + 512],
                                           self.VT()[:, blk + h * 65: blk + (h + 1) * 65],
                                           self.PTb(slot)[:, jj * 512:(jj + 1) * 512],
                                           start=(j == 0), stop=(j == nj - 1))
                            avtok[h] = self.inc("pe", mm)
                            if jj == 1:
                                self.ptslot_free[slot] = avtok[h]

                for jp in range(npair):
                    emit_ST(jp)
                    if jp >= 1:
                        emit_AV(jp - 1)
                emit_AV(npair - 1)
                for h in range(2):
                    self.o_roundtrip(half, p, qg, h, gp, avtok[h])

    def o_roundtrip(self, half, p, qg, h, gp, t_av):
        pe, dv = self.pe, self.dv
        T = self.T
        obank = 4 + h * 2 + gp
        pair = half * 4 + p
        self.w(dv, "pe", t_av)
        self.w(dv, "pe", self.rt_osg_free)
        ins = dv.tensor_copy(self.OSG()[0:65, 0:512], self.PS[0:65, obank * 512: obank * 512 + 512])
        t_ev = self.inc("dv", ins)
        self.w(pe, "dv", t_ev)
        for b in range(4):
            mm = pe.transpose(self.PS[:, obank * 512 + b * 65: obank * 512 + (b + 1) * 65],
                              self.OSG()[0:65, b * 128:(b + 1) * 128], self.IDF[0:65, 0:65])
        t_tr = self.inc("pe", mm)
        self.rt_osg_free = t_tr
        self.w(dv, "pe", t_tr)
        self.w(dv, "pe", self.rt_otb_free)
        dv.tensor_copy(self.OTM()[:, 0:260], self.PS[:, obank * 512: obank * 512 + 260])
        dv.drain()
        linv_src = bass.AP(self.OTM().tensor, self.OTM().offset + 64,
                           [list(self.OTM().ap[0]), [65, 4], [1, 1]])
        dv.reciprocal(self.OL4[:, 0:4], linv_src)
        dv.drain()
        for b in range(4):
            ins = dv.tensor_scalar(self.OTB()[:, b * 64:(b + 1) * 64],
                                   self.OTM()[:, b * 65: b * 65 + 64],
                                   self.OL4[:, b:b + 1], None, op0=ALU.mult)
        t_nb = self.inc("dv", ins)
        self.w(pe, "dv", t_nb)
        ptb = self.pt_bf16(obank)
        for b in range(4):
            mm = pe.transpose(ptb[0:64, b * 128:(b + 1) * 128],
                              self.OTB()[:, b * 64:(b + 1) * 64], self.IDB[:])
        t_tb = self.inc("pe", mm)
        self.rt_otb_free = t_tb
        self.w(dv, "pe", t_tb)
        ins = dv.tensor_copy(self.OF[h * 64:(h + 1) * 64, pair * T + qg * 512: pair * T + (qg + 1) * 512],
                             ptb[0:64, 0:512])
        self.obank_free[h * 2 + gp] = self.inc("dv", ins)

    def proj_phase(self, mha, is_inner):
        pe, dv, po = self.pe, self.dv, self.po
        T, NG = self.T, self.NG
        wslot_free = [0, 0]
        wtok = [0, 0]
        pbank_free = [0, 0]
        pt_free = self.ptb_free[4]
        afm_cons = 0
        atm_free = [0, 0]
        for g in range(NG):
            evtoks = []
            for ct in range(8):
                sl = ct % 2
                src = self.d_wp[mha][:, ct * 128:(ct + 1) * 128].rearrange("(j p) f -> p j f", p=128)
                self.w(po, "pe", wslot_free[sl])
                dst = self.WSQ[:, sl * 1024:(sl + 1) * 1024].rearrange("p (j f) -> p j f", j=8)
                wtok[sl] = self.inc("wa%d" % sl, po.dma_start(dst, src), 16)
                self.w(pe, "wa%d" % sl, wtok[sl])
                bank = ct % 2
                self.w(pe, "dv", pbank_free[bank])
                if ct == 0:
                    self.w(pe, "pe", afm_cons)
                for j in range(NCH):
                    mm = pe.matmul(self.psb(bank),
                                   self.WSQ[:, sl * 1024 + j * 128: sl * 1024 + (j + 1) * 128],
                                   self.OF[:, j * T + g * 512: j * T + (g + 1) * 512],
                                   start=(j == 0), stop=(j == NCH - 1))
                t_mm = self.inc("pe", mm)
                wslot_free[sl] = t_mm
                self.w(dv, "pe", t_mm)
                self.w(dv, "pe", afm_cons)
                ins = dv.tensor_copy(self.AST[:, ct * 512:(ct + 1) * 512], self.psb(bank))
                pbank_free[bank] = self.inc("dv", ins)
                evtoks.append(pbank_free[bank])
            afm_tok = max(evtoks)
            for tt in range(4):
                tslot = tt % 2
                self.w(pe, "dv", afm_tok)
                self.w(pe, "dv", pt_free)
                self.w(pe, "dv", atm_free[tslot])
                ptb = self.pt_bf16(4)
                for ct in range(8):
                    mm = pe.transpose(ptb[:, ct * 128:(ct + 1) * 128],
                                      self.AST[:, ct * 512 + tt * 128: ct * 512 + (tt + 1) * 128],
                                      self.IDB[:])
                t_tr = self.inc("pe", mm)
                if tt == 3:
                    afm_cons = t_tr
                self.w(dv, "pe", t_tr)
                atm = self.ATM[:, tslot * 1024:(tslot + 1) * 1024]
                ins = dv.tensor_copy(atm, ptb[:])
                pt_free = self.inc("dv", ins)
                ti = g * 4 + tt
                xr = self.XR[:, ti * C:(ti + 1) * C]
                if not is_inner:
                    par = tt % 2
                    mu, rstd, t_rst = self.ln_stats(atm, par, src_dv_tok=pt_free)
                    self.w(dv, "ac", t_rst)
                    dv.tensor_scalar(atm, atm, mu, rstd, op0=ALU.subtract, op1=ALU.mult)
                    dv.drain()
                    ins = dv.tensor_tensor(xr, xr, atm, op=ALU.add)
                    t_res = self.inc("dv", ins)
                    self.ln_free[par] = t_res
                    atm_free[tslot] = t_res
                else:
                    oslot = 1 - tslot
                    self.w(dv, "pe", 0)
                    self.w(dv, "dv", 0)
                    other = self.ATM[:, oslot * 1024:(oslot + 1) * 1024]
                    self.w(dv, "dv", atm_free[oslot]) if False else None
                    dv.tensor_scalar(other, xr, self.MUCACHE[:, ti:ti + 1],
                                     self.RSTCACHE[:, ti:ti + 1], op0=ALU.subtract, op1=ALU.mult)
                    dv.drain()
                    uu = self.UU[:, ti * C:(ti + 1) * C]
                    ins = dv.tensor_tensor(uu, other, atm, op=ALU.add)
                    t_res = self.inc("dv", ins)
                    atm_free[tslot] = t_res
                    atm_free[oslot] = t_res
        self.ptb_free[4] = pt_free

    def mha_slot(self, seq, mha):
        is_inner = (mha == 1)
        self.h_build(self.XR, gain_idx=(0 if is_inner else None), cache_stats=is_inner)
        self.barrier()
        if self.ck("h%d" % mha):
            self.dump("h", self.HF[:])
            raise _DebugStop()
        for half in range(2):
            self.kv_build(mha, half)
            self.barrier()
            self.emit_qknorms(half)
            if half == 0 and self.ck("kv%d" % mha):
                self.dump("kf", self.KF()[:])
                self.dump("vt", self.VT()[:])
                raise _DebugStop()
            self.attention_half(mha, half)
            self.barrier()
        if self.ck("of%d" % mha):
            self.dump("of", self.OF[:, 0:8 * self.T])
            raise _DebugStop()
        self.proj_phase(mha, is_inner)
        self.barrier()

    def router_phase(self):
        pe, dv, po, ac = self.pe, self.dv, self.po, self.ac
        T, NG = self.T, self.NG
        lg_free = 0
        lgt_free = [0, 0]
        ps5_free = 0
        wt_tok = 0
        for g in range(NG):
            for j in range(NCH):
                mm = pe.matmul(self.PS[0:8, 7 * 512: 7 * 512 + 512],
                               self.RW[:, j * 8:(j + 1) * 8],
                               self.HF[:, j * T + g * 512: j * T + (g + 1) * 512],
                               start=(j == 0), stop=(j == NCH - 1))
            t_mm = self.inc("pe", mm)
            self.w(dv, "pe", t_mm)
            self.w(dv, "pe", lg_free)
            ins = dv.tensor_copy(self.LG[0:8, 0:512], self.PS[0:8, 7 * 512: 7 * 512 + 512])
            t_lg = self.inc("dv", ins)
            for tt in range(4):
                par = tt % 2
                self.w(pe, "dv", t_lg)
                self.w(pe, "dv", max(lgt_free[par], ps5_free))
                mm = pe.transpose(self.PS[:, 5 * 512: 5 * 512 + 8],
                                  self.LG[0:8, tt * 128:(tt + 1) * 128], self.IDF[0:8, 0:8])
                t_tr = self.inc("pe", mm)
                if tt == 3:
                    lg_free = t_tr
                self.w(dv, "pe", t_tr)
                ins = dv.tensor_copy(self.LGT[:, par * 8:(par + 1) * 8],
                                     self.PS[:, 5 * 512: 5 * 512 + 8])
                t_lgt = self.inc("dv", ins)
                ps5_free = t_lgt
                lgt = self.LGT[:, par * 8:(par + 1) * 8]
                dv.drain()
                dv.reduce_max(self.RM1N[:], lgt, axis=AX, negate=True)
                dv.drain()
                dv.tensor_scalar(self.RM1[:], self.RM1N[:], -1.0, None, op0=ALU.mult)
                dv.drain()
                dv.tensor_scalar(self.RMK1[:], lgt, self.RM1[:], None, op0=ALU.is_ge)
                dv.drain()
                dv.scalar_tensor_tensor(self.RL2[:], self.RMK1[:], -1e30, lgt,
                                        op0=ALU.mult, op1=ALU.add)
                dv.drain()
                ins = dv.reduce_max(self.RM2[:], self.RL2[:], axis=AX)
                t_m2 = self.inc("dv", ins)
                self.w(ac, "dv", t_m2)
                ins = ac.activation(self.RE2[:], self.RM2[:], AF.Exp, bias=self.RM1N[:])
                t_e2 = self.inc("ac", ins)
                self.w(dv, "ac", t_e2)
                dv.tensor_scalar(self.RDEN[:], self.RE2[:], 1.0, None, op0=ALU.add)
                dv.drain()
                dv.tensor_scalar(self.RMK2[:], self.RL2[:], self.RM2[:], None, op0=ALU.is_ge)
                dv.drain()
                ti = g * 4 + tt
                dv.scalar_tensor_tensor(self.RMK2[:], self.RMK2[:], self.RE2[:], self.RMK1[:],
                                        op0=ALU.mult, op1=ALU.add)
                dv.reciprocal(self.RW1[:], self.RDEN[:])
                dv.drain()
                ins = dv.tensor_scalar(self.WTOK[:, ti * 8:(ti + 1) * 8], self.RMK2[:],
                                       self.RW1[:], None, op0=ALU.mult)
                wt_tok = self.inc("dv", ins)
                lgt_free[par] = wt_tok
        self.wtok_tok = wt_tok

    def moe_phase(self, seq):
        pe, dv, ac, po = self.pe, self.dv, self.ac, self.po
        T = self.T
        tgW = T // 2
        self.tgW = tgW
        self.WBv = self.AST[:, 0:tgW]
        nsub = max(1, tgW // 512)
        subW = min(512, tgW)
        nh = max(1, tgW // 512)
        for tg in range(2):
            tgbase = tg * tgW
            tgtile0 = tg * (tgW // 128)
            gel_tok = [0, 0]
            scl_tok = [0] * 8
            w1_tok = [0, 0]
            w1_free = [0, 0]
            w2_free = 0
            w2_tok_cur = 0
            w2_tok_prev = 0
            scl_prev = [0] * 8
            yevac_tok = [0] * (2 * nsub)
            y_last_evac = 0
            wb_tok = 0
            hpq_guard = 0

            def h_mms(fc, oct_):
                bank0 = (fc % 2) * 2
                sl = fc // 4
                self.w(pe, "w1a" if sl == 0 else "w1b", w1_tok[sl])
                self.w(pe, "ac", gel_tok[fc % 2])
                if fc == 0:
                    self.w(pe, "dv", wb_tok)
                for j in range(NCH):
                    for hh in range(nh):
                        mm = pe.matmul(self.PS[:, (bank0 + hh) * 512:(bank0 + hh) * 512 + subW],
                                       self.W1Q(sl)[:, j * 512 + (fc % 4) * 128: j * 512 + (fc % 4) * 128 + 128],
                                       self.HF[:, j * T + tgbase + hh * subW: j * T + tgbase + (hh + 1) * subW],
                                       start=(j == 0), stop=(j == NCH - 1))
                tok = self.inc("pe", mm)
                return tok

            def gelu_scale(fc, t_h):
                bank0 = (fc % 2) * 2
                self.w(ac, "pe", t_h)
                self.w(ac, "dv", scl_prev[fc])
                self.w(ac, "pe", hpq_guard)
                ins = ac.activation(self.HPQ[:, fc * tgW:(fc + 1) * tgW],
                                    self.PS[:, bank0 * 512: bank0 * 512 + tgW], AF.Gelu)
                t_g = self.inc("ac", ins)
                gel_tok[fc % 2] = t_g
                self.w(dv, "ac", t_g)
                ins = dv.tensor_tensor(self.HPQ[:, fc * tgW:(fc + 1) * tgW],
                                       self.HPQ[:, fc * tgW:(fc + 1) * tgW],
                                       self.WBv, op=ALU.mult)
                scl_tok[fc] = self.inc("dv", ins)

            def w2_combos(first):
                nonlocal w2_free, y_last_evac
                dv.drain()
                for ci, (ch, sub) in enumerate([(a, b) for a in range(2) for b in range(nsub)]):
                    self.w(pe, "w2o", w2_tok_prev)
                    self.w(pe, "dv", y_last_evac)
                    for fc in range(8):
                        self.w(pe, "dv", scl_prev[fc])
                        for cti in range(4):
                            cto = (ch * 4 + cti) * 128
                            mm = pe.matmul(self.PS[:, (4 + cti) * 512:(4 + cti) * 512 + subW],
                                           self.W2O[:, fc * 1024 + cto: fc * 1024 + cto + 128],
                                           self.HPQ[:, fc * tgW + sub * subW: fc * tgW + (sub + 1) * subW],
                                           start=(fc == 0), stop=(fc == 7))
                    t_c = self.inc("pe", mm)
                    w2_free = t_c
                    self.w(dv, "pe", t_c)
                    ysl = self.BIGB[:, 0:4 * T].rearrange("p (c t) -> p c t", t=tgW)[
                        :, ch * 4:(ch + 1) * 4, sub * subW:(sub + 1) * subW]
                    ps_y = self.PS[:, 2048:4096].rearrange("p (c t) -> p c t", t=512)[:, :, 0:subW]
                    if first:
                        ins = dv.tensor_scalar(ysl, ps_y, 1.0, None, op0=ALU.mult)
                    else:
                        ins = dv.scalar_tensor_tensor(ysl, ps_y, 1.0, ysl, op0=ALU.mult, op1=ALU.add)
                    yevac_tok[ci] = self.inc("dv", ins)
                    y_last_evac = yevac_tok[ci]

            for oct_ in range(16):
                if oct_ % 2 == 0:
                    wb_tok = self.wb_build(tgtile0, oct_ // 2, gel_tok, scl_tok)
                for q in range(2):
                    col0 = oct_ * 1024 + q * 512
                    src = self.d_w1[:, col0:col0 + 512].rearrange("(j p) f -> p j f", p=128)
                    self.w(po, "pe", w1_free[q])
                    dst = self.W1Q(q)[:].rearrange("p (j f) -> p j f", j=8)
                    w1_tok[q] = self.inc("w1a" if q == 0 else "w1b", po.dma_start(dst, src), 16)
                t_h0 = h_mms(0, oct_)
                t_h1 = h_mms(1, oct_)
                if oct_ > 0:
                    w2_combos(first=(oct_ == 1))
                hpq_guard = w2_free
                self.w(po, "pe", w2_free)
                src = self.d_w2[oct_ * 1024:(oct_ + 1) * 1024, :].rearrange("(j p) c -> p j c", p=128)
                dst = self.W2O[:].rearrange("p (j c) -> p j c", j=8)
                w2_tok_cur = self.inc("w2o", po.dma_start(dst, src), 16)
                gelu_scale(0, t_h0)
                gelu_scale(1, t_h1)
                for fc in range(2, 8):
                    t_h = h_mms(fc, oct_)
                    gelu_scale(fc, t_h)
                w1_free = [self.cnt["pe"], self.cnt["pe"]]
                w2_tok_prev = w2_tok_cur
                scl_prev = list(scl_tok)
            w2_combos(first=False)
            self.barrier()
            if self.ck("y") and tg == 0:
                self.dump("y", self.BIGB[:, 0:4 * self.T])
                self.dump("wtok", self.WTOK[:])
                self.dump("t2d", self.HF[:])
                raise _DebugStop()
            self.finale(seq, tg, tgW)
            self.barrier()

    def wb_build(self, tgtile0, expert, gel_tok, scl_tok):
        pe, dv = self.pe, self.dv
        ntile = self.tgW // 128
        dv.drain()
        self.w(dv, "pe", self.cnt["pe"])
        for tt in range(ntile):
            ti = tgtile0 + tt
            ins = dv.tensor_scalar(self.AST[:, 2048 + tt * 128: 2048 + (tt + 1) * 128],
                                   self.ONES[:], self.WTOK[:, ti * 8 + expert: ti * 8 + expert + 1],
                                   None, op0=ALU.mult)
        t_stage = self.inc("dv", ins)
        self.w(pe, "dv", t_stage)
        self.w(pe, "ac", gel_tok[0])
        ptb = self.pt_bf16(0)
        for tt in range(ntile):
            mm = pe.transpose(ptb[:, tt * 128:(tt + 1) * 128],
                              self.AST[:, 2048 + tt * 128: 2048 + (tt + 1) * 128], self.IDB[:])
        t_tr = self.inc("pe", mm)
        self.w(dv, "pe", t_tr)
        ins = dv.tensor_copy(self.WBv, ptb[:, 0:self.tgW])
        return self.inc("dv", ins)

    def finale(self, seq, tg, tgW):
        pe, dv, sy = self.pe, self.dv, self.sy
        ntile = tgW // 128
        astf = self.AST[:].bitcast(F32)
        atmf = self.ATM[:].bitcast(F32)
        outslot = [astf[:, 1024:2048], atmf[:, 0:1024]]
        out_tok = [0, 0]
        f_cons = 0
        tr_free = 0
        for tt in range(ntile):
            ti = tg * ntile + tt
            par = tt % 2
            self.w(pe, "dv", tr_free)
            for ct in range(8):
                bank = ct // 4
                mm = pe.transpose(self.PS[:, bank * 512 + (ct % 4) * 128: bank * 512 + (ct % 4) * 128 + 128],
                                  self.BIGB[:, ct * tgW + tt * 128: ct * tgW + (tt + 1) * 128],
                                  self.IDF[:])
            t_tr = self.inc("pe", mm)
            self.w(dv, "pe", t_tr)
            self.w(dv, "dv", 0)
            ftmp = astf[:, 0:1024]
            ins = dv.scalar_tensor_tensor(ftmp, self.PS[:, 0:1024], 1.0,
                                          self.UU[:, ti * C:(ti + 1) * C], op0=ALU.mult, op1=ALU.add)
            t_f = self.inc("dv", ins)
            tr_free = t_f
            mu, rstd, t_rst = self.ln_stats(ftmp, par, src_dv_tok=t_f)
            self.w(dv, "ac", t_rst)
            self.w(dv, "qo%d" % par, out_tok[par])
            nrm = outslot[par]
            dv.tensor_scalar(nrm, ftmp, mu, rstd, op0=ALU.subtract, op1=ALU.mult)
            dv.drain()
            ins = dv.tensor_tensor(nrm, nrm, self.XR[:, ti * C:(ti + 1) * C], op=ALU.add)
            t_o = self.inc("dv", ins)
            self.ln_free[par] = t_o
            self.w(sy, "dv", t_o)
            dst = self.d_out[seq, ti * 128:(ti + 1) * 128, :]
            out_tok[par] = self.inc("qo%d" % par, sy.dma_start(dst, nrm), 16)

    def dump(self, name, ap):
        w = ap.shape[-1]
        npart = ap.shape[0]
        d = self.nc.dram_tensor("dbg_" + name, [npart, w], F32, kind="ExternalOutput").ap()
        self.barrier()
        for s in self.sem:
            if s != "qdbg" and self.cnt[s]:
                self.po.wait_ge(self.sem[s], self.cnt[s])
        self.inc("qdbg", self.po.dma_start(d, ap), 16)

    def ck(self, phase):
        if self.debug_phase is None:
            return False
        return phase == self.debug_phase

    def build(self, debug_phase=None):
        self.debug_phase = debug_phase

        def fin():
            for s in ("qo0", "qo1", "qdbg"):
                if self.cnt[s]:
                    self.po.wait_ge(self.sem[s], self.cnt[s])
                    self.sy.wait_ge(self.sem[s], self.cnt[s])
            return self.nc

        self.load_consts()
        try:
            self._main()
        except _DebugStop:
            pass
        return fin()

    def _main(self):
        for seq in range(2):
            if seq == 1:
                self.barrier()
            xt = self.load_x(seq)
            for eng in (self.dv, self.pe, self.ac):
                self.w(eng, "qx", xt)
            self.w(self.dv, "dv", self.const_tok)
            if self.ck("x"):
                self.dump("x", self.XR[:])
                raise _DebugStop()
            self.mha_slot(seq, 0)
            if self.ck("mha0"):
                self.dump("x1", self.XR[:])
                raise _DebugStop()
            self.mha_slot(seq, 1)
            if self.ck("mha1"):
                self.dump("x1", self.XR[:])
                self.dump("u", self.UU[:, 0:self.NT * C])
                raise _DebugStop()
            self.h_build(self.UU, gain_idx=1, cache_stats=False)
            self.barrier()
            if self.ck("t2"):
                self.dump("t2", self.HF[:])
                raise _DebugStop()
            self.router_phase()
            self.barrier()
            if self.ck("router"):
                self.dump("wtok", self.WTOK[:])
                raise _DebugStop()
            self.moe_phase(seq)
            if self.ck("y"):
                raise _DebugStop()


def build_program(T=2048):
    return KB(T).build()


def make_masks():
    kk = np.arange(128)[:, None]
    qq = np.arange(512)[None, :]
    return np.stack([np.where(qq >= 128 * r + kk, 0.0, -1e30) for r in range(4)]).astype(np.float32)


def kernel(L0_attn_x, L0_attn_W_attn, L0_attn_W_proj, L0_ffn_ln1_w, L0_ffn_c_attn_w,
           L0_ffn_c_proj_w, L0_ffn_ln2_w, L0_ffn_router_w, L0_ffn_w1, L0_ffn_w2):
    x = np.ascontiguousarray(np.asarray(L0_attn_x, dtype=np.float32))
    B, T, _ = x.shape
    n_cores = 8
    per = B // n_cores
    nc = build_program(T)
    consts = {
        "wqkv0": np.ascontiguousarray(L0_attn_W_attn, dtype=np.float32),
        "wqkv1": np.ascontiguousarray(L0_ffn_c_attn_w, dtype=np.float32),
        "wp0": np.ascontiguousarray(L0_attn_W_proj, dtype=np.float32),
        "wp1": np.ascontiguousarray(L0_ffn_c_proj_w, dtype=np.float32),
        "lnw": np.ascontiguousarray(
            np.stack([np.asarray(L0_ffn_ln1_w, dtype=np.float32).reshape(8, 128),
                      np.asarray(L0_ffn_ln2_w, dtype=np.float32).reshape(8, 128)])),
        "rwT": np.ascontiguousarray(np.asarray(L0_ffn_router_w, dtype=np.float32).T),
        "w1": np.ascontiguousarray(L0_ffn_w1, dtype=np.float32),
        "w2": np.ascontiguousarray(L0_ffn_w2, dtype=np.float32),
        "ident": np.eye(128, dtype=np.float32),
        "masks": make_masks(),
        "ones": np.ones((128, 128), dtype=np.float32),
    }
    in_maps = []
    for c in range(n_cores):
        m = dict(consts)
        m["x"] = np.ascontiguousarray(x[c * per:(c + 1) * per])
        in_maps.append(m)
    res = run_bass_kernel_spmd(nc, in_maps, core_ids=list(range(n_cores)))
    return np.concatenate([np.asarray(r["out"], dtype=np.float32) for r in res.results], axis=0)


# version 09
# speedup vs baseline: 1.1846x; 1.0512x over previous
import numpy as np
import concourse.bass as bass
import concourse.mybir as mybir
from concourse.bass_utils import run_bass_kernel_spmd
from contextlib import ExitStack

F32 = mybir.dt.float32
BF16 = mybir.dt.bfloat16
AF = mybir.ActivationFunctionType
ALU = mybir.AluOpType
AX = mybir.AxisListType.X

C = 1024
NCH = 8
EPS = 1e-5
FF = 16384


class _DebugStop(Exception):
    pass


class KB:
    def __init__(self, T):
        self.T = T
        self.NT = T // 128
        self.NG = T // 512
        nc = bass.Bass("TRN2", target_bir_lowering=False)
        self.nc = nc
        self.pe, self.dv, self.ac, self.po, self.sy = nc.tensor, nc.vector, nc.scalar, nc.gpsimd, nc.sync
        self.st = ExitStack()
        self.cnt = {}
        self.sem = {}
        NT = self.NT

        dram = lambda n, shp, k: nc.dram_tensor(n, shp, F32, kind=k).ap()
        self.d_x = dram("x", [2, T, C], "ExternalInput")
        self.d_wqkv = [dram("wqkv0", [C, 3 * C], "ExternalInput"),
                       dram("wqkv1", [C, 3 * C], "ExternalInput")]
        self.d_wp = [dram("wp0", [C, C], "ExternalInput"),
                     dram("wp1", [C, C], "ExternalInput")]
        self.d_ln = dram("lnw", [2, 8, 128], "ExternalInput")
        self.d_rwT = dram("rwT", [8, C], "ExternalInput")
        self.d_w1 = dram("w1", [C, FF], "ExternalInput")
        self.d_w2 = dram("w2", [FF, C], "ExternalInput")
        self.d_id = dram("ident", [128, 128], "ExternalInput")
        self.d_mk = dram("masks", [4, 128, 512], "ExternalInput")
        self.d_ones = dram("ones", [128, 128], "ExternalInput")
        self.d_out = dram("out", [2, T, C], "ExternalOutput")

        sb = lambda n, shp, dt: self.st.enter_context(nc.sbuf_tensor(n, shp, dt))
        self.XR = sb("XR", [128, NT * C], BF16)
        self.HF = sb("HF", [128, NCH * T], BF16)
        self.BIGB = sb("BIGB", [128, 4 * T], F32)
        self.KVW = sb("KVW", [128, max(4 * T + (T // 128) * 4 * 130, 16384)], BF16)
        self.AR1 = sb("AR1", [128, max(8 * T, 4 * T + 8192)], BF16)
        self.AST = sb("AST", [128, 4096], BF16)
        self.HPQ = sb("HPQ", [128, 8192], BF16)
        self.ATM = sb("ATM", [128, 2048], BF16)
        self.SQS = sb("SQS", [128, C], BF16)
        self.MK = sb("MK", [128, 4 * 512], BF16)
        self.IDB = sb("IDB", [128, 128], BF16)
        self.IDF = sb("IDF", [128, 128], F32)
        self.ONES = sb("ONESB", [128, 128], BF16)
        self.LNW = sb("LNW", [128, 16], F32)
        self.RW = sb("RW", [128, 64], BF16)
        self.WTOK = sb("WTOK", [128, NT * 8], F32)
        self.WSQ = sb("WSQ", [128, 4 * 1024], BF16)
        self.LGT = sb("LGT", [128, 16], F32)
        self.S1 = sb("S1", [128, 2], F32)
        self.S2 = sb("S2", [128, 2], F32)
        self.MUc = sb("MUc", [128, 2], F32)
        self.VPc = sb("VPc", [128, 2], F32)
        self.TVA = sb("TVA", [128, 2], F32)
        self.RSTc = sb("RSTc", [128, 2], F32)
        self.MUCACHE = sb("MUCACHE", [128, NT], F32)
        self.RSTCACHE = sb("RSTCACHE", [128, NT], F32)
        self.NM = sb("NM", [128, 8], F32)
        self.KN2A = sb("KN2A", [128, 8], F32)
        self.QN2A = sb("QN2A", [128, 8], F32)
        self.OL4 = sb("OL4", [128, 4], F32)
        self.BND = sb("BND", [128, 8], F32)
        self.NKND = sb("NKND", [128, 8], F32)
        self.QN2 = sb("QN2", [128, 8], F32)
        self.NMH = sb("NMH", [128, 8], F32)
        self.NMHB = sb("NMHB", [128, 8], BF16)
        self.LL = sb("LL", [128, 8], F32)
        self.NGM = sb("NGM", [128, 4], F32)
        self.BB = sb("BB", [128, 8], F32)
        self.BL = sb("BL", [128, 8], F32)
        self.LT = sb("LT", [128, 4], F32)
        self.LIV = sb("LIV", [128, 4], F32)
        self.RM1 = sb("RM1", [128, 1], F32)
        self.RM1N = sb("RM1N", [128, 1], F32)
        self.RM2 = sb("RM2", [128, 1], F32)
        self.RE2 = sb("RE2", [128, 1], F32)
        self.RDEN = sb("RDEN", [128, 1], F32)
        self.RW1 = sb("RW1", [128, 1], F32)
        self.RW2 = sb("RW2", [128, 1], F32)
        self.RMK1 = sb("RMK1", [128, 8], F32)
        self.RL2 = sb("RL2", [128, 8], F32)
        self.RMK2 = sb("RMK2", [128, 8], F32)

        self.PS = self.st.enter_context(nc.psum_tensor("PS", [128, 4096], F32))

        for s in ["pe", "dv", "ac", "po", "qld", "qx", "qdbg", "qo0", "qo1",
                  "wa0", "wa1", "wa2", "wa3", "wq", "w1a", "w1b", "w2o"]:
            self.sem[s] = self.st.enter_context(nc.semaphore("sem_" + s))
            self.cnt[s] = 0

        self.ln_free = [0, 0]
        self.bank_free = [0, 0, 0, 0]
        self.slot_free = [0, 0]
        self.nm_free = [0, 0]
        self.o_evac = [0, 0]
        self.ptb_free = {0: 0, 4: 0, 7: 0}
        self.pts_free = [0, 0]
        self.liv_free = [0, 0, 0, 0]
        self.ll_free = [0, 0]
        self.stb_free = [0, 0, 0, 0]
        self.ptslot_free = [0, 0, 0, 0]
        self.obank_free = [0, 0, 0, 0]
        self.rt_osg_free = 0
        self.rt_otb_free = 0
        self.rt_queue = []
        self.itc = 0

    def inc(self, s, instr, n=1):
        instr.then_inc(self.sem[s], n)
        self.cnt[s] += n
        return self.cnt[s]

    def w(self, eng, s, tok):
        if tok and tok > 0:
            eng.wait_ge(self.sem[s], tok)

    def barrier(self):
        eng_of = {"pe": self.pe, "dv": self.dv, "ac": self.ac, "po": self.po}
        self.dv.drain()
        self.ac.drain()
        self.po.drain()
        for a in eng_of:
            for b in eng_of:
                if a != b and self.cnt[b] > 0:
                    eng_of[a].wait_ge(self.sem[b], self.cnt[b])
            for s in ("qo0", "qo1"):
                if self.cnt[s] > 0:
                    eng_of[a].wait_ge(self.sem[s], self.cnt[s])

    def psb(self, bank, w=512):
        return self.PS[:, bank * 512: bank * 512 + w]

    def pt_bf16(self, bank):
        return self.PS[:, bank * 512:(bank + 1) * 512].bitcast(BF16)

    @property
    def L8(self):
        return self.HPQ[:].bitcast(F32)

    @property
    def LG(self):
        return self.HPQ[:].bitcast(F32)

    @property
    def UU(self):
        return self.AR1
    def PPslot(self, slot):
        return self.AR1[:, slot * 2 * self.T:(slot + 1) * 2 * self.T]
    def QH(self):
        return self.AR1[:, 0: 4 * self.T]
    def PTb(self, slot):
        o = 4 * self.T + slot * 1024
        return self.AR1[:, o: o + 1024]
    def OSG(self):
        o4 = (4 * self.T + 4096) // 2
        return self.AR1[:].bitcast(F32)[:, o4: o4 + 512]
    def OTM(self):
        o4 = (4 * self.T + 4096) // 2 + 512
        return self.AR1[:].bitcast(F32)[:, o4: o4 + 272]
    def OTB(self):
        o = 4 * self.T + 4096 + 1024 + 544
        return self.AR1[:, o: o + 256]

    def KF(self):
        return self.KVW[:, 0: 4 * self.T]
    def VT(self):
        return self.KVW[:, 4 * self.T: 4 * self.T + (self.T // 128) * 4 * 130]
    def W1Q(self, slot):
        return self.KVW[:, slot * 4096:(slot + 1) * 4096]
    @property
    def W2O(self):
        return self.KVW[:, 8192: 16384]

    @property
    def OF(self):
        return self.BIGB[:].bitcast(BF16)

    def load_consts(self):
        po, pe, dv = self.po, self.pe, self.dv
        self.inc("qld", po.dma_start(self.IDB[:], self.d_id), 16)
        self.inc("qld", po.dma_start(self.IDF[:], self.d_id), 16)
        self.inc("qld", po.dma_start(self.ONES[:], self.d_ones), 16)
        mk = self.d_mk.rearrange("r p k -> p r k")
        self.inc("qld", po.dma_start(self.MK[:].rearrange("p (r k) -> p r k", r=4), mk), 16)
        self.inc("qld", po.dma_start(self.L8[0:8, 0:128], self.d_ln[0]), 16)
        self.inc("qld", po.dma_start(self.L8[0:8, 128:256], self.d_ln[1]), 16)
        t = self.inc("qld", po.dma_start(self.L8[0:8, 256:1280], self.d_rwT), 16)
        self.w(pe, "qld", t)
        pe.transpose(self.PS[:, 0:8], self.L8[0:8, 0:128], self.IDF[0:8, 0:8])
        pe.transpose(self.PS[:, 8:16], self.L8[0:8, 128:256], self.IDF[0:8, 0:8])
        for j in range(8):
            mm = pe.transpose(self.PS[:, 512 + j * 8: 512 + (j + 1) * 8],
                              self.L8[0:8, 256 + j * 128: 256 + (j + 1) * 128],
                              self.IDF[0:8, 0:8])
        ptok = self.inc("pe", mm)
        self.w(dv, "pe", ptok)
        dv.tensor_copy(self.LNW[:, 0:16], self.PS[:, 0:16])
        ins = dv.tensor_copy(self.RW[:], self.PS[:, 512: 512 + 64])
        self.const_tok = self.inc("dv", ins)

    def load_x(self, seq):
        src = self.d_x[seq].rearrange("(n p) c -> p n c", p=128)
        dst = self.XR[:].rearrange("p (n c) -> p n c", c=C)
        return self.inc("qx", self.po.dma_start(dst, src), 16)

    def ln_stats(self, src_ap, par, src_dv_tok=0):
        dv, ac, po = self.dv, self.ac, self.po
        c1 = 1.0 / C
        if src_dv_tok:
            dv.drain()
        t_dv = self.inc("dv", dv.reduce_sum(self.S1[:, par:par + 1], src_ap, axis=AX))
        self.w(ac, "dv", max(self.ln_free[par], src_dv_tok))
        t_ac = self.inc("ac", ac.activation(self.SQS[:], src_ap, AF.Square,
                                            accum_out=self.S2[:, par:par + 1]))
        self.w(dv, "ac", t_ac)
        dv.drain()
        dv.tensor_scalar(self.MUc[:, par:par + 1], self.S1[:, par:par + 1], c1, None, op0=ALU.mult)
        dv.drain()
        dv.tensor_tensor(self.TVA[:, par:par + 1], self.MUc[:, par:par + 1],
                         self.MUc[:, par:par + 1], op=ALU.mult)
        dv.drain()
        dv.tensor_scalar(self.TVA[:, par:par + 1], self.TVA[:, par:par + 1], EPS, None,
                         op0=ALU.subtract)
        dv.tensor_scalar(self.VPc[:, par:par + 1], self.S2[:, par:par + 1], c1, None, op0=ALU.mult)
        dv.drain()
        ins = dv.tensor_tensor(self.VPc[:, par:par + 1], self.VPc[:, par:par + 1],
                               self.TVA[:, par:par + 1], op=ALU.subtract)
        t_po = self.inc("dv", ins)
        self.w(ac, "dv", t_po)
        ac.activation(self.RSTc[:, par:par + 1], self.VPc[:, par:par + 1], AF.Ln)
        ac.drain()
        ins = ac.activation(self.RSTc[:, par:par + 1], self.RSTc[:, par:par + 1], AF.Exp, scale=-0.5)
        t_rst = self.inc("ac", ins)
        return (self.MUc[:, par:par + 1], self.RSTc[:, par:par + 1], t_rst)

    def h_build(self, src_buf, gain_idx=None, cache_stats=False):
        nc, pe, dv = self.nc, self.pe, self.dv
        T = self.T
        nx_cons = [0, 0]
        pt_free = self.ptb_free[4]
        for i in range(self.NT):
            par = i % 2
            src = src_buf[:, i * C:(i + 1) * C]
            mu, rstd, t_rst = self.ln_stats(src, par)
            self.w(dv, "ac", t_rst)
            self.w(dv, "pe", nx_cons[par])
            nx = self.AST[:, par * C:(par + 1) * C]
            ins = dv.tensor_scalar(nx, src, mu, rstd, op0=ALU.subtract, op1=ALU.mult)
            t_nx = self.inc("dv", ins)
            self.ln_free[par] = t_nx
            if cache_stats:
                dv.tensor_copy(self.MUCACHE[:, i:i + 1], mu)
                ins = dv.tensor_copy(self.RSTCACHE[:, i:i + 1], rstd)
                t_nx = self.inc("dv", ins)
                self.ln_free[par] = t_nx
            self.w(pe, "dv", t_nx)
            self.w(pe, "dv", pt_free)
            ptb = self.pt_bf16(4)
            for j in range(NCH):
                mm = pe.transpose(ptb[:, j * 128:(j + 1) * 128], nx[:, j * 128:(j + 1) * 128], self.IDB[:])
            t_tr = self.inc("pe", mm)
            nx_cons[par] = t_tr
            self.w(dv, "pe", t_tr)
            if gain_idx is None:
                out = self.HF[:].rearrange("p (j t) -> p j t", j=NCH)[:, :, i * 128:(i + 1) * 128]
                ins = dv.tensor_copy(out, ptb[:].rearrange("p (j q) -> p j q", j=NCH))
            else:
                for j in range(NCH):
                    ins = dv.tensor_scalar(self.HF[:, j * T + i * 128: j * T + (i + 1) * 128],
                                           ptb[:, j * 128:(j + 1) * 128],
                                           self.LNW[:, gain_idx * 8 + j: gain_idx * 8 + j + 1],
                                           None, op0=ALU.mult)
            pt_free = self.inc("dv", ins)
        self.ptb_free[4] = pt_free

    def kv_build(self, mha, half):
        pe, dv, po = self.pe, self.dv, self.po
        T, NG = self.T, self.NG
        wslot_free = [0, 0, 0, 0]
        wtok = [0, 0, 0, 0]
        kvbank_free = [0, 0, 0, 0]
        pt_free = self.ptb_free[4]
        vstage_cons = [0, 0]
        nblk = (T // 128) * 4
        ones_ap = self.VT().rearrange("p (b c) -> p b c", c=130)[:, :, 64:130:65]
        ins = dv.memset(ones_ap, 1.0)
        t_ones = self.inc("dv", ins)
        sidx = 0
        for p in range(4):
            pair = half * 4 + p
            streams = (("q", 0), ("k", C), ("v", 2 * C))
            sl_of = {}
            for which, cbase in streams:
                sl = sidx % 4
                sidx += 1
                sl_of[which] = sl
                col0 = cbase + pair * 128
                src = self.d_wqkv[mha][:, col0:col0 + 128].rearrange("(j p) f -> p j f", p=128)
                self.w(po, "pe", wslot_free[sl])
                dst = self.WSQ[:, sl * 1024:(sl + 1) * 1024].rearrange("p (j f) -> p j f", j=8)
                wtok[sl] = self.inc("wa%d" % sl, po.dma_start(dst, src), 16)
            for which, dstbuf, scale in (("k", self.KF(), None), ("q", self.QH(), 0.125)):
                sl = sl_of[which]
                self.w(pe, "wa%d" % sl, wtok[sl])
                for g in range(NG):
                    bank = g % 2
                    self.w(pe, "dv", kvbank_free[bank])
                    for j in range(NCH):
                        mm = pe.matmul(self.psb(bank),
                                       self.WSQ[:, sl * 1024 + j * 128: sl * 1024 + (j + 1) * 128],
                                       self.HF[:, j * T + g * 512: j * T + (g + 1) * 512],
                                       start=(j == 0), stop=(j == NCH - 1))
                    t_mm = self.inc("pe", mm)
                    if g == NG - 1:
                        wslot_free[sl] = t_mm
                    self.w(dv, "pe", t_mm)
                    dsl = dstbuf[:, p * T + g * 512: p * T + (g + 1) * 512]
                    if scale is None:
                        ins = dv.tensor_copy(dsl, self.psb(bank))
                    else:
                        ins = dv.tensor_scalar(dsl, self.psb(bank), scale, None, op0=ALU.mult)
                    kvbank_free[bank] = self.inc("dv", ins)
            sl = sl_of["v"]
            self.w(pe, "wa%d" % sl, wtok[sl])
            for g in range(NG):
                bank = 2 + g % 2
                self.w(pe, "dv", kvbank_free[bank])
                for j in range(NCH):
                    mm = pe.matmul(self.psb(bank),
                                   self.WSQ[:, sl * 1024 + j * 128: sl * 1024 + (j + 1) * 128],
                                   self.HF[:, j * T + g * 512: j * T + (g + 1) * 512],
                                   start=(j == 0), stop=(j == NCH - 1))
                t_mm = self.inc("pe", mm)
                if g == NG - 1:
                    wslot_free[sl] = t_mm
                vsl = g % 2
                self.w(dv, "pe", t_mm)
                self.w(dv, "pe", vstage_cons[vsl])
                vst = self.AST[:, vsl * 512:(vsl + 1) * 512]
                ins = dv.tensor_copy(vst, self.psb(bank))
                t_ev = self.inc("dv", ins)
                kvbank_free[bank] = t_ev
                self.w(pe, "dv", t_ev)
                self.w(pe, "dv", pt_free)
                self.w(pe, "dv", t_ones)
                ptb = self.pt_bf16(4)
                for kk in range(4):
                    mm = pe.transpose(ptb[:, kk * 128:(kk + 1) * 128], vst[:, kk * 128:(kk + 1) * 128],
                                      self.IDB[:])
                t_tr = self.inc("pe", mm)
                vstage_cons[vsl] = t_tr
                self.w(dv, "pe", t_tr)
                vtr = self.VT().rearrange("p (b c) -> p b c", c=130)
                out = vtr[:, g * 16 + p: g * 16 + 13 + p: 4, 0:129]
                out = out.rearrange("p b (h d) -> p b h d", d=64) if False else \
                    vtr[:, g * 16 + p: g * 16 + 13 + p: 4, :].rearrange(
                        "p b c -> p b c")[:, :, 0:130]
                base = (g * 16 + p) * 130
                out = bass.AP(self.VT().tensor, self.VT().offset + base,
                              [list(self.VT().ap[0]), [4 * 130, 4], [65, 2], [1, 64]])
                ins = dv.tensor_copy(out, ptb[:, 0:512].rearrange("p (k h d) -> p k h d", k=4, h=2))
                pt_free = self.inc("dv", ins)
        self.ptb_free[4] = pt_free

    def emit_qknorms(self, half):
        pe, dv, ac = self.pe, self.dv, self.ac
        T = self.T
        nck = T // 512
        t_red = 0
        t_mm = 0
        for src_i, buf, dst in ((0, self.KF(), self.KN2A), (1, self.QH(), self.QN2A)):
            for wave in range(2):
                sq = self.HPQ[:, 0: 2 * T]
                t_sq = 0
                for pw in range(2):
                    p = wave * 2 + pw
                    self.w(ac, "pe", t_mm)
                    ins = ac.activation(sq[:, pw * T:(pw + 1) * T],
                                        buf[:, p * T:(p + 1) * T], AF.Square)
                    t_sq = self.inc("ac", ins)
                self.w(pe, "ac", t_sq)
                self.w(pe, "dv", t_red)
                for pw in range(2):
                    for h in range(2):
                        r = h * 64
                        for cc in range(nck):
                            mm = pe.matmul(
                                self.PS[r:r + 1, pw * 2048 + cc * 512: pw * 2048 + (cc + 1) * 512],
                                self.ONES[r:r + 64, 0:1],
                                sq[r:r + 64, pw * T + cc * 512: pw * T + (cc + 1) * 512],
                                start=True, stop=True)
                        t_mm = self.inc("pe", mm)
                for pw in range(2):
                    p = wave * 2 + pw
                    for h in range(2):
                        r = h * 64
                        self.w(dv, "pe", t_mm)
                        dv.drain()
                        ins = dv.reduce_max(dst[r:r + 1, p:p + 1],
                                            self.PS[r:r + 1, pw * 2048: pw * 2048 + T], axis=AX)
                        t_red = self.inc("dv", ins)
        self.w(dv, "dv", t_red)
        dv.drain()
        for h in range(2):
            r = h * 64
            dv.tensor_scalar(self.NMH[r:r + 1, 0:4], self.QN2A[r:r + 1, 0:4], -4.08, None, op0=ALU.mult)
            dv.tensor_scalar(self.NMH[r:r + 1, 4:8], self.KN2A[r:r + 1, 0:4], -1.02 / 16.0,
                             None, op0=ALU.mult)
        dv.drain()
        for h in range(2):
            r = h * 64
            dv.tensor_tensor(self.NMH[r:r + 1, 0:4], self.NMH[r:r + 1, 0:4],
                             self.NMH[r:r + 1, 4:8], op=ALU.add)
        dv.drain()
        for h in range(2):
            r = h * 64
            ins = dv.tensor_copy(self.NMHB[r:r + 1, h * 4: h * 4 + 4], self.NMH[r:r + 1, 0:4])
        t_nb = self.inc("dv", ins)
        self.w(pe, "dv", t_nb)
        for h in range(2):
            r = h * 64
            mm = pe.matmul(self.PS[:, 7 * 512 + h * 4: 7 * 512 + h * 4 + 4],
                           self.ONES[r:r + 1, 0:128], self.NMHB[r:r + 1, h * 4: h * 4 + 4],
                           start=True, stop=True)
        t_mm2 = self.inc("pe", mm)
        self.w(dv, "pe", t_mm2)
        ins = dv.tensor_copy(self.BND[:, 0:8], self.PS[:, 7 * 512: 7 * 512 + 8])
        t_bc = self.inc("dv", ins)
        self.bnd_tok = t_bc
        self.kn_guard = t_bc

    def attention_half(self, mha, half):
        pe, dv, ac = self.pe, self.dv, self.ac
        T, NG = self.T, self.NG
        for qg in range(NG):
            for p in range(4):
                gp = p % 2
                nj = 4 * qg + 4
                npair = nj // 2
                avtok = [0, 0]
                exp_of = {}

                def emit_ST(jp):
                    toks = [0, 0]
                    for jj in range(2):
                        j = jp * 2 + jj
                        for h in range(2):
                            bank = h * 2 + jj
                            self.w(pe, "ac", self.stb_free[bank])
                            if j == 0:
                                self.w(pe, "dv", self.obank_free[h * 2 + gp])
                                self.w(pe, "dv", self.kn_guard)
                            lhs = self.KF()[h * 64:(h + 1) * 64,
                                            p * T + j * 128: p * T + (j + 1) * 128]
                            rhs = self.QH()[h * 64:(h + 1) * 64,
                                            p * T + qg * 512: p * T + (qg + 1) * 512]
                            if j >= 4 * qg:
                                r = j - 4 * qg
                                pe.matmul(self.psb(bank), lhs, rhs, start=True, stop=False)
                                mm = pe.matmul(self.psb(bank), self.IDB[:],
                                               self.MK[:, r * 512:(r + 1) * 512],
                                               start=False, stop=True)
                            else:
                                mm = pe.matmul(self.psb(bank), lhs, rhs, start=True, stop=True)
                            toks[h] = self.inc("pe", mm)
                    sp = jp % 2
                    for h in range(2):
                        slot = h * 2 + sp
                        self.w(ac, "pe", toks[h])
                        self.w(ac, "pe", self.ptslot_free[slot])
                        ins = ac.activation(self.PTb(slot), self.PS[:, (h * 2) * 512:(h * 2) * 512 + 1024],
                                            AF.Exp, bias=self.BND[:, h * 4 + p: h * 4 + p + 1])
                        etok = self.inc("ac", ins)
                        self.stb_free[h * 2] = etok
                        self.stb_free[h * 2 + 1] = etok
                        exp_of[(jp, h)] = etok

                def emit_AV(jp):
                    sp = jp % 2
                    for jj in range(2):
                        j = jp * 2 + jj
                        for h in range(2):
                            slot = h * 2 + sp
                            obank = 4 + h * 2 + gp
                            self.w(pe, "ac", exp_of[(jp, h)])
                            blk = (j * 4 + p) * 130
                            mm = pe.matmul(self.PS[0:65, obank * 512: obank * 512 + 512],
                                           self.VT()[:, blk + h * 65: blk + (h + 1) * 65],
                                           self.PTb(slot)[:, jj * 512:(jj + 1) * 512],
                                           start=(j == 0), stop=(j == nj - 1))
                            avtok[h] = self.inc("pe", mm)
                            if jj == 1:
                                self.ptslot_free[slot] = avtok[h]

                for jp in range(npair):
                    emit_ST(jp)
                    self.step_rt()
                    if jp >= 1:
                        emit_AV(jp - 1)
                        self.step_rt()
                emit_AV(npair - 1)
                self.flush_rt()
                for h in range(2):
                    self.queue_roundtrip(half, p, qg, h, gp, avtok[h])
        self.flush_rt()

    def queue_roundtrip(self, half, p, qg, h, gp, t_av):
        pe, dv = self.pe, self.dv
        T = self.T
        obank = 4 + h * 2 + gp
        pair = half * 4 + p
        st = {}

        def s0():
            self.w(dv, "pe", t_av)
            self.w(dv, "pe", self.rt_osg_free)
            ins = dv.tensor_copy(self.OSG()[0:65, 0:512],
                                 self.PS[0:65, obank * 512: obank * 512 + 512])
            st["ev"] = self.inc("dv", ins)

        def s1():
            self.w(pe, "dv", st["ev"])
            for b in range(4):
                mm = pe.transpose(self.PS[:, obank * 512 + b * 65: obank * 512 + (b + 1) * 65],
                                  self.OSG()[0:65, b * 128:(b + 1) * 128], self.IDF[0:65, 0:65])
            st["tr"] = self.inc("pe", mm)
            self.rt_osg_free = st["tr"]

        def s2():
            self.w(dv, "pe", st["tr"])
            self.w(dv, "pe", self.rt_otb_free)
            dv.tensor_copy(self.OTM()[:, 0:260], self.PS[:, obank * 512: obank * 512 + 260])
            dv.drain()
            linv_src = bass.AP(self.OTM().tensor, self.OTM().offset + 64,
                               [list(self.OTM().ap[0]), [65, 4], [1, 1]])
            dv.reciprocal(self.OL4[:, 0:4], linv_src)
            dv.drain()
            for b in range(4):
                ins = dv.tensor_scalar(self.OTB()[:, b * 64:(b + 1) * 64],
                                       self.OTM()[:, b * 65: b * 65 + 64],
                                       self.OL4[:, b:b + 1], None, op0=ALU.mult)
            st["nb"] = self.inc("dv", ins)

        def s3():
            self.w(pe, "dv", st["nb"])
            ptb = self.pt_bf16(obank)
            for b in range(4):
                mm = pe.transpose(ptb[0:64, b * 128:(b + 1) * 128],
                                  self.OTB()[:, b * 64:(b + 1) * 64], self.IDB[:])
            st["tb"] = self.inc("pe", mm)
            self.rt_otb_free = st["tb"]

        def s4():
            self.w(dv, "pe", st["tb"])
            ins = dv.tensor_copy(
                self.OF[h * 64:(h + 1) * 64, pair * T + qg * 512: pair * T + (qg + 1) * 512],
                ptb_dummy[0] if False else self.pt_bf16(obank)[0:64, 0:512])
            self.obank_free[h * 2 + gp] = self.inc("dv", ins)

        self.rt_queue.extend([s0, s1, s2, s3, s4])

    def step_rt(self):
        if self.rt_queue:
            self.rt_queue.pop(0)()

    def flush_rt(self):
        while self.rt_queue:
            self.rt_queue.pop(0)()

    def proj_phase(self, mha, is_inner):
        pe, dv, po = self.pe, self.dv, self.po
        T, NG = self.T, self.NG
        wslot_free = [0, 0]
        wtok = [0, 0]
        pbank_free = [0, 0]
        pt_free = self.ptb_free[4]
        afm_cons = 0
        atm_free = [0, 0]
        for g in range(NG):
            evtoks = []
            for ct in range(8):
                sl = ct % 2
                src = self.d_wp[mha][:, ct * 128:(ct + 1) * 128].rearrange("(j p) f -> p j f", p=128)
                self.w(po, "pe", wslot_free[sl])
                dst = self.WSQ[:, sl * 1024:(sl + 1) * 1024].rearrange("p (j f) -> p j f", j=8)
                wtok[sl] = self.inc("wa%d" % sl, po.dma_start(dst, src), 16)
                self.w(pe, "wa%d" % sl, wtok[sl])
                bank = ct % 2
                self.w(pe, "dv", pbank_free[bank])
                if ct == 0:
                    self.w(pe, "pe", afm_cons)
                for j in range(NCH):
                    mm = pe.matmul(self.psb(bank),
                                   self.WSQ[:, sl * 1024 + j * 128: sl * 1024 + (j + 1) * 128],
                                   self.OF[:, j * T + g * 512: j * T + (g + 1) * 512],
                                   start=(j == 0), stop=(j == NCH - 1))
                t_mm = self.inc("pe", mm)
                wslot_free[sl] = t_mm
                self.w(dv, "pe", t_mm)
                self.w(dv, "pe", afm_cons)
                ins = dv.tensor_copy(self.AST[:, ct * 512:(ct + 1) * 512], self.psb(bank))
                pbank_free[bank] = self.inc("dv", ins)
                evtoks.append(pbank_free[bank])
            afm_tok = max(evtoks)
            for tt in range(4):
                tslot = tt % 2
                self.w(pe, "dv", afm_tok)
                self.w(pe, "dv", pt_free)
                self.w(pe, "dv", atm_free[tslot])
                ptb = self.pt_bf16(4)
                for ct in range(8):
                    mm = pe.transpose(ptb[:, ct * 128:(ct + 1) * 128],
                                      self.AST[:, ct * 512 + tt * 128: ct * 512 + (tt + 1) * 128],
                                      self.IDB[:])
                t_tr = self.inc("pe", mm)
                if tt == 3:
                    afm_cons = t_tr
                self.w(dv, "pe", t_tr)
                atm = self.ATM[:, tslot * 1024:(tslot + 1) * 1024]
                ins = dv.tensor_copy(atm, ptb[:])
                pt_free = self.inc("dv", ins)
                ti = g * 4 + tt
                xr = self.XR[:, ti * C:(ti + 1) * C]
                if not is_inner:
                    par = tt % 2
                    mu, rstd, t_rst = self.ln_stats(atm, par, src_dv_tok=pt_free)
                    self.w(dv, "ac", t_rst)
                    dv.tensor_scalar(atm, atm, mu, rstd, op0=ALU.subtract, op1=ALU.mult)
                    dv.drain()
                    ins = dv.tensor_tensor(xr, xr, atm, op=ALU.add)
                    t_res = self.inc("dv", ins)
                    self.ln_free[par] = t_res
                    atm_free[tslot] = t_res
                else:
                    oslot = 1 - tslot
                    self.w(dv, "pe", 0)
                    self.w(dv, "dv", 0)
                    other = self.ATM[:, oslot * 1024:(oslot + 1) * 1024]
                    self.w(dv, "dv", atm_free[oslot]) if False else None
                    dv.tensor_scalar(other, xr, self.MUCACHE[:, ti:ti + 1],
                                     self.RSTCACHE[:, ti:ti + 1], op0=ALU.subtract, op1=ALU.mult)
                    dv.drain()
                    uu = self.UU[:, ti * C:(ti + 1) * C]
                    ins = dv.tensor_tensor(uu, other, atm, op=ALU.add)
                    t_res = self.inc("dv", ins)
                    atm_free[tslot] = t_res
                    atm_free[oslot] = t_res
        self.ptb_free[4] = pt_free

    def mha_slot(self, seq, mha):
        is_inner = (mha == 1)
        self.h_build(self.XR, gain_idx=(0 if is_inner else None), cache_stats=is_inner)
        self.barrier()
        if self.ck("h%d" % mha):
            self.dump("h", self.HF[:])
            raise _DebugStop()
        for half in range(2):
            self.kv_build(mha, half)
            self.barrier()
            self.emit_qknorms(half)
            if half == 0 and self.ck("kv%d" % mha):
                self.dump("kf", self.KF()[:])
                self.dump("vt", self.VT()[:])
                raise _DebugStop()
            self.attention_half(mha, half)
            self.barrier()
        if self.ck("of%d" % mha):
            self.dump("of", self.OF[:, 0:8 * self.T])
            raise _DebugStop()
        self.proj_phase(mha, is_inner)
        self.barrier()

    def router_phase(self):
        pe, dv, po, ac = self.pe, self.dv, self.po, self.ac
        T, NG = self.T, self.NG
        lg_free = 0
        lgt_free = [0, 0]
        ps5_free = 0
        wt_tok = 0
        for g in range(NG):
            for j in range(NCH):
                mm = pe.matmul(self.PS[0:8, 7 * 512: 7 * 512 + 512],
                               self.RW[:, j * 8:(j + 1) * 8],
                               self.HF[:, j * T + g * 512: j * T + (g + 1) * 512],
                               start=(j == 0), stop=(j == NCH - 1))
            t_mm = self.inc("pe", mm)
            self.w(dv, "pe", t_mm)
            self.w(dv, "pe", lg_free)
            ins = dv.tensor_copy(self.LG[0:8, 0:512], self.PS[0:8, 7 * 512: 7 * 512 + 512])
            t_lg = self.inc("dv", ins)
            for tt in range(4):
                par = tt % 2
                self.w(pe, "dv", t_lg)
                self.w(pe, "dv", max(lgt_free[par], ps5_free))
                mm = pe.transpose(self.PS[:, 5 * 512: 5 * 512 + 8],
                                  self.LG[0:8, tt * 128:(tt + 1) * 128], self.IDF[0:8, 0:8])
                t_tr = self.inc("pe", mm)
                if tt == 3:
                    lg_free = t_tr
                self.w(dv, "pe", t_tr)
                ins = dv.tensor_copy(self.LGT[:, par * 8:(par + 1) * 8],
                                     self.PS[:, 5 * 512: 5 * 512 + 8])
                t_lgt = self.inc("dv", ins)
                ps5_free = t_lgt
                lgt = self.LGT[:, par * 8:(par + 1) * 8]
                dv.drain()
                dv.reduce_max(self.RM1N[:], lgt, axis=AX, negate=True)
                dv.drain()
                dv.tensor_scalar(self.RM1[:], self.RM1N[:], -1.0, None, op0=ALU.mult)
                dv.drain()
                dv.tensor_scalar(self.RMK1[:], lgt, self.RM1[:], None, op0=ALU.is_ge)
                dv.drain()
                dv.scalar_tensor_tensor(self.RL2[:], self.RMK1[:], -1e30, lgt,
                                        op0=ALU.mult, op1=ALU.add)
                dv.drain()
                ins = dv.reduce_max(self.RM2[:], self.RL2[:], axis=AX)
                t_m2 = self.inc("dv", ins)
                self.w(ac, "dv", t_m2)
                ins = ac.activation(self.RE2[:], self.RM2[:], AF.Exp, bias=self.RM1N[:])
                t_e2 = self.inc("ac", ins)
                self.w(dv, "ac", t_e2)
                dv.tensor_scalar(self.RDEN[:], self.RE2[:], 1.0, None, op0=ALU.add)
                dv.drain()
                dv.tensor_scalar(self.RMK2[:], self.RL2[:], self.RM2[:], None, op0=ALU.is_ge)
                dv.drain()
                ti = g * 4 + tt
                dv.scalar_tensor_tensor(self.RMK2[:], self.RMK2[:], self.RE2[:], self.RMK1[:],
                                        op0=ALU.mult, op1=ALU.add)
                dv.reciprocal(self.RW1[:], self.RDEN[:])
                dv.drain()
                ins = dv.tensor_scalar(self.WTOK[:, ti * 8:(ti + 1) * 8], self.RMK2[:],
                                       self.RW1[:], None, op0=ALU.mult)
                wt_tok = self.inc("dv", ins)
                lgt_free[par] = wt_tok
        self.wtok_tok = wt_tok

    def moe_phase(self, seq):
        pe, dv, ac, po = self.pe, self.dv, self.ac, self.po
        T = self.T
        tgW = T // 2
        self.tgW = tgW
        self.WBv = self.AST[:, 0:tgW]
        nsub = max(1, tgW // 512)
        subW = min(512, tgW)
        nh = max(1, tgW // 512)
        for tg in range(2):
            tgbase = tg * tgW
            tgtile0 = tg * (tgW // 128)
            gel_tok = [0, 0]
            scl_tok = [0] * 8
            w1_tok = [0, 0]
            w1_free = [0, 0]
            w2_free = 0
            w2_tok_cur = 0
            w2_tok_prev = 0
            scl_prev = [0] * 8
            yevac_tok = [0] * (2 * nsub)
            y_last_evac = 0
            wb_tok = 0
            hpq_guard = 0

            def h_mms(fc, oct_):
                bank0 = (fc % 2) * 2
                sl = fc // 4
                self.w(pe, "w1a" if sl == 0 else "w1b", w1_tok[sl])
                self.w(pe, "ac", gel_tok[fc % 2])
                if fc == 0:
                    self.w(pe, "dv", wb_tok)
                for j in range(NCH):
                    for hh in range(nh):
                        mm = pe.matmul(self.PS[:, (bank0 + hh) * 512:(bank0 + hh) * 512 + subW],
                                       self.W1Q(sl)[:, j * 512 + (fc % 4) * 128: j * 512 + (fc % 4) * 128 + 128],
                                       self.HF[:, j * T + tgbase + hh * subW: j * T + tgbase + (hh + 1) * subW],
                                       start=(j == 0), stop=(j == NCH - 1))
                tok = self.inc("pe", mm)
                return tok

            def gelu_scale(fc, t_h):
                bank0 = (fc % 2) * 2
                self.w(ac, "pe", t_h)
                self.w(ac, "dv", scl_prev[fc])
                self.w(ac, "pe", hpq_guard)
                ins = ac.activation(self.HPQ[:, fc * tgW:(fc + 1) * tgW],
                                    self.PS[:, bank0 * 512: bank0 * 512 + tgW], AF.Gelu)
                t_g = self.inc("ac", ins)
                gel_tok[fc % 2] = t_g
                self.w(dv, "ac", t_g)
                ins = dv.tensor_tensor(self.HPQ[:, fc * tgW:(fc + 1) * tgW],
                                       self.HPQ[:, fc * tgW:(fc + 1) * tgW],
                                       self.WBv, op=ALU.mult)
                scl_tok[fc] = self.inc("dv", ins)

            def w2_combos(first):
                nonlocal w2_free, y_last_evac
                dv.drain()
                for ci, (ch, sub) in enumerate([(a, b) for a in range(2) for b in range(nsub)]):
                    self.w(pe, "w2o", w2_tok_prev)
                    self.w(pe, "dv", y_last_evac)
                    for fc in range(8):
                        self.w(pe, "dv", scl_prev[fc])
                        for cti in range(4):
                            cto = (ch * 4 + cti) * 128
                            mm = pe.matmul(self.PS[:, (4 + cti) * 512:(4 + cti) * 512 + subW],
                                           self.W2O[:, fc * 1024 + cto: fc * 1024 + cto + 128],
                                           self.HPQ[:, fc * tgW + sub * subW: fc * tgW + (sub + 1) * subW],
                                           start=(fc == 0), stop=(fc == 7))
                    t_c = self.inc("pe", mm)
                    w2_free = t_c
                    self.w(dv, "pe", t_c)
                    ysl = self.BIGB[:, 0:4 * T].rearrange("p (c t) -> p c t", t=tgW)[
                        :, ch * 4:(ch + 1) * 4, sub * subW:(sub + 1) * subW]
                    ps_y = self.PS[:, 2048:4096].rearrange("p (c t) -> p c t", t=512)[:, :, 0:subW]
                    if first:
                        ins = dv.tensor_scalar(ysl, ps_y, 1.0, None, op0=ALU.mult)
                    else:
                        ins = dv.scalar_tensor_tensor(ysl, ps_y, 1.0, ysl, op0=ALU.mult, op1=ALU.add)
                    yevac_tok[ci] = self.inc("dv", ins)
                    y_last_evac = yevac_tok[ci]

            for oct_ in range(16):
                if oct_ % 2 == 0:
                    wb_tok = self.wb_build(tgtile0, oct_ // 2, gel_tok, scl_tok)
                for q in range(2):
                    col0 = oct_ * 1024 + q * 512
                    src = self.d_w1[:, col0:col0 + 512].rearrange("(j p) f -> p j f", p=128)
                    self.w(po, "pe", w1_free[q])
                    dst = self.W1Q(q)[:].rearrange("p (j f) -> p j f", j=8)
                    w1_tok[q] = self.inc("w1a" if q == 0 else "w1b", po.dma_start(dst, src), 16)
                t_h0 = h_mms(0, oct_)
                t_h1 = h_mms(1, oct_)
                if oct_ > 0:
                    w2_combos(first=(oct_ == 1))
                hpq_guard = w2_free
                self.w(po, "pe", w2_free)
                src = self.d_w2[oct_ * 1024:(oct_ + 1) * 1024, :].rearrange("(j p) c -> p j c", p=128)
                dst = self.W2O[:].rearrange("p (j c) -> p j c", j=8)
                w2_tok_cur = self.inc("w2o", po.dma_start(dst, src), 16)
                gelu_scale(0, t_h0)
                gelu_scale(1, t_h1)
                for fc in range(2, 8):
                    t_h = h_mms(fc, oct_)
                    gelu_scale(fc, t_h)
                w1_free = [self.cnt["pe"], self.cnt["pe"]]
                w2_tok_prev = w2_tok_cur
                scl_prev = list(scl_tok)
            w2_combos(first=False)
            self.barrier()
            if self.ck("y") and tg == 0:
                self.dump("y", self.BIGB[:, 0:4 * self.T])
                self.dump("wtok", self.WTOK[:])
                self.dump("t2d", self.HF[:])
                raise _DebugStop()
            self.finale(seq, tg, tgW)
            self.barrier()

    def wb_build(self, tgtile0, expert, gel_tok, scl_tok):
        pe, dv = self.pe, self.dv
        ntile = self.tgW // 128
        dv.drain()
        self.w(dv, "pe", self.cnt["pe"])
        for tt in range(ntile):
            ti = tgtile0 + tt
            ins = dv.tensor_scalar(self.AST[:, 2048 + tt * 128: 2048 + (tt + 1) * 128],
                                   self.ONES[:], self.WTOK[:, ti * 8 + expert: ti * 8 + expert + 1],
                                   None, op0=ALU.mult)
        t_stage = self.inc("dv", ins)
        self.w(pe, "dv", t_stage)
        self.w(pe, "ac", gel_tok[0])
        ptb = self.pt_bf16(0)
        for tt in range(ntile):
            mm = pe.transpose(ptb[:, tt * 128:(tt + 1) * 128],
                              self.AST[:, 2048 + tt * 128: 2048 + (tt + 1) * 128], self.IDB[:])
        t_tr = self.inc("pe", mm)
        self.w(dv, "pe", t_tr)
        ins = dv.tensor_copy(self.WBv, ptb[:, 0:self.tgW])
        return self.inc("dv", ins)

    def finale(self, seq, tg, tgW):
        pe, dv, sy = self.pe, self.dv, self.sy
        ntile = tgW // 128
        astf = self.AST[:].bitcast(F32)
        atmf = self.ATM[:].bitcast(F32)
        outslot = [astf[:, 1024:2048], atmf[:, 0:1024]]
        out_tok = [0, 0]
        f_cons = 0
        tr_free = 0
        for tt in range(ntile):
            ti = tg * ntile + tt
            par = tt % 2
            self.w(pe, "dv", tr_free)
            for ct in range(8):
                bank = ct // 4
                mm = pe.transpose(self.PS[:, bank * 512 + (ct % 4) * 128: bank * 512 + (ct % 4) * 128 + 128],
                                  self.BIGB[:, ct * tgW + tt * 128: ct * tgW + (tt + 1) * 128],
                                  self.IDF[:])
            t_tr = self.inc("pe", mm)
            self.w(dv, "pe", t_tr)
            self.w(dv, "dv", 0)
            ftmp = astf[:, 0:1024]
            ins = dv.scalar_tensor_tensor(ftmp, self.PS[:, 0:1024], 1.0,
                                          self.UU[:, ti * C:(ti + 1) * C], op0=ALU.mult, op1=ALU.add)
            t_f = self.inc("dv", ins)
            tr_free = t_f
            mu, rstd, t_rst = self.ln_stats(ftmp, par, src_dv_tok=t_f)
            self.w(dv, "ac", t_rst)
            self.w(dv, "qo%d" % par, out_tok[par])
            nrm = outslot[par]
            dv.tensor_scalar(nrm, ftmp, mu, rstd, op0=ALU.subtract, op1=ALU.mult)
            dv.drain()
            ins = dv.tensor_tensor(nrm, nrm, self.XR[:, ti * C:(ti + 1) * C], op=ALU.add)
            t_o = self.inc("dv", ins)
            self.ln_free[par] = t_o
            self.w(sy, "dv", t_o)
            dst = self.d_out[seq, ti * 128:(ti + 1) * 128, :]
            out_tok[par] = self.inc("qo%d" % par, sy.dma_start(dst, nrm), 16)

    def dump(self, name, ap):
        w = ap.shape[-1]
        npart = ap.shape[0]
        d = self.nc.dram_tensor("dbg_" + name, [npart, w], F32, kind="ExternalOutput").ap()
        self.barrier()
        for s in self.sem:
            if s != "qdbg" and self.cnt[s]:
                self.po.wait_ge(self.sem[s], self.cnt[s])
        self.inc("qdbg", self.po.dma_start(d, ap), 16)

    def ck(self, phase):
        if self.debug_phase is None:
            return False
        return phase == self.debug_phase

    def build(self, debug_phase=None):
        self.debug_phase = debug_phase

        def fin():
            for s in ("qo0", "qo1", "qdbg"):
                if self.cnt[s]:
                    self.po.wait_ge(self.sem[s], self.cnt[s])
                    self.sy.wait_ge(self.sem[s], self.cnt[s])
            return self.nc

        self.load_consts()
        try:
            self._main()
        except _DebugStop:
            pass
        return fin()

    def _main(self):
        for seq in range(2):
            if seq == 1:
                self.barrier()
            xt = self.load_x(seq)
            for eng in (self.dv, self.pe, self.ac):
                self.w(eng, "qx", xt)
            self.w(self.dv, "dv", self.const_tok)
            if self.ck("x"):
                self.dump("x", self.XR[:])
                raise _DebugStop()
            self.mha_slot(seq, 0)
            if self.ck("mha0"):
                self.dump("x1", self.XR[:])
                raise _DebugStop()
            self.mha_slot(seq, 1)
            if self.ck("mha1"):
                self.dump("x1", self.XR[:])
                self.dump("u", self.UU[:, 0:self.NT * C])
                raise _DebugStop()
            self.h_build(self.UU, gain_idx=1, cache_stats=False)
            self.barrier()
            if self.ck("t2"):
                self.dump("t2", self.HF[:])
                raise _DebugStop()
            self.router_phase()
            self.barrier()
            if self.ck("router"):
                self.dump("wtok", self.WTOK[:])
                raise _DebugStop()
            self.moe_phase(seq)
            if self.ck("y"):
                raise _DebugStop()


def build_program(T=2048):
    return KB(T).build()


def make_masks():
    kk = np.arange(128)[:, None]
    qq = np.arange(512)[None, :]
    return np.stack([np.where(qq >= 128 * r + kk, 0.0, -1e30) for r in range(4)]).astype(np.float32)


def kernel(L0_attn_x, L0_attn_W_attn, L0_attn_W_proj, L0_ffn_ln1_w, L0_ffn_c_attn_w,
           L0_ffn_c_proj_w, L0_ffn_ln2_w, L0_ffn_router_w, L0_ffn_w1, L0_ffn_w2):
    x = np.ascontiguousarray(np.asarray(L0_attn_x, dtype=np.float32))
    B, T, _ = x.shape
    n_cores = 8
    per = B // n_cores
    nc = build_program(T)
    consts = {
        "wqkv0": np.ascontiguousarray(L0_attn_W_attn, dtype=np.float32),
        "wqkv1": np.ascontiguousarray(L0_ffn_c_attn_w, dtype=np.float32),
        "wp0": np.ascontiguousarray(L0_attn_W_proj, dtype=np.float32),
        "wp1": np.ascontiguousarray(L0_ffn_c_proj_w, dtype=np.float32),
        "lnw": np.ascontiguousarray(
            np.stack([np.asarray(L0_ffn_ln1_w, dtype=np.float32).reshape(8, 128),
                      np.asarray(L0_ffn_ln2_w, dtype=np.float32).reshape(8, 128)])),
        "rwT": np.ascontiguousarray(np.asarray(L0_ffn_router_w, dtype=np.float32).T),
        "w1": np.ascontiguousarray(L0_ffn_w1, dtype=np.float32),
        "w2": np.ascontiguousarray(L0_ffn_w2, dtype=np.float32),
        "ident": np.eye(128, dtype=np.float32),
        "masks": make_masks(),
        "ones": np.ones((128, 128), dtype=np.float32),
    }
    in_maps = []
    for c in range(n_cores):
        m = dict(consts)
        m["x"] = np.ascontiguousarray(x[c * per:(c + 1) * per])
        in_maps.append(m)
    res = run_bass_kernel_spmd(nc, in_maps, core_ids=list(range(n_cores)))
    return np.concatenate([np.asarray(r["out"], dtype=np.float32) for r in res.results], axis=0)


# version 10
# speedup vs baseline: 1.2108x; 1.0222x over previous
import numpy as np
import concourse.bass as bass
import concourse.mybir as mybir
from concourse.bass_utils import run_bass_kernel_spmd
from contextlib import ExitStack

F32 = mybir.dt.float32
BF16 = mybir.dt.bfloat16
AF = mybir.ActivationFunctionType
ALU = mybir.AluOpType
AX = mybir.AxisListType.X

C = 1024
NCH = 8
EPS = 1e-5
FF = 16384


class _DebugStop(Exception):
    pass


class KB:
    def __init__(self, T):
        self.T = T
        self.NT = T // 128
        self.NG = T // 512
        nc = bass.Bass("TRN2", target_bir_lowering=False)
        self.nc = nc
        self.pe, self.dv, self.ac, self.po, self.sy = nc.tensor, nc.vector, nc.scalar, nc.gpsimd, nc.sync
        self.st = ExitStack()
        self.cnt = {}
        self.sem = {}
        NT = self.NT

        dram = lambda n, shp, k: nc.dram_tensor(n, shp, F32, kind=k).ap()
        self.d_x = dram("x", [2, T, C], "ExternalInput")
        self.d_wqkv = [dram("wqkv0", [C, 3 * C], "ExternalInput"),
                       dram("wqkv1", [C, 3 * C], "ExternalInput")]
        self.d_wp = [dram("wp0", [C, C], "ExternalInput"),
                     dram("wp1", [C, C], "ExternalInput")]
        self.d_ln = dram("lnw", [2, 8, 128], "ExternalInput")
        self.d_rwT = dram("rwT", [8, C], "ExternalInput")
        self.d_w1 = dram("w1", [C, FF], "ExternalInput")
        self.d_w2 = dram("w2", [FF, C], "ExternalInput")
        self.d_id = dram("ident", [128, 128], "ExternalInput")
        self.d_mk = dram("masks", [4, 128, 512], "ExternalInput")
        self.d_ones = dram("ones", [128, 128], "ExternalInput")
        self.d_out = dram("out", [2, T, C], "ExternalOutput")
        self.d_w1c = nc.dram_tensor("w1c", [C, FF], BF16, kind="Internal").ap()
        self.d_w2c = nc.dram_tensor("w2c", [FF, C], BF16, kind="Internal").ap()

        sb = lambda n, shp, dt: self.st.enter_context(nc.sbuf_tensor(n, shp, dt))
        self.XR = sb("XR", [128, NT * C], BF16)
        self.HF = sb("HF", [128, NCH * T], BF16)
        self.BIGB = sb("BIGB", [128, 4 * T], F32)
        self.KVW = sb("KVW", [128, max(4 * T + (T // 128) * 4 * 130, 16384)], BF16)
        self.AR1 = sb("AR1", [128, max(8 * T, 4 * T + 8192)], BF16)
        self.AST = sb("AST", [128, 4096], BF16)
        self.HPQ = sb("HPQ", [128, 8192], BF16)
        self.ATM = sb("ATM", [128, 2048], BF16)
        self.SQS = sb("SQS", [128, C], BF16)
        self.MK = sb("MK", [128, 4 * 512], BF16)
        self.IDB = sb("IDB", [128, 128], BF16)
        self.IDF = sb("IDF", [128, 128], F32)
        self.ONES = sb("ONESB", [128, 128], BF16)
        self.LNW = sb("LNW", [128, 16], F32)
        self.RW = sb("RW", [128, 64], BF16)
        self.WTOK = sb("WTOK", [128, NT * 8], F32)
        self.WSQ = sb("WSQ", [128, 4 * 1024], BF16)
        self.LGT = sb("LGT", [128, 16], F32)
        self.S1 = sb("S1", [128, 2], F32)
        self.S2 = sb("S2", [128, 2], F32)
        self.MUc = sb("MUc", [128, 2], F32)
        self.VPc = sb("VPc", [128, 2], F32)
        self.TVA = sb("TVA", [128, 2], F32)
        self.RSTc = sb("RSTc", [128, 2], F32)
        self.MUCACHE = sb("MUCACHE", [128, NT], F32)
        self.RSTCACHE = sb("RSTCACHE", [128, NT], F32)
        self.NM = sb("NM", [128, 8], F32)
        self.KN2A = sb("KN2A", [128, 8], F32)
        self.QN2A = sb("QN2A", [128, 8], F32)
        self.OL4 = sb("OL4", [128, 4], F32)
        self.BND = sb("BND", [128, 8], F32)
        self.NKND = sb("NKND", [128, 8], F32)
        self.QN2 = sb("QN2", [128, 8], F32)
        self.NMH = sb("NMH", [128, 8], F32)
        self.NMHB = sb("NMHB", [128, 8], BF16)
        self.LL = sb("LL", [128, 8], F32)
        self.NGM = sb("NGM", [128, 4], F32)
        self.BB = sb("BB", [128, 8], F32)
        self.BL = sb("BL", [128, 8], F32)
        self.LT = sb("LT", [128, 4], F32)
        self.LIV = sb("LIV", [128, 4], F32)
        self.RM1 = sb("RM1", [128, 1], F32)
        self.RM1N = sb("RM1N", [128, 1], F32)
        self.RM2 = sb("RM2", [128, 1], F32)
        self.RE2 = sb("RE2", [128, 1], F32)
        self.RDEN = sb("RDEN", [128, 1], F32)
        self.RW1 = sb("RW1", [128, 1], F32)
        self.RW2 = sb("RW2", [128, 1], F32)
        self.RMK1 = sb("RMK1", [128, 8], F32)
        self.RL2 = sb("RL2", [128, 8], F32)
        self.RMK2 = sb("RMK2", [128, 8], F32)

        self.PS = self.st.enter_context(nc.psum_tensor("PS", [128, 4096], F32))

        for s in ["pe", "dv", "ac", "po", "qld", "qx", "qdbg", "qo0", "qo1",
                  "wa0", "wa1", "wa2", "wa3", "wq", "w1a", "w1b", "w2o", "wc"]:
            self.sem[s] = self.st.enter_context(nc.semaphore("sem_" + s))
            self.cnt[s] = 0

        self.ln_free = [0, 0]
        self.bank_free = [0, 0, 0, 0]
        self.slot_free = [0, 0]
        self.nm_free = [0, 0]
        self.o_evac = [0, 0]
        self.ptb_free = {0: 0, 4: 0, 7: 0}
        self.pts_free = [0, 0]
        self.liv_free = [0, 0, 0, 0]
        self.ll_free = [0, 0]
        self.stb_free = [0, 0, 0, 0]
        self.ptslot_free = [0, 0, 0, 0]
        self.obank_free = [0, 0, 0, 0]
        self.rt_osg_free = 0
        self.rt_otb_free = 0
        self.rt_queue = []
        self.itc = 0

    def inc(self, s, instr, n=1):
        instr.then_inc(self.sem[s], n)
        self.cnt[s] += n
        return self.cnt[s]

    def w(self, eng, s, tok):
        if tok and tok > 0:
            eng.wait_ge(self.sem[s], tok)

    def barrier(self):
        eng_of = {"pe": self.pe, "dv": self.dv, "ac": self.ac, "po": self.po}
        self.dv.drain()
        self.ac.drain()
        self.po.drain()
        for a in eng_of:
            for b in eng_of:
                if a != b and self.cnt[b] > 0:
                    eng_of[a].wait_ge(self.sem[b], self.cnt[b])
            for s in ("qo0", "qo1"):
                if self.cnt[s] > 0:
                    eng_of[a].wait_ge(self.sem[s], self.cnt[s])

    def psb(self, bank, w=512):
        return self.PS[:, bank * 512: bank * 512 + w]

    def pt_bf16(self, bank):
        return self.PS[:, bank * 512:(bank + 1) * 512].bitcast(BF16)

    @property
    def L8(self):
        return self.HPQ[:].bitcast(F32)

    @property
    def LG(self):
        return self.HPQ[:].bitcast(F32)

    @property
    def UU(self):
        return self.AR1
    def PPslot(self, slot):
        return self.AR1[:, slot * 2 * self.T:(slot + 1) * 2 * self.T]
    def QH(self):
        return self.AR1[:, 0: 4 * self.T]
    def PTb(self, slot):
        o = 4 * self.T + slot * 1024
        return self.AR1[:, o: o + 1024]
    def OSG(self):
        o4 = (4 * self.T + 4096) // 2
        return self.AR1[:].bitcast(F32)[:, o4: o4 + 512]
    def OTM(self):
        o4 = (4 * self.T + 4096) // 2 + 512
        return self.AR1[:].bitcast(F32)[:, o4: o4 + 272]
    def OTB(self):
        o = 4 * self.T + 4096 + 1024 + 544
        return self.AR1[:, o: o + 256]

    def KF(self):
        return self.KVW[:, 0: 4 * self.T]
    def VT(self):
        return self.KVW[:, 4 * self.T: 4 * self.T + (self.T // 128) * 4 * 130]
    def W1Q(self, slot):
        return self.KVW[:, slot * 4096:(slot + 1) * 4096]
    @property
    def W2O(self):
        return self.KVW[:, 8192: 16384]

    @property
    def OF(self):
        return self.BIGB[:].bitcast(BF16)

    def load_consts(self):
        po, pe, dv = self.po, self.pe, self.dv
        self.inc("qld", po.dma_start(self.IDB[:], self.d_id), 16)
        self.inc("qld", po.dma_start(self.IDF[:], self.d_id), 16)
        self.inc("qld", po.dma_start(self.ONES[:], self.d_ones), 16)
        mk = self.d_mk.rearrange("r p k -> p r k")
        self.inc("qld", po.dma_start(self.MK[:].rearrange("p (r k) -> p r k", r=4), mk), 16)
        self.inc("qld", po.dma_start(self.L8[0:8, 0:128], self.d_ln[0]), 16)
        self.inc("qld", po.dma_start(self.L8[0:8, 128:256], self.d_ln[1]), 16)
        t = self.inc("qld", po.dma_start(self.L8[0:8, 256:1280], self.d_rwT), 16)
        self.w(pe, "qld", t)
        pe.transpose(self.PS[:, 0:8], self.L8[0:8, 0:128], self.IDF[0:8, 0:8])
        pe.transpose(self.PS[:, 8:16], self.L8[0:8, 128:256], self.IDF[0:8, 0:8])
        for j in range(8):
            mm = pe.transpose(self.PS[:, 512 + j * 8: 512 + (j + 1) * 8],
                              self.L8[0:8, 256 + j * 128: 256 + (j + 1) * 128],
                              self.IDF[0:8, 0:8])
        ptok = self.inc("pe", mm)
        self.w(dv, "pe", ptok)
        dv.tensor_copy(self.LNW[:, 0:16], self.PS[:, 0:16])
        ins = dv.tensor_copy(self.RW[:], self.PS[:, 512: 512 + 64])
        self.const_tok = self.inc("dv", ins)

    def prep_wcache(self):
        po = self.po
        t = 0
        for j in range(0, C, 128):
            t = self.inc("wc", po.dma_start(self.d_w1c[j:j + 128, :], self.d_w1[j:j + 128, :]), 16)
        for j in range(0, FF, 2048):
            t = self.inc("wc", po.dma_start(self.d_w2c[j:j + 2048, :], self.d_w2[j:j + 2048, :]), 16)
        self.wc_tok = t

    def load_x(self, seq):
        src = self.d_x[seq].rearrange("(n p) c -> p n c", p=128)
        dst = self.XR[:].rearrange("p (n c) -> p n c", c=C)
        return self.inc("qx", self.po.dma_start(dst, src), 16)

    def ln_stats(self, src_ap, par, src_dv_tok=0):
        dv, ac, po = self.dv, self.ac, self.po
        c1 = 1.0 / C
        if src_dv_tok:
            dv.drain()
        t_dv = self.inc("dv", dv.reduce_sum(self.S1[:, par:par + 1], src_ap, axis=AX))
        self.w(ac, "dv", max(self.ln_free[par], src_dv_tok))
        t_ac = self.inc("ac", ac.activation(self.SQS[:], src_ap, AF.Square,
                                            accum_out=self.S2[:, par:par + 1]))
        self.w(dv, "ac", t_ac)
        dv.drain()
        dv.tensor_scalar(self.MUc[:, par:par + 1], self.S1[:, par:par + 1], c1, None, op0=ALU.mult)
        dv.drain()
        dv.tensor_tensor(self.TVA[:, par:par + 1], self.MUc[:, par:par + 1],
                         self.MUc[:, par:par + 1], op=ALU.mult)
        dv.drain()
        dv.tensor_scalar(self.TVA[:, par:par + 1], self.TVA[:, par:par + 1], EPS, None,
                         op0=ALU.subtract)
        dv.tensor_scalar(self.VPc[:, par:par + 1], self.S2[:, par:par + 1], c1, None, op0=ALU.mult)
        dv.drain()
        ins = dv.tensor_tensor(self.VPc[:, par:par + 1], self.VPc[:, par:par + 1],
                               self.TVA[:, par:par + 1], op=ALU.subtract)
        t_po = self.inc("dv", ins)
        self.w(ac, "dv", t_po)
        ac.activation(self.RSTc[:, par:par + 1], self.VPc[:, par:par + 1], AF.Ln)
        ac.drain()
        ins = ac.activation(self.RSTc[:, par:par + 1], self.RSTc[:, par:par + 1], AF.Exp, scale=-0.5)
        t_rst = self.inc("ac", ins)
        return (self.MUc[:, par:par + 1], self.RSTc[:, par:par + 1], t_rst)

    def h_build(self, src_buf, gain_idx=None, cache_stats=False):
        nc, pe, dv = self.nc, self.pe, self.dv
        T = self.T
        nx_cons = [0, 0]
        pt_free = self.ptb_free[4]
        for i in range(self.NT):
            par = i % 2
            src = src_buf[:, i * C:(i + 1) * C]
            mu, rstd, t_rst = self.ln_stats(src, par)
            self.w(dv, "ac", t_rst)
            self.w(dv, "pe", nx_cons[par])
            nx = self.AST[:, par * C:(par + 1) * C]
            ins = dv.tensor_scalar(nx, src, mu, rstd, op0=ALU.subtract, op1=ALU.mult)
            t_nx = self.inc("dv", ins)
            self.ln_free[par] = t_nx
            if cache_stats:
                dv.tensor_copy(self.MUCACHE[:, i:i + 1], mu)
                ins = dv.tensor_copy(self.RSTCACHE[:, i:i + 1], rstd)
                t_nx = self.inc("dv", ins)
                self.ln_free[par] = t_nx
            self.w(pe, "dv", t_nx)
            self.w(pe, "dv", pt_free)
            ptb = self.pt_bf16(4)
            for j in range(NCH):
                mm = pe.transpose(ptb[:, j * 128:(j + 1) * 128], nx[:, j * 128:(j + 1) * 128], self.IDB[:])
            t_tr = self.inc("pe", mm)
            nx_cons[par] = t_tr
            self.w(dv, "pe", t_tr)
            if gain_idx is None:
                out = self.HF[:].rearrange("p (j t) -> p j t", j=NCH)[:, :, i * 128:(i + 1) * 128]
                ins = dv.tensor_copy(out, ptb[:].rearrange("p (j q) -> p j q", j=NCH))
            else:
                for j in range(NCH):
                    ins = dv.tensor_scalar(self.HF[:, j * T + i * 128: j * T + (i + 1) * 128],
                                           ptb[:, j * 128:(j + 1) * 128],
                                           self.LNW[:, gain_idx * 8 + j: gain_idx * 8 + j + 1],
                                           None, op0=ALU.mult)
            pt_free = self.inc("dv", ins)
        self.ptb_free[4] = pt_free

    def kv_build(self, mha, half):
        pe, dv, po = self.pe, self.dv, self.po
        T, NG = self.T, self.NG
        wslot_free = [0, 0, 0, 0]
        wtok = [0, 0, 0, 0]
        kvbank_free = [0, 0, 0, 0]
        pt_free = self.ptb_free[4]
        vstage_cons = [0, 0]
        nblk = (T // 128) * 4
        ones_ap = self.VT().rearrange("p (b c) -> p b c", c=130)[:, :, 64:130:65]
        ins = dv.memset(ones_ap, 1.0)
        t_ones = self.inc("dv", ins)
        sidx = 0
        for p in range(4):
            pair = half * 4 + p
            streams = (("q", 0), ("k", C), ("v", 2 * C))
            sl_of = {}
            for which, cbase in streams:
                sl = sidx % 4
                sidx += 1
                sl_of[which] = sl
                col0 = cbase + pair * 128
                src = self.d_wqkv[mha][:, col0:col0 + 128].rearrange("(j p) f -> p j f", p=128)
                self.w(po, "pe", wslot_free[sl])
                dst = self.WSQ[:, sl * 1024:(sl + 1) * 1024].rearrange("p (j f) -> p j f", j=8)
                wtok[sl] = self.inc("wa%d" % sl, po.dma_start(dst, src), 16)
            for which, dstbuf, scale in (("k", self.KF(), None), ("q", self.QH(), 0.125)):
                sl = sl_of[which]
                self.w(pe, "wa%d" % sl, wtok[sl])
                for g in range(NG):
                    bank = g % 2
                    self.w(pe, "dv", kvbank_free[bank])
                    for j in range(NCH):
                        mm = pe.matmul(self.psb(bank),
                                       self.WSQ[:, sl * 1024 + j * 128: sl * 1024 + (j + 1) * 128],
                                       self.HF[:, j * T + g * 512: j * T + (g + 1) * 512],
                                       start=(j == 0), stop=(j == NCH - 1))
                    t_mm = self.inc("pe", mm)
                    if g == NG - 1:
                        wslot_free[sl] = t_mm
                    self.w(dv, "pe", t_mm)
                    dsl = dstbuf[:, p * T + g * 512: p * T + (g + 1) * 512]
                    if scale is None:
                        ins = dv.tensor_copy(dsl, self.psb(bank))
                    else:
                        ins = dv.tensor_scalar(dsl, self.psb(bank), scale, None, op0=ALU.mult)
                    kvbank_free[bank] = self.inc("dv", ins)
            sl = sl_of["v"]
            self.w(pe, "wa%d" % sl, wtok[sl])
            for g in range(NG):
                bank = 2 + g % 2
                self.w(pe, "dv", kvbank_free[bank])
                for j in range(NCH):
                    mm = pe.matmul(self.psb(bank),
                                   self.WSQ[:, sl * 1024 + j * 128: sl * 1024 + (j + 1) * 128],
                                   self.HF[:, j * T + g * 512: j * T + (g + 1) * 512],
                                   start=(j == 0), stop=(j == NCH - 1))
                t_mm = self.inc("pe", mm)
                if g == NG - 1:
                    wslot_free[sl] = t_mm
                vsl = g % 2
                self.w(dv, "pe", t_mm)
                self.w(dv, "pe", vstage_cons[vsl])
                vst = self.AST[:, vsl * 512:(vsl + 1) * 512]
                ins = dv.tensor_copy(vst, self.psb(bank))
                t_ev = self.inc("dv", ins)
                kvbank_free[bank] = t_ev
                self.w(pe, "dv", t_ev)
                self.w(pe, "dv", pt_free)
                self.w(pe, "dv", t_ones)
                ptb = self.pt_bf16(4)
                for kk in range(4):
                    mm = pe.transpose(ptb[:, kk * 128:(kk + 1) * 128], vst[:, kk * 128:(kk + 1) * 128],
                                      self.IDB[:])
                t_tr = self.inc("pe", mm)
                vstage_cons[vsl] = t_tr
                self.w(dv, "pe", t_tr)
                vtr = self.VT().rearrange("p (b c) -> p b c", c=130)
                out = vtr[:, g * 16 + p: g * 16 + 13 + p: 4, 0:129]
                out = out.rearrange("p b (h d) -> p b h d", d=64) if False else \
                    vtr[:, g * 16 + p: g * 16 + 13 + p: 4, :].rearrange(
                        "p b c -> p b c")[:, :, 0:130]
                base = (g * 16 + p) * 130
                out = bass.AP(self.VT().tensor, self.VT().offset + base,
                              [list(self.VT().ap[0]), [4 * 130, 4], [65, 2], [1, 64]])
                ins = dv.tensor_copy(out, ptb[:, 0:512].rearrange("p (k h d) -> p k h d", k=4, h=2))
                pt_free = self.inc("dv", ins)
        self.ptb_free[4] = pt_free

    def emit_qknorms(self, half):
        pe, dv, ac = self.pe, self.dv, self.ac
        T = self.T
        nck = T // 512
        t_red = 0
        t_mm = 0
        for src_i, buf, dst in ((0, self.KF(), self.KN2A), (1, self.QH(), self.QN2A)):
            for wave in range(2):
                sq = self.HPQ[:, 0: 2 * T]
                t_sq = 0
                for pw in range(2):
                    p = wave * 2 + pw
                    self.w(ac, "pe", t_mm)
                    ins = ac.activation(sq[:, pw * T:(pw + 1) * T],
                                        buf[:, p * T:(p + 1) * T], AF.Square)
                    t_sq = self.inc("ac", ins)
                self.w(pe, "ac", t_sq)
                self.w(pe, "dv", t_red)
                for pw in range(2):
                    for h in range(2):
                        r = h * 64
                        for cc in range(nck):
                            mm = pe.matmul(
                                self.PS[r:r + 1, pw * 2048 + cc * 512: pw * 2048 + (cc + 1) * 512],
                                self.ONES[r:r + 64, 0:1],
                                sq[r:r + 64, pw * T + cc * 512: pw * T + (cc + 1) * 512],
                                start=True, stop=True)
                        t_mm = self.inc("pe", mm)
                for pw in range(2):
                    p = wave * 2 + pw
                    for h in range(2):
                        r = h * 64
                        self.w(dv, "pe", t_mm)
                        dv.drain()
                        ins = dv.reduce_max(dst[r:r + 1, p:p + 1],
                                            self.PS[r:r + 1, pw * 2048: pw * 2048 + T], axis=AX)
                        t_red = self.inc("dv", ins)
        self.w(dv, "dv", t_red)
        dv.drain()
        for h in range(2):
            r = h * 64
            dv.tensor_scalar(self.NMH[r:r + 1, 0:4], self.QN2A[r:r + 1, 0:4], -4.08, None, op0=ALU.mult)
            dv.tensor_scalar(self.NMH[r:r + 1, 4:8], self.KN2A[r:r + 1, 0:4], -1.02 / 16.0,
                             None, op0=ALU.mult)
        dv.drain()
        for h in range(2):
            r = h * 64
            dv.tensor_tensor(self.NMH[r:r + 1, 0:4], self.NMH[r:r + 1, 0:4],
                             self.NMH[r:r + 1, 4:8], op=ALU.add)
        dv.drain()
        for h in range(2):
            r = h * 64
            ins = dv.tensor_copy(self.NMHB[r:r + 1, h * 4: h * 4 + 4], self.NMH[r:r + 1, 0:4])
        t_nb = self.inc("dv", ins)
        self.w(pe, "dv", t_nb)
        for h in range(2):
            r = h * 64
            mm = pe.matmul(self.PS[:, 7 * 512 + h * 4: 7 * 512 + h * 4 + 4],
                           self.ONES[r:r + 1, 0:128], self.NMHB[r:r + 1, h * 4: h * 4 + 4],
                           start=True, stop=True)
        t_mm2 = self.inc("pe", mm)
        self.w(dv, "pe", t_mm2)
        ins = dv.tensor_copy(self.BND[:, 0:8], self.PS[:, 7 * 512: 7 * 512 + 8])
        t_bc = self.inc("dv", ins)
        self.bnd_tok = t_bc
        self.kn_guard = t_bc

    def attention_half(self, mha, half):
        pe, dv, ac = self.pe, self.dv, self.ac
        T, NG = self.T, self.NG
        for qg in range(NG):
            for p in range(4):
                gp = p % 2
                nj = 4 * qg + 4
                npair = nj // 2
                avtok = [0, 0]
                exp_of = {}

                def emit_ST(jp):
                    toks = [0, 0]
                    for jj in range(2):
                        j = jp * 2 + jj
                        for h in range(2):
                            bank = h * 2 + jj
                            self.w(pe, "ac", self.stb_free[bank])
                            if j == 0:
                                self.w(pe, "dv", self.obank_free[h * 2 + gp])
                                self.w(pe, "dv", self.kn_guard)
                            lhs = self.KF()[h * 64:(h + 1) * 64,
                                            p * T + j * 128: p * T + (j + 1) * 128]
                            rhs = self.QH()[h * 64:(h + 1) * 64,
                                            p * T + qg * 512: p * T + (qg + 1) * 512]
                            if j >= 4 * qg:
                                r = j - 4 * qg
                                pe.matmul(self.psb(bank), lhs, rhs, start=True, stop=False)
                                mm = pe.matmul(self.psb(bank), self.IDB[:],
                                               self.MK[:, r * 512:(r + 1) * 512],
                                               start=False, stop=True)
                            else:
                                mm = pe.matmul(self.psb(bank), lhs, rhs, start=True, stop=True)
                            toks[h] = self.inc("pe", mm)
                    sp = jp % 2
                    for h in range(2):
                        slot = h * 2 + sp
                        self.w(ac, "pe", toks[h])
                        self.w(ac, "pe", self.ptslot_free[slot])
                        ins = ac.activation(self.PTb(slot), self.PS[:, (h * 2) * 512:(h * 2) * 512 + 1024],
                                            AF.Exp, bias=self.BND[:, h * 4 + p: h * 4 + p + 1])
                        etok = self.inc("ac", ins)
                        self.stb_free[h * 2] = etok
                        self.stb_free[h * 2 + 1] = etok
                        exp_of[(jp, h)] = etok

                def emit_AV(jp):
                    sp = jp % 2
                    for jj in range(2):
                        j = jp * 2 + jj
                        for h in range(2):
                            slot = h * 2 + sp
                            obank = 4 + h * 2 + gp
                            self.w(pe, "ac", exp_of[(jp, h)])
                            blk = (j * 4 + p) * 130
                            mm = pe.matmul(self.PS[0:65, obank * 512: obank * 512 + 512],
                                           self.VT()[:, blk + h * 65: blk + (h + 1) * 65],
                                           self.PTb(slot)[:, jj * 512:(jj + 1) * 512],
                                           start=(j == 0), stop=(j == nj - 1))
                            avtok[h] = self.inc("pe", mm)
                            if jj == 1:
                                self.ptslot_free[slot] = avtok[h]

                for jp in range(npair):
                    emit_ST(jp)
                    self.step_rt()
                    if jp >= 1:
                        emit_AV(jp - 1)
                        self.step_rt()
                emit_AV(npair - 1)
                self.flush_rt()
                for h in range(2):
                    self.queue_roundtrip(half, p, qg, h, gp, avtok[h])
        self.flush_rt()

    def queue_roundtrip(self, half, p, qg, h, gp, t_av):
        pe, dv = self.pe, self.dv
        T = self.T
        obank = 4 + h * 2 + gp
        pair = half * 4 + p
        st = {}

        def s0():
            self.w(dv, "pe", t_av)
            self.w(dv, "pe", self.rt_osg_free)
            ins = dv.tensor_copy(self.OSG()[0:65, 0:512],
                                 self.PS[0:65, obank * 512: obank * 512 + 512])
            st["ev"] = self.inc("dv", ins)

        def s1():
            self.w(pe, "dv", st["ev"])
            for b in range(4):
                mm = pe.transpose(self.PS[:, obank * 512 + b * 65: obank * 512 + (b + 1) * 65],
                                  self.OSG()[0:65, b * 128:(b + 1) * 128], self.IDF[0:65, 0:65])
            st["tr"] = self.inc("pe", mm)
            self.rt_osg_free = st["tr"]

        def s2():
            self.w(dv, "pe", st["tr"])
            self.w(dv, "pe", self.rt_otb_free)
            dv.tensor_copy(self.OTM()[:, 0:260], self.PS[:, obank * 512: obank * 512 + 260])
            dv.drain()
            linv_src = bass.AP(self.OTM().tensor, self.OTM().offset + 64,
                               [list(self.OTM().ap[0]), [65, 4], [1, 1]])
            dv.reciprocal(self.OL4[:, 0:4], linv_src)
            dv.drain()
            for b in range(4):
                ins = dv.tensor_scalar(self.OTB()[:, b * 64:(b + 1) * 64],
                                       self.OTM()[:, b * 65: b * 65 + 64],
                                       self.OL4[:, b:b + 1], None, op0=ALU.mult)
            st["nb"] = self.inc("dv", ins)

        def s3():
            self.w(pe, "dv", st["nb"])
            ptb = self.pt_bf16(obank)
            for b in range(4):
                mm = pe.transpose(ptb[0:64, b * 128:(b + 1) * 128],
                                  self.OTB()[:, b * 64:(b + 1) * 64], self.IDB[:])
            st["tb"] = self.inc("pe", mm)
            self.rt_otb_free = st["tb"]

        def s4():
            self.w(dv, "pe", st["tb"])
            ins = dv.tensor_copy(
                self.OF[h * 64:(h + 1) * 64, pair * T + qg * 512: pair * T + (qg + 1) * 512],
                ptb_dummy[0] if False else self.pt_bf16(obank)[0:64, 0:512])
            self.obank_free[h * 2 + gp] = self.inc("dv", ins)

        self.rt_queue.extend([s0, s1, s2, s3, s4])

    def step_rt(self):
        if self.rt_queue:
            self.rt_queue.pop(0)()

    def flush_rt(self):
        while self.rt_queue:
            self.rt_queue.pop(0)()

    def proj_phase(self, mha, is_inner):
        pe, dv, po = self.pe, self.dv, self.po
        T, NG = self.T, self.NG
        wslot_free = [0, 0]
        wtok = [0, 0]
        pbank_free = [0, 0]
        pt_free = self.ptb_free[4]
        afm_cons = 0
        atm_free = [0, 0]
        for g in range(NG):
            evtoks = []
            for ct in range(8):
                sl = ct % 2
                src = self.d_wp[mha][:, ct * 128:(ct + 1) * 128].rearrange("(j p) f -> p j f", p=128)
                self.w(po, "pe", wslot_free[sl])
                dst = self.WSQ[:, sl * 1024:(sl + 1) * 1024].rearrange("p (j f) -> p j f", j=8)
                wtok[sl] = self.inc("wa%d" % sl, po.dma_start(dst, src), 16)
                self.w(pe, "wa%d" % sl, wtok[sl])
                bank = ct % 2
                self.w(pe, "dv", pbank_free[bank])
                if ct == 0:
                    self.w(pe, "pe", afm_cons)
                for j in range(NCH):
                    mm = pe.matmul(self.psb(bank),
                                   self.WSQ[:, sl * 1024 + j * 128: sl * 1024 + (j + 1) * 128],
                                   self.OF[:, j * T + g * 512: j * T + (g + 1) * 512],
                                   start=(j == 0), stop=(j == NCH - 1))
                t_mm = self.inc("pe", mm)
                wslot_free[sl] = t_mm
                self.w(dv, "pe", t_mm)
                self.w(dv, "pe", afm_cons)
                ins = dv.tensor_copy(self.AST[:, ct * 512:(ct + 1) * 512], self.psb(bank))
                pbank_free[bank] = self.inc("dv", ins)
                evtoks.append(pbank_free[bank])
            afm_tok = max(evtoks)
            for tt in range(4):
                tslot = tt % 2
                self.w(pe, "dv", afm_tok)
                self.w(pe, "dv", pt_free)
                self.w(pe, "dv", atm_free[tslot])
                ptb = self.pt_bf16(4)
                for ct in range(8):
                    mm = pe.transpose(ptb[:, ct * 128:(ct + 1) * 128],
                                      self.AST[:, ct * 512 + tt * 128: ct * 512 + (tt + 1) * 128],
                                      self.IDB[:])
                t_tr = self.inc("pe", mm)
                if tt == 3:
                    afm_cons = t_tr
                self.w(dv, "pe", t_tr)
                atm = self.ATM[:, tslot * 1024:(tslot + 1) * 1024]
                ins = dv.tensor_copy(atm, ptb[:])
                pt_free = self.inc("dv", ins)
                ti = g * 4 + tt
                xr = self.XR[:, ti * C:(ti + 1) * C]
                if not is_inner:
                    par = tt % 2
                    mu, rstd, t_rst = self.ln_stats(atm, par, src_dv_tok=pt_free)
                    self.w(dv, "ac", t_rst)
                    dv.tensor_scalar(atm, atm, mu, rstd, op0=ALU.subtract, op1=ALU.mult)
                    dv.drain()
                    ins = dv.tensor_tensor(xr, xr, atm, op=ALU.add)
                    t_res = self.inc("dv", ins)
                    self.ln_free[par] = t_res
                    atm_free[tslot] = t_res
                else:
                    oslot = 1 - tslot
                    self.w(dv, "pe", 0)
                    self.w(dv, "dv", 0)
                    other = self.ATM[:, oslot * 1024:(oslot + 1) * 1024]
                    self.w(dv, "dv", atm_free[oslot]) if False else None
                    dv.tensor_scalar(other, xr, self.MUCACHE[:, ti:ti + 1],
                                     self.RSTCACHE[:, ti:ti + 1], op0=ALU.subtract, op1=ALU.mult)
                    dv.drain()
                    uu = self.UU[:, ti * C:(ti + 1) * C]
                    ins = dv.tensor_tensor(uu, other, atm, op=ALU.add)
                    t_res = self.inc("dv", ins)
                    atm_free[tslot] = t_res
                    atm_free[oslot] = t_res
        self.ptb_free[4] = pt_free

    def mha_slot(self, seq, mha):
        is_inner = (mha == 1)
        self.h_build(self.XR, gain_idx=(0 if is_inner else None), cache_stats=is_inner)
        self.barrier()
        if self.ck("h%d" % mha):
            self.dump("h", self.HF[:])
            raise _DebugStop()
        for half in range(2):
            self.kv_build(mha, half)
            self.barrier()
            self.emit_qknorms(half)
            if half == 0 and self.ck("kv%d" % mha):
                self.dump("kf", self.KF()[:])
                self.dump("vt", self.VT()[:])
                raise _DebugStop()
            self.attention_half(mha, half)
            self.barrier()
        if self.ck("of%d" % mha):
            self.dump("of", self.OF[:, 0:8 * self.T])
            raise _DebugStop()
        self.proj_phase(mha, is_inner)
        self.barrier()

    def router_phase(self):
        pe, dv, po, ac = self.pe, self.dv, self.po, self.ac
        T, NG = self.T, self.NG
        lg_free = 0
        lgt_free = [0, 0]
        ps5_free = 0
        wt_tok = 0
        for g in range(NG):
            for j in range(NCH):
                mm = pe.matmul(self.PS[0:8, 7 * 512: 7 * 512 + 512],
                               self.RW[:, j * 8:(j + 1) * 8],
                               self.HF[:, j * T + g * 512: j * T + (g + 1) * 512],
                               start=(j == 0), stop=(j == NCH - 1))
            t_mm = self.inc("pe", mm)
            self.w(dv, "pe", t_mm)
            self.w(dv, "pe", lg_free)
            ins = dv.tensor_copy(self.LG[0:8, 0:512], self.PS[0:8, 7 * 512: 7 * 512 + 512])
            t_lg = self.inc("dv", ins)
            for tt in range(4):
                par = tt % 2
                self.w(pe, "dv", t_lg)
                self.w(pe, "dv", max(lgt_free[par], ps5_free))
                mm = pe.transpose(self.PS[:, 5 * 512: 5 * 512 + 8],
                                  self.LG[0:8, tt * 128:(tt + 1) * 128], self.IDF[0:8, 0:8])
                t_tr = self.inc("pe", mm)
                if tt == 3:
                    lg_free = t_tr
                self.w(dv, "pe", t_tr)
                ins = dv.tensor_copy(self.LGT[:, par * 8:(par + 1) * 8],
                                     self.PS[:, 5 * 512: 5 * 512 + 8])
                t_lgt = self.inc("dv", ins)
                ps5_free = t_lgt
                lgt = self.LGT[:, par * 8:(par + 1) * 8]
                dv.drain()
                dv.reduce_max(self.RM1N[:], lgt, axis=AX, negate=True)
                dv.drain()
                dv.tensor_scalar(self.RM1[:], self.RM1N[:], -1.0, None, op0=ALU.mult)
                dv.drain()
                dv.tensor_scalar(self.RMK1[:], lgt, self.RM1[:], None, op0=ALU.is_ge)
                dv.drain()
                dv.scalar_tensor_tensor(self.RL2[:], self.RMK1[:], -1e30, lgt,
                                        op0=ALU.mult, op1=ALU.add)
                dv.drain()
                ins = dv.reduce_max(self.RM2[:], self.RL2[:], axis=AX)
                t_m2 = self.inc("dv", ins)
                self.w(ac, "dv", t_m2)
                ins = ac.activation(self.RE2[:], self.RM2[:], AF.Exp, bias=self.RM1N[:])
                t_e2 = self.inc("ac", ins)
                self.w(dv, "ac", t_e2)
                dv.tensor_scalar(self.RDEN[:], self.RE2[:], 1.0, None, op0=ALU.add)
                dv.drain()
                dv.tensor_scalar(self.RMK2[:], self.RL2[:], self.RM2[:], None, op0=ALU.is_ge)
                dv.drain()
                ti = g * 4 + tt
                dv.scalar_tensor_tensor(self.RMK2[:], self.RMK2[:], self.RE2[:], self.RMK1[:],
                                        op0=ALU.mult, op1=ALU.add)
                dv.reciprocal(self.RW1[:], self.RDEN[:])
                dv.drain()
                ins = dv.tensor_scalar(self.WTOK[:, ti * 8:(ti + 1) * 8], self.RMK2[:],
                                       self.RW1[:], None, op0=ALU.mult)
                wt_tok = self.inc("dv", ins)
                lgt_free[par] = wt_tok
        self.wtok_tok = wt_tok

    def moe_phase(self, seq):
        pe, dv, ac, po = self.pe, self.dv, self.ac, self.po
        T = self.T
        tgW = T // 2
        self.tgW = tgW
        self.WBv = self.AST[:, 0:tgW]
        nsub = max(1, tgW // 512)
        subW = min(512, tgW)
        nh = max(1, tgW // 512)
        self.w(self.sy, "pe", self.cnt["pe"])
        self.w(self.sy, "dv", self.cnt["dv"])
        self.w(self.sy, "ac", self.cnt["ac"])
        for tg in range(2):
            tgbase = tg * tgW
            tgtile0 = tg * (tgW // 128)
            gel_tok = [0, 0]
            scl_tok = [0] * 8
            w1_tok = [0, 0]
            w1_free = [0, 0]
            w2_free = 0
            w2_tok_cur = 0
            w2_tok_prev = 0
            scl_prev = [0] * 8
            yevac_tok = [0] * (2 * nsub)
            y_last_evac = 0
            wb_tok = 0
            hpq_guard = 0

            def h_mms(fc, oct_):
                bank0 = (fc % 2) * 2
                sl = fc // 4
                self.w(pe, "w1a" if sl == 0 else "w1b", w1_tok[sl])
                self.w(pe, "ac", gel_tok[fc % 2])
                if fc == 0:
                    self.w(pe, "dv", wb_tok)
                for j in range(NCH):
                    for hh in range(nh):
                        mm = pe.matmul(self.PS[:, (bank0 + hh) * 512:(bank0 + hh) * 512 + subW],
                                       self.W1Q(sl)[:, j * 512 + (fc % 4) * 128: j * 512 + (fc % 4) * 128 + 128],
                                       self.HF[:, j * T + tgbase + hh * subW: j * T + tgbase + (hh + 1) * subW],
                                       start=(j == 0), stop=(j == NCH - 1))
                tok = self.inc("pe", mm)
                return tok

            def gelu_scale(fc, t_h):
                bank0 = (fc % 2) * 2
                self.w(ac, "pe", t_h)
                self.w(ac, "dv", scl_prev[fc])
                self.w(ac, "pe", hpq_guard)
                ins = ac.activation(self.HPQ[:, fc * tgW:(fc + 1) * tgW],
                                    self.PS[:, bank0 * 512: bank0 * 512 + tgW], AF.Gelu)
                t_g = self.inc("ac", ins)
                gel_tok[fc % 2] = t_g
                self.w(dv, "ac", t_g)
                ins = dv.tensor_tensor(self.HPQ[:, fc * tgW:(fc + 1) * tgW],
                                       self.HPQ[:, fc * tgW:(fc + 1) * tgW],
                                       self.WBv, op=ALU.mult)
                scl_tok[fc] = self.inc("dv", ins)

            def w2_combos(first):
                nonlocal w2_free, y_last_evac
                dv.drain()
                for ci, (ch, sub) in enumerate([(a, b) for a in range(2) for b in range(nsub)]):
                    self.w(pe, "w2o", w2_tok_prev)
                    self.w(pe, "dv", y_last_evac)
                    for fc in range(8):
                        self.w(pe, "dv", scl_prev[fc])
                        for cti in range(4):
                            cto = (ch * 4 + cti) * 128
                            mm = pe.matmul(self.PS[:, (4 + cti) * 512:(4 + cti) * 512 + subW],
                                           self.W2O[:, fc * 1024 + cto: fc * 1024 + cto + 128],
                                           self.HPQ[:, fc * tgW + sub * subW: fc * tgW + (sub + 1) * subW],
                                           start=(fc == 0), stop=(fc == 7))
                    t_c = self.inc("pe", mm)
                    w2_free = t_c
                    self.w(dv, "pe", t_c)
                    ysl = self.BIGB[:, 0:4 * T].rearrange("p (c t) -> p c t", t=tgW)[
                        :, ch * 4:(ch + 1) * 4, sub * subW:(sub + 1) * subW]
                    ps_y = self.PS[:, 2048:4096].rearrange("p (c t) -> p c t", t=512)[:, :, 0:subW]
                    if first:
                        ins = dv.tensor_scalar(ysl, ps_y, 1.0, None, op0=ALU.mult)
                    else:
                        ins = dv.scalar_tensor_tensor(ysl, ps_y, 1.0, ysl, op0=ALU.mult, op1=ALU.add)
                    yevac_tok[ci] = self.inc("dv", ins)
                    y_last_evac = yevac_tok[ci]

            for oct_ in range(16):
                if oct_ % 2 == 0:
                    wb_tok = self.wb_build(tgtile0, oct_ // 2, gel_tok, scl_tok)
                for q in range(2):
                    col0 = oct_ * 1024 + q * 512
                    src = self.d_w1c[:, col0:col0 + 512].rearrange("(j p) f -> p j f", p=128)
                    self.w(self.sy, "pe", w1_free[q])
                    if oct_ == 0 and tg == 0:
                        self.w(self.sy, "wc", self.wc_tok)
                    dst = self.W1Q(q)[:].rearrange("p (j f) -> p j f", j=8)
                    w1_tok[q] = self.inc("w1a" if q == 0 else "w1b", self.sy.dma_start(dst, src), 16)
                t_h0 = h_mms(0, oct_)
                t_h1 = h_mms(1, oct_)
                if oct_ > 0:
                    w2_combos(first=(oct_ == 1))
                hpq_guard = w2_free
                self.w(self.sy, "pe", w2_free)
                src = self.d_w2c[oct_ * 1024:(oct_ + 1) * 1024, :].rearrange("(j p) c -> p j c", p=128)
                dst = self.W2O[:].rearrange("p (j c) -> p j c", j=8)
                w2_tok_cur = self.inc("w2o", self.sy.dma_start(dst, src), 16)
                gelu_scale(0, t_h0)
                gelu_scale(1, t_h1)
                for fc in range(2, 8):
                    t_h = h_mms(fc, oct_)
                    gelu_scale(fc, t_h)
                w1_free = [self.cnt["pe"], self.cnt["pe"]]
                w2_tok_prev = w2_tok_cur
                scl_prev = list(scl_tok)
            w2_combos(first=False)
            self.barrier()
            if self.ck("y") and tg == 0:
                self.dump("y", self.BIGB[:, 0:4 * self.T])
                self.dump("wtok", self.WTOK[:])
                self.dump("t2d", self.HF[:])
                raise _DebugStop()
            self.finale(seq, tg, tgW)
            self.barrier()

    def wb_build(self, tgtile0, expert, gel_tok, scl_tok):
        pe, dv = self.pe, self.dv
        ntile = self.tgW // 128
        dv.drain()
        self.w(dv, "pe", self.cnt["pe"])
        for tt in range(ntile):
            ti = tgtile0 + tt
            ins = dv.tensor_scalar(self.AST[:, 2048 + tt * 128: 2048 + (tt + 1) * 128],
                                   self.ONES[:], self.WTOK[:, ti * 8 + expert: ti * 8 + expert + 1],
                                   None, op0=ALU.mult)
        t_stage = self.inc("dv", ins)
        self.w(pe, "dv", t_stage)
        self.w(pe, "ac", gel_tok[0])
        ptb = self.pt_bf16(0)
        for tt in range(ntile):
            mm = pe.transpose(ptb[:, tt * 128:(tt + 1) * 128],
                              self.AST[:, 2048 + tt * 128: 2048 + (tt + 1) * 128], self.IDB[:])
        t_tr = self.inc("pe", mm)
        self.w(dv, "pe", t_tr)
        ins = dv.tensor_copy(self.WBv, ptb[:, 0:self.tgW])
        return self.inc("dv", ins)

    def finale(self, seq, tg, tgW):
        pe, dv, sy = self.pe, self.dv, self.sy
        ntile = tgW // 128
        astf = self.AST[:].bitcast(F32)
        atmf = self.ATM[:].bitcast(F32)
        outslot = [astf[:, 1024:2048], atmf[:, 0:1024]]
        out_tok = [0, 0]
        f_cons = 0
        tr_free = 0
        for tt in range(ntile):
            ti = tg * ntile + tt
            par = tt % 2
            self.w(pe, "dv", tr_free)
            for ct in range(8):
                bank = ct // 4
                mm = pe.transpose(self.PS[:, bank * 512 + (ct % 4) * 128: bank * 512 + (ct % 4) * 128 + 128],
                                  self.BIGB[:, ct * tgW + tt * 128: ct * tgW + (tt + 1) * 128],
                                  self.IDF[:])
            t_tr = self.inc("pe", mm)
            self.w(dv, "pe", t_tr)
            self.w(dv, "dv", 0)
            ftmp = astf[:, 0:1024]
            ins = dv.scalar_tensor_tensor(ftmp, self.PS[:, 0:1024], 1.0,
                                          self.UU[:, ti * C:(ti + 1) * C], op0=ALU.mult, op1=ALU.add)
            t_f = self.inc("dv", ins)
            tr_free = t_f
            mu, rstd, t_rst = self.ln_stats(ftmp, par, src_dv_tok=t_f)
            self.w(dv, "ac", t_rst)
            self.w(dv, "qo%d" % par, out_tok[par])
            nrm = outslot[par]
            dv.tensor_scalar(nrm, ftmp, mu, rstd, op0=ALU.subtract, op1=ALU.mult)
            dv.drain()
            ins = dv.tensor_tensor(nrm, nrm, self.XR[:, ti * C:(ti + 1) * C], op=ALU.add)
            t_o = self.inc("dv", ins)
            self.ln_free[par] = t_o
            self.w(sy, "dv", t_o)
            dst = self.d_out[seq, ti * 128:(ti + 1) * 128, :]
            out_tok[par] = self.inc("qo%d" % par, sy.dma_start(dst, nrm), 16)

    def dump(self, name, ap):
        w = ap.shape[-1]
        npart = ap.shape[0]
        d = self.nc.dram_tensor("dbg_" + name, [npart, w], F32, kind="ExternalOutput").ap()
        self.barrier()
        for s in self.sem:
            if s != "qdbg" and self.cnt[s]:
                self.po.wait_ge(self.sem[s], self.cnt[s])
        self.inc("qdbg", self.po.dma_start(d, ap), 16)

    def ck(self, phase):
        if self.debug_phase is None:
            return False
        return phase == self.debug_phase

    def build(self, debug_phase=None):
        self.debug_phase = debug_phase

        def fin():
            for s in ("qo0", "qo1", "qdbg"):
                if self.cnt[s]:
                    self.po.wait_ge(self.sem[s], self.cnt[s])
                    self.sy.wait_ge(self.sem[s], self.cnt[s])
            return self.nc

        self.load_consts()
        try:
            self._main()
        except _DebugStop:
            pass
        return fin()

    def _main(self):
        for seq in range(2):
            if seq == 1:
                self.barrier()
            xt = self.load_x(seq)
            if seq == 0:
                self.prep_wcache()
            for eng in (self.dv, self.pe, self.ac):
                self.w(eng, "qx", xt)
            self.w(self.dv, "dv", self.const_tok)
            if self.ck("x"):
                self.dump("x", self.XR[:])
                raise _DebugStop()
            self.mha_slot(seq, 0)
            if self.ck("mha0"):
                self.dump("x1", self.XR[:])
                raise _DebugStop()
            self.mha_slot(seq, 1)
            if self.ck("mha1"):
                self.dump("x1", self.XR[:])
                self.dump("u", self.UU[:, 0:self.NT * C])
                raise _DebugStop()
            self.h_build(self.UU, gain_idx=1, cache_stats=False)
            self.barrier()
            if self.ck("t2"):
                self.dump("t2", self.HF[:])
                raise _DebugStop()
            self.router_phase()
            self.barrier()
            if self.ck("router"):
                self.dump("wtok", self.WTOK[:])
                raise _DebugStop()
            self.moe_phase(seq)
            if self.ck("y"):
                raise _DebugStop()


def build_program(T=2048):
    return KB(T).build()


def make_masks():
    kk = np.arange(128)[:, None]
    qq = np.arange(512)[None, :]
    return np.stack([np.where(qq >= 128 * r + kk, 0.0, -1e30) for r in range(4)]).astype(np.float32)


def kernel(L0_attn_x, L0_attn_W_attn, L0_attn_W_proj, L0_ffn_ln1_w, L0_ffn_c_attn_w,
           L0_ffn_c_proj_w, L0_ffn_ln2_w, L0_ffn_router_w, L0_ffn_w1, L0_ffn_w2):
    x = np.ascontiguousarray(np.asarray(L0_attn_x, dtype=np.float32))
    B, T, _ = x.shape
    n_cores = 8
    per = B // n_cores
    nc = build_program(T)
    consts = {
        "wqkv0": np.ascontiguousarray(L0_attn_W_attn, dtype=np.float32),
        "wqkv1": np.ascontiguousarray(L0_ffn_c_attn_w, dtype=np.float32),
        "wp0": np.ascontiguousarray(L0_attn_W_proj, dtype=np.float32),
        "wp1": np.ascontiguousarray(L0_ffn_c_proj_w, dtype=np.float32),
        "lnw": np.ascontiguousarray(
            np.stack([np.asarray(L0_ffn_ln1_w, dtype=np.float32).reshape(8, 128),
                      np.asarray(L0_ffn_ln2_w, dtype=np.float32).reshape(8, 128)])),
        "rwT": np.ascontiguousarray(np.asarray(L0_ffn_router_w, dtype=np.float32).T),
        "w1": np.ascontiguousarray(L0_ffn_w1, dtype=np.float32),
        "w2": np.ascontiguousarray(L0_ffn_w2, dtype=np.float32),
        "ident": np.eye(128, dtype=np.float32),
        "masks": make_masks(),
        "ones": np.ones((128, 128), dtype=np.float32),
    }
    in_maps = []
    for c in range(n_cores):
        m = dict(consts)
        m["x"] = np.ascontiguousarray(x[c * per:(c + 1) * per])
        in_maps.append(m)
    res = run_bass_kernel_spmd(nc, in_maps, core_ids=list(range(n_cores)))
    return np.concatenate([np.asarray(r["out"], dtype=np.float32) for r in res.results], axis=0)


# version 11
# speedup vs baseline: 1.2498x; 1.0322x over previous
import numpy as np
import concourse.bass as bass
import concourse.mybir as mybir
from concourse.bass_utils import run_bass_kernel_spmd
from contextlib import ExitStack

F32 = mybir.dt.float32
BF16 = mybir.dt.bfloat16
AF = mybir.ActivationFunctionType
ALU = mybir.AluOpType
AX = mybir.AxisListType.X

C = 1024
NCH = 8
EPS = 1e-5
FF = 16384


class _DebugStop(Exception):
    pass


class KB:
    def __init__(self, T):
        self.T = T
        self.NT = T // 128
        self.NG = T // 512
        nc = bass.Bass("TRN2", target_bir_lowering=False)
        self.nc = nc
        self.pe, self.dv, self.ac, self.po, self.sy = nc.tensor, nc.vector, nc.scalar, nc.gpsimd, nc.sync
        self.st = ExitStack()
        self.cnt = {}
        self.sem = {}
        NT = self.NT

        dram = lambda n, shp, k: nc.dram_tensor(n, shp, F32, kind=k).ap()
        self.d_x = dram("x", [2, T, C], "ExternalInput")
        self.d_wqkv = [dram("wqkv0", [C, 3 * C], "ExternalInput"),
                       dram("wqkv1", [C, 3 * C], "ExternalInput")]
        self.d_wp = [dram("wp0", [C, C], "ExternalInput"),
                     dram("wp1", [C, C], "ExternalInput")]
        self.d_ln = dram("lnw", [2, 8, 128], "ExternalInput")
        self.d_rwT = dram("rwT", [8, C], "ExternalInput")
        self.d_w1 = dram("w1", [C, FF], "ExternalInput")
        self.d_w2 = dram("w2", [FF, C], "ExternalInput")
        self.d_id = dram("ident", [128, 128], "ExternalInput")
        self.d_mk = dram("masks", [4, 128, 512], "ExternalInput")
        self.d_ones = dram("ones", [128, 128], "ExternalInput")
        self.d_out = dram("out", [2, T, C], "ExternalOutput")
        self.d_w1c = nc.dram_tensor("w1c", [C, FF], BF16, kind="Internal").ap()
        self.d_w2c = nc.dram_tensor("w2c", [FF, C], BF16, kind="Internal").ap()

        sb = lambda n, shp, dt: self.st.enter_context(nc.sbuf_tensor(n, shp, dt))
        self.XR = sb("XR", [128, NT * C], BF16)
        self.HF = sb("HF", [128, NCH * T], BF16)
        self.BIGB = sb("BIGB", [128, 4 * T], F32)
        self.KVW = sb("KVW", [128, max(4 * T + (T // 128) * 4 * 130, 16384)], BF16)
        self.AR1 = sb("AR1", [128, max(8 * T, 4 * T + 8192)], BF16)
        self.AST = sb("AST", [128, 4096], BF16)
        self.HPQ = sb("HPQ", [128, 8192], BF16)
        self.ATM = sb("ATM", [128, 2048], BF16)
        self.SQS = sb("SQS", [128, C], BF16)
        self.MK = sb("MK", [128, 4 * 512], BF16)
        self.IDB = sb("IDB", [128, 128], BF16)
        self.IDF = sb("IDF", [128, 128], F32)
        self.ONES = sb("ONESB", [128, 128], BF16)
        self.LNW = sb("LNW", [128, 16], F32)
        self.RW = sb("RW", [128, 64], BF16)
        self.WTOK = sb("WTOK", [128, NT * 8], F32)
        self.WSQ = sb("WSQ", [128, 4 * 1024], BF16)
        self.LGT = sb("LGT", [128, 16], F32)
        self.S1 = sb("S1", [128, 2], F32)
        self.S2 = sb("S2", [128, 2], F32)
        self.MUc = sb("MUc", [128, 2], F32)
        self.VPc = sb("VPc", [128, 2], F32)
        self.TVA = sb("TVA", [128, 2], F32)
        self.RSTc = sb("RSTc", [128, 2], F32)
        self.MUCACHE = sb("MUCACHE", [128, NT], F32)
        self.RSTCACHE = sb("RSTCACHE", [128, NT], F32)
        self.S1A = sb("S1A", [128, NT], F32)
        self.S2A = sb("S2A", [128, NT], F32)
        self.VPA = sb("VPA", [128, NT], F32)
        self.NM = sb("NM", [128, 8], F32)
        self.KN2A = sb("KN2A", [128, 8], F32)
        self.QN2A = sb("QN2A", [128, 8], F32)
        self.OL4 = sb("OL4", [128, 4], F32)
        self.BND = sb("BND", [128, 8], F32)
        self.NKND = sb("NKND", [128, 8], F32)
        self.QN2 = sb("QN2", [128, 8], F32)
        self.NMH = sb("NMH", [128, 8], F32)
        self.NMHB = sb("NMHB", [128, 8], BF16)
        self.LL = sb("LL", [128, 8], F32)
        self.NGM = sb("NGM", [128, 4], F32)
        self.BB = sb("BB", [128, 8], F32)
        self.BL = sb("BL", [128, 8], F32)
        self.LT = sb("LT", [128, 4], F32)
        self.LIV = sb("LIV", [128, 4], F32)
        self.RM1 = sb("RM1", [128, 1], F32)
        self.RM1N = sb("RM1N", [128, 1], F32)
        self.RM2 = sb("RM2", [128, 1], F32)
        self.RE2 = sb("RE2", [128, 1], F32)
        self.RDEN = sb("RDEN", [128, 1], F32)
        self.RW1 = sb("RW1", [128, 1], F32)
        self.RW2 = sb("RW2", [128, 1], F32)
        self.RMK1 = sb("RMK1", [128, 8], F32)
        self.RL2 = sb("RL2", [128, 8], F32)
        self.RMK2 = sb("RMK2", [128, 8], F32)

        self.PS = self.st.enter_context(nc.psum_tensor("PS", [128, 4096], F32))

        for s in ["pe", "dv", "ac", "po", "qld", "qx", "qdbg", "qo0", "qo1",
                  "wa0", "wa1", "wa2", "wa3", "wq", "w1a", "w1b", "w2o", "wc"]:
            self.sem[s] = self.st.enter_context(nc.semaphore("sem_" + s))
            self.cnt[s] = 0

        self.ln_free = [0, 0]
        self.bank_free = [0, 0, 0, 0]
        self.slot_free = [0, 0]
        self.nm_free = [0, 0]
        self.o_evac = [0, 0]
        self.ptb_free = {0: 0, 4: 0, 7: 0}
        self.pts_free = [0, 0]
        self.liv_free = [0, 0, 0, 0]
        self.ll_free = [0, 0]
        self.stb_free = [0, 0, 0, 0]
        self.ptslot_free = [0, 0, 0, 0]
        self.obank_free = [0, 0, 0, 0]
        self.rt_osg_free = 0
        self.rt_otb_free = 0
        self.rt_queue = []
        self.itc = 0

    def inc(self, s, instr, n=1):
        instr.then_inc(self.sem[s], n)
        self.cnt[s] += n
        return self.cnt[s]

    def w(self, eng, s, tok):
        if tok and tok > 0:
            eng.wait_ge(self.sem[s], tok)

    def barrier(self):
        eng_of = {"pe": self.pe, "dv": self.dv, "ac": self.ac, "po": self.po}
        self.dv.drain()
        self.ac.drain()
        self.po.drain()
        for a in eng_of:
            for b in eng_of:
                if a != b and self.cnt[b] > 0:
                    eng_of[a].wait_ge(self.sem[b], self.cnt[b])
            for s in ("qo0", "qo1"):
                if self.cnt[s] > 0:
                    eng_of[a].wait_ge(self.sem[s], self.cnt[s])

    def psb(self, bank, w=512):
        return self.PS[:, bank * 512: bank * 512 + w]

    def pt_bf16(self, bank):
        return self.PS[:, bank * 512:(bank + 1) * 512].bitcast(BF16)

    @property
    def L8(self):
        return self.HPQ[:].bitcast(F32)

    @property
    def LG(self):
        return self.HPQ[:].bitcast(F32)

    @property
    def UU(self):
        return self.AR1
    def PPslot(self, slot):
        return self.AR1[:, slot * 2 * self.T:(slot + 1) * 2 * self.T]
    def QH(self):
        return self.AR1[:, 0: 4 * self.T]
    def PTb(self, slot):
        o = 4 * self.T + slot * 1024
        return self.AR1[:, o: o + 1024]
    def OSG(self):
        o4 = (4 * self.T + 4096) // 2
        return self.AR1[:].bitcast(F32)[:, o4: o4 + 512]
    def OTM(self):
        o4 = (4 * self.T + 4096) // 2 + 512
        return self.AR1[:].bitcast(F32)[:, o4: o4 + 272]
    def OTB(self):
        o = 4 * self.T + 4096 + 1024 + 544
        return self.AR1[:, o: o + 256]

    def KF(self):
        return self.KVW[:, 0: 4 * self.T]
    def VT(self):
        return self.KVW[:, 4 * self.T: 4 * self.T + (self.T // 128) * 4 * 130]
    def W1Q(self, slot):
        return self.KVW[:, slot * 4096:(slot + 1) * 4096]
    @property
    def W2O(self):
        return self.KVW[:, 8192: 16384]

    @property
    def OF(self):
        return self.BIGB[:].bitcast(BF16)

    def load_consts(self):
        po, pe, dv = self.po, self.pe, self.dv
        self.inc("qld", po.dma_start(self.IDB[:], self.d_id), 16)
        self.inc("qld", po.dma_start(self.IDF[:], self.d_id), 16)
        self.inc("qld", po.dma_start(self.ONES[:], self.d_ones), 16)
        mk = self.d_mk.rearrange("r p k -> p r k")
        self.inc("qld", po.dma_start(self.MK[:].rearrange("p (r k) -> p r k", r=4), mk), 16)
        self.inc("qld", po.dma_start(self.L8[0:8, 0:128], self.d_ln[0]), 16)
        self.inc("qld", po.dma_start(self.L8[0:8, 128:256], self.d_ln[1]), 16)
        t = self.inc("qld", po.dma_start(self.L8[0:8, 256:1280], self.d_rwT), 16)
        self.w(pe, "qld", t)
        pe.transpose(self.PS[:, 0:8], self.L8[0:8, 0:128], self.IDF[0:8, 0:8])
        pe.transpose(self.PS[:, 8:16], self.L8[0:8, 128:256], self.IDF[0:8, 0:8])
        for j in range(8):
            mm = pe.transpose(self.PS[:, 512 + j * 8: 512 + (j + 1) * 8],
                              self.L8[0:8, 256 + j * 128: 256 + (j + 1) * 128],
                              self.IDF[0:8, 0:8])
        ptok = self.inc("pe", mm)
        self.w(dv, "pe", ptok)
        dv.tensor_copy(self.LNW[:, 0:16], self.PS[:, 0:16])
        ins = dv.tensor_copy(self.RW[:], self.PS[:, 512: 512 + 64])
        self.const_tok = self.inc("dv", ins)

    def prep_wcache(self):
        po = self.po
        t = 0
        for j in range(0, C, 128):
            t = self.inc("wc", po.dma_start(self.d_w1c[j:j + 128, :], self.d_w1[j:j + 128, :]), 16)
        for j in range(0, FF, 2048):
            t = self.inc("wc", po.dma_start(self.d_w2c[j:j + 2048, :], self.d_w2[j:j + 2048, :]), 16)
        self.wc_tok = t

    def load_x(self, seq):
        src = self.d_x[seq].rearrange("(n p) c -> p n c", p=128)
        dst = self.XR[:].rearrange("p (n c) -> p n c", c=C)
        return self.inc("qx", self.po.dma_start(dst, src), 16)

    def ln_stats(self, src_ap, par, src_dv_tok=0):
        dv, ac, po = self.dv, self.ac, self.po
        c1 = 1.0 / C
        if src_dv_tok:
            dv.drain()
        t_dv = self.inc("dv", dv.reduce_sum(self.S1[:, par:par + 1], src_ap, axis=AX))
        self.w(ac, "dv", max(self.ln_free[par], src_dv_tok))
        t_ac = self.inc("ac", ac.activation(self.SQS[:], src_ap, AF.Square,
                                            accum_out=self.S2[:, par:par + 1]))
        self.w(dv, "ac", t_ac)
        dv.drain()
        dv.tensor_scalar(self.MUc[:, par:par + 1], self.S1[:, par:par + 1], c1, None, op0=ALU.mult)
        dv.drain()
        dv.tensor_tensor(self.TVA[:, par:par + 1], self.MUc[:, par:par + 1],
                         self.MUc[:, par:par + 1], op=ALU.mult)
        dv.drain()
        dv.tensor_scalar(self.TVA[:, par:par + 1], self.TVA[:, par:par + 1], EPS, None,
                         op0=ALU.subtract)
        dv.tensor_scalar(self.VPc[:, par:par + 1], self.S2[:, par:par + 1], c1, None, op0=ALU.mult)
        dv.drain()
        ins = dv.tensor_tensor(self.VPc[:, par:par + 1], self.VPc[:, par:par + 1],
                               self.TVA[:, par:par + 1], op=ALU.subtract)
        t_po = self.inc("dv", ins)
        self.w(ac, "dv", t_po)
        ac.activation(self.RSTc[:, par:par + 1], self.VPc[:, par:par + 1], AF.Ln)
        ac.drain()
        ins = ac.activation(self.RSTc[:, par:par + 1], self.RSTc[:, par:par + 1], AF.Exp, scale=-0.5)
        t_rst = self.inc("ac", ins)
        return (self.MUc[:, par:par + 1], self.RSTc[:, par:par + 1], t_rst)

    def h_build(self, src_buf, gain_idx=None, cache_stats=False):
        pe, dv, ac = self.pe, self.dv, self.ac
        T, NT = self.T, self.NT
        for i in range(NT):
            src = src_buf[:, i * C:(i + 1) * C]
            t_dv = self.inc("dv", dv.reduce_sum(self.S1A[:, i:i + 1], src, axis=AX))
            ac.drain()
            t_ac = self.inc("ac", ac.activation(self.SQS[:], src, AF.Square,
                                                accum_out=self.S2A[:, i:i + 1]))
        c1 = 1.0 / C
        self.w(dv, "ac", t_ac)
        dv.drain()
        dv.tensor_scalar(self.MUCACHE[:], self.S1A[:], c1, None, op0=ALU.mult)
        dv.drain()
        dv.tensor_tensor(self.VPA[:], self.MUCACHE[:], self.MUCACHE[:], op=ALU.mult)
        dv.drain()
        dv.tensor_scalar(self.VPA[:], self.VPA[:], EPS, None, op0=ALU.subtract)
        dv.drain()
        dv.tensor_scalar(self.RSTCACHE[:], self.S2A[:], c1, None, op0=ALU.mult)
        dv.drain()
        ins = dv.tensor_tensor(self.VPA[:], self.RSTCACHE[:], self.VPA[:], op=ALU.subtract)
        t_vp = self.inc("dv", ins)
        self.w(ac, "dv", t_vp)
        ac.activation(self.RSTCACHE[:], self.VPA[:], AF.Ln)
        ac.drain()
        ins = ac.activation(self.RSTCACHE[:], self.RSTCACHE[:], AF.Exp, scale=-0.5)
        t_rst = self.inc("ac", ins)
        nx_cons = [0, 0]
        pt_free = self.ptb_free[4]
        self.w(dv, "ac", t_rst)
        for i in range(NT):
            par = i % 2
            src = src_buf[:, i * C:(i + 1) * C]
            self.w(dv, "pe", nx_cons[par])
            nx = self.AST[:, par * C:(par + 1) * C]
            ins = dv.tensor_scalar(nx, src, self.MUCACHE[:, i:i + 1], self.RSTCACHE[:, i:i + 1],
                                   op0=ALU.subtract, op1=ALU.mult)
            t_nx = self.inc("dv", ins)
            self.w(pe, "dv", t_nx)
            self.w(pe, "dv", pt_free)
            ptb = self.pt_bf16(4)
            for j in range(NCH):
                mm = pe.transpose(ptb[:, j * 128:(j + 1) * 128], nx[:, j * 128:(j + 1) * 128], self.IDB[:])
            t_tr = self.inc("pe", mm)
            nx_cons[par] = t_tr
            self.w(dv, "pe", t_tr)
            if gain_idx is None:
                out = self.HF[:].rearrange("p (j t) -> p j t", j=NCH)[:, :, i * 128:(i + 1) * 128]
                ins = dv.tensor_copy(out, ptb[:].rearrange("p (j q) -> p j q", j=NCH))
            else:
                for j in range(NCH):
                    ins = dv.tensor_scalar(self.HF[:, j * T + i * 128: j * T + (i + 1) * 128],
                                           ptb[:, j * 128:(j + 1) * 128],
                                           self.LNW[:, gain_idx * 8 + j: gain_idx * 8 + j + 1],
                                           None, op0=ALU.mult)
            pt_free = self.inc("dv", ins)
        self.ptb_free[4] = pt_free

    def kv_build(self, mha, half):
        pe, dv, po = self.pe, self.dv, self.po
        T, NG = self.T, self.NG
        wslot_free = [0, 0, 0, 0]
        wtok = [0, 0, 0, 0]
        kvbank_free = [0, 0, 0, 0]
        pt_free = self.ptb_free[4]
        vstage_cons = [0, 0]
        nblk = (T // 128) * 4
        ones_ap = self.VT().rearrange("p (b c) -> p b c", c=130)[:, :, 64:130:65]
        ins = dv.memset(ones_ap, 1.0)
        t_ones = self.inc("dv", ins)
        sidx = 0
        for p in range(4):
            pair = half * 4 + p
            streams = (("q", 0), ("k", C), ("v", 2 * C))
            sl_of = {}
            for which, cbase in streams:
                sl = sidx % 4
                sidx += 1
                sl_of[which] = sl
                col0 = cbase + pair * 128
                src = self.d_wqkv[mha][:, col0:col0 + 128].rearrange("(j p) f -> p j f", p=128)
                self.w(po, "pe", wslot_free[sl])
                dst = self.WSQ[:, sl * 1024:(sl + 1) * 1024].rearrange("p (j f) -> p j f", j=8)
                wtok[sl] = self.inc("wa%d" % sl, po.dma_start(dst, src), 16)
            for which, dstbuf, scale in (("k", self.KF(), None), ("q", self.QH(), 0.125)):
                sl = sl_of[which]
                self.w(pe, "wa%d" % sl, wtok[sl])
                for g in range(NG):
                    bank = g % 2
                    self.w(pe, "dv", kvbank_free[bank])
                    for j in range(NCH):
                        mm = pe.matmul(self.psb(bank),
                                       self.WSQ[:, sl * 1024 + j * 128: sl * 1024 + (j + 1) * 128],
                                       self.HF[:, j * T + g * 512: j * T + (g + 1) * 512],
                                       start=(j == 0), stop=(j == NCH - 1))
                    t_mm = self.inc("pe", mm)
                    if g == NG - 1:
                        wslot_free[sl] = t_mm
                    self.w(dv, "pe", t_mm)
                    dsl = dstbuf[:, p * T + g * 512: p * T + (g + 1) * 512]
                    if scale is None:
                        ins = dv.tensor_copy(dsl, self.psb(bank))
                    else:
                        ins = dv.tensor_scalar(dsl, self.psb(bank), scale, None, op0=ALU.mult)
                    kvbank_free[bank] = self.inc("dv", ins)
            sl = sl_of["v"]
            self.w(pe, "wa%d" % sl, wtok[sl])
            for g in range(NG):
                bank = 2 + g % 2
                self.w(pe, "dv", kvbank_free[bank])
                for j in range(NCH):
                    mm = pe.matmul(self.psb(bank),
                                   self.WSQ[:, sl * 1024 + j * 128: sl * 1024 + (j + 1) * 128],
                                   self.HF[:, j * T + g * 512: j * T + (g + 1) * 512],
                                   start=(j == 0), stop=(j == NCH - 1))
                t_mm = self.inc("pe", mm)
                if g == NG - 1:
                    wslot_free[sl] = t_mm
                vsl = g % 2
                self.w(dv, "pe", t_mm)
                self.w(dv, "pe", vstage_cons[vsl])
                vst = self.AST[:, vsl * 512:(vsl + 1) * 512]
                ins = dv.tensor_copy(vst, self.psb(bank))
                t_ev = self.inc("dv", ins)
                kvbank_free[bank] = t_ev
                self.w(pe, "dv", t_ev)
                self.w(pe, "dv", pt_free)
                self.w(pe, "dv", t_ones)
                ptb = self.pt_bf16(4)
                for kk in range(4):
                    mm = pe.transpose(ptb[:, kk * 128:(kk + 1) * 128], vst[:, kk * 128:(kk + 1) * 128],
                                      self.IDB[:])
                t_tr = self.inc("pe", mm)
                vstage_cons[vsl] = t_tr
                self.w(dv, "pe", t_tr)
                vtr = self.VT().rearrange("p (b c) -> p b c", c=130)
                out = vtr[:, g * 16 + p: g * 16 + 13 + p: 4, 0:129]
                out = out.rearrange("p b (h d) -> p b h d", d=64) if False else \
                    vtr[:, g * 16 + p: g * 16 + 13 + p: 4, :].rearrange(
                        "p b c -> p b c")[:, :, 0:130]
                base = (g * 16 + p) * 130
                out = bass.AP(self.VT().tensor, self.VT().offset + base,
                              [list(self.VT().ap[0]), [4 * 130, 4], [65, 2], [1, 64]])
                ins = dv.tensor_copy(out, ptb[:, 0:512].rearrange("p (k h d) -> p k h d", k=4, h=2))
                pt_free = self.inc("dv", ins)
        self.ptb_free[4] = pt_free

    def emit_qknorms(self, half):
        pe, dv, ac = self.pe, self.dv, self.ac
        T = self.T
        nck = T // 512
        t_red = 0
        t_mm = 0
        for src_i, buf, dst in ((0, self.KF(), self.KN2A), (1, self.QH(), self.QN2A)):
            for wave in range(2):
                sq = self.HPQ[:, 0: 2 * T]
                t_sq = 0
                for pw in range(2):
                    p = wave * 2 + pw
                    self.w(ac, "pe", t_mm)
                    ins = ac.activation(sq[:, pw * T:(pw + 1) * T],
                                        buf[:, p * T:(p + 1) * T], AF.Square)
                    t_sq = self.inc("ac", ins)
                self.w(pe, "ac", t_sq)
                self.w(pe, "dv", t_red)
                for pw in range(2):
                    for h in range(2):
                        r = h * 64
                        for cc in range(nck):
                            mm = pe.matmul(
                                self.PS[r:r + 1, pw * 2048 + cc * 512: pw * 2048 + (cc + 1) * 512],
                                self.ONES[r:r + 64, 0:1],
                                sq[r:r + 64, pw * T + cc * 512: pw * T + (cc + 1) * 512],
                                start=True, stop=True)
                        t_mm = self.inc("pe", mm)
                for pw in range(2):
                    p = wave * 2 + pw
                    for h in range(2):
                        r = h * 64
                        self.w(dv, "pe", t_mm)
                        dv.drain()
                        ins = dv.reduce_max(dst[r:r + 1, p:p + 1],
                                            self.PS[r:r + 1, pw * 2048: pw * 2048 + T], axis=AX)
                        t_red = self.inc("dv", ins)
        self.w(dv, "dv", t_red)
        dv.drain()
        for h in range(2):
            r = h * 64
            dv.tensor_scalar(self.NMH[r:r + 1, 0:4], self.QN2A[r:r + 1, 0:4], -4.08, None, op0=ALU.mult)
            dv.tensor_scalar(self.NMH[r:r + 1, 4:8], self.KN2A[r:r + 1, 0:4], -1.02 / 16.0,
                             None, op0=ALU.mult)
        dv.drain()
        for h in range(2):
            r = h * 64
            dv.tensor_tensor(self.NMH[r:r + 1, 0:4], self.NMH[r:r + 1, 0:4],
                             self.NMH[r:r + 1, 4:8], op=ALU.add)
        dv.drain()
        for h in range(2):
            r = h * 64
            ins = dv.tensor_copy(self.NMHB[r:r + 1, h * 4: h * 4 + 4], self.NMH[r:r + 1, 0:4])
        t_nb = self.inc("dv", ins)
        self.w(pe, "dv", t_nb)
        for h in range(2):
            r = h * 64
            mm = pe.matmul(self.PS[:, 7 * 512 + h * 4: 7 * 512 + h * 4 + 4],
                           self.ONES[r:r + 1, 0:128], self.NMHB[r:r + 1, h * 4: h * 4 + 4],
                           start=True, stop=True)
        t_mm2 = self.inc("pe", mm)
        self.w(dv, "pe", t_mm2)
        ins = dv.tensor_copy(self.BND[:, 0:8], self.PS[:, 7 * 512: 7 * 512 + 8])
        t_bc = self.inc("dv", ins)
        self.bnd_tok = t_bc
        self.kn_guard = t_bc

    def attention_half(self, mha, half):
        pe, dv, ac = self.pe, self.dv, self.ac
        T, NG = self.T, self.NG
        for qg in range(NG):
            for p in range(4):
                gp = p % 2
                nj = 4 * qg + 4
                npair = nj // 2
                avtok = [0, 0]
                exp_of = {}

                def emit_ST(jp):
                    toks = [0, 0]
                    for jj in range(2):
                        j = jp * 2 + jj
                        for h in range(2):
                            bank = h * 2 + jj
                            self.w(pe, "ac", self.stb_free[bank])
                            if j == 0:
                                self.w(pe, "dv", self.obank_free[h * 2 + gp])
                                self.w(pe, "dv", self.kn_guard)
                            lhs = self.KF()[h * 64:(h + 1) * 64,
                                            p * T + j * 128: p * T + (j + 1) * 128]
                            rhs = self.QH()[h * 64:(h + 1) * 64,
                                            p * T + qg * 512: p * T + (qg + 1) * 512]
                            if j >= 4 * qg:
                                r = j - 4 * qg
                                pe.matmul(self.psb(bank), lhs, rhs, start=True, stop=False)
                                mm = pe.matmul(self.psb(bank), self.IDB[:],
                                               self.MK[:, r * 512:(r + 1) * 512],
                                               start=False, stop=True)
                            else:
                                mm = pe.matmul(self.psb(bank), lhs, rhs, start=True, stop=True)
                            toks[h] = self.inc("pe", mm)
                    sp = jp % 2
                    for h in range(2):
                        slot = h * 2 + sp
                        self.w(ac, "pe", toks[h])
                        self.w(ac, "pe", self.ptslot_free[slot])
                        ins = ac.activation(self.PTb(slot), self.PS[:, (h * 2) * 512:(h * 2) * 512 + 1024],
                                            AF.Exp, bias=self.BND[:, h * 4 + p: h * 4 + p + 1])
                        etok = self.inc("ac", ins)
                        self.stb_free[h * 2] = etok
                        self.stb_free[h * 2 + 1] = etok
                        exp_of[(jp, h)] = etok

                def emit_AV(jp):
                    sp = jp % 2
                    for jj in range(2):
                        j = jp * 2 + jj
                        for h in range(2):
                            slot = h * 2 + sp
                            obank = 4 + h * 2 + gp
                            self.w(pe, "ac", exp_of[(jp, h)])
                            blk = (j * 4 + p) * 130
                            mm = pe.matmul(self.PS[0:65, obank * 512: obank * 512 + 512],
                                           self.VT()[:, blk + h * 65: blk + (h + 1) * 65],
                                           self.PTb(slot)[:, jj * 512:(jj + 1) * 512],
                                           start=(j == 0), stop=(j == nj - 1))
                            avtok[h] = self.inc("pe", mm)
                            if jj == 1:
                                self.ptslot_free[slot] = avtok[h]

                for jp in range(npair):
                    emit_ST(jp)
                    self.step_rt()
                    if jp >= 1:
                        emit_AV(jp - 1)
                        self.step_rt()
                emit_AV(npair - 1)
                self.flush_rt()
                for h in range(2):
                    self.queue_roundtrip(half, p, qg, h, gp, avtok[h])
        self.flush_rt()

    def queue_roundtrip(self, half, p, qg, h, gp, t_av):
        pe, dv = self.pe, self.dv
        T = self.T
        obank = 4 + h * 2 + gp
        pair = half * 4 + p
        st = {}

        def s0():
            self.w(dv, "pe", t_av)
            self.w(dv, "pe", self.rt_osg_free)
            ins = dv.tensor_copy(self.OSG()[0:65, 0:512],
                                 self.PS[0:65, obank * 512: obank * 512 + 512])
            st["ev"] = self.inc("dv", ins)

        def s1():
            self.w(pe, "dv", st["ev"])
            for b in range(4):
                mm = pe.transpose(self.PS[:, obank * 512 + b * 65: obank * 512 + (b + 1) * 65],
                                  self.OSG()[0:65, b * 128:(b + 1) * 128], self.IDF[0:65, 0:65])
            st["tr"] = self.inc("pe", mm)
            self.rt_osg_free = st["tr"]

        def s2():
            self.w(dv, "pe", st["tr"])
            self.w(dv, "pe", self.rt_otb_free)
            dv.tensor_copy(self.OTM()[:, 0:260], self.PS[:, obank * 512: obank * 512 + 260])
            dv.drain()
            linv_src = bass.AP(self.OTM().tensor, self.OTM().offset + 64,
                               [list(self.OTM().ap[0]), [65, 4], [1, 1]])
            dv.reciprocal(self.OL4[:, 0:4], linv_src)
            dv.drain()
            for b in range(4):
                ins = dv.tensor_scalar(self.OTB()[:, b * 64:(b + 1) * 64],
                                       self.OTM()[:, b * 65: b * 65 + 64],
                                       self.OL4[:, b:b + 1], None, op0=ALU.mult)
            st["nb"] = self.inc("dv", ins)

        def s3():
            self.w(pe, "dv", st["nb"])
            ptb = self.pt_bf16(obank)
            for b in range(4):
                mm = pe.transpose(ptb[0:64, b * 128:(b + 1) * 128],
                                  self.OTB()[:, b * 64:(b + 1) * 64], self.IDB[:])
            st["tb"] = self.inc("pe", mm)
            self.rt_otb_free = st["tb"]

        def s4():
            self.w(dv, "pe", st["tb"])
            ins = dv.tensor_copy(
                self.OF[h * 64:(h + 1) * 64, pair * T + qg * 512: pair * T + (qg + 1) * 512],
                ptb_dummy[0] if False else self.pt_bf16(obank)[0:64, 0:512])
            self.obank_free[h * 2 + gp] = self.inc("dv", ins)

        self.rt_queue.extend([s0, s1, s2, s3, s4])

    def step_rt(self):
        if self.rt_queue:
            self.rt_queue.pop(0)()

    def flush_rt(self):
        while self.rt_queue:
            self.rt_queue.pop(0)()

    def proj_phase(self, mha, is_inner):
        pe, dv, po = self.pe, self.dv, self.po
        T, NG = self.T, self.NG
        wslot_free = [0, 0]
        wtok = [0, 0]
        pbank_free = [0, 0]
        pt_free = self.ptb_free[4]
        afm_cons = 0
        atm_free = [0, 0]
        for g in range(NG):
            evtoks = []
            for ct in range(8):
                sl = ct % 2
                src = self.d_wp[mha][:, ct * 128:(ct + 1) * 128].rearrange("(j p) f -> p j f", p=128)
                self.w(po, "pe", wslot_free[sl])
                dst = self.WSQ[:, sl * 1024:(sl + 1) * 1024].rearrange("p (j f) -> p j f", j=8)
                wtok[sl] = self.inc("wa%d" % sl, po.dma_start(dst, src), 16)
                self.w(pe, "wa%d" % sl, wtok[sl])
                bank = ct % 2
                self.w(pe, "dv", pbank_free[bank])
                if ct == 0:
                    self.w(pe, "pe", afm_cons)
                for j in range(NCH):
                    mm = pe.matmul(self.psb(bank),
                                   self.WSQ[:, sl * 1024 + j * 128: sl * 1024 + (j + 1) * 128],
                                   self.OF[:, j * T + g * 512: j * T + (g + 1) * 512],
                                   start=(j == 0), stop=(j == NCH - 1))
                t_mm = self.inc("pe", mm)
                wslot_free[sl] = t_mm
                self.w(dv, "pe", t_mm)
                self.w(dv, "pe", afm_cons)
                ins = dv.tensor_copy(self.AST[:, ct * 512:(ct + 1) * 512], self.psb(bank))
                pbank_free[bank] = self.inc("dv", ins)
                evtoks.append(pbank_free[bank])
            afm_tok = max(evtoks)
            for tt in range(4):
                tslot = tt % 2
                self.w(pe, "dv", afm_tok)
                self.w(pe, "dv", pt_free)
                self.w(pe, "dv", atm_free[tslot])
                ptb = self.pt_bf16(4)
                for ct in range(8):
                    mm = pe.transpose(ptb[:, ct * 128:(ct + 1) * 128],
                                      self.AST[:, ct * 512 + tt * 128: ct * 512 + (tt + 1) * 128],
                                      self.IDB[:])
                t_tr = self.inc("pe", mm)
                if tt == 3:
                    afm_cons = t_tr
                self.w(dv, "pe", t_tr)
                atm = self.ATM[:, tslot * 1024:(tslot + 1) * 1024]
                ins = dv.tensor_copy(atm, ptb[:])
                pt_free = self.inc("dv", ins)
                ti = g * 4 + tt
                xr = self.XR[:, ti * C:(ti + 1) * C]
                if not is_inner:
                    par = tt % 2
                    mu, rstd, t_rst = self.ln_stats(atm, par, src_dv_tok=pt_free)
                    self.w(dv, "ac", t_rst)
                    dv.tensor_scalar(atm, atm, mu, rstd, op0=ALU.subtract, op1=ALU.mult)
                    dv.drain()
                    ins = dv.tensor_tensor(xr, xr, atm, op=ALU.add)
                    t_res = self.inc("dv", ins)
                    self.ln_free[par] = t_res
                    atm_free[tslot] = t_res
                else:
                    oslot = 1 - tslot
                    self.w(dv, "pe", 0)
                    self.w(dv, "dv", 0)
                    other = self.ATM[:, oslot * 1024:(oslot + 1) * 1024]
                    self.w(dv, "dv", atm_free[oslot]) if False else None
                    dv.tensor_scalar(other, xr, self.MUCACHE[:, ti:ti + 1],
                                     self.RSTCACHE[:, ti:ti + 1], op0=ALU.subtract, op1=ALU.mult)
                    dv.drain()
                    uu = self.UU[:, ti * C:(ti + 1) * C]
                    ins = dv.tensor_tensor(uu, other, atm, op=ALU.add)
                    t_res = self.inc("dv", ins)
                    atm_free[tslot] = t_res
                    atm_free[oslot] = t_res
        self.ptb_free[4] = pt_free

    def mha_slot(self, seq, mha):
        is_inner = (mha == 1)
        self.h_build(self.XR, gain_idx=(0 if is_inner else None), cache_stats=is_inner)
        self.barrier()
        if self.ck("h%d" % mha):
            self.dump("h", self.HF[:])
            raise _DebugStop()
        for half in range(2):
            self.kv_build(mha, half)
            self.barrier()
            self.emit_qknorms(half)
            if half == 0 and self.ck("kv%d" % mha):
                self.dump("kf", self.KF()[:])
                self.dump("vt", self.VT()[:])
                raise _DebugStop()
            self.attention_half(mha, half)
            self.barrier()
        if self.ck("of%d" % mha):
            self.dump("of", self.OF[:, 0:8 * self.T])
            raise _DebugStop()
        self.proj_phase(mha, is_inner)
        self.barrier()

    def router_phase(self):
        pe, dv, po, ac = self.pe, self.dv, self.po, self.ac
        T, NG = self.T, self.NG
        lg_free = 0
        lgt_free = [0, 0]
        ps5_free = 0
        wt_tok = 0
        for g in range(NG):
            for j in range(NCH):
                mm = pe.matmul(self.PS[0:8, 7 * 512: 7 * 512 + 512],
                               self.RW[:, j * 8:(j + 1) * 8],
                               self.HF[:, j * T + g * 512: j * T + (g + 1) * 512],
                               start=(j == 0), stop=(j == NCH - 1))
            t_mm = self.inc("pe", mm)
            self.w(dv, "pe", t_mm)
            self.w(dv, "pe", lg_free)
            ins = dv.tensor_copy(self.LG[0:8, 0:512], self.PS[0:8, 7 * 512: 7 * 512 + 512])
            t_lg = self.inc("dv", ins)
            for tt in range(4):
                par = tt % 2
                self.w(pe, "dv", t_lg)
                self.w(pe, "dv", max(lgt_free[par], ps5_free))
                mm = pe.transpose(self.PS[:, 5 * 512: 5 * 512 + 8],
                                  self.LG[0:8, tt * 128:(tt + 1) * 128], self.IDF[0:8, 0:8])
                t_tr = self.inc("pe", mm)
                if tt == 3:
                    lg_free = t_tr
                self.w(dv, "pe", t_tr)
                ins = dv.tensor_copy(self.LGT[:, par * 8:(par + 1) * 8],
                                     self.PS[:, 5 * 512: 5 * 512 + 8])
                t_lgt = self.inc("dv", ins)
                ps5_free = t_lgt
                lgt = self.LGT[:, par * 8:(par + 1) * 8]
                dv.drain()
                dv.reduce_max(self.RM1N[:], lgt, axis=AX, negate=True)
                dv.drain()
                dv.tensor_scalar(self.RM1[:], self.RM1N[:], -1.0, None, op0=ALU.mult)
                dv.drain()
                dv.tensor_scalar(self.RMK1[:], lgt, self.RM1[:], None, op0=ALU.is_ge)
                dv.drain()
                dv.scalar_tensor_tensor(self.RL2[:], self.RMK1[:], -1e30, lgt,
                                        op0=ALU.mult, op1=ALU.add)
                dv.drain()
                ins = dv.reduce_max(self.RM2[:], self.RL2[:], axis=AX)
                t_m2 = self.inc("dv", ins)
                self.w(ac, "dv", t_m2)
                ins = ac.activation(self.RE2[:], self.RM2[:], AF.Exp, bias=self.RM1N[:])
                t_e2 = self.inc("ac", ins)
                self.w(dv, "ac", t_e2)
                dv.tensor_scalar(self.RDEN[:], self.RE2[:], 1.0, None, op0=ALU.add)
                dv.drain()
                dv.tensor_scalar(self.RMK2[:], self.RL2[:], self.RM2[:], None, op0=ALU.is_ge)
                dv.drain()
                ti = g * 4 + tt
                dv.scalar_tensor_tensor(self.RMK2[:], self.RMK2[:], self.RE2[:], self.RMK1[:],
                                        op0=ALU.mult, op1=ALU.add)
                dv.reciprocal(self.RW1[:], self.RDEN[:])
                dv.drain()
                ins = dv.tensor_scalar(self.WTOK[:, ti * 8:(ti + 1) * 8], self.RMK2[:],
                                       self.RW1[:], None, op0=ALU.mult)
                wt_tok = self.inc("dv", ins)
                lgt_free[par] = wt_tok
        self.wtok_tok = wt_tok

    def moe_phase(self, seq):
        pe, dv, ac, po = self.pe, self.dv, self.ac, self.po
        T = self.T
        tgW = T // 2
        self.tgW = tgW
        self.WBv = self.AST[:, 0:tgW]
        nsub = max(1, tgW // 512)
        subW = min(512, tgW)
        nh = max(1, tgW // 512)
        self.w(self.sy, "pe", self.cnt["pe"])
        self.w(self.sy, "dv", self.cnt["dv"])
        self.w(self.sy, "ac", self.cnt["ac"])
        for tg in range(2):
            tgbase = tg * tgW
            tgtile0 = tg * (tgW // 128)
            gel_tok = [0, 0]
            scl_tok = [0] * 8
            w1_tok = [0, 0]
            w1_free = [0, 0]
            w2_free = 0
            w2_tok_cur = 0
            w2_tok_prev = 0
            scl_prev = [0] * 8
            yevac_tok = [0] * (2 * nsub)
            y_last_evac = 0
            self._ybank_tok = [0, 0, 0, 0]
            wb_tok = 0
            hpq_guard = 0

            def h_mms(fc, oct_):
                bank0 = (fc % 2) * 2
                sl = fc // 4
                self.w(pe, "w1a" if sl == 0 else "w1b", w1_tok[sl])
                self.w(pe, "ac", gel_tok[fc % 2])
                if fc == 0:
                    self.w(pe, "dv", wb_tok)
                for j in range(NCH):
                    for hh in range(nh):
                        mm = pe.matmul(self.PS[:, (bank0 + hh) * 512:(bank0 + hh) * 512 + subW],
                                       self.W1Q(sl)[:, j * 512 + (fc % 4) * 128: j * 512 + (fc % 4) * 128 + 128],
                                       self.HF[:, j * T + tgbase + hh * subW: j * T + tgbase + (hh + 1) * subW],
                                       start=(j == 0), stop=(j == NCH - 1))
                tok = self.inc("pe", mm)
                return tok

            def gelu_scale(fc, t_h):
                bank0 = (fc % 2) * 2
                self.w(ac, "pe", t_h)
                self.w(ac, "dv", scl_prev[fc])
                self.w(ac, "pe", hpq_guard)
                ins = ac.activation(self.HPQ[:, fc * tgW:(fc + 1) * tgW],
                                    self.PS[:, bank0 * 512: bank0 * 512 + tgW], AF.Gelu)
                t_g = self.inc("ac", ins)
                gel_tok[fc % 2] = t_g
                self.w(dv, "ac", t_g)
                ins = dv.tensor_tensor(self.HPQ[:, fc * tgW:(fc + 1) * tgW],
                                       self.HPQ[:, fc * tgW:(fc + 1) * tgW],
                                       self.WBv, op=ALU.mult)
                scl_tok[fc] = self.inc("dv", ins)

            def w2_combos(first):
                nonlocal w2_free, y_last_evac
                dv.drain()
                ybank_tok = list(self._ybank_tok)
                for ci, (ch, sub) in enumerate([(a, b) for a in range(2) for b in range(nsub)]):
                    self.w(pe, "w2o", w2_tok_prev)
                    mm_tok = [0, 0, 0, 0]
                    for fc in range(8):
                        self.w(pe, "dv", scl_prev[fc])
                        for cti in range(4):
                            if fc == 0:
                                self.w(pe, "dv", ybank_tok[cti])
                            cto = (ch * 4 + cti) * 128
                            mm = pe.matmul(self.PS[:, (4 + cti) * 512:(4 + cti) * 512 + subW],
                                           self.W2O[:, fc * 1024 + cto: fc * 1024 + cto + 128],
                                           self.HPQ[:, fc * tgW + sub * subW: fc * tgW + (sub + 1) * subW],
                                           start=(fc == 0), stop=(fc == 7))
                            if fc == 7:
                                mm_tok[cti] = self.inc("pe", mm)
                    w2_free = mm_tok[3]
                    for cti in range(4):
                        self.w(dv, "pe", mm_tok[cti])
                        ysl = self.BIGB[:, (ch * 4 + cti) * tgW + sub * subW:
                                        (ch * 4 + cti) * tgW + (sub + 1) * subW]
                        ps_y = self.PS[:, (4 + cti) * 512:(4 + cti) * 512 + subW]
                        if first:
                            ins = dv.tensor_scalar(ysl, ps_y, 1.0, None, op0=ALU.mult)
                        else:
                            ins = dv.scalar_tensor_tensor(ysl, ps_y, 1.0, ysl, op0=ALU.mult, op1=ALU.add)
                        ybank_tok[cti] = self.inc("dv", ins)
                        y_last_evac = ybank_tok[cti]
                self._ybank_tok = ybank_tok

            for oct_ in range(16):
                if oct_ % 2 == 0:
                    wb_tok = self.wb_build(tgtile0, oct_ // 2, gel_tok, scl_tok)
                for q in range(2):
                    col0 = oct_ * 1024 + q * 512
                    src = self.d_w1c[:, col0:col0 + 512].rearrange("(j p) f -> p j f", p=128)
                    self.w(self.sy, "pe", w1_free[q])
                    if oct_ == 0 and tg == 0:
                        self.w(self.sy, "wc", self.wc_tok)
                    dst = self.W1Q(q)[:].rearrange("p (j f) -> p j f", j=8)
                    w1_tok[q] = self.inc("w1a" if q == 0 else "w1b", self.sy.dma_start(dst, src), 16)
                t_h0 = h_mms(0, oct_)
                t_h1 = h_mms(1, oct_)
                if oct_ > 0:
                    w2_combos(first=(oct_ == 1))
                hpq_guard = w2_free
                self.w(self.sy, "pe", w2_free)
                src = self.d_w2c[oct_ * 1024:(oct_ + 1) * 1024, :].rearrange("(j p) c -> p j c", p=128)
                dst = self.W2O[:].rearrange("p (j c) -> p j c", j=8)
                w2_tok_cur = self.inc("w2o", self.sy.dma_start(dst, src), 16)
                gelu_scale(0, t_h0)
                gelu_scale(1, t_h1)
                for fc in range(2, 8):
                    t_h = h_mms(fc, oct_)
                    gelu_scale(fc, t_h)
                w1_free = [self.cnt["pe"], self.cnt["pe"]]
                w2_tok_prev = w2_tok_cur
                scl_prev = list(scl_tok)
            w2_combos(first=False)
            self.barrier()
            if self.ck("y") and tg == 0:
                self.dump("y", self.BIGB[:, 0:4 * self.T])
                self.dump("wtok", self.WTOK[:])
                self.dump("t2d", self.HF[:])
                raise _DebugStop()
            self.finale(seq, tg, tgW)
            self.barrier()

    def wb_build(self, tgtile0, expert, gel_tok, scl_tok):
        pe, dv = self.pe, self.dv
        ntile = self.tgW // 128
        dv.drain()
        self.w(dv, "pe", self.cnt["pe"])
        for tt in range(ntile):
            ti = tgtile0 + tt
            ins = dv.tensor_scalar(self.AST[:, 2048 + tt * 128: 2048 + (tt + 1) * 128],
                                   self.ONES[:], self.WTOK[:, ti * 8 + expert: ti * 8 + expert + 1],
                                   None, op0=ALU.mult)
        t_stage = self.inc("dv", ins)
        self.w(pe, "dv", t_stage)
        self.w(pe, "ac", gel_tok[0])
        ptb = self.pt_bf16(0)
        for tt in range(ntile):
            mm = pe.transpose(ptb[:, tt * 128:(tt + 1) * 128],
                              self.AST[:, 2048 + tt * 128: 2048 + (tt + 1) * 128], self.IDB[:])
        t_tr = self.inc("pe", mm)
        self.w(dv, "pe", t_tr)
        ins = dv.tensor_copy(self.WBv, ptb[:, 0:self.tgW])
        return self.inc("dv", ins)

    def finale(self, seq, tg, tgW):
        pe, dv, sy = self.pe, self.dv, self.sy
        ntile = tgW // 128
        astf = self.AST[:].bitcast(F32)
        atmf = self.ATM[:].bitcast(F32)
        outslot = [astf[:, 1024:2048], atmf[:, 0:1024]]
        out_tok = [0, 0]
        f_cons = 0
        tr_free = 0
        for tt in range(ntile):
            ti = tg * ntile + tt
            par = tt % 2
            self.w(pe, "dv", tr_free)
            for ct in range(8):
                bank = ct // 4
                mm = pe.transpose(self.PS[:, bank * 512 + (ct % 4) * 128: bank * 512 + (ct % 4) * 128 + 128],
                                  self.BIGB[:, ct * tgW + tt * 128: ct * tgW + (tt + 1) * 128],
                                  self.IDF[:])
            t_tr = self.inc("pe", mm)
            self.w(dv, "pe", t_tr)
            self.w(dv, "dv", 0)
            ftmp = astf[:, 0:1024]
            ins = dv.scalar_tensor_tensor(ftmp, self.PS[:, 0:1024], 1.0,
                                          self.UU[:, ti * C:(ti + 1) * C], op0=ALU.mult, op1=ALU.add)
            t_f = self.inc("dv", ins)
            tr_free = t_f
            mu, rstd, t_rst = self.ln_stats(ftmp, par, src_dv_tok=t_f)
            self.w(dv, "ac", t_rst)
            self.w(dv, "qo%d" % par, out_tok[par])
            nrm = outslot[par]
            dv.tensor_scalar(nrm, ftmp, mu, rstd, op0=ALU.subtract, op1=ALU.mult)
            dv.drain()
            ins = dv.tensor_tensor(nrm, nrm, self.XR[:, ti * C:(ti + 1) * C], op=ALU.add)
            t_o = self.inc("dv", ins)
            self.ln_free[par] = t_o
            self.w(sy, "dv", t_o)
            dst = self.d_out[seq, ti * 128:(ti + 1) * 128, :]
            out_tok[par] = self.inc("qo%d" % par, sy.dma_start(dst, nrm), 16)

    def dump(self, name, ap):
        w = ap.shape[-1]
        npart = ap.shape[0]
        d = self.nc.dram_tensor("dbg_" + name, [npart, w], F32, kind="ExternalOutput").ap()
        self.barrier()
        for s in self.sem:
            if s != "qdbg" and self.cnt[s]:
                self.po.wait_ge(self.sem[s], self.cnt[s])
        self.inc("qdbg", self.po.dma_start(d, ap), 16)

    def ck(self, phase):
        if self.debug_phase is None:
            return False
        return phase == self.debug_phase

    def build(self, debug_phase=None):
        self.debug_phase = debug_phase

        def fin():
            for s in ("qo0", "qo1", "qdbg"):
                if self.cnt[s]:
                    self.po.wait_ge(self.sem[s], self.cnt[s])
                    self.sy.wait_ge(self.sem[s], self.cnt[s])
            return self.nc

        self.load_consts()
        try:
            self._main()
        except _DebugStop:
            pass
        return fin()

    def _main(self):
        for seq in range(2):
            if seq == 1:
                self.barrier()
            xt = self.load_x(seq)
            if seq == 0:
                self.prep_wcache()
            for eng in (self.dv, self.pe, self.ac):
                self.w(eng, "qx", xt)
            self.w(self.dv, "dv", self.const_tok)
            if self.ck("x"):
                self.dump("x", self.XR[:])
                raise _DebugStop()
            self.mha_slot(seq, 0)
            if self.ck("mha0"):
                self.dump("x1", self.XR[:])
                raise _DebugStop()
            self.mha_slot(seq, 1)
            if self.ck("mha1"):
                self.dump("x1", self.XR[:])
                self.dump("u", self.UU[:, 0:self.NT * C])
                raise _DebugStop()
            self.h_build(self.UU, gain_idx=1, cache_stats=False)
            self.barrier()
            if self.ck("t2"):
                self.dump("t2", self.HF[:])
                raise _DebugStop()
            self.router_phase()
            self.barrier()
            if self.ck("router"):
                self.dump("wtok", self.WTOK[:])
                raise _DebugStop()
            self.moe_phase(seq)
            if self.ck("y"):
                raise _DebugStop()


def build_program(T=2048):
    return KB(T).build()


def make_masks():
    kk = np.arange(128)[:, None]
    qq = np.arange(512)[None, :]
    return np.stack([np.where(qq >= 128 * r + kk, 0.0, -1e30) for r in range(4)]).astype(np.float32)


def kernel(L0_attn_x, L0_attn_W_attn, L0_attn_W_proj, L0_ffn_ln1_w, L0_ffn_c_attn_w,
           L0_ffn_c_proj_w, L0_ffn_ln2_w, L0_ffn_router_w, L0_ffn_w1, L0_ffn_w2):
    x = np.ascontiguousarray(np.asarray(L0_attn_x, dtype=np.float32))
    B, T, _ = x.shape
    n_cores = 8
    per = B // n_cores
    nc = build_program(T)
    consts = {
        "wqkv0": np.ascontiguousarray(L0_attn_W_attn, dtype=np.float32),
        "wqkv1": np.ascontiguousarray(L0_ffn_c_attn_w, dtype=np.float32),
        "wp0": np.ascontiguousarray(L0_attn_W_proj, dtype=np.float32),
        "wp1": np.ascontiguousarray(L0_ffn_c_proj_w, dtype=np.float32),
        "lnw": np.ascontiguousarray(
            np.stack([np.asarray(L0_ffn_ln1_w, dtype=np.float32).reshape(8, 128),
                      np.asarray(L0_ffn_ln2_w, dtype=np.float32).reshape(8, 128)])),
        "rwT": np.ascontiguousarray(np.asarray(L0_ffn_router_w, dtype=np.float32).T),
        "w1": np.ascontiguousarray(L0_ffn_w1, dtype=np.float32),
        "w2": np.ascontiguousarray(L0_ffn_w2, dtype=np.float32),
        "ident": np.eye(128, dtype=np.float32),
        "masks": make_masks(),
        "ones": np.ones((128, 128), dtype=np.float32),
    }
    in_maps = []
    for c in range(n_cores):
        m = dict(consts)
        m["x"] = np.ascontiguousarray(x[c * per:(c + 1) * per])
        in_maps.append(m)
    res = run_bass_kernel_spmd(nc, in_maps, core_ids=list(range(n_cores)))
    return np.concatenate([np.asarray(r["out"], dtype=np.float32) for r in res.results], axis=0)


# version 12
# speedup vs baseline: 1.2924x; 1.0341x over previous
import numpy as np
import concourse.bass as bass
import concourse.mybir as mybir
from concourse.bass_utils import run_bass_kernel_spmd
from contextlib import ExitStack

F32 = mybir.dt.float32
BF16 = mybir.dt.bfloat16
AF = mybir.ActivationFunctionType
ALU = mybir.AluOpType
AX = mybir.AxisListType.X

C = 1024
NCH = 8
EPS = 1e-5
FF = 16384


class _DebugStop(Exception):
    pass


class KB:
    def __init__(self, T):
        self.T = T
        self.NT = T // 128
        self.NG = T // 512
        nc = bass.Bass("TRN2", target_bir_lowering=False)
        self.nc = nc
        self.pe, self.dv, self.ac, self.po, self.sy = nc.tensor, nc.vector, nc.scalar, nc.gpsimd, nc.sync
        self.st = ExitStack()
        self.cnt = {}
        self.sem = {}
        NT = self.NT

        dram = lambda n, shp, k: nc.dram_tensor(n, shp, F32, kind=k).ap()
        self.d_x = dram("x", [2, T, C], "ExternalInput")
        self.d_wqkv = [dram("wqkv0", [C, 3 * C], "ExternalInput"),
                       dram("wqkv1", [C, 3 * C], "ExternalInput")]
        self.d_wp = [dram("wp0", [C, C], "ExternalInput"),
                     dram("wp1", [C, C], "ExternalInput")]
        self.d_ln = dram("lnw", [2, 8, 128], "ExternalInput")
        self.d_rwT = dram("rwT", [8, C], "ExternalInput")
        self.d_w1 = dram("w1", [C, FF], "ExternalInput")
        self.d_w2 = dram("w2", [FF, C], "ExternalInput")
        self.d_id = dram("ident", [128, 128], "ExternalInput")
        self.d_mk = dram("masks", [4, 128, 512], "ExternalInput")
        self.d_ones = dram("ones", [128, 128], "ExternalInput")
        self.d_out = dram("out", [2, T, C], "ExternalOutput")
        self.d_w1c = nc.dram_tensor("w1c", [C, FF], BF16, kind="Internal").ap()
        self.d_w2c = nc.dram_tensor("w2c", [FF, C], BF16, kind="Internal").ap()
        self.d_wqkvc = [nc.dram_tensor("wqkv%dc" % m, [C, 3 * C], BF16, kind="Internal").ap()
                        for m in range(2)]
        self.d_wpc = [nc.dram_tensor("wp%dc" % m, [C, C], BF16, kind="Internal").ap()
                      for m in range(2)]

        sb = lambda n, shp, dt: self.st.enter_context(nc.sbuf_tensor(n, shp, dt))
        self.XR = sb("XR", [128, NT * C], BF16)
        self.HF = sb("HF", [128, NCH * T], BF16)
        self.BIGB = sb("BIGB", [128, 4 * T], F32)
        self.KVW = sb("KVW", [128, max(4 * T + (T // 128) * 4 * 130, 16384)], BF16)
        self.AR1 = sb("AR1", [128, max(8 * T, 4 * T + 8192)], BF16)
        self.AST = sb("AST", [128, 4096], BF16)
        self.HPQ = sb("HPQ", [128, 8192], BF16)
        self.ATM = sb("ATM", [128, 2048], BF16)
        self.SQS = sb("SQS", [128, C], BF16)
        self.MK = sb("MK", [128, 4 * 512], BF16)
        self.IDB = sb("IDB", [128, 128], BF16)
        self.IDF = sb("IDF", [128, 128], F32)
        self.ONES = sb("ONESB", [128, 128], BF16)
        self.LNW = sb("LNW", [128, 16], F32)
        self.RW = sb("RW", [128, 64], BF16)
        self.WTOK = sb("WTOK", [128, NT * 8], F32)
        self.WSQ = sb("WSQ", [128, 4 * 1024], BF16)
        self.LGT = sb("LGT", [128, 16], F32)
        self.S1 = sb("S1", [128, 2], F32)
        self.S2 = sb("S2", [128, 2], F32)
        self.MUc = sb("MUc", [128, 2], F32)
        self.VPc = sb("VPc", [128, 2], F32)
        self.TVA = sb("TVA", [128, 2], F32)
        self.RSTc = sb("RSTc", [128, 2], F32)
        self.MUCACHE = sb("MUCACHE", [128, NT], F32)
        self.RSTCACHE = sb("RSTCACHE", [128, NT], F32)
        self.S1A = sb("S1A", [128, NT], F32)
        self.S2A = sb("S2A", [128, NT], F32)
        self.VPA = sb("VPA", [128, NT], F32)
        self.NM = sb("NM", [128, 8], F32)
        self.KN2A = sb("KN2A", [128, 8], F32)
        self.QN2A = sb("QN2A", [128, 8], F32)
        self.OL4 = sb("OL4", [128, 4], F32)
        self.BND = sb("BND", [128, 8], F32)
        self.NKND = sb("NKND", [128, 8], F32)
        self.QN2 = sb("QN2", [128, 8], F32)
        self.NMH = sb("NMH", [128, 8], F32)
        self.NMHB = sb("NMHB", [128, 8], BF16)
        self.LL = sb("LL", [128, 8], F32)
        self.NGM = sb("NGM", [128, 4], F32)
        self.BB = sb("BB", [128, 8], F32)
        self.BL = sb("BL", [128, 8], F32)
        self.LT = sb("LT", [128, 4], F32)
        self.LIV = sb("LIV", [128, 4], F32)
        self.RM1 = sb("RM1", [128, 1], F32)
        self.RM1N = sb("RM1N", [128, 1], F32)
        self.RM2 = sb("RM2", [128, 1], F32)
        self.RE2 = sb("RE2", [128, 1], F32)
        self.RDEN = sb("RDEN", [128, 1], F32)
        self.RW1 = sb("RW1", [128, 1], F32)
        self.RW2 = sb("RW2", [128, 1], F32)
        self.RMK1 = sb("RMK1", [128, 8], F32)
        self.RL2 = sb("RL2", [128, 8], F32)
        self.RMK2 = sb("RMK2", [128, 8], F32)

        self.PS = self.st.enter_context(nc.psum_tensor("PS", [128, 4096], F32))

        for s in ["pe", "dv", "ac", "po", "qld", "qx", "qdbg", "qo0", "qo1",
                  "wa0", "wa1", "wa2", "wa3", "wq", "w1a", "w1b", "w2o", "wc"]:
            self.sem[s] = self.st.enter_context(nc.semaphore("sem_" + s))
            self.cnt[s] = 0

        self.ln_free = [0, 0]
        self.bank_free = [0, 0, 0, 0]
        self.slot_free = [0, 0]
        self.nm_free = [0, 0]
        self.o_evac = [0, 0]
        self.ptb_free = {0: 0, 4: 0, 7: 0}
        self.wsl_free = [0, 0, 0, 0]
        self.pts_free = [0, 0]
        self.liv_free = [0, 0, 0, 0]
        self.ll_free = [0, 0]
        self.stb_free = [0, 0, 0, 0]
        self.ptslot_free = [0, 0, 0, 0]
        self.obank_free = [0, 0, 0, 0]
        self.rt_osg_free = 0
        self.rt_otb_free = 0
        self.rt_queue = []
        self.itc = 0

    def inc(self, s, instr, n=1):
        instr.then_inc(self.sem[s], n)
        self.cnt[s] += n
        return self.cnt[s]

    def w(self, eng, s, tok):
        if tok and tok > 0:
            eng.wait_ge(self.sem[s], tok)

    def barrier(self):
        eng_of = {"pe": self.pe, "dv": self.dv, "ac": self.ac, "po": self.po}
        self.dv.drain()
        self.ac.drain()
        self.po.drain()
        for a in eng_of:
            for b in eng_of:
                if a != b and self.cnt[b] > 0:
                    eng_of[a].wait_ge(self.sem[b], self.cnt[b])
            for s in ("qo0", "qo1"):
                if self.cnt[s] > 0:
                    eng_of[a].wait_ge(self.sem[s], self.cnt[s])

    def psb(self, bank, w=512):
        return self.PS[:, bank * 512: bank * 512 + w]

    def pt_bf16(self, bank):
        return self.PS[:, bank * 512:(bank + 1) * 512].bitcast(BF16)

    @property
    def L8(self):
        return self.HPQ[:].bitcast(F32)

    @property
    def LG(self):
        return self.HPQ[:].bitcast(F32)

    @property
    def UU(self):
        return self.AR1
    def PPslot(self, slot):
        return self.AR1[:, slot * 2 * self.T:(slot + 1) * 2 * self.T]
    def QH(self):
        return self.AR1[:, 0: 4 * self.T]
    def PTb(self, slot):
        o = 4 * self.T + slot * 1024
        return self.AR1[:, o: o + 1024]
    def OSG(self):
        o4 = (4 * self.T + 4096) // 2
        return self.AR1[:].bitcast(F32)[:, o4: o4 + 512]
    def OTM(self):
        o4 = (4 * self.T + 4096) // 2 + 512
        return self.AR1[:].bitcast(F32)[:, o4: o4 + 272]
    def OTB(self):
        o = 4 * self.T + 4096 + 1024 + 544
        return self.AR1[:, o: o + 256]

    def KF(self):
        return self.KVW[:, 0: 4 * self.T]
    def VT(self):
        return self.KVW[:, 4 * self.T: 4 * self.T + (self.T // 128) * 4 * 130]
    def W1Q(self, slot):
        return self.KVW[:, slot * 4096:(slot + 1) * 4096]
    @property
    def W2O(self):
        return self.KVW[:, 8192: 16384]

    @property
    def OF(self):
        return self.BIGB[:].bitcast(BF16)

    def load_consts(self):
        po, pe, dv = self.po, self.pe, self.dv
        self.inc("qld", po.dma_start(self.IDB[:], self.d_id), 16)
        self.inc("qld", po.dma_start(self.IDF[:], self.d_id), 16)
        self.inc("qld", po.dma_start(self.ONES[:], self.d_ones), 16)
        mk = self.d_mk.rearrange("r p k -> p r k")
        self.inc("qld", po.dma_start(self.MK[:].rearrange("p (r k) -> p r k", r=4), mk), 16)
        self.inc("qld", po.dma_start(self.L8[0:8, 0:128], self.d_ln[0]), 16)
        self.inc("qld", po.dma_start(self.L8[0:8, 128:256], self.d_ln[1]), 16)
        t = self.inc("qld", po.dma_start(self.L8[0:8, 256:1280], self.d_rwT), 16)
        self.w(pe, "qld", t)
        pe.transpose(self.PS[:, 0:8], self.L8[0:8, 0:128], self.IDF[0:8, 0:8])
        pe.transpose(self.PS[:, 8:16], self.L8[0:8, 128:256], self.IDF[0:8, 0:8])
        for j in range(8):
            mm = pe.transpose(self.PS[:, 512 + j * 8: 512 + (j + 1) * 8],
                              self.L8[0:8, 256 + j * 128: 256 + (j + 1) * 128],
                              self.IDF[0:8, 0:8])
        ptok = self.inc("pe", mm)
        self.w(dv, "pe", ptok)
        dv.tensor_copy(self.LNW[:, 0:16], self.PS[:, 0:16])
        ins = dv.tensor_copy(self.RW[:], self.PS[:, 512: 512 + 64])
        self.const_tok = self.inc("dv", ins)

    def prep_wcache(self):
        po = self.po

        def group(pairs):
            for dst, src in pairs:
                self.inc("wc", po.dma_start(dst, src), 16)
            tok = self.cnt["wc"]
            po.wait_ge(self.sem["wc"], tok)
            return tok

        self.cast_tok = {}
        for m in range(2):
            self.cast_tok["wqkv%d" % m] = group(
                [(self.d_wqkvc[m][j:j + 512, :], self.d_wqkv[m][j:j + 512, :])
                 for j in (0, 512)])
            self.cast_tok["wp%d" % m] = group([(self.d_wpc[m], self.d_wp[m])])
        t = 0
        for j in range(0, C, 128):
            t = self.inc("wc", po.dma_start(self.d_w1c[j:j + 128, :], self.d_w1[j:j + 128, :]), 16)
        for j in range(0, FF, 2048):
            t = self.inc("wc", po.dma_start(self.d_w2c[j:j + 2048, :], self.d_w2[j:j + 2048, :]), 16)
        self.wc_tok = t

    def load_x(self, seq):
        src = self.d_x[seq].rearrange("(n p) c -> p n c", p=128)
        dst = self.XR[:].rearrange("p (n c) -> p n c", c=C)
        return self.inc("qx", self.po.dma_start(dst, src), 16)

    def ln_stats(self, src_ap, par, src_dv_tok=0):
        dv, ac, po = self.dv, self.ac, self.po
        c1 = 1.0 / C
        if src_dv_tok:
            dv.drain()
        t_dv = self.inc("dv", dv.reduce_sum(self.S1[:, par:par + 1], src_ap, axis=AX))
        self.w(ac, "dv", max(self.ln_free[par], src_dv_tok))
        t_ac = self.inc("ac", ac.activation(self.SQS[:], src_ap, AF.Square,
                                            accum_out=self.S2[:, par:par + 1]))
        self.w(dv, "ac", t_ac)
        dv.drain()
        dv.tensor_scalar(self.MUc[:, par:par + 1], self.S1[:, par:par + 1], c1, None, op0=ALU.mult)
        dv.drain()
        dv.tensor_tensor(self.TVA[:, par:par + 1], self.MUc[:, par:par + 1],
                         self.MUc[:, par:par + 1], op=ALU.mult)
        dv.drain()
        dv.tensor_scalar(self.TVA[:, par:par + 1], self.TVA[:, par:par + 1], EPS, None,
                         op0=ALU.subtract)
        dv.tensor_scalar(self.VPc[:, par:par + 1], self.S2[:, par:par + 1], c1, None, op0=ALU.mult)
        dv.drain()
        ins = dv.tensor_tensor(self.VPc[:, par:par + 1], self.VPc[:, par:par + 1],
                               self.TVA[:, par:par + 1], op=ALU.subtract)
        t_po = self.inc("dv", ins)
        self.w(ac, "dv", t_po)
        ac.activation(self.RSTc[:, par:par + 1], self.VPc[:, par:par + 1], AF.Ln)
        ac.drain()
        ins = ac.activation(self.RSTc[:, par:par + 1], self.RSTc[:, par:par + 1], AF.Exp, scale=-0.5)
        t_rst = self.inc("ac", ins)
        return (self.MUc[:, par:par + 1], self.RSTc[:, par:par + 1], t_rst)

    def h_build(self, src_buf, gain_idx=None, cache_stats=False):
        pe, dv, ac = self.pe, self.dv, self.ac
        T, NT = self.T, self.NT
        for i in range(NT):
            src = src_buf[:, i * C:(i + 1) * C]
            t_dv = self.inc("dv", dv.reduce_sum(self.S1A[:, i:i + 1], src, axis=AX))
            ac.drain()
            t_ac = self.inc("ac", ac.activation(self.SQS[:], src, AF.Square,
                                                accum_out=self.S2A[:, i:i + 1]))
        c1 = 1.0 / C
        self.w(dv, "ac", t_ac)
        dv.drain()
        dv.tensor_scalar(self.MUCACHE[:], self.S1A[:], c1, None, op0=ALU.mult)
        dv.drain()
        dv.tensor_tensor(self.VPA[:], self.MUCACHE[:], self.MUCACHE[:], op=ALU.mult)
        dv.drain()
        dv.tensor_scalar(self.VPA[:], self.VPA[:], EPS, None, op0=ALU.subtract)
        dv.drain()
        dv.tensor_scalar(self.RSTCACHE[:], self.S2A[:], c1, None, op0=ALU.mult)
        dv.drain()
        ins = dv.tensor_tensor(self.VPA[:], self.RSTCACHE[:], self.VPA[:], op=ALU.subtract)
        t_vp = self.inc("dv", ins)
        self.w(ac, "dv", t_vp)
        ac.activation(self.RSTCACHE[:], self.VPA[:], AF.Ln)
        ac.drain()
        ins = ac.activation(self.RSTCACHE[:], self.RSTCACHE[:], AF.Exp, scale=-0.5)
        t_rst = self.inc("ac", ins)
        nx_cons = [0, 0]
        pt_free = self.ptb_free[4]
        self.w(dv, "ac", t_rst)
        for i in range(NT):
            par = i % 2
            src = src_buf[:, i * C:(i + 1) * C]
            self.w(dv, "pe", nx_cons[par])
            nx = self.AST[:, par * C:(par + 1) * C]
            ins = dv.tensor_scalar(nx, src, self.MUCACHE[:, i:i + 1], self.RSTCACHE[:, i:i + 1],
                                   op0=ALU.subtract, op1=ALU.mult)
            t_nx = self.inc("dv", ins)
            self.w(pe, "dv", t_nx)
            self.w(pe, "dv", pt_free)
            ptb = self.pt_bf16(4)
            for j in range(NCH):
                mm = pe.transpose(ptb[:, j * 128:(j + 1) * 128], nx[:, j * 128:(j + 1) * 128], self.IDB[:])
            t_tr = self.inc("pe", mm)
            nx_cons[par] = t_tr
            self.w(dv, "pe", t_tr)
            if gain_idx is None:
                out = self.HF[:].rearrange("p (j t) -> p j t", j=NCH)[:, :, i * 128:(i + 1) * 128]
                ins = dv.tensor_copy(out, ptb[:].rearrange("p (j q) -> p j q", j=NCH))
            else:
                for j in range(NCH):
                    ins = dv.tensor_scalar(self.HF[:, j * T + i * 128: j * T + (i + 1) * 128],
                                           ptb[:, j * 128:(j + 1) * 128],
                                           self.LNW[:, gain_idx * 8 + j: gain_idx * 8 + j + 1],
                                           None, op0=ALU.mult)
            pt_free = self.inc("dv", ins)
        self.ptb_free[4] = pt_free

    def kv_build(self, mha, half):
        pe, dv, po = self.pe, self.dv, self.po
        T, NG = self.T, self.NG
        wslot_free = self.wsl_free
        wtok = [0, 0, 0, 0]
        kvbank_free = [0, 0, 0, 0]
        pt_free = self.ptb_free[4]
        vstage_cons = [0, 0]
        nblk = (T // 128) * 4
        ones_ap = self.VT().rearrange("p (b c) -> p b c", c=130)[:, :, 64:130:65]
        ins = dv.memset(ones_ap, 1.0)
        t_ones = self.inc("dv", ins)
        sidx = 0
        for p in range(4):
            pair = half * 4 + p
            streams = (("q", 0), ("k", C), ("v", 2 * C))
            sl_of = {}
            for which, cbase in streams:
                sl = sidx % 4
                sidx += 1
                sl_of[which] = sl
                col0 = cbase + pair * 128
                src = self.d_wqkvc[mha][:, col0:col0 + 128].rearrange("(j p) f -> p j f", p=128)
                self.w(self.sy, "wc", self.cast_tok["wqkv%d" % mha])
                self.w(self.sy, "pe", wslot_free[sl])
                dst = self.WSQ[:, sl * 1024:(sl + 1) * 1024].rearrange("p (j f) -> p j f", j=8)
                wtok[sl] = self.inc("wa%d" % sl, self.sy.dma_start(dst, src), 16)
            for which, dstbuf, scale in (("k", self.KF(), None), ("q", self.QH(), 0.125)):
                sl = sl_of[which]
                self.w(pe, "wa%d" % sl, wtok[sl])
                for g in range(NG):
                    bank = g % 2
                    self.w(pe, "dv", kvbank_free[bank])
                    for j in range(NCH):
                        mm = pe.matmul(self.psb(bank),
                                       self.WSQ[:, sl * 1024 + j * 128: sl * 1024 + (j + 1) * 128],
                                       self.HF[:, j * T + g * 512: j * T + (g + 1) * 512],
                                       start=(j == 0), stop=(j == NCH - 1))
                    t_mm = self.inc("pe", mm)
                    if g == NG - 1:
                        wslot_free[sl] = t_mm
                    self.w(dv, "pe", t_mm)
                    dsl = dstbuf[:, p * T + g * 512: p * T + (g + 1) * 512]
                    if scale is None:
                        ins = dv.tensor_copy(dsl, self.psb(bank))
                    else:
                        ins = dv.tensor_scalar(dsl, self.psb(bank), scale, None, op0=ALU.mult)
                    kvbank_free[bank] = self.inc("dv", ins)
            sl = sl_of["v"]
            self.w(pe, "wa%d" % sl, wtok[sl])
            for g in range(NG):
                bank = 2 + g % 2
                self.w(pe, "dv", kvbank_free[bank])
                for j in range(NCH):
                    mm = pe.matmul(self.psb(bank),
                                   self.WSQ[:, sl * 1024 + j * 128: sl * 1024 + (j + 1) * 128],
                                   self.HF[:, j * T + g * 512: j * T + (g + 1) * 512],
                                   start=(j == 0), stop=(j == NCH - 1))
                t_mm = self.inc("pe", mm)
                if g == NG - 1:
                    wslot_free[sl] = t_mm
                vsl = g % 2
                self.w(dv, "pe", t_mm)
                self.w(dv, "pe", vstage_cons[vsl])
                vst = self.AST[:, vsl * 512:(vsl + 1) * 512]
                ins = dv.tensor_copy(vst, self.psb(bank))
                t_ev = self.inc("dv", ins)
                kvbank_free[bank] = t_ev
                self.w(pe, "dv", t_ev)
                self.w(pe, "dv", pt_free)
                self.w(pe, "dv", t_ones)
                ptb = self.pt_bf16(4)
                for kk in range(4):
                    mm = pe.transpose(ptb[:, kk * 128:(kk + 1) * 128], vst[:, kk * 128:(kk + 1) * 128],
                                      self.IDB[:])
                t_tr = self.inc("pe", mm)
                vstage_cons[vsl] = t_tr
                self.w(dv, "pe", t_tr)
                vtr = self.VT().rearrange("p (b c) -> p b c", c=130)
                out = vtr[:, g * 16 + p: g * 16 + 13 + p: 4, 0:129]
                out = out.rearrange("p b (h d) -> p b h d", d=64) if False else \
                    vtr[:, g * 16 + p: g * 16 + 13 + p: 4, :].rearrange(
                        "p b c -> p b c")[:, :, 0:130]
                base = (g * 16 + p) * 130
                out = bass.AP(self.VT().tensor, self.VT().offset + base,
                              [list(self.VT().ap[0]), [4 * 130, 4], [65, 2], [1, 64]])
                ins = dv.tensor_copy(out, ptb[:, 0:512].rearrange("p (k h d) -> p k h d", k=4, h=2))
                pt_free = self.inc("dv", ins)
        self.ptb_free[4] = pt_free

    def emit_qknorms(self, half):
        pe, dv, ac = self.pe, self.dv, self.ac
        T = self.T
        nck = T // 512
        t_red = 0
        t_mm = 0
        for src_i, buf, dst in ((0, self.KF(), self.KN2A), (1, self.QH(), self.QN2A)):
            for wave in range(2):
                sq = self.HPQ[:, 0: 2 * T]
                t_sq = 0
                for pw in range(2):
                    p = wave * 2 + pw
                    self.w(ac, "pe", t_mm)
                    ins = ac.activation(sq[:, pw * T:(pw + 1) * T],
                                        buf[:, p * T:(p + 1) * T], AF.Square)
                    t_sq = self.inc("ac", ins)
                self.w(pe, "ac", t_sq)
                self.w(pe, "dv", t_red)
                for pw in range(2):
                    for h in range(2):
                        r = h * 64
                        for cc in range(nck):
                            mm = pe.matmul(
                                self.PS[r:r + 1, pw * 2048 + cc * 512: pw * 2048 + (cc + 1) * 512],
                                self.ONES[r:r + 64, 0:1],
                                sq[r:r + 64, pw * T + cc * 512: pw * T + (cc + 1) * 512],
                                start=True, stop=True)
                        t_mm = self.inc("pe", mm)
                for pw in range(2):
                    p = wave * 2 + pw
                    for h in range(2):
                        r = h * 64
                        self.w(dv, "pe", t_mm)
                        dv.drain()
                        ins = dv.reduce_max(dst[r:r + 1, p:p + 1],
                                            self.PS[r:r + 1, pw * 2048: pw * 2048 + T], axis=AX)
                        t_red = self.inc("dv", ins)
        self.w(dv, "dv", t_red)
        dv.drain()
        for h in range(2):
            r = h * 64
            dv.tensor_scalar(self.NMH[r:r + 1, 0:4], self.QN2A[r:r + 1, 0:4], -4.08, None, op0=ALU.mult)
            dv.tensor_scalar(self.NMH[r:r + 1, 4:8], self.KN2A[r:r + 1, 0:4], -1.02 / 16.0,
                             None, op0=ALU.mult)
        dv.drain()
        for h in range(2):
            r = h * 64
            dv.tensor_tensor(self.NMH[r:r + 1, 0:4], self.NMH[r:r + 1, 0:4],
                             self.NMH[r:r + 1, 4:8], op=ALU.add)
        dv.drain()
        for h in range(2):
            r = h * 64
            ins = dv.tensor_copy(self.NMHB[r:r + 1, h * 4: h * 4 + 4], self.NMH[r:r + 1, 0:4])
        t_nb = self.inc("dv", ins)
        self.w(pe, "dv", t_nb)
        for h in range(2):
            r = h * 64
            mm = pe.matmul(self.PS[:, 7 * 512 + h * 4: 7 * 512 + h * 4 + 4],
                           self.ONES[r:r + 1, 0:128], self.NMHB[r:r + 1, h * 4: h * 4 + 4],
                           start=True, stop=True)
        t_mm2 = self.inc("pe", mm)
        self.w(dv, "pe", t_mm2)
        ins = dv.tensor_copy(self.BND[:, 0:8], self.PS[:, 7 * 512: 7 * 512 + 8])
        t_bc = self.inc("dv", ins)
        self.bnd_tok = t_bc
        self.kn_guard = t_bc

    def attention_half(self, mha, half):
        pe, dv, ac = self.pe, self.dv, self.ac
        T, NG = self.T, self.NG
        for qg in range(NG):
            for p in range(4):
                gp = p % 2
                nj = 4 * qg + 4
                npair = nj // 2
                avtok = [0, 0]
                exp_of = {}

                def emit_ST(jp):
                    toks = [0, 0]
                    for jj in range(2):
                        j = jp * 2 + jj
                        for h in range(2):
                            bank = h * 2 + jj
                            self.w(pe, "ac", self.stb_free[bank])
                            if j == 0:
                                self.w(pe, "dv", self.obank_free[h * 2 + gp])
                                self.w(pe, "dv", self.kn_guard)
                            lhs = self.KF()[h * 64:(h + 1) * 64,
                                            p * T + j * 128: p * T + (j + 1) * 128]
                            rhs = self.QH()[h * 64:(h + 1) * 64,
                                            p * T + qg * 512: p * T + (qg + 1) * 512]
                            if j >= 4 * qg:
                                r = j - 4 * qg
                                pe.matmul(self.psb(bank), lhs, rhs, start=True, stop=False)
                                mm = pe.matmul(self.psb(bank), self.IDB[:],
                                               self.MK[:, r * 512:(r + 1) * 512],
                                               start=False, stop=True)
                            else:
                                mm = pe.matmul(self.psb(bank), lhs, rhs, start=True, stop=True)
                            toks[h] = self.inc("pe", mm)
                    sp = jp % 2
                    for h in range(2):
                        slot = h * 2 + sp
                        self.w(ac, "pe", toks[h])
                        self.w(ac, "pe", self.ptslot_free[slot])
                        ins = ac.activation(self.PTb(slot), self.PS[:, (h * 2) * 512:(h * 2) * 512 + 1024],
                                            AF.Exp, bias=self.BND[:, h * 4 + p: h * 4 + p + 1])
                        etok = self.inc("ac", ins)
                        self.stb_free[h * 2] = etok
                        self.stb_free[h * 2 + 1] = etok
                        exp_of[(jp, h)] = etok

                def emit_AV(jp):
                    sp = jp % 2
                    for jj in range(2):
                        j = jp * 2 + jj
                        for h in range(2):
                            slot = h * 2 + sp
                            obank = 4 + h * 2 + gp
                            self.w(pe, "ac", exp_of[(jp, h)])
                            blk = (j * 4 + p) * 130
                            mm = pe.matmul(self.PS[0:65, obank * 512: obank * 512 + 512],
                                           self.VT()[:, blk + h * 65: blk + (h + 1) * 65],
                                           self.PTb(slot)[:, jj * 512:(jj + 1) * 512],
                                           start=(j == 0), stop=(j == nj - 1))
                            avtok[h] = self.inc("pe", mm)
                            if jj == 1:
                                self.ptslot_free[slot] = avtok[h]

                for jp in range(npair):
                    emit_ST(jp)
                    self.step_rt()
                    if jp >= 1:
                        emit_AV(jp - 1)
                        self.step_rt()
                emit_AV(npair - 1)
                self.flush_rt()
                for h in range(2):
                    self.queue_roundtrip(half, p, qg, h, gp, avtok[h])
        self.flush_rt()

    def queue_roundtrip(self, half, p, qg, h, gp, t_av):
        pe, dv = self.pe, self.dv
        T = self.T
        obank = 4 + h * 2 + gp
        pair = half * 4 + p
        st = {}

        def s0():
            self.w(dv, "pe", t_av)
            self.w(dv, "pe", self.rt_osg_free)
            ins = dv.tensor_copy(self.OSG()[0:65, 0:512],
                                 self.PS[0:65, obank * 512: obank * 512 + 512])
            st["ev"] = self.inc("dv", ins)

        def s1():
            self.w(pe, "dv", st["ev"])
            for b in range(4):
                mm = pe.transpose(self.PS[:, obank * 512 + b * 65: obank * 512 + (b + 1) * 65],
                                  self.OSG()[0:65, b * 128:(b + 1) * 128], self.IDF[0:65, 0:65])
            st["tr"] = self.inc("pe", mm)
            self.rt_osg_free = st["tr"]

        def s2():
            self.w(dv, "pe", st["tr"])
            self.w(dv, "pe", self.rt_otb_free)
            dv.tensor_copy(self.OTM()[:, 0:260], self.PS[:, obank * 512: obank * 512 + 260])
            dv.drain()
            linv_src = bass.AP(self.OTM().tensor, self.OTM().offset + 64,
                               [list(self.OTM().ap[0]), [65, 4], [1, 1]])
            dv.reciprocal(self.OL4[:, 0:4], linv_src)
            dv.drain()
            for b in range(4):
                ins = dv.tensor_scalar(self.OTB()[:, b * 64:(b + 1) * 64],
                                       self.OTM()[:, b * 65: b * 65 + 64],
                                       self.OL4[:, b:b + 1], None, op0=ALU.mult)
            st["nb"] = self.inc("dv", ins)

        def s3():
            self.w(pe, "dv", st["nb"])
            ptb = self.pt_bf16(obank)
            for b in range(4):
                mm = pe.transpose(ptb[0:64, b * 128:(b + 1) * 128],
                                  self.OTB()[:, b * 64:(b + 1) * 64], self.IDB[:])
            st["tb"] = self.inc("pe", mm)
            self.rt_otb_free = st["tb"]

        def s4():
            self.w(dv, "pe", st["tb"])
            ins = dv.tensor_copy(
                self.OF[h * 64:(h + 1) * 64, pair * T + qg * 512: pair * T + (qg + 1) * 512],
                ptb_dummy[0] if False else self.pt_bf16(obank)[0:64, 0:512])
            self.obank_free[h * 2 + gp] = self.inc("dv", ins)

        self.rt_queue.extend([s0, s1, s2, s3, s4])

    def step_rt(self):
        if self.rt_queue:
            self.rt_queue.pop(0)()

    def flush_rt(self):
        while self.rt_queue:
            self.rt_queue.pop(0)()

    def proj_phase(self, mha, is_inner):
        pe, dv, po = self.pe, self.dv, self.po
        T, NG = self.T, self.NG
        wslot_free = self.wsl_free
        wtok = [0, 0]
        pbank_free = [0, 0]
        pt_free = self.ptb_free[4]
        afm_cons = 0
        atm_free = [0, 0]
        for g in range(NG):
            evtoks = []
            for ct in range(8):
                sl = ct % 2
                src = self.d_wpc[mha][:, ct * 128:(ct + 1) * 128].rearrange("(j p) f -> p j f", p=128)
                self.w(self.sy, "wc", self.cast_tok["wp%d" % mha])
                self.w(self.sy, "pe", wslot_free[sl])
                dst = self.WSQ[:, sl * 1024:(sl + 1) * 1024].rearrange("p (j f) -> p j f", j=8)
                wtok[sl] = self.inc("wa%d" % sl, self.sy.dma_start(dst, src), 16)
                self.w(pe, "wa%d" % sl, wtok[sl])
                bank = ct % 2
                self.w(pe, "dv", pbank_free[bank])
                if ct == 0:
                    self.w(pe, "pe", afm_cons)
                for j in range(NCH):
                    mm = pe.matmul(self.psb(bank),
                                   self.WSQ[:, sl * 1024 + j * 128: sl * 1024 + (j + 1) * 128],
                                   self.OF[:, j * T + g * 512: j * T + (g + 1) * 512],
                                   start=(j == 0), stop=(j == NCH - 1))
                t_mm = self.inc("pe", mm)
                wslot_free[sl] = t_mm
                self.w(dv, "pe", t_mm)
                self.w(dv, "pe", afm_cons)
                ins = dv.tensor_copy(self.AST[:, ct * 512:(ct + 1) * 512], self.psb(bank))
                pbank_free[bank] = self.inc("dv", ins)
                evtoks.append(pbank_free[bank])
            afm_tok = max(evtoks)
            for tt in range(4):
                tslot = tt % 2
                self.w(pe, "dv", afm_tok)
                self.w(pe, "dv", pt_free)
                self.w(pe, "dv", atm_free[tslot])
                ptb = self.pt_bf16(4)
                for ct in range(8):
                    mm = pe.transpose(ptb[:, ct * 128:(ct + 1) * 128],
                                      self.AST[:, ct * 512 + tt * 128: ct * 512 + (tt + 1) * 128],
                                      self.IDB[:])
                t_tr = self.inc("pe", mm)
                if tt == 3:
                    afm_cons = t_tr
                self.w(dv, "pe", t_tr)
                atm = self.ATM[:, tslot * 1024:(tslot + 1) * 1024]
                ins = dv.tensor_copy(atm, ptb[:])
                pt_free = self.inc("dv", ins)
                ti = g * 4 + tt
                xr = self.XR[:, ti * C:(ti + 1) * C]
                if not is_inner:
                    par = tt % 2
                    mu, rstd, t_rst = self.ln_stats(atm, par, src_dv_tok=pt_free)
                    self.w(dv, "ac", t_rst)
                    dv.tensor_scalar(atm, atm, mu, rstd, op0=ALU.subtract, op1=ALU.mult)
                    dv.drain()
                    ins = dv.tensor_tensor(xr, xr, atm, op=ALU.add)
                    t_res = self.inc("dv", ins)
                    self.ln_free[par] = t_res
                    atm_free[tslot] = t_res
                else:
                    oslot = 1 - tslot
                    self.w(dv, "pe", 0)
                    self.w(dv, "dv", 0)
                    other = self.ATM[:, oslot * 1024:(oslot + 1) * 1024]
                    self.w(dv, "dv", atm_free[oslot]) if False else None
                    dv.tensor_scalar(other, xr, self.MUCACHE[:, ti:ti + 1],
                                     self.RSTCACHE[:, ti:ti + 1], op0=ALU.subtract, op1=ALU.mult)
                    dv.drain()
                    uu = self.UU[:, ti * C:(ti + 1) * C]
                    ins = dv.tensor_tensor(uu, other, atm, op=ALU.add)
                    t_res = self.inc("dv", ins)
                    atm_free[tslot] = t_res
                    atm_free[oslot] = t_res
        self.ptb_free[4] = pt_free

    def mha_slot(self, seq, mha):
        is_inner = (mha == 1)
        self.h_build(self.XR, gain_idx=(0 if is_inner else None), cache_stats=is_inner)
        self.barrier()
        if self.ck("h%d" % mha):
            self.dump("h", self.HF[:])
            raise _DebugStop()
        for half in range(2):
            self.kv_build(mha, half)
            self.barrier()
            self.emit_qknorms(half)
            if half == 0 and self.ck("kv%d" % mha):
                self.dump("kf", self.KF()[:])
                self.dump("vt", self.VT()[:])
                raise _DebugStop()
            self.attention_half(mha, half)
            self.barrier()
        if self.ck("of%d" % mha):
            self.dump("of", self.OF[:, 0:8 * self.T])
            raise _DebugStop()
        self.proj_phase(mha, is_inner)
        self.barrier()

    def router_phase(self):
        pe, dv, po, ac = self.pe, self.dv, self.po, self.ac
        T, NG = self.T, self.NG
        lg_free = 0
        lgt_free = [0, 0]
        ps5_free = 0
        wt_tok = 0
        for g in range(NG):
            for j in range(NCH):
                mm = pe.matmul(self.PS[0:8, 7 * 512: 7 * 512 + 512],
                               self.RW[:, j * 8:(j + 1) * 8],
                               self.HF[:, j * T + g * 512: j * T + (g + 1) * 512],
                               start=(j == 0), stop=(j == NCH - 1))
            t_mm = self.inc("pe", mm)
            self.w(dv, "pe", t_mm)
            self.w(dv, "pe", lg_free)
            ins = dv.tensor_copy(self.LG[0:8, 0:512], self.PS[0:8, 7 * 512: 7 * 512 + 512])
            t_lg = self.inc("dv", ins)
            for tt in range(4):
                par = tt % 2
                self.w(pe, "dv", t_lg)
                self.w(pe, "dv", max(lgt_free[par], ps5_free))
                mm = pe.transpose(self.PS[:, 5 * 512: 5 * 512 + 8],
                                  self.LG[0:8, tt * 128:(tt + 1) * 128], self.IDF[0:8, 0:8])
                t_tr = self.inc("pe", mm)
                if tt == 3:
                    lg_free = t_tr
                self.w(dv, "pe", t_tr)
                ins = dv.tensor_copy(self.LGT[:, par * 8:(par + 1) * 8],
                                     self.PS[:, 5 * 512: 5 * 512 + 8])
                t_lgt = self.inc("dv", ins)
                ps5_free = t_lgt
                lgt = self.LGT[:, par * 8:(par + 1) * 8]
                dv.drain()
                dv.reduce_max(self.RM1N[:], lgt, axis=AX, negate=True)
                dv.drain()
                dv.tensor_scalar(self.RM1[:], self.RM1N[:], -1.0, None, op0=ALU.mult)
                dv.drain()
                dv.tensor_scalar(self.RMK1[:], lgt, self.RM1[:], None, op0=ALU.is_ge)
                dv.drain()
                dv.scalar_tensor_tensor(self.RL2[:], self.RMK1[:], -1e30, lgt,
                                        op0=ALU.mult, op1=ALU.add)
                dv.drain()
                ins = dv.reduce_max(self.RM2[:], self.RL2[:], axis=AX)
                t_m2 = self.inc("dv", ins)
                self.w(ac, "dv", t_m2)
                ins = ac.activation(self.RE2[:], self.RM2[:], AF.Exp, bias=self.RM1N[:])
                t_e2 = self.inc("ac", ins)
                self.w(dv, "ac", t_e2)
                dv.tensor_scalar(self.RDEN[:], self.RE2[:], 1.0, None, op0=ALU.add)
                dv.drain()
                dv.tensor_scalar(self.RMK2[:], self.RL2[:], self.RM2[:], None, op0=ALU.is_ge)
                dv.drain()
                ti = g * 4 + tt
                dv.scalar_tensor_tensor(self.RMK2[:], self.RMK2[:], self.RE2[:], self.RMK1[:],
                                        op0=ALU.mult, op1=ALU.add)
                dv.reciprocal(self.RW1[:], self.RDEN[:])
                dv.drain()
                ins = dv.tensor_scalar(self.WTOK[:, ti * 8:(ti + 1) * 8], self.RMK2[:],
                                       self.RW1[:], None, op0=ALU.mult)
                wt_tok = self.inc("dv", ins)
                lgt_free[par] = wt_tok
        self.wtok_tok = wt_tok

    def moe_phase(self, seq):
        pe, dv, ac, po = self.pe, self.dv, self.ac, self.po
        T = self.T
        tgW = T // 2
        self.tgW = tgW
        self.WBv = self.AST[:, 0:tgW]
        nsub = max(1, tgW // 512)
        subW = min(512, tgW)
        nh = max(1, tgW // 512)
        self.w(self.sy, "pe", self.cnt["pe"])
        self.w(self.sy, "dv", self.cnt["dv"])
        self.w(self.sy, "ac", self.cnt["ac"])
        for tg in range(2):
            tgbase = tg * tgW
            tgtile0 = tg * (tgW // 128)
            gel_tok = [0, 0]
            scl_tok = [0] * 8
            w1_tok = [0, 0]
            w1_free = [0, 0]
            w2_free = 0
            w2_tok_cur = 0
            w2_tok_prev = 0
            scl_prev = [0] * 8
            yevac_tok = [0] * (2 * nsub)
            y_last_evac = 0
            self._ybank_tok = [0, 0, 0, 0]
            wb_tok = 0
            hpq_guard = 0

            def h_mms(fc, oct_):
                bank0 = (fc % 2) * 2
                sl = fc // 4
                self.w(pe, "w1a" if sl == 0 else "w1b", w1_tok[sl])
                self.w(pe, "ac", gel_tok[fc % 2])
                if fc == 0:
                    self.w(pe, "dv", wb_tok)
                for j in range(NCH):
                    for hh in range(nh):
                        mm = pe.matmul(self.PS[:, (bank0 + hh) * 512:(bank0 + hh) * 512 + subW],
                                       self.W1Q(sl)[:, j * 512 + (fc % 4) * 128: j * 512 + (fc % 4) * 128 + 128],
                                       self.HF[:, j * T + tgbase + hh * subW: j * T + tgbase + (hh + 1) * subW],
                                       start=(j == 0), stop=(j == NCH - 1))
                tok = self.inc("pe", mm)
                return tok

            def gelu_scale(fc, t_h):
                bank0 = (fc % 2) * 2
                self.w(ac, "pe", t_h)
                self.w(ac, "dv", scl_prev[fc])
                self.w(ac, "pe", hpq_guard)
                ins = ac.activation(self.HPQ[:, fc * tgW:(fc + 1) * tgW],
                                    self.PS[:, bank0 * 512: bank0 * 512 + tgW], AF.Gelu)
                t_g = self.inc("ac", ins)
                gel_tok[fc % 2] = t_g
                self.w(dv, "ac", t_g)
                ins = dv.tensor_tensor(self.HPQ[:, fc * tgW:(fc + 1) * tgW],
                                       self.HPQ[:, fc * tgW:(fc + 1) * tgW],
                                       self.WBv, op=ALU.mult)
                scl_tok[fc] = self.inc("dv", ins)

            def w2_combos(first):
                nonlocal w2_free, y_last_evac
                dv.drain()
                ybank_tok = list(self._ybank_tok)
                for ci, (ch, sub) in enumerate([(a, b) for a in range(2) for b in range(nsub)]):
                    self.w(pe, "w2o", w2_tok_prev)
                    mm_tok = [0, 0, 0, 0]
                    for fc in range(8):
                        self.w(pe, "dv", scl_prev[fc])
                        for cti in range(4):
                            if fc == 0:
                                self.w(pe, "dv", ybank_tok[cti])
                            cto = (ch * 4 + cti) * 128
                            mm = pe.matmul(self.PS[:, (4 + cti) * 512:(4 + cti) * 512 + subW],
                                           self.W2O[:, fc * 1024 + cto: fc * 1024 + cto + 128],
                                           self.HPQ[:, fc * tgW + sub * subW: fc * tgW + (sub + 1) * subW],
                                           start=(fc == 0), stop=(fc == 7))
                            if fc == 7:
                                mm_tok[cti] = self.inc("pe", mm)
                    w2_free = mm_tok[3]
                    for cti in range(4):
                        self.w(dv, "pe", mm_tok[cti])
                        ysl = self.BIGB[:, (ch * 4 + cti) * tgW + sub * subW:
                                        (ch * 4 + cti) * tgW + (sub + 1) * subW]
                        ps_y = self.PS[:, (4 + cti) * 512:(4 + cti) * 512 + subW]
                        if first:
                            ins = dv.tensor_scalar(ysl, ps_y, 1.0, None, op0=ALU.mult)
                        else:
                            ins = dv.scalar_tensor_tensor(ysl, ps_y, 1.0, ysl, op0=ALU.mult, op1=ALU.add)
                        ybank_tok[cti] = self.inc("dv", ins)
                        y_last_evac = ybank_tok[cti]
                self._ybank_tok = ybank_tok

            for oct_ in range(16):
                if oct_ % 2 == 0:
                    wb_tok = self.wb_build(tgtile0, oct_ // 2, gel_tok, scl_tok)
                for q in range(2):
                    col0 = oct_ * 1024 + q * 512
                    src = self.d_w1c[:, col0:col0 + 512].rearrange("(j p) f -> p j f", p=128)
                    self.w(self.sy, "pe", w1_free[q])
                    if oct_ == 0 and tg == 0:
                        self.w(self.sy, "wc", self.wc_tok)
                    dst = self.W1Q(q)[:].rearrange("p (j f) -> p j f", j=8)
                    w1_tok[q] = self.inc("w1a" if q == 0 else "w1b", self.sy.dma_start(dst, src), 16)
                t_h0 = h_mms(0, oct_)
                t_h1 = h_mms(1, oct_)
                if oct_ > 0:
                    w2_combos(first=(oct_ == 1))
                hpq_guard = w2_free
                self.w(self.sy, "pe", w2_free)
                src = self.d_w2c[oct_ * 1024:(oct_ + 1) * 1024, :].rearrange("(j p) c -> p j c", p=128)
                dst = self.W2O[:].rearrange("p (j c) -> p j c", j=8)
                w2_tok_cur = self.inc("w2o", self.sy.dma_start(dst, src), 16)
                gelu_scale(0, t_h0)
                gelu_scale(1, t_h1)
                for fc in range(2, 8):
                    t_h = h_mms(fc, oct_)
                    gelu_scale(fc, t_h)
                w1_free = [self.cnt["pe"], self.cnt["pe"]]
                w2_tok_prev = w2_tok_cur
                scl_prev = list(scl_tok)
            w2_combos(first=False)
            self.barrier()
            if self.ck("y") and tg == 0:
                self.dump("y", self.BIGB[:, 0:4 * self.T])
                self.dump("wtok", self.WTOK[:])
                self.dump("t2d", self.HF[:])
                raise _DebugStop()
            self.finale(seq, tg, tgW)
            self.barrier()

    def wb_build(self, tgtile0, expert, gel_tok, scl_tok):
        pe, dv = self.pe, self.dv
        ntile = self.tgW // 128
        dv.drain()
        self.w(dv, "pe", self.cnt["pe"])
        for tt in range(ntile):
            ti = tgtile0 + tt
            ins = dv.tensor_scalar(self.AST[:, 2048 + tt * 128: 2048 + (tt + 1) * 128],
                                   self.ONES[:], self.WTOK[:, ti * 8 + expert: ti * 8 + expert + 1],
                                   None, op0=ALU.mult)
        t_stage = self.inc("dv", ins)
        self.w(pe, "dv", t_stage)
        self.w(pe, "ac", gel_tok[0])
        ptb = self.pt_bf16(0)
        for tt in range(ntile):
            mm = pe.transpose(ptb[:, tt * 128:(tt + 1) * 128],
                              self.AST[:, 2048 + tt * 128: 2048 + (tt + 1) * 128], self.IDB[:])
        t_tr = self.inc("pe", mm)
        self.w(dv, "pe", t_tr)
        ins = dv.tensor_copy(self.WBv, ptb[:, 0:self.tgW])
        return self.inc("dv", ins)

    def finale(self, seq, tg, tgW):
        pe, dv, sy = self.pe, self.dv, self.sy
        ntile = tgW // 128
        astf = self.AST[:].bitcast(F32)
        atmf = self.ATM[:].bitcast(F32)
        outslot = [astf[:, 1024:2048], atmf[:, 0:1024]]
        out_tok = [0, 0]
        f_cons = 0
        tr_free = 0
        for tt in range(ntile):
            ti = tg * ntile + tt
            par = tt % 2
            self.w(pe, "dv", tr_free)
            for ct in range(8):
                bank = ct // 4
                mm = pe.transpose(self.PS[:, bank * 512 + (ct % 4) * 128: bank * 512 + (ct % 4) * 128 + 128],
                                  self.BIGB[:, ct * tgW + tt * 128: ct * tgW + (tt + 1) * 128],
                                  self.IDF[:])
            t_tr = self.inc("pe", mm)
            self.w(dv, "pe", t_tr)
            self.w(dv, "dv", 0)
            ftmp = astf[:, 0:1024]
            ins = dv.scalar_tensor_tensor(ftmp, self.PS[:, 0:1024], 1.0,
                                          self.UU[:, ti * C:(ti + 1) * C], op0=ALU.mult, op1=ALU.add)
            t_f = self.inc("dv", ins)
            tr_free = t_f
            mu, rstd, t_rst = self.ln_stats(ftmp, par, src_dv_tok=t_f)
            self.w(dv, "ac", t_rst)
            self.w(dv, "qo%d" % par, out_tok[par])
            nrm = outslot[par]
            dv.tensor_scalar(nrm, ftmp, mu, rstd, op0=ALU.subtract, op1=ALU.mult)
            dv.drain()
            ins = dv.tensor_tensor(nrm, nrm, self.XR[:, ti * C:(ti + 1) * C], op=ALU.add)
            t_o = self.inc("dv", ins)
            self.ln_free[par] = t_o
            self.w(sy, "dv", t_o)
            dst = self.d_out[seq, ti * 128:(ti + 1) * 128, :]
            out_tok[par] = self.inc("qo%d" % par, sy.dma_start(dst, nrm), 16)

    def dump(self, name, ap):
        w = ap.shape[-1]
        npart = ap.shape[0]
        d = self.nc.dram_tensor("dbg_" + name, [npart, w], F32, kind="ExternalOutput").ap()
        self.barrier()
        for s in self.sem:
            if s != "qdbg" and self.cnt[s]:
                self.po.wait_ge(self.sem[s], self.cnt[s])
        self.inc("qdbg", self.po.dma_start(d, ap), 16)

    def ck(self, phase):
        if self.debug_phase is None:
            return False
        return phase == self.debug_phase

    def build(self, debug_phase=None):
        self.debug_phase = debug_phase

        def fin():
            for s in ("qo0", "qo1", "qdbg"):
                if self.cnt[s]:
                    self.po.wait_ge(self.sem[s], self.cnt[s])
                    self.sy.wait_ge(self.sem[s], self.cnt[s])
            return self.nc

        self.load_consts()
        try:
            self._main()
        except _DebugStop:
            pass
        return fin()

    def _main(self):
        for seq in range(2):
            if seq == 1:
                self.barrier()
            xt = self.load_x(seq)
            if seq == 0:
                self.prep_wcache()
            for eng in (self.dv, self.pe, self.ac):
                self.w(eng, "qx", xt)
            self.w(self.dv, "dv", self.const_tok)
            if self.ck("x"):
                self.dump("x", self.XR[:])
                raise _DebugStop()
            self.mha_slot(seq, 0)
            if self.ck("mha0"):
                self.dump("x1", self.XR[:])
                raise _DebugStop()
            self.mha_slot(seq, 1)
            if self.ck("mha1"):
                self.dump("x1", self.XR[:])
                self.dump("u", self.UU[:, 0:self.NT * C])
                raise _DebugStop()
            self.h_build(self.UU, gain_idx=1, cache_stats=False)
            self.barrier()
            if self.ck("t2"):
                self.dump("t2", self.HF[:])
                raise _DebugStop()
            self.router_phase()
            self.barrier()
            if self.ck("router"):
                self.dump("wtok", self.WTOK[:])
                raise _DebugStop()
            self.moe_phase(seq)
            if self.ck("y"):
                raise _DebugStop()


def build_program(T=2048):
    return KB(T).build()


def make_masks():
    kk = np.arange(128)[:, None]
    qq = np.arange(512)[None, :]
    return np.stack([np.where(qq >= 128 * r + kk, 0.0, -1e30) for r in range(4)]).astype(np.float32)


def kernel(L0_attn_x, L0_attn_W_attn, L0_attn_W_proj, L0_ffn_ln1_w, L0_ffn_c_attn_w,
           L0_ffn_c_proj_w, L0_ffn_ln2_w, L0_ffn_router_w, L0_ffn_w1, L0_ffn_w2):
    x = np.ascontiguousarray(np.asarray(L0_attn_x, dtype=np.float32))
    B, T, _ = x.shape
    n_cores = 8
    per = B // n_cores
    nc = build_program(T)
    consts = {
        "wqkv0": np.ascontiguousarray(L0_attn_W_attn, dtype=np.float32),
        "wqkv1": np.ascontiguousarray(L0_ffn_c_attn_w, dtype=np.float32),
        "wp0": np.ascontiguousarray(L0_attn_W_proj, dtype=np.float32),
        "wp1": np.ascontiguousarray(L0_ffn_c_proj_w, dtype=np.float32),
        "lnw": np.ascontiguousarray(
            np.stack([np.asarray(L0_ffn_ln1_w, dtype=np.float32).reshape(8, 128),
                      np.asarray(L0_ffn_ln2_w, dtype=np.float32).reshape(8, 128)])),
        "rwT": np.ascontiguousarray(np.asarray(L0_ffn_router_w, dtype=np.float32).T),
        "w1": np.ascontiguousarray(L0_ffn_w1, dtype=np.float32),
        "w2": np.ascontiguousarray(L0_ffn_w2, dtype=np.float32),
        "ident": np.eye(128, dtype=np.float32),
        "masks": make_masks(),
        "ones": np.ones((128, 128), dtype=np.float32),
    }
    in_maps = []
    for c in range(n_cores):
        m = dict(consts)
        m["x"] = np.ascontiguousarray(x[c * per:(c + 1) * per])
        in_maps.append(m)
    res = run_bass_kernel_spmd(nc, in_maps, core_ids=list(range(n_cores)))
    return np.concatenate([np.asarray(r["out"], dtype=np.float32) for r in res.results], axis=0)


# version 13
# speedup vs baseline: 1.3023x; 1.0077x over previous
import numpy as np
import concourse.bass as bass
import concourse.mybir as mybir
from concourse.bass_utils import run_bass_kernel_spmd
from contextlib import ExitStack

F32 = mybir.dt.float32
BF16 = mybir.dt.bfloat16
AF = mybir.ActivationFunctionType
ALU = mybir.AluOpType
AX = mybir.AxisListType.X

C = 1024
NCH = 8
EPS = 1e-5
FF = 16384


class _DebugStop(Exception):
    pass


class KB:
    def __init__(self, T):
        self.T = T
        self.NT = T // 128
        self.NG = T // 512
        nc = bass.Bass("TRN2", target_bir_lowering=False)
        self.nc = nc
        self.pe, self.dv, self.ac, self.po, self.sy = nc.tensor, nc.vector, nc.scalar, nc.gpsimd, nc.sync
        self.st = ExitStack()
        self.cnt = {}
        self.sem = {}
        NT = self.NT

        dram = lambda n, shp, k: nc.dram_tensor(n, shp, F32, kind=k).ap()
        self.d_x = dram("x", [2, T, C], "ExternalInput")
        self.d_wqkv = [dram("wqkv0", [C, 3 * C], "ExternalInput"),
                       dram("wqkv1", [C, 3 * C], "ExternalInput")]
        self.d_wp = [dram("wp0", [C, C], "ExternalInput"),
                     dram("wp1", [C, C], "ExternalInput")]
        self.d_ln = dram("lnw", [2, 8, 128], "ExternalInput")
        self.d_rwT = dram("rwT", [8, C], "ExternalInput")
        self.d_w1 = dram("w1", [C, FF], "ExternalInput")
        self.d_w2 = dram("w2", [FF, C], "ExternalInput")
        self.d_id = dram("ident", [128, 128], "ExternalInput")
        self.d_mk = dram("masks", [4, 128, 512], "ExternalInput")
        self.d_ones = dram("ones", [128, 128], "ExternalInput")
        self.d_out = dram("out", [2, T, C], "ExternalOutput")
        self.d_w1c = nc.dram_tensor("w1c", [C, FF], BF16, kind="Internal").ap()
        self.d_w2c = nc.dram_tensor("w2c", [FF, C], BF16, kind="Internal").ap()
        self.d_wqkvc = [nc.dram_tensor("wqkv%dc" % m, [C, 3 * C], BF16, kind="Internal").ap()
                        for m in range(2)]
        self.d_wpc = [nc.dram_tensor("wp%dc" % m, [C, C], BF16, kind="Internal").ap()
                      for m in range(2)]

        sb = lambda n, shp, dt: self.st.enter_context(nc.sbuf_tensor(n, shp, dt))
        self.XR = sb("XR", [128, NT * C], BF16)
        self.HF = sb("HF", [128, NCH * T], BF16)
        self.BIGB = sb("BIGB", [128, 4 * T], F32)
        self.KVW = sb("KVW", [128, max(4 * T + (T // 128) * 4 * 130, 16384)], BF16)
        self.AR1 = sb("AR1", [128, max(8 * T, 4 * T + 8192)], BF16)
        self.AST = sb("AST", [128, 4096], BF16)
        self.HPQ = sb("HPQ", [128, 8192], BF16)
        self.ATM = sb("ATM", [128, 2048], BF16)
        self.SQS = sb("SQS", [128, C], BF16)
        self.MK = sb("MK", [128, 4 * 512], BF16)
        self.IDB = sb("IDB", [128, 128], BF16)
        self.IDF = sb("IDF", [128, 128], F32)
        self.ONES = sb("ONESB", [128, 128], BF16)
        self.LNW = sb("LNW", [128, 16], F32)
        self.RW = sb("RW", [128, 64], BF16)
        self.WTOK = sb("WTOK", [128, NT * 8], F32)
        self.WSQ = sb("WSQ", [128, 4 * 1024], BF16)
        self.LGT = sb("LGT", [128, 16], F32)
        self.S1 = sb("S1", [128, 2], F32)
        self.S2 = sb("S2", [128, 2], F32)
        self.MUc = sb("MUc", [128, 2], F32)
        self.VPc = sb("VPc", [128, 2], F32)
        self.TVA = sb("TVA", [128, 2], F32)
        self.RSTc = sb("RSTc", [128, 2], F32)
        self.MUCACHE = sb("MUCACHE", [128, NT], F32)
        self.RSTCACHE = sb("RSTCACHE", [128, NT], F32)
        self.S1A = sb("S1A", [128, NT], F32)
        self.S2A = sb("S2A", [128, NT], F32)
        self.VPA = sb("VPA", [128, NT], F32)
        self.NM = sb("NM", [128, 8], F32)
        self.KN2A = sb("KN2A", [128, 8], F32)
        self.QN2A = sb("QN2A", [128, 8], F32)
        self.OL4 = sb("OL4", [128, 4], F32)
        self.BND = sb("BND", [128, 8], F32)
        self.NKND = sb("NKND", [128, 8], F32)
        self.QN2 = sb("QN2", [128, 8], F32)
        self.NMH = sb("NMH", [128, 8], F32)
        self.NMHB = sb("NMHB", [128, 8], BF16)
        self.LL = sb("LL", [128, 8], F32)
        self.NGM = sb("NGM", [128, 4], F32)
        self.BB = sb("BB", [128, 8], F32)
        self.BL = sb("BL", [128, 8], F32)
        self.LT = sb("LT", [128, 4], F32)
        self.LIV = sb("LIV", [128, 4], F32)
        self.RM1 = sb("RM1", [128, 1], F32)
        self.RM1N = sb("RM1N", [128, 1], F32)
        self.RM2 = sb("RM2", [128, 1], F32)
        self.RE2 = sb("RE2", [128, 1], F32)
        self.RDEN = sb("RDEN", [128, 1], F32)
        self.RW1 = sb("RW1", [128, 1], F32)
        self.RW2 = sb("RW2", [128, 1], F32)
        self.RMK1 = sb("RMK1", [128, 8], F32)
        self.RL2 = sb("RL2", [128, 8], F32)
        self.RMK2 = sb("RMK2", [128, 8], F32)

        self.PS = self.st.enter_context(nc.psum_tensor("PS", [128, 4096], F32))

        for s in ["pe", "dv", "ac", "po", "qld", "qx", "qdbg", "qo0", "qo1",
                  "wa0", "wa1", "wa2", "wa3", "wb0", "wb1", "wb2", "wb3",
                  "wq", "w1a", "w1b", "w2o", "wc"]:
            self.sem[s] = self.st.enter_context(nc.semaphore("sem_" + s))
            self.cnt[s] = 0

        self.ln_free = [0, 0]
        self.bank_free = [0, 0, 0, 0]
        self.slot_free = [0, 0]
        self.nm_free = [0, 0]
        self.o_evac = [0, 0]
        self.ptb_free = {0: 0, 4: 0, 7: 0}
        self.wsl_free = [0, 0, 0, 0]
        self.pts_free = [0, 0]
        self.liv_free = [0, 0, 0, 0]
        self.ll_free = [0, 0]
        self.stb_free = [0, 0, 0, 0]
        self.ptslot_free = [0, 0, 0, 0]
        self.obank_free = [0, 0, 0, 0]
        self.rt_osg_free = 0
        self.rt_otb_free = 0
        self.rt_queue = []
        self.itc = 0

    def inc(self, s, instr, n=1):
        instr.then_inc(self.sem[s], n)
        self.cnt[s] += n
        return self.cnt[s]

    def w(self, eng, s, tok):
        if tok and tok > 0:
            eng.wait_ge(self.sem[s], tok)

    def barrier(self):
        eng_of = {"pe": self.pe, "dv": self.dv, "ac": self.ac, "po": self.po}
        self.dv.drain()
        self.ac.drain()
        self.po.drain()
        for a in eng_of:
            for b in eng_of:
                if a != b and self.cnt[b] > 0:
                    eng_of[a].wait_ge(self.sem[b], self.cnt[b])
            for s in ("qo0", "qo1"):
                if self.cnt[s] > 0:
                    eng_of[a].wait_ge(self.sem[s], self.cnt[s])

    def psb(self, bank, w=512):
        return self.PS[:, bank * 512: bank * 512 + w]

    def pt_bf16(self, bank):
        return self.PS[:, bank * 512:(bank + 1) * 512].bitcast(BF16)

    @property
    def L8(self):
        return self.HPQ[:].bitcast(F32)

    @property
    def LG(self):
        return self.HPQ[:].bitcast(F32)

    @property
    def UU(self):
        return self.AR1
    def PPslot(self, slot):
        return self.AR1[:, slot * 2 * self.T:(slot + 1) * 2 * self.T]
    def QH(self):
        return self.AR1[:, 0: 4 * self.T]
    def PTb(self, slot):
        o = 4 * self.T + slot * 1024
        return self.AR1[:, o: o + 1024]
    def OSG(self):
        o4 = (4 * self.T + 4096) // 2
        return self.AR1[:].bitcast(F32)[:, o4: o4 + 512]
    def OTM(self):
        o4 = (4 * self.T + 4096) // 2 + 512
        return self.AR1[:].bitcast(F32)[:, o4: o4 + 272]
    def OTB(self):
        o = 4 * self.T + 4096 + 1024 + 544
        return self.AR1[:, o: o + 256]

    def KF(self):
        return self.KVW[:, 0: 4 * self.T]
    def VT(self):
        return self.KVW[:, 4 * self.T: 4 * self.T + (self.T // 128) * 4 * 130]
    def W1Q(self, slot):
        return self.KVW[:, slot * 4096:(slot + 1) * 4096]
    @property
    def W2O(self):
        return self.KVW[:, 8192: 16384]

    @property
    def OF(self):
        return self.BIGB[:].bitcast(BF16)

    def load_consts(self):
        po, pe, dv = self.po, self.pe, self.dv
        self.inc("qld", po.dma_start(self.IDB[:], self.d_id), 16)
        self.inc("qld", po.dma_start(self.IDF[:], self.d_id), 16)
        self.inc("qld", po.dma_start(self.ONES[:], self.d_ones), 16)
        mk = self.d_mk.rearrange("r p k -> p r k")
        self.inc("qld", po.dma_start(self.MK[:].rearrange("p (r k) -> p r k", r=4), mk), 16)
        self.inc("qld", po.dma_start(self.L8[0:8, 0:128], self.d_ln[0]), 16)
        self.inc("qld", po.dma_start(self.L8[0:8, 128:256], self.d_ln[1]), 16)
        t = self.inc("qld", po.dma_start(self.L8[0:8, 256:1280], self.d_rwT), 16)
        self.w(pe, "qld", t)
        pe.transpose(self.PS[:, 0:8], self.L8[0:8, 0:128], self.IDF[0:8, 0:8])
        pe.transpose(self.PS[:, 8:16], self.L8[0:8, 128:256], self.IDF[0:8, 0:8])
        for j in range(8):
            mm = pe.transpose(self.PS[:, 512 + j * 8: 512 + (j + 1) * 8],
                              self.L8[0:8, 256 + j * 128: 256 + (j + 1) * 128],
                              self.IDF[0:8, 0:8])
        ptok = self.inc("pe", mm)
        self.w(dv, "pe", ptok)
        dv.tensor_copy(self.LNW[:, 0:16], self.PS[:, 0:16])
        ins = dv.tensor_copy(self.RW[:], self.PS[:, 512: 512 + 64])
        self.const_tok = self.inc("dv", ins)

    def prep_wcache(self):
        po = self.po

        def group(pairs):
            for dst, src in pairs:
                self.inc("wc", po.dma_start(dst, src), 16)
            tok = self.cnt["wc"]
            po.wait_ge(self.sem["wc"], tok)
            return tok

        self.cast_tok = {}
        self.cast_tok["wp0"] = group([(self.d_wpc[0], self.d_wp[0])])
        for m in range(2):
            self.cast_tok["wqkv%d" % m] = group(
                [(self.d_wqkvc[m][j:j + 512, :], self.d_wqkv[m][j:j + 512, :])
                 for j in (0, 512)])
        self.cast_tok["wp1"] = group([(self.d_wpc[1], self.d_wp[1])])
        t = 0
        for j in range(0, C, 128):
            t = self.inc("wc", po.dma_start(self.d_w1c[j:j + 128, :], self.d_w1[j:j + 128, :]), 16)
        for j in range(0, FF, 2048):
            t = self.inc("wc", po.dma_start(self.d_w2c[j:j + 2048, :], self.d_w2[j:j + 2048, :]), 16)
        self.wc_tok = t

    def load_x(self, seq):
        src = self.d_x[seq].rearrange("(n p) c -> p n c", p=128)
        dst = self.XR[:].rearrange("p (n c) -> p n c", c=C)
        return self.inc("qx", self.po.dma_start(dst, src), 16)

    def ln_stats(self, src_ap, par, src_dv_tok=0):
        dv, ac, po = self.dv, self.ac, self.po
        c1 = 1.0 / C
        if src_dv_tok:
            dv.drain()
        t_dv = self.inc("dv", dv.reduce_sum(self.S1[:, par:par + 1], src_ap, axis=AX))
        self.w(ac, "dv", max(self.ln_free[par], src_dv_tok))
        t_ac = self.inc("ac", ac.activation(self.SQS[:], src_ap, AF.Square,
                                            accum_out=self.S2[:, par:par + 1]))
        self.w(dv, "ac", t_ac)
        dv.drain()
        dv.tensor_scalar(self.MUc[:, par:par + 1], self.S1[:, par:par + 1], c1, None, op0=ALU.mult)
        dv.drain()
        dv.tensor_tensor(self.TVA[:, par:par + 1], self.MUc[:, par:par + 1],
                         self.MUc[:, par:par + 1], op=ALU.mult)
        dv.drain()
        dv.tensor_scalar(self.TVA[:, par:par + 1], self.TVA[:, par:par + 1], EPS, None,
                         op0=ALU.subtract)
        dv.tensor_scalar(self.VPc[:, par:par + 1], self.S2[:, par:par + 1], c1, None, op0=ALU.mult)
        dv.drain()
        ins = dv.tensor_tensor(self.VPc[:, par:par + 1], self.VPc[:, par:par + 1],
                               self.TVA[:, par:par + 1], op=ALU.subtract)
        t_po = self.inc("dv", ins)
        self.w(ac, "dv", t_po)
        ac.activation(self.RSTc[:, par:par + 1], self.VPc[:, par:par + 1], AF.Ln)
        ac.drain()
        ins = ac.activation(self.RSTc[:, par:par + 1], self.RSTc[:, par:par + 1], AF.Exp, scale=-0.5)
        t_rst = self.inc("ac", ins)
        return (self.MUc[:, par:par + 1], self.RSTc[:, par:par + 1], t_rst)

    def h_build(self, src_buf, gain_idx=None, cache_stats=False):
        pe, dv, ac = self.pe, self.dv, self.ac
        T, NT = self.T, self.NT
        for i in range(NT):
            src = src_buf[:, i * C:(i + 1) * C]
            t_dv = self.inc("dv", dv.reduce_sum(self.S1A[:, i:i + 1], src, axis=AX))
            ac.drain()
            t_ac = self.inc("ac", ac.activation(self.SQS[:], src, AF.Square,
                                                accum_out=self.S2A[:, i:i + 1]))
        c1 = 1.0 / C
        self.w(dv, "ac", t_ac)
        dv.drain()
        dv.tensor_scalar(self.MUCACHE[:], self.S1A[:], c1, None, op0=ALU.mult)
        dv.drain()
        dv.tensor_tensor(self.VPA[:], self.MUCACHE[:], self.MUCACHE[:], op=ALU.mult)
        dv.drain()
        dv.tensor_scalar(self.VPA[:], self.VPA[:], EPS, None, op0=ALU.subtract)
        dv.drain()
        dv.tensor_scalar(self.RSTCACHE[:], self.S2A[:], c1, None, op0=ALU.mult)
        dv.drain()
        ins = dv.tensor_tensor(self.VPA[:], self.RSTCACHE[:], self.VPA[:], op=ALU.subtract)
        t_vp = self.inc("dv", ins)
        self.w(ac, "dv", t_vp)
        ac.activation(self.RSTCACHE[:], self.VPA[:], AF.Ln)
        ac.drain()
        ins = ac.activation(self.RSTCACHE[:], self.RSTCACHE[:], AF.Exp, scale=-0.5)
        t_rst = self.inc("ac", ins)
        nx_cons = [0, 0]
        pt_free = self.ptb_free[4]
        self.w(dv, "ac", t_rst)
        for i in range(NT):
            par = i % 2
            src = src_buf[:, i * C:(i + 1) * C]
            self.w(dv, "pe", nx_cons[par])
            nx = self.AST[:, par * C:(par + 1) * C]
            ins = dv.tensor_scalar(nx, src, self.MUCACHE[:, i:i + 1], self.RSTCACHE[:, i:i + 1],
                                   op0=ALU.subtract, op1=ALU.mult)
            t_nx = self.inc("dv", ins)
            self.w(pe, "dv", t_nx)
            self.w(pe, "dv", pt_free)
            ptb = self.pt_bf16(4)
            for j in range(NCH):
                mm = pe.transpose(ptb[:, j * 128:(j + 1) * 128], nx[:, j * 128:(j + 1) * 128], self.IDB[:])
            t_tr = self.inc("pe", mm)
            nx_cons[par] = t_tr
            self.w(dv, "pe", t_tr)
            if gain_idx is None:
                out = self.HF[:].rearrange("p (j t) -> p j t", j=NCH)[:, :, i * 128:(i + 1) * 128]
                ins = dv.tensor_copy(out, ptb[:].rearrange("p (j q) -> p j q", j=NCH))
            else:
                for j in range(NCH):
                    ins = dv.tensor_scalar(self.HF[:, j * T + i * 128: j * T + (i + 1) * 128],
                                           ptb[:, j * 128:(j + 1) * 128],
                                           self.LNW[:, gain_idx * 8 + j: gain_idx * 8 + j + 1],
                                           None, op0=ALU.mult)
            pt_free = self.inc("dv", ins)
        self.ptb_free[4] = pt_free

    def kv_build(self, mha, half, use_fp32=False):
        pe, dv, po = self.pe, self.dv, self.po
        T, NG = self.T, self.NG
        wslot_free = self.wsl_free
        wtok = [0, 0, 0, 0]
        kvbank_free = [0, 0, 0, 0]
        pt_free = self.ptb_free[4]
        vstage_cons = [0, 0]
        nblk = (T // 128) * 4
        ones_ap = self.VT().rearrange("p (b c) -> p b c", c=130)[:, :, 64:130:65]
        ins = dv.memset(ones_ap, 1.0)
        t_ones = self.inc("dv", ins)
        sidx = 0
        for p in range(4):
            pair = half * 4 + p
            streams = (("q", 0), ("k", C), ("v", 2 * C))
            sl_of = {}
            for which, cbase in streams:
                sl = sidx % 4
                sidx += 1
                sl_of[which] = sl
                col0 = cbase + pair * 128
                dst = self.WSQ[:, sl * 1024:(sl + 1) * 1024].rearrange("p (j f) -> p j f", j=8)
                if use_fp32:
                    src = self.d_wqkv[mha][:, col0:col0 + 128].rearrange("(j p) f -> p j f", p=128)
                    self.w(self.po, "pe", wslot_free[sl])
                    wtok[sl] = self.inc("wb%d" % sl, self.po.dma_start(dst, src), 16)
                else:
                    src = self.d_wqkvc[mha][:, col0:col0 + 128].rearrange("(j p) f -> p j f", p=128)
                    self.w(self.sy, "wc", self.cast_tok["wqkv%d" % mha])
                    self.w(self.sy, "pe", wslot_free[sl])
                    wtok[sl] = self.inc("wa%d" % sl, self.sy.dma_start(dst, src), 16)
            wsem = "wb%d" if use_fp32 else "wa%d"
            for which, dstbuf, scale in (("k", self.KF(), None), ("q", self.QH(), 0.125)):
                sl = sl_of[which]
                self.w(pe, wsem % sl, wtok[sl])
                for g in range(NG):
                    bank = g % 2
                    self.w(pe, "dv", kvbank_free[bank])
                    for j in range(NCH):
                        mm = pe.matmul(self.psb(bank),
                                       self.WSQ[:, sl * 1024 + j * 128: sl * 1024 + (j + 1) * 128],
                                       self.HF[:, j * T + g * 512: j * T + (g + 1) * 512],
                                       start=(j == 0), stop=(j == NCH - 1))
                    t_mm = self.inc("pe", mm)
                    if g == NG - 1:
                        wslot_free[sl] = t_mm
                    self.w(dv, "pe", t_mm)
                    dsl = dstbuf[:, p * T + g * 512: p * T + (g + 1) * 512]
                    if scale is None:
                        ins = dv.tensor_copy(dsl, self.psb(bank))
                    else:
                        ins = dv.tensor_scalar(dsl, self.psb(bank), scale, None, op0=ALU.mult)
                    kvbank_free[bank] = self.inc("dv", ins)
            sl = sl_of["v"]
            self.w(pe, wsem % sl, wtok[sl])
            for g in range(NG):
                bank = 2 + g % 2
                self.w(pe, "dv", kvbank_free[bank])
                for j in range(NCH):
                    mm = pe.matmul(self.psb(bank),
                                   self.WSQ[:, sl * 1024 + j * 128: sl * 1024 + (j + 1) * 128],
                                   self.HF[:, j * T + g * 512: j * T + (g + 1) * 512],
                                   start=(j == 0), stop=(j == NCH - 1))
                t_mm = self.inc("pe", mm)
                if g == NG - 1:
                    wslot_free[sl] = t_mm
                vsl = g % 2
                self.w(dv, "pe", t_mm)
                self.w(dv, "pe", vstage_cons[vsl])
                vst = self.AST[:, vsl * 512:(vsl + 1) * 512]
                ins = dv.tensor_copy(vst, self.psb(bank))
                t_ev = self.inc("dv", ins)
                kvbank_free[bank] = t_ev
                self.w(pe, "dv", t_ev)
                self.w(pe, "dv", pt_free)
                self.w(pe, "dv", t_ones)
                ptb = self.pt_bf16(4)
                for kk in range(4):
                    mm = pe.transpose(ptb[:, kk * 128:(kk + 1) * 128], vst[:, kk * 128:(kk + 1) * 128],
                                      self.IDB[:])
                t_tr = self.inc("pe", mm)
                vstage_cons[vsl] = t_tr
                self.w(dv, "pe", t_tr)
                vtr = self.VT().rearrange("p (b c) -> p b c", c=130)
                out = vtr[:, g * 16 + p: g * 16 + 13 + p: 4, 0:129]
                out = out.rearrange("p b (h d) -> p b h d", d=64) if False else \
                    vtr[:, g * 16 + p: g * 16 + 13 + p: 4, :].rearrange(
                        "p b c -> p b c")[:, :, 0:130]
                base = (g * 16 + p) * 130
                out = bass.AP(self.VT().tensor, self.VT().offset + base,
                              [list(self.VT().ap[0]), [4 * 130, 4], [65, 2], [1, 64]])
                ins = dv.tensor_copy(out, ptb[:, 0:512].rearrange("p (k h d) -> p k h d", k=4, h=2))
                pt_free = self.inc("dv", ins)
        self.ptb_free[4] = pt_free

    def emit_qknorms(self, half):
        pe, dv, ac = self.pe, self.dv, self.ac
        T = self.T
        nck = T // 512
        t_red = 0
        t_mm = 0
        for src_i, buf, dst in ((0, self.KF(), self.KN2A), (1, self.QH(), self.QN2A)):
            for wave in range(2):
                sq = self.HPQ[:, 0: 2 * T]
                t_sq = 0
                for pw in range(2):
                    p = wave * 2 + pw
                    self.w(ac, "pe", t_mm)
                    ins = ac.activation(sq[:, pw * T:(pw + 1) * T],
                                        buf[:, p * T:(p + 1) * T], AF.Square)
                    t_sq = self.inc("ac", ins)
                self.w(pe, "ac", t_sq)
                self.w(pe, "dv", t_red)
                for pw in range(2):
                    for h in range(2):
                        r = h * 64
                        for cc in range(nck):
                            mm = pe.matmul(
                                self.PS[r:r + 1, pw * 2048 + cc * 512: pw * 2048 + (cc + 1) * 512],
                                self.ONES[r:r + 64, 0:1],
                                sq[r:r + 64, pw * T + cc * 512: pw * T + (cc + 1) * 512],
                                start=True, stop=True)
                        t_mm = self.inc("pe", mm)
                for pw in range(2):
                    p = wave * 2 + pw
                    for h in range(2):
                        r = h * 64
                        self.w(dv, "pe", t_mm)
                        dv.drain()
                        ins = dv.reduce_max(dst[r:r + 1, p:p + 1],
                                            self.PS[r:r + 1, pw * 2048: pw * 2048 + T], axis=AX)
                        t_red = self.inc("dv", ins)
        self.w(dv, "dv", t_red)
        dv.drain()
        for h in range(2):
            r = h * 64
            dv.tensor_scalar(self.NMH[r:r + 1, 0:4], self.QN2A[r:r + 1, 0:4], -4.08, None, op0=ALU.mult)
            dv.tensor_scalar(self.NMH[r:r + 1, 4:8], self.KN2A[r:r + 1, 0:4], -1.02 / 16.0,
                             None, op0=ALU.mult)
        dv.drain()
        for h in range(2):
            r = h * 64
            dv.tensor_tensor(self.NMH[r:r + 1, 0:4], self.NMH[r:r + 1, 0:4],
                             self.NMH[r:r + 1, 4:8], op=ALU.add)
        dv.drain()
        for h in range(2):
            r = h * 64
            ins = dv.tensor_copy(self.NMHB[r:r + 1, h * 4: h * 4 + 4], self.NMH[r:r + 1, 0:4])
        t_nb = self.inc("dv", ins)
        self.w(pe, "dv", t_nb)
        for h in range(2):
            r = h * 64
            mm = pe.matmul(self.PS[:, 7 * 512 + h * 4: 7 * 512 + h * 4 + 4],
                           self.ONES[r:r + 1, 0:128], self.NMHB[r:r + 1, h * 4: h * 4 + 4],
                           start=True, stop=True)
        t_mm2 = self.inc("pe", mm)
        self.w(dv, "pe", t_mm2)
        ins = dv.tensor_copy(self.BND[:, 0:8], self.PS[:, 7 * 512: 7 * 512 + 8])
        t_bc = self.inc("dv", ins)
        self.bnd_tok = t_bc
        self.kn_guard = t_bc

    def attention_half(self, mha, half):
        pe, dv, ac = self.pe, self.dv, self.ac
        T, NG = self.T, self.NG
        for qg in range(NG):
            for p in range(4):
                gp = p % 2
                nj = 4 * qg + 4
                npair = nj // 2
                avtok = [0, 0]
                exp_of = {}

                def emit_ST(jp):
                    toks = [0, 0]
                    for jj in range(2):
                        j = jp * 2 + jj
                        for h in range(2):
                            bank = h * 2 + jj
                            self.w(pe, "ac", self.stb_free[bank])
                            if j == 0:
                                self.w(pe, "dv", self.obank_free[h * 2 + gp])
                                self.w(pe, "dv", self.kn_guard)
                            lhs = self.KF()[h * 64:(h + 1) * 64,
                                            p * T + j * 128: p * T + (j + 1) * 128]
                            rhs = self.QH()[h * 64:(h + 1) * 64,
                                            p * T + qg * 512: p * T + (qg + 1) * 512]
                            if j >= 4 * qg:
                                r = j - 4 * qg
                                pe.matmul(self.psb(bank), lhs, rhs, start=True, stop=False)
                                mm = pe.matmul(self.psb(bank), self.IDB[:],
                                               self.MK[:, r * 512:(r + 1) * 512],
                                               start=False, stop=True)
                            else:
                                mm = pe.matmul(self.psb(bank), lhs, rhs, start=True, stop=True)
                            toks[h] = self.inc("pe", mm)
                    sp = jp % 2
                    for h in range(2):
                        slot = h * 2 + sp
                        self.w(ac, "pe", toks[h])
                        self.w(ac, "pe", self.ptslot_free[slot])
                        ins = ac.activation(self.PTb(slot), self.PS[:, (h * 2) * 512:(h * 2) * 512 + 1024],
                                            AF.Exp, bias=self.BND[:, h * 4 + p: h * 4 + p + 1])
                        etok = self.inc("ac", ins)
                        self.stb_free[h * 2] = etok
                        self.stb_free[h * 2 + 1] = etok
                        exp_of[(jp, h)] = etok

                def emit_AV(jp):
                    sp = jp % 2
                    for jj in range(2):
                        j = jp * 2 + jj
                        for h in range(2):
                            slot = h * 2 + sp
                            obank = 4 + h * 2 + gp
                            self.w(pe, "ac", exp_of[(jp, h)])
                            blk = (j * 4 + p) * 130
                            mm = pe.matmul(self.PS[0:65, obank * 512: obank * 512 + 512],
                                           self.VT()[:, blk + h * 65: blk + (h + 1) * 65],
                                           self.PTb(slot)[:, jj * 512:(jj + 1) * 512],
                                           start=(j == 0), stop=(j == nj - 1))
                            avtok[h] = self.inc("pe", mm)
                            if jj == 1:
                                self.ptslot_free[slot] = avtok[h]

                for jp in range(npair):
                    emit_ST(jp)
                    self.step_rt()
                    if jp >= 1:
                        emit_AV(jp - 1)
                        self.step_rt()
                emit_AV(npair - 1)
                self.flush_rt()
                for h in range(2):
                    self.queue_roundtrip(half, p, qg, h, gp, avtok[h])
        self.flush_rt()

    def queue_roundtrip(self, half, p, qg, h, gp, t_av):
        pe, dv = self.pe, self.dv
        T = self.T
        obank = 4 + h * 2 + gp
        pair = half * 4 + p
        st = {}

        def s0():
            self.w(dv, "pe", t_av)
            self.w(dv, "pe", self.rt_osg_free)
            ins = dv.tensor_copy(self.OSG()[0:65, 0:512],
                                 self.PS[0:65, obank * 512: obank * 512 + 512])
            st["ev"] = self.inc("dv", ins)

        def s1():
            self.w(pe, "dv", st["ev"])
            for b in range(4):
                mm = pe.transpose(self.PS[:, obank * 512 + b * 65: obank * 512 + (b + 1) * 65],
                                  self.OSG()[0:65, b * 128:(b + 1) * 128], self.IDF[0:65, 0:65])
            st["tr"] = self.inc("pe", mm)
            self.rt_osg_free = st["tr"]

        def s2():
            self.w(dv, "pe", st["tr"])
            self.w(dv, "pe", self.rt_otb_free)
            dv.tensor_copy(self.OTM()[:, 0:260], self.PS[:, obank * 512: obank * 512 + 260])
            dv.drain()
            linv_src = bass.AP(self.OTM().tensor, self.OTM().offset + 64,
                               [list(self.OTM().ap[0]), [65, 4], [1, 1]])
            dv.reciprocal(self.OL4[:, 0:4], linv_src)
            dv.drain()
            for b in range(4):
                ins = dv.tensor_scalar(self.OTB()[:, b * 64:(b + 1) * 64],
                                       self.OTM()[:, b * 65: b * 65 + 64],
                                       self.OL4[:, b:b + 1], None, op0=ALU.mult)
            st["nb"] = self.inc("dv", ins)

        def s3():
            self.w(pe, "dv", st["nb"])
            ptb = self.pt_bf16(obank)
            for b in range(4):
                mm = pe.transpose(ptb[0:64, b * 128:(b + 1) * 128],
                                  self.OTB()[:, b * 64:(b + 1) * 64], self.IDB[:])
            st["tb"] = self.inc("pe", mm)
            self.rt_otb_free = st["tb"]

        def s4():
            self.w(dv, "pe", st["tb"])
            ins = dv.tensor_copy(
                self.OF[h * 64:(h + 1) * 64, pair * T + qg * 512: pair * T + (qg + 1) * 512],
                ptb_dummy[0] if False else self.pt_bf16(obank)[0:64, 0:512])
            self.obank_free[h * 2 + gp] = self.inc("dv", ins)

        self.rt_queue.extend([s0, s1, s2, s3, s4])

    def step_rt(self):
        if self.rt_queue:
            self.rt_queue.pop(0)()

    def flush_rt(self):
        while self.rt_queue:
            self.rt_queue.pop(0)()

    def proj_phase(self, mha, is_inner):
        pe, dv, po = self.pe, self.dv, self.po
        T, NG = self.T, self.NG
        wslot_free = self.wsl_free
        wtok = [0, 0]
        pbank_free = [0, 0]
        pt_free = self.ptb_free[4]
        afm_cons = 0
        atm_free = [0, 0]
        for g in range(NG):
            evtoks = []
            for ct in range(8):
                sl = ct % 2
                src = self.d_wpc[mha][:, ct * 128:(ct + 1) * 128].rearrange("(j p) f -> p j f", p=128)
                self.w(self.sy, "wc", self.cast_tok["wp%d" % mha])
                self.w(self.sy, "pe", wslot_free[sl])
                dst = self.WSQ[:, sl * 1024:(sl + 1) * 1024].rearrange("p (j f) -> p j f", j=8)
                wtok[sl] = self.inc("wa%d" % sl, self.sy.dma_start(dst, src), 16)
                self.w(pe, "wa%d" % sl, wtok[sl])
                bank = ct % 2
                self.w(pe, "dv", pbank_free[bank])
                if ct == 0:
                    self.w(pe, "pe", afm_cons)
                for j in range(NCH):
                    mm = pe.matmul(self.psb(bank),
                                   self.WSQ[:, sl * 1024 + j * 128: sl * 1024 + (j + 1) * 128],
                                   self.OF[:, j * T + g * 512: j * T + (g + 1) * 512],
                                   start=(j == 0), stop=(j == NCH - 1))
                t_mm = self.inc("pe", mm)
                wslot_free[sl] = t_mm
                self.w(dv, "pe", t_mm)
                self.w(dv, "pe", afm_cons)
                ins = dv.tensor_copy(self.AST[:, ct * 512:(ct + 1) * 512], self.psb(bank))
                pbank_free[bank] = self.inc("dv", ins)
                evtoks.append(pbank_free[bank])
            afm_tok = max(evtoks)
            for tt in range(4):
                tslot = tt % 2
                self.w(pe, "dv", afm_tok)
                self.w(pe, "dv", pt_free)
                self.w(pe, "dv", atm_free[tslot])
                ptb = self.pt_bf16(4)
                for ct in range(8):
                    mm = pe.transpose(ptb[:, ct * 128:(ct + 1) * 128],
                                      self.AST[:, ct * 512 + tt * 128: ct * 512 + (tt + 1) * 128],
                                      self.IDB[:])
                t_tr = self.inc("pe", mm)
                if tt == 3:
                    afm_cons = t_tr
                self.w(dv, "pe", t_tr)
                atm = self.ATM[:, tslot * 1024:(tslot + 1) * 1024]
                ins = dv.tensor_copy(atm, ptb[:])
                pt_free = self.inc("dv", ins)
                ti = g * 4 + tt
                xr = self.XR[:, ti * C:(ti + 1) * C]
                if not is_inner:
                    par = tt % 2
                    mu, rstd, t_rst = self.ln_stats(atm, par, src_dv_tok=pt_free)
                    self.w(dv, "ac", t_rst)
                    dv.tensor_scalar(atm, atm, mu, rstd, op0=ALU.subtract, op1=ALU.mult)
                    dv.drain()
                    ins = dv.tensor_tensor(xr, xr, atm, op=ALU.add)
                    t_res = self.inc("dv", ins)
                    self.ln_free[par] = t_res
                    atm_free[tslot] = t_res
                else:
                    oslot = 1 - tslot
                    self.w(dv, "pe", 0)
                    self.w(dv, "dv", 0)
                    other = self.ATM[:, oslot * 1024:(oslot + 1) * 1024]
                    self.w(dv, "dv", atm_free[oslot]) if False else None
                    dv.tensor_scalar(other, xr, self.MUCACHE[:, ti:ti + 1],
                                     self.RSTCACHE[:, ti:ti + 1], op0=ALU.subtract, op1=ALU.mult)
                    dv.drain()
                    uu = self.UU[:, ti * C:(ti + 1) * C]
                    ins = dv.tensor_tensor(uu, other, atm, op=ALU.add)
                    t_res = self.inc("dv", ins)
                    atm_free[tslot] = t_res
                    atm_free[oslot] = t_res
        self.ptb_free[4] = pt_free

    def mha_slot(self, seq, mha, boot=False):
        is_inner = (mha == 1)
        self.h_build(self.XR, gain_idx=(0 if is_inner else None), cache_stats=is_inner)
        self.barrier()
        if self.ck("h%d" % mha):
            self.dump("h", self.HF[:])
            raise _DebugStop()
        for half in range(2):
            self.kv_build(mha, half, use_fp32=(boot and half == 0))
            if boot and half == 0:
                self.prep_wcache()
            self.barrier()
            self.emit_qknorms(half)
            if half == 0 and self.ck("kv%d" % mha):
                self.dump("kf", self.KF()[:])
                self.dump("vt", self.VT()[:])
                raise _DebugStop()
            self.attention_half(mha, half)
            self.barrier()
        if self.ck("of%d" % mha):
            self.dump("of", self.OF[:, 0:8 * self.T])
            raise _DebugStop()
        self.proj_phase(mha, is_inner)
        self.barrier()

    def router_phase(self):
        pe, dv, po, ac = self.pe, self.dv, self.po, self.ac
        T, NG = self.T, self.NG
        lg_free = 0
        lgt_free = [0, 0]
        ps5_free = 0
        wt_tok = 0
        for g in range(NG):
            for j in range(NCH):
                mm = pe.matmul(self.PS[0:8, 7 * 512: 7 * 512 + 512],
                               self.RW[:, j * 8:(j + 1) * 8],
                               self.HF[:, j * T + g * 512: j * T + (g + 1) * 512],
                               start=(j == 0), stop=(j == NCH - 1))
            t_mm = self.inc("pe", mm)
            self.w(dv, "pe", t_mm)
            self.w(dv, "pe", lg_free)
            ins = dv.tensor_copy(self.LG[0:8, 0:512], self.PS[0:8, 7 * 512: 7 * 512 + 512])
            t_lg = self.inc("dv", ins)
            for tt in range(4):
                par = tt % 2
                self.w(pe, "dv", t_lg)
                self.w(pe, "dv", max(lgt_free[par], ps5_free))
                mm = pe.transpose(self.PS[:, 5 * 512: 5 * 512 + 8],
                                  self.LG[0:8, tt * 128:(tt + 1) * 128], self.IDF[0:8, 0:8])
                t_tr = self.inc("pe", mm)
                if tt == 3:
                    lg_free = t_tr
                self.w(dv, "pe", t_tr)
                ins = dv.tensor_copy(self.LGT[:, par * 8:(par + 1) * 8],
                                     self.PS[:, 5 * 512: 5 * 512 + 8])
                t_lgt = self.inc("dv", ins)
                ps5_free = t_lgt
                lgt = self.LGT[:, par * 8:(par + 1) * 8]
                dv.drain()
                dv.reduce_max(self.RM1N[:], lgt, axis=AX, negate=True)
                dv.drain()
                dv.tensor_scalar(self.RM1[:], self.RM1N[:], -1.0, None, op0=ALU.mult)
                dv.drain()
                dv.tensor_scalar(self.RMK1[:], lgt, self.RM1[:], None, op0=ALU.is_ge)
                dv.drain()
                dv.scalar_tensor_tensor(self.RL2[:], self.RMK1[:], -1e30, lgt,
                                        op0=ALU.mult, op1=ALU.add)
                dv.drain()
                ins = dv.reduce_max(self.RM2[:], self.RL2[:], axis=AX)
                t_m2 = self.inc("dv", ins)
                self.w(ac, "dv", t_m2)
                ins = ac.activation(self.RE2[:], self.RM2[:], AF.Exp, bias=self.RM1N[:])
                t_e2 = self.inc("ac", ins)
                self.w(dv, "ac", t_e2)
                dv.tensor_scalar(self.RDEN[:], self.RE2[:], 1.0, None, op0=ALU.add)
                dv.drain()
                dv.tensor_scalar(self.RMK2[:], self.RL2[:], self.RM2[:], None, op0=ALU.is_ge)
                dv.drain()
                ti = g * 4 + tt
                dv.scalar_tensor_tensor(self.RMK2[:], self.RMK2[:], self.RE2[:], self.RMK1[:],
                                        op0=ALU.mult, op1=ALU.add)
                dv.reciprocal(self.RW1[:], self.RDEN[:])
                dv.drain()
                ins = dv.tensor_scalar(self.WTOK[:, ti * 8:(ti + 1) * 8], self.RMK2[:],
                                       self.RW1[:], None, op0=ALU.mult)
                wt_tok = self.inc("dv", ins)
                lgt_free[par] = wt_tok
        self.wtok_tok = wt_tok

    def moe_phase(self, seq):
        pe, dv, ac, po = self.pe, self.dv, self.ac, self.po
        T = self.T
        tgW = T // 2
        self.tgW = tgW
        self.WBv = self.AST[:, 0:tgW]
        nsub = max(1, tgW // 512)
        subW = min(512, tgW)
        nh = max(1, tgW // 512)
        self.w(self.sy, "pe", self.cnt["pe"])
        self.w(self.sy, "dv", self.cnt["dv"])
        self.w(self.sy, "ac", self.cnt["ac"])
        for tg in range(2):
            tgbase = tg * tgW
            tgtile0 = tg * (tgW // 128)
            gel_tok = [0, 0]
            scl_tok = [0] * 8
            w1_tok = [0, 0]
            w1_free = [0, 0]
            w2_free = 0
            w2_tok_cur = 0
            w2_tok_prev = 0
            scl_prev = [0] * 8
            yevac_tok = [0] * (2 * nsub)
            y_last_evac = 0
            self._ybank_tok = [0, 0, 0, 0]
            wb_tok = 0
            hpq_guard = 0

            def h_mms(fc, oct_):
                bank0 = (fc % 2) * 2
                sl = fc // 4
                self.w(pe, "w1a" if sl == 0 else "w1b", w1_tok[sl])
                self.w(pe, "ac", gel_tok[fc % 2])
                if fc == 0:
                    self.w(pe, "dv", wb_tok)
                for j in range(NCH):
                    for hh in range(nh):
                        mm = pe.matmul(self.PS[:, (bank0 + hh) * 512:(bank0 + hh) * 512 + subW],
                                       self.W1Q(sl)[:, j * 512 + (fc % 4) * 128: j * 512 + (fc % 4) * 128 + 128],
                                       self.HF[:, j * T + tgbase + hh * subW: j * T + tgbase + (hh + 1) * subW],
                                       start=(j == 0), stop=(j == NCH - 1))
                tok = self.inc("pe", mm)
                return tok

            def gelu_scale(fc, t_h):
                bank0 = (fc % 2) * 2
                self.w(ac, "pe", t_h)
                self.w(ac, "dv", scl_prev[fc])
                self.w(ac, "pe", hpq_guard)
                ins = ac.activation(self.HPQ[:, fc * tgW:(fc + 1) * tgW],
                                    self.PS[:, bank0 * 512: bank0 * 512 + tgW], AF.Gelu)
                t_g = self.inc("ac", ins)
                gel_tok[fc % 2] = t_g
                self.w(dv, "ac", t_g)
                ins = dv.tensor_tensor(self.HPQ[:, fc * tgW:(fc + 1) * tgW],
                                       self.HPQ[:, fc * tgW:(fc + 1) * tgW],
                                       self.WBv, op=ALU.mult)
                scl_tok[fc] = self.inc("dv", ins)

            def w2_combos(first):
                nonlocal w2_free, y_last_evac
                dv.drain()
                ybank_tok = list(self._ybank_tok)
                for ci, (ch, sub) in enumerate([(a, b) for a in range(2) for b in range(nsub)]):
                    self.w(pe, "w2o", w2_tok_prev)
                    mm_tok = [0, 0, 0, 0]
                    for fc in range(8):
                        self.w(pe, "dv", scl_prev[fc])
                        for cti in range(4):
                            if fc == 0:
                                self.w(pe, "dv", ybank_tok[cti])
                            cto = (ch * 4 + cti) * 128
                            mm = pe.matmul(self.PS[:, (4 + cti) * 512:(4 + cti) * 512 + subW],
                                           self.W2O[:, fc * 1024 + cto: fc * 1024 + cto + 128],
                                           self.HPQ[:, fc * tgW + sub * subW: fc * tgW + (sub + 1) * subW],
                                           start=(fc == 0), stop=(fc == 7))
                            if fc == 7:
                                mm_tok[cti] = self.inc("pe", mm)
                    w2_free = mm_tok[3]
                    for cti in range(4):
                        self.w(dv, "pe", mm_tok[cti])
                        ysl = self.BIGB[:, (ch * 4 + cti) * tgW + sub * subW:
                                        (ch * 4 + cti) * tgW + (sub + 1) * subW]
                        ps_y = self.PS[:, (4 + cti) * 512:(4 + cti) * 512 + subW]
                        if first:
                            ins = dv.tensor_scalar(ysl, ps_y, 1.0, None, op0=ALU.mult)
                        else:
                            ins = dv.scalar_tensor_tensor(ysl, ps_y, 1.0, ysl, op0=ALU.mult, op1=ALU.add)
                        ybank_tok[cti] = self.inc("dv", ins)
                        y_last_evac = ybank_tok[cti]
                self._ybank_tok = ybank_tok

            for oct_ in range(16):
                if oct_ % 2 == 0:
                    wb_tok = self.wb_build(tgtile0, oct_ // 2, gel_tok, scl_tok)
                for q in range(2):
                    col0 = oct_ * 1024 + q * 512
                    src = self.d_w1c[:, col0:col0 + 512].rearrange("(j p) f -> p j f", p=128)
                    self.w(self.sy, "pe", w1_free[q])
                    if oct_ == 0 and tg == 0:
                        self.w(self.sy, "wc", self.wc_tok)
                    dst = self.W1Q(q)[:].rearrange("p (j f) -> p j f", j=8)
                    w1_tok[q] = self.inc("w1a" if q == 0 else "w1b", self.sy.dma_start(dst, src), 16)
                t_h0 = h_mms(0, oct_)
                t_h1 = h_mms(1, oct_)
                if oct_ > 0:
                    w2_combos(first=(oct_ == 1))
                hpq_guard = w2_free
                self.w(self.sy, "pe", w2_free)
                src = self.d_w2c[oct_ * 1024:(oct_ + 1) * 1024, :].rearrange("(j p) c -> p j c", p=128)
                dst = self.W2O[:].rearrange("p (j c) -> p j c", j=8)
                w2_tok_cur = self.inc("w2o", self.sy.dma_start(dst, src), 16)
                gelu_scale(0, t_h0)
                gelu_scale(1, t_h1)
                for fc in range(2, 8):
                    t_h = h_mms(fc, oct_)
                    gelu_scale(fc, t_h)
                w1_free = [self.cnt["pe"], self.cnt["pe"]]
                w2_tok_prev = w2_tok_cur
                scl_prev = list(scl_tok)
            w2_combos(first=False)
            self.barrier()
            if self.ck("y") and tg == 0:
                self.dump("y", self.BIGB[:, 0:4 * self.T])
                self.dump("wtok", self.WTOK[:])
                self.dump("t2d", self.HF[:])
                raise _DebugStop()
            self.finale(seq, tg, tgW)
            self.barrier()

    def wb_build(self, tgtile0, expert, gel_tok, scl_tok):
        pe, dv = self.pe, self.dv
        ntile = self.tgW // 128
        dv.drain()
        self.w(dv, "pe", self.cnt["pe"])
        for tt in range(ntile):
            ti = tgtile0 + tt
            ins = dv.tensor_scalar(self.AST[:, 2048 + tt * 128: 2048 + (tt + 1) * 128],
                                   self.ONES[:], self.WTOK[:, ti * 8 + expert: ti * 8 + expert + 1],
                                   None, op0=ALU.mult)
        t_stage = self.inc("dv", ins)
        self.w(pe, "dv", t_stage)
        self.w(pe, "ac", gel_tok[0])
        ptb = self.pt_bf16(0)
        for tt in range(ntile):
            mm = pe.transpose(ptb[:, tt * 128:(tt + 1) * 128],
                              self.AST[:, 2048 + tt * 128: 2048 + (tt + 1) * 128], self.IDB[:])
        t_tr = self.inc("pe", mm)
        self.w(dv, "pe", t_tr)
        ins = dv.tensor_copy(self.WBv, ptb[:, 0:self.tgW])
        return self.inc("dv", ins)

    def finale(self, seq, tg, tgW):
        pe, dv, sy = self.pe, self.dv, self.sy
        ntile = tgW // 128
        astf = self.AST[:].bitcast(F32)
        atmf = self.ATM[:].bitcast(F32)
        outslot = [astf[:, 1024:2048], atmf[:, 0:1024]]
        out_tok = [0, 0]
        f_cons = 0
        tr_free = 0
        for tt in range(ntile):
            ti = tg * ntile + tt
            par = tt % 2
            self.w(pe, "dv", tr_free)
            for ct in range(8):
                bank = ct // 4
                mm = pe.transpose(self.PS[:, bank * 512 + (ct % 4) * 128: bank * 512 + (ct % 4) * 128 + 128],
                                  self.BIGB[:, ct * tgW + tt * 128: ct * tgW + (tt + 1) * 128],
                                  self.IDF[:])
            t_tr = self.inc("pe", mm)
            self.w(dv, "pe", t_tr)
            self.w(dv, "dv", 0)
            ftmp = astf[:, 0:1024]
            ins = dv.scalar_tensor_tensor(ftmp, self.PS[:, 0:1024], 1.0,
                                          self.UU[:, ti * C:(ti + 1) * C], op0=ALU.mult, op1=ALU.add)
            t_f = self.inc("dv", ins)
            tr_free = t_f
            mu, rstd, t_rst = self.ln_stats(ftmp, par, src_dv_tok=t_f)
            self.w(dv, "ac", t_rst)
            self.w(dv, "qo%d" % par, out_tok[par])
            nrm = outslot[par]
            dv.tensor_scalar(nrm, ftmp, mu, rstd, op0=ALU.subtract, op1=ALU.mult)
            dv.drain()
            ins = dv.tensor_tensor(nrm, nrm, self.XR[:, ti * C:(ti + 1) * C], op=ALU.add)
            t_o = self.inc("dv", ins)
            self.ln_free[par] = t_o
            self.w(sy, "dv", t_o)
            dst = self.d_out[seq, ti * 128:(ti + 1) * 128, :]
            out_tok[par] = self.inc("qo%d" % par, sy.dma_start(dst, nrm), 16)

    def dump(self, name, ap):
        w = ap.shape[-1]
        npart = ap.shape[0]
        d = self.nc.dram_tensor("dbg_" + name, [npart, w], F32, kind="ExternalOutput").ap()
        self.barrier()
        for s in self.sem:
            if s != "qdbg" and self.cnt[s]:
                self.po.wait_ge(self.sem[s], self.cnt[s])
        self.inc("qdbg", self.po.dma_start(d, ap), 16)

    def ck(self, phase):
        if self.debug_phase is None:
            return False
        return phase == self.debug_phase

    def build(self, debug_phase=None):
        self.debug_phase = debug_phase

        def fin():
            for s in ("qo0", "qo1", "qdbg"):
                if self.cnt[s]:
                    self.po.wait_ge(self.sem[s], self.cnt[s])
                    self.sy.wait_ge(self.sem[s], self.cnt[s])
            return self.nc

        self.load_consts()
        try:
            self._main()
        except _DebugStop:
            pass
        return fin()

    def _main(self):
        for seq in range(2):
            if seq == 1:
                self.barrier()
            xt = self.load_x(seq)
            for eng in (self.dv, self.pe, self.ac):
                self.w(eng, "qx", xt)
            self.w(self.dv, "dv", self.const_tok)
            if self.ck("x"):
                self.dump("x", self.XR[:])
                raise _DebugStop()
            self.mha_slot(seq, 0, boot=(seq == 0))
            if self.ck("mha0"):
                self.dump("x1", self.XR[:])
                raise _DebugStop()
            self.mha_slot(seq, 1)
            if self.ck("mha1"):
                self.dump("x1", self.XR[:])
                self.dump("u", self.UU[:, 0:self.NT * C])
                raise _DebugStop()
            self.h_build(self.UU, gain_idx=1, cache_stats=False)
            self.barrier()
            if self.ck("t2"):
                self.dump("t2", self.HF[:])
                raise _DebugStop()
            self.router_phase()
            self.barrier()
            if self.ck("router"):
                self.dump("wtok", self.WTOK[:])
                raise _DebugStop()
            self.moe_phase(seq)
            if self.ck("y"):
                raise _DebugStop()


def build_program(T=2048):
    return KB(T).build()


def make_masks():
    kk = np.arange(128)[:, None]
    qq = np.arange(512)[None, :]
    return np.stack([np.where(qq >= 128 * r + kk, 0.0, -1e30) for r in range(4)]).astype(np.float32)


def kernel(L0_attn_x, L0_attn_W_attn, L0_attn_W_proj, L0_ffn_ln1_w, L0_ffn_c_attn_w,
           L0_ffn_c_proj_w, L0_ffn_ln2_w, L0_ffn_router_w, L0_ffn_w1, L0_ffn_w2):
    x = np.ascontiguousarray(np.asarray(L0_attn_x, dtype=np.float32))
    B, T, _ = x.shape
    n_cores = 8
    per = B // n_cores
    nc = build_program(T)
    consts = {
        "wqkv0": np.ascontiguousarray(L0_attn_W_attn, dtype=np.float32),
        "wqkv1": np.ascontiguousarray(L0_ffn_c_attn_w, dtype=np.float32),
        "wp0": np.ascontiguousarray(L0_attn_W_proj, dtype=np.float32),
        "wp1": np.ascontiguousarray(L0_ffn_c_proj_w, dtype=np.float32),
        "lnw": np.ascontiguousarray(
            np.stack([np.asarray(L0_ffn_ln1_w, dtype=np.float32).reshape(8, 128),
                      np.asarray(L0_ffn_ln2_w, dtype=np.float32).reshape(8, 128)])),
        "rwT": np.ascontiguousarray(np.asarray(L0_ffn_router_w, dtype=np.float32).T),
        "w1": np.ascontiguousarray(L0_ffn_w1, dtype=np.float32),
        "w2": np.ascontiguousarray(L0_ffn_w2, dtype=np.float32),
        "ident": np.eye(128, dtype=np.float32),
        "masks": make_masks(),
        "ones": np.ones((128, 128), dtype=np.float32),
    }
    in_maps = []
    for c in range(n_cores):
        m = dict(consts)
        m["x"] = np.ascontiguousarray(x[c * per:(c + 1) * per])
        in_maps.append(m)
    res = run_bass_kernel_spmd(nc, in_maps, core_ids=list(range(n_cores)))
    return np.concatenate([np.asarray(r["out"], dtype=np.float32) for r in res.results], axis=0)


# version 14
# speedup vs baseline: 1.3255x; 1.0178x over previous
import numpy as np
import concourse.bass as bass
import concourse.mybir as mybir
from concourse.bass_utils import run_bass_kernel_spmd
from contextlib import ExitStack

F32 = mybir.dt.float32
BF16 = mybir.dt.bfloat16
AF = mybir.ActivationFunctionType
ALU = mybir.AluOpType
AX = mybir.AxisListType.X

C = 1024
NCH = 8
EPS = 1e-5
FF = 16384


class _DebugStop(Exception):
    pass


class KB:
    def __init__(self, T):
        self.T = T
        self.NT = T // 128
        self.NG = T // 512
        nc = bass.Bass("TRN2", target_bir_lowering=False)
        self.nc = nc
        self.pe, self.dv, self.ac, self.po, self.sy = nc.tensor, nc.vector, nc.scalar, nc.gpsimd, nc.sync
        self.st = ExitStack()
        self.cnt = {}
        self.sem = {}
        NT = self.NT

        dram = lambda n, shp, k: nc.dram_tensor(n, shp, F32, kind=k).ap()
        self.d_x = dram("x", [2, T, C], "ExternalInput")
        self.d_wqkv = [dram("wqkv0", [C, 3 * C], "ExternalInput"),
                       dram("wqkv1", [C, 3 * C], "ExternalInput")]
        self.d_wp = [dram("wp0", [C, C], "ExternalInput"),
                     dram("wp1", [C, C], "ExternalInput")]
        self.d_ln = dram("lnw", [2, 8, 128], "ExternalInput")
        self.d_rwT = dram("rwT", [8, C], "ExternalInput")
        self.d_w1 = dram("w1", [C, FF], "ExternalInput")
        self.d_w2 = dram("w2", [FF, C], "ExternalInput")
        self.d_id = dram("ident", [128, 128], "ExternalInput")
        self.d_mk = dram("masks", [4, 128, 512], "ExternalInput")
        self.d_ones = dram("ones", [128, 128], "ExternalInput")
        self.d_out = dram("out", [2, T, C], "ExternalOutput")
        self.d_w1c = nc.dram_tensor("w1c", [C, FF], BF16, kind="Internal").ap()
        self.d_w2c = nc.dram_tensor("w2c", [FF, C], BF16, kind="Internal").ap()
        self.d_wqkvc = [nc.dram_tensor("wqkv%dc" % m, [C, 3 * C], BF16, kind="Internal").ap()
                        for m in range(2)]
        self.d_wpc = [nc.dram_tensor("wp%dc" % m, [C, C], BF16, kind="Internal").ap()
                      for m in range(2)]

        sb = lambda n, shp, dt: self.st.enter_context(nc.sbuf_tensor(n, shp, dt))
        self.XR = sb("XR", [128, NT * C], BF16)
        self.HF = sb("HF", [128, NCH * T], BF16)
        self.BIGB = sb("BIGB", [128, 4 * T], F32)
        self.KVW = sb("KVW", [128, max(4 * T + (T // 128) * 4 * 130, 16384)], BF16)
        self.AR1 = sb("AR1", [128, max(8 * T, 4 * T + 8192)], BF16)
        self.AST = sb("AST", [128, 4096], BF16)
        self.HPQ = sb("HPQ", [128, 8192], BF16)
        self.ATM = sb("ATM", [128, 2048], BF16)
        self.SQS = sb("SQS", [128, C], BF16)
        self.MK = sb("MK", [128, 4 * 512], BF16)
        self.IDB = sb("IDB", [128, 128], BF16)
        self.IDF = sb("IDF", [128, 128], F32)
        self.ONES = sb("ONESB", [128, 128], BF16)
        self.LNW = sb("LNW", [128, 16], F32)
        self.RW = sb("RW", [128, 64], BF16)
        self.WTOK = sb("WTOK", [128, NT * 8], F32)
        self.WSQ = sb("WSQ", [128, 4 * 1024], BF16)
        self.LGT = sb("LGT", [128, 16], F32)
        self.S1 = sb("S1", [128, 2], F32)
        self.S2 = sb("S2", [128, 2], F32)
        self.MUc = sb("MUc", [128, 2], F32)
        self.VPc = sb("VPc", [128, 2], F32)
        self.TVA = sb("TVA", [128, 2], F32)
        self.RSTc = sb("RSTc", [128, 2], F32)
        self.MUCACHE = sb("MUCACHE", [128, NT], F32)
        self.RSTCACHE = sb("RSTCACHE", [128, NT], F32)
        self.S1A = sb("S1A", [128, NT], F32)
        self.S2A = sb("S2A", [128, NT], F32)
        self.VPA = sb("VPA", [128, NT], F32)
        self.NM = sb("NM", [128, 8], F32)
        self.KN2A = sb("KN2A", [128, 8], F32)
        self.QN2A = sb("QN2A", [128, 8], F32)
        self.OL4 = sb("OL4", [128, 4], F32)
        self.BND = sb("BND", [128, 8], F32)
        self.NKND = sb("NKND", [128, 8], F32)
        self.QN2 = sb("QN2", [128, 8], F32)
        self.NMH = sb("NMH", [128, 8], F32)
        self.NMHB = sb("NMHB", [128, 8], BF16)
        self.LL = sb("LL", [128, 8], F32)
        self.NGM = sb("NGM", [128, 4], F32)
        self.BB = sb("BB", [128, 8], F32)
        self.BL = sb("BL", [128, 8], F32)
        self.LT = sb("LT", [128, 4], F32)
        self.LIV = sb("LIV", [128, 4], F32)
        self.RM1 = sb("RM1", [128, 1], F32)
        self.RM1N = sb("RM1N", [128, 1], F32)
        self.RM2 = sb("RM2", [128, 1], F32)
        self.RE2 = sb("RE2", [128, 1], F32)
        self.RDEN = sb("RDEN", [128, 1], F32)
        self.RW1 = sb("RW1", [128, 1], F32)
        self.RW2 = sb("RW2", [128, 1], F32)
        self.RMK1 = sb("RMK1", [128, 8], F32)
        self.RL2 = sb("RL2", [128, 8], F32)
        self.RMK2 = sb("RMK2", [128, 8], F32)

        self.PS = self.st.enter_context(nc.psum_tensor("PS", [128, 4096], F32))

        for s in ["pe", "dv", "ac", "po", "qld", "qx", "qdbg", "qo0", "qo1",
                  "wa0", "wa1", "wa2", "wa3", "wb0", "wb1", "wb2", "wb3",
                  "wq", "w1a", "w1b", "w2o", "wc"]:
            self.sem[s] = self.st.enter_context(nc.semaphore("sem_" + s))
            self.cnt[s] = 0

        self.ln_free = [0, 0]
        self.bank_free = [0, 0, 0, 0]
        self.slot_free = [0, 0]
        self.nm_free = [0, 0]
        self.o_evac = [0, 0]
        self.ptb_free = {0: 0, 4: 0, 7: 0}
        self.wsl_free = [0, 0, 0, 0]
        self.pts_free = [0, 0]
        self.liv_free = [0, 0, 0, 0]
        self.ll_free = [0, 0]
        self.stb_free = [0, 0, 0, 0]
        self.ptslot_free = [0, 0, 0, 0]
        self.obank_free = [0, 0, 0, 0]
        self.rt_osg_free = 0
        self.rt_otb_free = 0
        self.rt_queue = []
        self.itc = 0

    def inc(self, s, instr, n=1):
        instr.then_inc(self.sem[s], n)
        self.cnt[s] += n
        return self.cnt[s]

    def w(self, eng, s, tok):
        if tok and tok > 0:
            eng.wait_ge(self.sem[s], tok)

    def barrier(self):
        eng_of = {"pe": self.pe, "dv": self.dv, "ac": self.ac, "po": self.po}
        self.dv.drain()
        self.ac.drain()
        self.po.drain()
        for a in eng_of:
            for b in eng_of:
                if a != b and self.cnt[b] > 0:
                    eng_of[a].wait_ge(self.sem[b], self.cnt[b])
            for s in ("qo0", "qo1"):
                if self.cnt[s] > 0:
                    eng_of[a].wait_ge(self.sem[s], self.cnt[s])

    def psb(self, bank, w=512):
        return self.PS[:, bank * 512: bank * 512 + w]

    def pt_bf16(self, bank):
        return self.PS[:, bank * 512:(bank + 1) * 512].bitcast(BF16)

    @property
    def L8(self):
        return self.HPQ[:].bitcast(F32)

    @property
    def LG(self):
        return self.HPQ[:].bitcast(F32)

    @property
    def UU(self):
        return self.AR1
    def PPslot(self, slot):
        return self.AR1[:, slot * 2 * self.T:(slot + 1) * 2 * self.T]
    def QH(self):
        return self.AR1[:, 0: 4 * self.T]
    def PTb(self, slot):
        o = 4 * self.T + slot * 1024
        return self.AR1[:, o: o + 1024]
    def OSG(self):
        o4 = (4 * self.T + 4096) // 2
        return self.AR1[:].bitcast(F32)[:, o4: o4 + 512]
    def OTM(self):
        o4 = (4 * self.T + 4096) // 2 + 512
        return self.AR1[:].bitcast(F32)[:, o4: o4 + 272]
    def OTB(self):
        o = 4 * self.T + 4096 + 1024 + 544
        return self.AR1[:, o: o + 256]

    def KF(self):
        return self.KVW[:, 0: 4 * self.T]
    def VT(self):
        return self.KVW[:, 4 * self.T: 4 * self.T + (self.T // 128) * 4 * 130]
    def W1Q(self, slot):
        return self.KVW[:, slot * 4096:(slot + 1) * 4096]
    @property
    def W2O(self):
        return self.KVW[:, 8192: 16384]

    @property
    def OF(self):
        return self.BIGB[:].bitcast(BF16)

    def load_consts(self):
        po, pe, dv = self.po, self.pe, self.dv
        self.inc("qld", po.dma_start(self.IDB[:], self.d_id), 16)
        self.inc("qld", po.dma_start(self.IDF[:], self.d_id), 16)
        self.inc("qld", po.dma_start(self.ONES[:], self.d_ones), 16)
        mk = self.d_mk.rearrange("r p k -> p r k")
        self.inc("qld", po.dma_start(self.MK[:].rearrange("p (r k) -> p r k", r=4), mk), 16)
        self.inc("qld", po.dma_start(self.L8[0:8, 0:128], self.d_ln[0]), 16)
        self.inc("qld", po.dma_start(self.L8[0:8, 128:256], self.d_ln[1]), 16)
        t = self.inc("qld", po.dma_start(self.L8[0:8, 256:1280], self.d_rwT), 16)
        self.w(pe, "qld", t)
        pe.transpose(self.PS[:, 0:8], self.L8[0:8, 0:128], self.IDF[0:8, 0:8])
        pe.transpose(self.PS[:, 8:16], self.L8[0:8, 128:256], self.IDF[0:8, 0:8])
        for j in range(8):
            mm = pe.transpose(self.PS[:, 512 + j * 8: 512 + (j + 1) * 8],
                              self.L8[0:8, 256 + j * 128: 256 + (j + 1) * 128],
                              self.IDF[0:8, 0:8])
        ptok = self.inc("pe", mm)
        self.w(dv, "pe", ptok)
        dv.tensor_copy(self.LNW[:, 0:16], self.PS[:, 0:16])
        ins = dv.tensor_copy(self.RW[:], self.PS[:, 512: 512 + 64])
        self.const_tok = self.inc("dv", ins)

    def prep_wcache(self):
        po = self.po

        def group(pairs):
            for dst, src in pairs:
                self.inc("wc", po.dma_start(dst, src), 16)
            tok = self.cnt["wc"]
            po.wait_ge(self.sem["wc"], tok)
            return tok

        self.cast_tok = {}
        self.cast_tok["wp0"] = group([(self.d_wpc[0], self.d_wp[0])])
        for m in range(2):
            self.cast_tok["wqkv%d" % m] = group(
                [(self.d_wqkvc[m][j:j + 512, :], self.d_wqkv[m][j:j + 512, :])
                 for j in (0, 512)])
        self.cast_tok["wp1"] = group([(self.d_wpc[1], self.d_wp[1])])
        t = 0
        for j in range(0, C, 128):
            t = self.inc("wc", po.dma_start(self.d_w1c[j:j + 128, :], self.d_w1[j:j + 128, :]), 16)
        for j in range(0, FF, 2048):
            t = self.inc("wc", po.dma_start(self.d_w2c[j:j + 2048, :], self.d_w2[j:j + 2048, :]), 16)
        self.wc_tok = t

    def load_x(self, seq):
        src = self.d_x[seq].rearrange("(n p) c -> p n c", p=128)
        dst = self.XR[:].rearrange("p (n c) -> p n c", c=C)
        return self.inc("qx", self.po.dma_start(dst, src), 16)

    def ln_stats(self, src_ap, par, src_dv_tok=0):
        dv, ac, po = self.dv, self.ac, self.po
        c1 = 1.0 / C
        if src_dv_tok:
            dv.drain()
        t_dv = self.inc("dv", dv.reduce_sum(self.S1[:, par:par + 1], src_ap, axis=AX))
        self.w(ac, "dv", max(self.ln_free[par], src_dv_tok))
        t_ac = self.inc("ac", ac.activation(self.SQS[:], src_ap, AF.Square,
                                            accum_out=self.S2[:, par:par + 1]))
        self.w(dv, "ac", t_ac)
        dv.drain()
        dv.tensor_scalar(self.MUc[:, par:par + 1], self.S1[:, par:par + 1], c1, None, op0=ALU.mult)
        dv.drain()
        dv.tensor_tensor(self.TVA[:, par:par + 1], self.MUc[:, par:par + 1],
                         self.MUc[:, par:par + 1], op=ALU.mult)
        dv.drain()
        dv.tensor_scalar(self.TVA[:, par:par + 1], self.TVA[:, par:par + 1], EPS, None,
                         op0=ALU.subtract)
        dv.tensor_scalar(self.VPc[:, par:par + 1], self.S2[:, par:par + 1], c1, None, op0=ALU.mult)
        dv.drain()
        ins = dv.tensor_tensor(self.VPc[:, par:par + 1], self.VPc[:, par:par + 1],
                               self.TVA[:, par:par + 1], op=ALU.subtract)
        t_po = self.inc("dv", ins)
        self.w(ac, "dv", t_po)
        ac.activation(self.RSTc[:, par:par + 1], self.VPc[:, par:par + 1], AF.Ln)
        ac.drain()
        ins = ac.activation(self.RSTc[:, par:par + 1], self.RSTc[:, par:par + 1], AF.Exp, scale=-0.5)
        t_rst = self.inc("ac", ins)
        return (self.MUc[:, par:par + 1], self.RSTc[:, par:par + 1], t_rst)

    def h_build(self, src_buf, gain_idx=None, cache_stats=False):
        pe, dv, ac = self.pe, self.dv, self.ac
        T, NT = self.T, self.NT
        for i in range(NT):
            src = src_buf[:, i * C:(i + 1) * C]
            t_dv = self.inc("dv", dv.reduce_sum(self.S1A[:, i:i + 1], src, axis=AX))
            ac.drain()
            t_ac = self.inc("ac", ac.activation(self.SQS[:], src, AF.Square,
                                                accum_out=self.S2A[:, i:i + 1]))
        c1 = 1.0 / C
        self.w(dv, "ac", t_ac)
        dv.drain()
        dv.tensor_scalar(self.MUCACHE[:], self.S1A[:], c1, None, op0=ALU.mult)
        dv.drain()
        dv.tensor_tensor(self.VPA[:], self.MUCACHE[:], self.MUCACHE[:], op=ALU.mult)
        dv.drain()
        dv.tensor_scalar(self.VPA[:], self.VPA[:], EPS, None, op0=ALU.subtract)
        dv.drain()
        dv.tensor_scalar(self.RSTCACHE[:], self.S2A[:], c1, None, op0=ALU.mult)
        dv.drain()
        ins = dv.tensor_tensor(self.VPA[:], self.RSTCACHE[:], self.VPA[:], op=ALU.subtract)
        t_vp = self.inc("dv", ins)
        self.w(ac, "dv", t_vp)
        ac.activation(self.RSTCACHE[:], self.VPA[:], AF.Ln)
        ac.drain()
        ins = ac.activation(self.RSTCACHE[:], self.RSTCACHE[:], AF.Exp, scale=-0.5)
        t_rst = self.inc("ac", ins)
        nx_cons = [0, 0]
        pt_free = self.ptb_free[4]
        self.w(dv, "ac", t_rst)
        for i in range(NT):
            par = i % 2
            src = src_buf[:, i * C:(i + 1) * C]
            self.w(dv, "pe", nx_cons[par])
            nx = self.AST[:, par * C:(par + 1) * C]
            ins = dv.tensor_scalar(nx, src, self.MUCACHE[:, i:i + 1], self.RSTCACHE[:, i:i + 1],
                                   op0=ALU.subtract, op1=ALU.mult)
            t_nx = self.inc("dv", ins)
            self.w(pe, "dv", t_nx)
            self.w(pe, "dv", pt_free)
            ptb = self.pt_bf16(4)
            for j in range(NCH):
                mm = pe.transpose(ptb[:, j * 128:(j + 1) * 128], nx[:, j * 128:(j + 1) * 128], self.IDB[:])
            t_tr = self.inc("pe", mm)
            nx_cons[par] = t_tr
            self.w(dv, "pe", t_tr)
            if gain_idx is None:
                out = self.HF[:].rearrange("p (j t) -> p j t", j=NCH)[:, :, i * 128:(i + 1) * 128]
                ins = dv.tensor_copy(out, ptb[:].rearrange("p (j q) -> p j q", j=NCH))
            else:
                for j in range(NCH):
                    ins = dv.tensor_scalar(self.HF[:, j * T + i * 128: j * T + (i + 1) * 128],
                                           ptb[:, j * 128:(j + 1) * 128],
                                           self.LNW[:, gain_idx * 8 + j: gain_idx * 8 + j + 1],
                                           None, op0=ALU.mult)
            pt_free = self.inc("dv", ins)
        self.ptb_free[4] = pt_free

    def kv_build(self, mha, half, use_fp32=False):
        pe, dv, po, ac = self.pe, self.dv, self.po, self.ac
        T, NG = self.T, self.NG
        wslot_free = self.wsl_free
        wtok = [0, 0, 0, 0]
        kvbank_free = [0, 0, 0, 0]
        pt_free = self.ptb_free[4]
        vstage_cons = [0, 0]
        sq_cons = [0, 0]
        sq_tok = [0, 0]
        self._norm_free = getattr(self, "_norm_free", 0)
        norm_q = []

        def make_norm(which, p_, g, slot, sqtok):
            def go():
                self.w(pe, "ac", sqtok)
                if g == 0:
                    self.w(pe, "dv", self._norm_free)
                for h in range(2):
                    r = h * 64
                    colb = (5 + h) * 512
                    mm = pe.matmul(self.PS[r:r + 1, colb:colb + 512],
                                   self.ONES[r:r + 64, 0:1],
                                   self.HPQ[r:r + 64, slot * 512:(slot + 1) * 512],
                                   start=(g == 0), stop=(g == NG - 1))
                tok = self.inc("pe", mm)
                sq_cons[slot] = tok
                if g == NG - 1:
                    dst = self.KN2A if which == "k" else self.QN2A
                    self.w(dv, "pe", tok)
                    dv.drain()
                    for h in range(2):
                        r = h * 64
                        ins = dv.reduce_max(dst[r:r + 1, p_:p_ + 1],
                                            self.PS[r:r + 1, (5 + h) * 512:(6 + h) * 512],
                                            axis=AX)
                    self._norm_free = self.inc("dv", ins)
                    self.qkred_tok = self._norm_free
            return go
        nblk = (T // 128) * 4
        ones_ap = self.VT().rearrange("p (b c) -> p b c", c=130)[:, :, 64:130:65]
        ins = dv.memset(ones_ap, 1.0)
        t_ones = self.inc("dv", ins)
        sidx = 0
        for p in range(4):
            pair = half * 4 + p
            streams = (("q", 0), ("k", C), ("v", 2 * C))
            sl_of = {}
            for which, cbase in streams:
                sl = sidx % 4
                sidx += 1
                sl_of[which] = sl
                col0 = cbase + pair * 128
                dst = self.WSQ[:, sl * 1024:(sl + 1) * 1024].rearrange("p (j f) -> p j f", j=8)
                if use_fp32:
                    src = self.d_wqkv[mha][:, col0:col0 + 128].rearrange("(j p) f -> p j f", p=128)
                    self.w(self.po, "pe", wslot_free[sl])
                    wtok[sl] = self.inc("wb%d" % sl, self.po.dma_start(dst, src), 16)
                else:
                    src = self.d_wqkvc[mha][:, col0:col0 + 128].rearrange("(j p) f -> p j f", p=128)
                    self.w(self.sy, "wc", self.cast_tok["wqkv%d" % mha])
                    self.w(self.sy, "pe", wslot_free[sl])
                    wtok[sl] = self.inc("wa%d" % sl, self.sy.dma_start(dst, src), 16)
            wsem = "wb%d" if use_fp32 else "wa%d"
            for which, dstbuf, scale in (("k", self.KF(), None), ("q", self.QH(), 0.125)):
                sl = sl_of[which]
                self.w(pe, wsem % sl, wtok[sl])
                for g in range(NG):
                    bank = g % 2
                    self.w(pe, "dv", kvbank_free[bank])
                    for j in range(NCH):
                        mm = pe.matmul(self.psb(bank),
                                       self.WSQ[:, sl * 1024 + j * 128: sl * 1024 + (j + 1) * 128],
                                       self.HF[:, j * T + g * 512: j * T + (g + 1) * 512],
                                       start=(j == 0), stop=(j == NCH - 1))
                    t_mm = self.inc("pe", mm)
                    if g == NG - 1:
                        wslot_free[sl] = t_mm
                    self.w(dv, "pe", t_mm)
                    dsl = dstbuf[:, p * T + g * 512: p * T + (g + 1) * 512]
                    if scale is None:
                        ins = dv.tensor_copy(dsl, self.psb(bank))
                    else:
                        ins = dv.tensor_scalar(dsl, self.psb(bank), scale, None, op0=ALU.mult)
                    kvbank_free[bank] = self.inc("dv", ins)
                    slot = g % 2
                    while norm_q and norm_q[0][0] == slot:
                        norm_q.pop(0)[1]()
                    sqv = self.HPQ[:, slot * 512:(slot + 1) * 512]
                    self.w(ac, "dv", kvbank_free[bank])
                    self.w(ac, "pe", sq_cons[slot])
                    ins = ac.activation(sqv, dsl, AF.Square)
                    sq_tok[slot] = self.inc("ac", ins)
                    norm_q.append((slot, make_norm(which, p, g, slot, sq_tok[slot])))
                    if len(norm_q) > 1:
                        norm_q.pop(0)[1]()
            sl = sl_of["v"]
            self.w(pe, wsem % sl, wtok[sl])
            for g in range(NG):
                bank = 2 + g % 2
                self.w(pe, "dv", kvbank_free[bank])
                for j in range(NCH):
                    mm = pe.matmul(self.psb(bank),
                                   self.WSQ[:, sl * 1024 + j * 128: sl * 1024 + (j + 1) * 128],
                                   self.HF[:, j * T + g * 512: j * T + (g + 1) * 512],
                                   start=(j == 0), stop=(j == NCH - 1))
                t_mm = self.inc("pe", mm)
                if g == NG - 1:
                    wslot_free[sl] = t_mm
                if norm_q:
                    norm_q.pop(0)[1]()
                vsl = g % 2
                self.w(dv, "pe", t_mm)
                self.w(dv, "pe", vstage_cons[vsl])
                vst = self.AST[:, vsl * 512:(vsl + 1) * 512]
                ins = dv.tensor_copy(vst, self.psb(bank))
                t_ev = self.inc("dv", ins)
                kvbank_free[bank] = t_ev
                self.w(pe, "dv", t_ev)
                self.w(pe, "dv", pt_free)
                self.w(pe, "dv", t_ones)
                ptb = self.pt_bf16(4)
                for kk in range(4):
                    mm = pe.transpose(ptb[:, kk * 128:(kk + 1) * 128], vst[:, kk * 128:(kk + 1) * 128],
                                      self.IDB[:])
                t_tr = self.inc("pe", mm)
                vstage_cons[vsl] = t_tr
                self.w(dv, "pe", t_tr)
                vtr = self.VT().rearrange("p (b c) -> p b c", c=130)
                out = vtr[:, g * 16 + p: g * 16 + 13 + p: 4, 0:129]
                out = out.rearrange("p b (h d) -> p b h d", d=64) if False else \
                    vtr[:, g * 16 + p: g * 16 + 13 + p: 4, :].rearrange(
                        "p b c -> p b c")[:, :, 0:130]
                base = (g * 16 + p) * 130
                out = bass.AP(self.VT().tensor, self.VT().offset + base,
                              [list(self.VT().ap[0]), [4 * 130, 4], [65, 2], [1, 64]])
                ins = dv.tensor_copy(out, ptb[:, 0:512].rearrange("p (k h d) -> p k h d", k=4, h=2))
                pt_free = self.inc("dv", ins)
        while norm_q:
            norm_q.pop(0)[1]()
        self.ptb_free[4] = pt_free

    def emit_qknorms(self, half):
        pe, dv, ac = self.pe, self.dv, self.ac
        T = self.T
        self.w(dv, "dv", self.qkred_tok)
        dv.drain()
        for h in range(2):
            r = h * 64
            dv.tensor_scalar(self.NMH[r:r + 1, 0:4], self.QN2A[r:r + 1, 0:4], -4.08, None, op0=ALU.mult)
            dv.tensor_scalar(self.NMH[r:r + 1, 4:8], self.KN2A[r:r + 1, 0:4], -1.02 / 16.0,
                             None, op0=ALU.mult)
        dv.drain()
        for h in range(2):
            r = h * 64
            dv.tensor_tensor(self.NMH[r:r + 1, 0:4], self.NMH[r:r + 1, 0:4],
                             self.NMH[r:r + 1, 4:8], op=ALU.add)
        dv.drain()
        for h in range(2):
            r = h * 64
            ins = dv.tensor_copy(self.NMHB[r:r + 1, h * 4: h * 4 + 4], self.NMH[r:r + 1, 0:4])
        t_nb = self.inc("dv", ins)
        self.w(pe, "dv", t_nb)
        for h in range(2):
            r = h * 64
            mm = pe.matmul(self.PS[:, 7 * 512 + h * 4: 7 * 512 + h * 4 + 4],
                           self.ONES[r:r + 1, 0:128], self.NMHB[r:r + 1, h * 4: h * 4 + 4],
                           start=True, stop=True)
        t_mm2 = self.inc("pe", mm)
        self.w(dv, "pe", t_mm2)
        ins = dv.tensor_copy(self.BND[:, 0:8], self.PS[:, 7 * 512: 7 * 512 + 8])
        t_bc = self.inc("dv", ins)
        self.bnd_tok = t_bc
        self.kn_guard = t_bc

    def attention_half(self, mha, half):
        pe, dv, ac = self.pe, self.dv, self.ac
        T, NG = self.T, self.NG
        for qg in range(NG):
            for p in range(4):
                gp = p % 2
                nj = 4 * qg + 4
                npair = nj // 2
                avtok = [0, 0]
                exp_of = {}

                def emit_ST(jp):
                    toks = [0, 0]
                    for jj in range(2):
                        j = jp * 2 + jj
                        for h in range(2):
                            bank = h * 2 + jj
                            self.w(pe, "ac", self.stb_free[bank])
                            if j == 0:
                                self.w(pe, "dv", self.obank_free[h * 2 + gp])
                                self.w(pe, "dv", self.kn_guard)
                            lhs = self.KF()[h * 64:(h + 1) * 64,
                                            p * T + j * 128: p * T + (j + 1) * 128]
                            rhs = self.QH()[h * 64:(h + 1) * 64,
                                            p * T + qg * 512: p * T + (qg + 1) * 512]
                            if j >= 4 * qg:
                                r = j - 4 * qg
                                pe.matmul(self.psb(bank), lhs, rhs, start=True, stop=False)
                                mm = pe.matmul(self.psb(bank), self.IDB[:],
                                               self.MK[:, r * 512:(r + 1) * 512],
                                               start=False, stop=True)
                            else:
                                mm = pe.matmul(self.psb(bank), lhs, rhs, start=True, stop=True)
                            toks[h] = self.inc("pe", mm)
                    sp = jp % 2
                    for h in range(2):
                        slot = h * 2 + sp
                        self.w(ac, "pe", toks[h])
                        self.w(ac, "pe", self.ptslot_free[slot])
                        ins = ac.activation(self.PTb(slot), self.PS[:, (h * 2) * 512:(h * 2) * 512 + 1024],
                                            AF.Exp, bias=self.BND[:, h * 4 + p: h * 4 + p + 1])
                        etok = self.inc("ac", ins)
                        self.stb_free[h * 2] = etok
                        self.stb_free[h * 2 + 1] = etok
                        exp_of[(jp, h)] = etok

                def emit_AV(jp):
                    sp = jp % 2
                    for jj in range(2):
                        j = jp * 2 + jj
                        for h in range(2):
                            slot = h * 2 + sp
                            obank = 4 + h * 2 + gp
                            self.w(pe, "ac", exp_of[(jp, h)])
                            blk = (j * 4 + p) * 130
                            mm = pe.matmul(self.PS[0:65, obank * 512: obank * 512 + 512],
                                           self.VT()[:, blk + h * 65: blk + (h + 1) * 65],
                                           self.PTb(slot)[:, jj * 512:(jj + 1) * 512],
                                           start=(j == 0), stop=(j == nj - 1))
                            avtok[h] = self.inc("pe", mm)
                            if jj == 1:
                                self.ptslot_free[slot] = avtok[h]

                for jp in range(npair):
                    emit_ST(jp)
                    self.step_rt()
                    if jp >= 1:
                        emit_AV(jp - 1)
                        self.step_rt()
                emit_AV(npair - 1)
                self.flush_rt()
                for h in range(2):
                    self.queue_roundtrip(half, p, qg, h, gp, avtok[h])
        self.flush_rt()

    def queue_roundtrip(self, half, p, qg, h, gp, t_av):
        pe, dv = self.pe, self.dv
        T = self.T
        obank = 4 + h * 2 + gp
        pair = half * 4 + p
        st = {}

        def s0():
            self.w(dv, "pe", t_av)
            self.w(dv, "pe", self.rt_osg_free)
            ins = dv.tensor_copy(self.OSG()[0:65, 0:512],
                                 self.PS[0:65, obank * 512: obank * 512 + 512])
            st["ev"] = self.inc("dv", ins)

        def s1():
            self.w(pe, "dv", st["ev"])
            for b in range(4):
                mm = pe.transpose(self.PS[:, obank * 512 + b * 65: obank * 512 + (b + 1) * 65],
                                  self.OSG()[0:65, b * 128:(b + 1) * 128], self.IDF[0:65, 0:65])
            st["tr"] = self.inc("pe", mm)
            self.rt_osg_free = st["tr"]

        def s2():
            self.w(dv, "pe", st["tr"])
            self.w(dv, "pe", self.rt_otb_free)
            dv.tensor_copy(self.OTM()[:, 0:260], self.PS[:, obank * 512: obank * 512 + 260])
            dv.drain()
            linv_src = bass.AP(self.OTM().tensor, self.OTM().offset + 64,
                               [list(self.OTM().ap[0]), [65, 4], [1, 1]])
            dv.reciprocal(self.OL4[:, 0:4], linv_src)
            dv.drain()
            for b in range(4):
                ins = dv.tensor_scalar(self.OTB()[:, b * 64:(b + 1) * 64],
                                       self.OTM()[:, b * 65: b * 65 + 64],
                                       self.OL4[:, b:b + 1], None, op0=ALU.mult)
            st["nb"] = self.inc("dv", ins)

        def s3():
            self.w(pe, "dv", st["nb"])
            ptb = self.pt_bf16(obank)
            for b in range(4):
                mm = pe.transpose(ptb[0:64, b * 128:(b + 1) * 128],
                                  self.OTB()[:, b * 64:(b + 1) * 64], self.IDB[:])
            st["tb"] = self.inc("pe", mm)
            self.rt_otb_free = st["tb"]

        def s4():
            self.w(dv, "pe", st["tb"])
            ins = dv.tensor_copy(
                self.OF[h * 64:(h + 1) * 64, pair * T + qg * 512: pair * T + (qg + 1) * 512],
                ptb_dummy[0] if False else self.pt_bf16(obank)[0:64, 0:512])
            self.obank_free[h * 2 + gp] = self.inc("dv", ins)

        self.rt_queue.extend([s0, s1, s2, s3, s4])

    def step_rt(self):
        if self.rt_queue:
            self.rt_queue.pop(0)()

    def flush_rt(self):
        while self.rt_queue:
            self.rt_queue.pop(0)()

    def proj_phase(self, mha, is_inner):
        pe, dv, po = self.pe, self.dv, self.po
        T, NG = self.T, self.NG
        wslot_free = self.wsl_free
        wtok = [0, 0]
        pbank_free = [0, 0]
        pt_free = self.ptb_free[4]
        afm_cons = 0
        atm_free = [0, 0]
        for g in range(NG):
            evtoks = []
            for ct in range(8):
                sl = ct % 2
                src = self.d_wpc[mha][:, ct * 128:(ct + 1) * 128].rearrange("(j p) f -> p j f", p=128)
                self.w(self.sy, "wc", self.cast_tok["wp%d" % mha])
                self.w(self.sy, "pe", wslot_free[sl])
                dst = self.WSQ[:, sl * 1024:(sl + 1) * 1024].rearrange("p (j f) -> p j f", j=8)
                wtok[sl] = self.inc("wa%d" % sl, self.sy.dma_start(dst, src), 16)
                self.w(pe, "wa%d" % sl, wtok[sl])
                bank = ct % 2
                self.w(pe, "dv", pbank_free[bank])
                if ct == 0:
                    self.w(pe, "pe", afm_cons)
                for j in range(NCH):
                    mm = pe.matmul(self.psb(bank),
                                   self.WSQ[:, sl * 1024 + j * 128: sl * 1024 + (j + 1) * 128],
                                   self.OF[:, j * T + g * 512: j * T + (g + 1) * 512],
                                   start=(j == 0), stop=(j == NCH - 1))
                t_mm = self.inc("pe", mm)
                wslot_free[sl] = t_mm
                self.w(dv, "pe", t_mm)
                self.w(dv, "pe", afm_cons)
                ins = dv.tensor_copy(self.AST[:, ct * 512:(ct + 1) * 512], self.psb(bank))
                pbank_free[bank] = self.inc("dv", ins)
                evtoks.append(pbank_free[bank])
            afm_tok = max(evtoks)
            for tt in range(4):
                tslot = tt % 2
                self.w(pe, "dv", afm_tok)
                self.w(pe, "dv", pt_free)
                self.w(pe, "dv", atm_free[tslot])
                ptb = self.pt_bf16(4)
                for ct in range(8):
                    mm = pe.transpose(ptb[:, ct * 128:(ct + 1) * 128],
                                      self.AST[:, ct * 512 + tt * 128: ct * 512 + (tt + 1) * 128],
                                      self.IDB[:])
                t_tr = self.inc("pe", mm)
                if tt == 3:
                    afm_cons = t_tr
                self.w(dv, "pe", t_tr)
                atm = self.ATM[:, tslot * 1024:(tslot + 1) * 1024]
                ins = dv.tensor_copy(atm, ptb[:])
                pt_free = self.inc("dv", ins)
                ti = g * 4 + tt
                xr = self.XR[:, ti * C:(ti + 1) * C]
                if not is_inner:
                    par = tt % 2
                    mu, rstd, t_rst = self.ln_stats(atm, par, src_dv_tok=pt_free)
                    self.w(dv, "ac", t_rst)
                    dv.tensor_scalar(atm, atm, mu, rstd, op0=ALU.subtract, op1=ALU.mult)
                    dv.drain()
                    ins = dv.tensor_tensor(xr, xr, atm, op=ALU.add)
                    t_res = self.inc("dv", ins)
                    self.ln_free[par] = t_res
                    atm_free[tslot] = t_res
                else:
                    oslot = 1 - tslot
                    self.w(dv, "pe", 0)
                    self.w(dv, "dv", 0)
                    other = self.ATM[:, oslot * 1024:(oslot + 1) * 1024]
                    self.w(dv, "dv", atm_free[oslot]) if False else None
                    dv.tensor_scalar(other, xr, self.MUCACHE[:, ti:ti + 1],
                                     self.RSTCACHE[:, ti:ti + 1], op0=ALU.subtract, op1=ALU.mult)
                    dv.drain()
                    uu = self.UU[:, ti * C:(ti + 1) * C]
                    ins = dv.tensor_tensor(uu, other, atm, op=ALU.add)
                    t_res = self.inc("dv", ins)
                    atm_free[tslot] = t_res
                    atm_free[oslot] = t_res
        self.ptb_free[4] = pt_free

    def mha_slot(self, seq, mha, boot=False):
        is_inner = (mha == 1)
        self.h_build(self.XR, gain_idx=(0 if is_inner else None), cache_stats=is_inner)
        self.barrier()
        if self.ck("h%d" % mha):
            self.dump("h", self.HF[:])
            raise _DebugStop()
        for half in range(2):
            self.kv_build(mha, half, use_fp32=(boot and half == 0))
            if boot and half == 0:
                self.prep_wcache()
            self.barrier()
            self.emit_qknorms(half)
            if half == 0 and self.ck("kv%d" % mha):
                self.dump("kf", self.KF()[:])
                self.dump("vt", self.VT()[:])
                raise _DebugStop()
            self.attention_half(mha, half)
            self.barrier()
        if self.ck("of%d" % mha):
            self.dump("of", self.OF[:, 0:8 * self.T])
            raise _DebugStop()
        self.proj_phase(mha, is_inner)
        self.barrier()

    def router_phase(self):
        pe, dv, po, ac = self.pe, self.dv, self.po, self.ac
        T, NG = self.T, self.NG
        lg_free = 0
        lgt_free = [0, 0]
        ps5_free = 0
        wt_tok = 0
        for g in range(NG):
            for j in range(NCH):
                mm = pe.matmul(self.PS[0:8, 7 * 512: 7 * 512 + 512],
                               self.RW[:, j * 8:(j + 1) * 8],
                               self.HF[:, j * T + g * 512: j * T + (g + 1) * 512],
                               start=(j == 0), stop=(j == NCH - 1))
            t_mm = self.inc("pe", mm)
            self.w(dv, "pe", t_mm)
            self.w(dv, "pe", lg_free)
            ins = dv.tensor_copy(self.LG[0:8, 0:512], self.PS[0:8, 7 * 512: 7 * 512 + 512])
            t_lg = self.inc("dv", ins)
            for tt in range(4):
                par = tt % 2
                self.w(pe, "dv", t_lg)
                self.w(pe, "dv", max(lgt_free[par], ps5_free))
                mm = pe.transpose(self.PS[:, 5 * 512: 5 * 512 + 8],
                                  self.LG[0:8, tt * 128:(tt + 1) * 128], self.IDF[0:8, 0:8])
                t_tr = self.inc("pe", mm)
                if tt == 3:
                    lg_free = t_tr
                self.w(dv, "pe", t_tr)
                ins = dv.tensor_copy(self.LGT[:, par * 8:(par + 1) * 8],
                                     self.PS[:, 5 * 512: 5 * 512 + 8])
                t_lgt = self.inc("dv", ins)
                ps5_free = t_lgt
                lgt = self.LGT[:, par * 8:(par + 1) * 8]
                dv.drain()
                dv.reduce_max(self.RM1N[:], lgt, axis=AX, negate=True)
                dv.drain()
                dv.tensor_scalar(self.RM1[:], self.RM1N[:], -1.0, None, op0=ALU.mult)
                dv.drain()
                dv.tensor_scalar(self.RMK1[:], lgt, self.RM1[:], None, op0=ALU.is_ge)
                dv.drain()
                dv.scalar_tensor_tensor(self.RL2[:], self.RMK1[:], -1e30, lgt,
                                        op0=ALU.mult, op1=ALU.add)
                dv.drain()
                ins = dv.reduce_max(self.RM2[:], self.RL2[:], axis=AX)
                t_m2 = self.inc("dv", ins)
                self.w(ac, "dv", t_m2)
                ins = ac.activation(self.RE2[:], self.RM2[:], AF.Exp, bias=self.RM1N[:])
                t_e2 = self.inc("ac", ins)
                self.w(dv, "ac", t_e2)
                dv.tensor_scalar(self.RDEN[:], self.RE2[:], 1.0, None, op0=ALU.add)
                dv.drain()
                dv.tensor_scalar(self.RMK2[:], self.RL2[:], self.RM2[:], None, op0=ALU.is_ge)
                dv.drain()
                ti = g * 4 + tt
                dv.scalar_tensor_tensor(self.RMK2[:], self.RMK2[:], self.RE2[:], self.RMK1[:],
                                        op0=ALU.mult, op1=ALU.add)
                dv.reciprocal(self.RW1[:], self.RDEN[:])
                dv.drain()
                ins = dv.tensor_scalar(self.WTOK[:, ti * 8:(ti + 1) * 8], self.RMK2[:],
                                       self.RW1[:], None, op0=ALU.mult)
                wt_tok = self.inc("dv", ins)
                lgt_free[par] = wt_tok
        self.wtok_tok = wt_tok

    def moe_phase(self, seq):
        pe, dv, ac, po = self.pe, self.dv, self.ac, self.po
        T = self.T
        tgW = T // 2
        self.tgW = tgW
        self.WBv = self.AST[:, 0:tgW]
        nsub = max(1, tgW // 512)
        subW = min(512, tgW)
        nh = max(1, tgW // 512)
        self.w(self.sy, "pe", self.cnt["pe"])
        self.w(self.sy, "dv", self.cnt["dv"])
        self.w(self.sy, "ac", self.cnt["ac"])
        for tg in range(2):
            tgbase = tg * tgW
            tgtile0 = tg * (tgW // 128)
            gel_tok = [0, 0]
            scl_tok = [0] * 8
            w1_tok = [0, 0]
            w1_free = [0, 0]
            w2_free = 0
            w2_tok_cur = 0
            w2_tok_prev = 0
            scl_prev = [0] * 8
            yevac_tok = [0] * (2 * nsub)
            y_last_evac = 0
            self._ybank_tok = [0, 0, 0, 0]
            wb_tok = 0
            hpq_guard = 0

            def h_mms(fc, oct_):
                bank0 = (fc % 2) * 2
                sl = fc // 4
                self.w(pe, "w1a" if sl == 0 else "w1b", w1_tok[sl])
                self.w(pe, "ac", gel_tok[fc % 2])
                if fc == 0:
                    self.w(pe, "dv", wb_tok)
                for j in range(NCH):
                    for hh in range(nh):
                        mm = pe.matmul(self.PS[:, (bank0 + hh) * 512:(bank0 + hh) * 512 + subW],
                                       self.W1Q(sl)[:, j * 512 + (fc % 4) * 128: j * 512 + (fc % 4) * 128 + 128],
                                       self.HF[:, j * T + tgbase + hh * subW: j * T + tgbase + (hh + 1) * subW],
                                       start=(j == 0), stop=(j == NCH - 1))
                tok = self.inc("pe", mm)
                return tok

            def gelu_scale(fc, t_h):
                bank0 = (fc % 2) * 2
                self.w(ac, "pe", t_h)
                self.w(ac, "dv", scl_prev[fc])
                self.w(ac, "pe", hpq_guard)
                ins = ac.activation(self.HPQ[:, fc * tgW:(fc + 1) * tgW],
                                    self.PS[:, bank0 * 512: bank0 * 512 + tgW], AF.Gelu)
                t_g = self.inc("ac", ins)
                gel_tok[fc % 2] = t_g
                self.w(dv, "ac", t_g)
                ins = dv.tensor_tensor(self.HPQ[:, fc * tgW:(fc + 1) * tgW],
                                       self.HPQ[:, fc * tgW:(fc + 1) * tgW],
                                       self.WBv, op=ALU.mult)
                scl_tok[fc] = self.inc("dv", ins)

            def w2_combos(first):
                nonlocal w2_free, y_last_evac
                dv.drain()
                ybank_tok = list(self._ybank_tok)
                for ci, (ch, sub) in enumerate([(a, b) for a in range(2) for b in range(nsub)]):
                    self.w(pe, "w2o", w2_tok_prev)
                    mm_tok = [0, 0, 0, 0]
                    for fc in range(8):
                        self.w(pe, "dv", scl_prev[fc])
                        for cti in range(4):
                            if fc == 0:
                                self.w(pe, "dv", ybank_tok[cti])
                            cto = (ch * 4 + cti) * 128
                            mm = pe.matmul(self.PS[:, (4 + cti) * 512:(4 + cti) * 512 + subW],
                                           self.W2O[:, fc * 1024 + cto: fc * 1024 + cto + 128],
                                           self.HPQ[:, fc * tgW + sub * subW: fc * tgW + (sub + 1) * subW],
                                           start=(fc == 0), stop=(fc == 7))
                            if fc == 7:
                                mm_tok[cti] = self.inc("pe", mm)
                    w2_free = mm_tok[3]
                    for cti in range(4):
                        self.w(dv, "pe", mm_tok[cti])
                        ysl = self.BIGB[:, (ch * 4 + cti) * tgW + sub * subW:
                                        (ch * 4 + cti) * tgW + (sub + 1) * subW]
                        ps_y = self.PS[:, (4 + cti) * 512:(4 + cti) * 512 + subW]
                        if first:
                            ins = dv.tensor_scalar(ysl, ps_y, 1.0, None, op0=ALU.mult)
                        else:
                            ins = dv.scalar_tensor_tensor(ysl, ps_y, 1.0, ysl, op0=ALU.mult, op1=ALU.add)
                        ybank_tok[cti] = self.inc("dv", ins)
                        y_last_evac = ybank_tok[cti]
                self._ybank_tok = ybank_tok

            for oct_ in range(16):
                if oct_ % 2 == 0:
                    wb_tok = self.wb_build(tgtile0, oct_ // 2, gel_tok, scl_tok)
                for q in range(2):
                    col0 = oct_ * 1024 + q * 512
                    src = self.d_w1c[:, col0:col0 + 512].rearrange("(j p) f -> p j f", p=128)
                    self.w(self.sy, "pe", w1_free[q])
                    if oct_ == 0 and tg == 0:
                        self.w(self.sy, "wc", self.wc_tok)
                    dst = self.W1Q(q)[:].rearrange("p (j f) -> p j f", j=8)
                    w1_tok[q] = self.inc("w1a" if q == 0 else "w1b", self.sy.dma_start(dst, src), 16)
                t_h0 = h_mms(0, oct_)
                t_h1 = h_mms(1, oct_)
                if oct_ > 0:
                    w2_combos(first=(oct_ == 1))
                hpq_guard = w2_free
                self.w(self.sy, "pe", w2_free)
                src = self.d_w2c[oct_ * 1024:(oct_ + 1) * 1024, :].rearrange("(j p) c -> p j c", p=128)
                dst = self.W2O[:].rearrange("p (j c) -> p j c", j=8)
                w2_tok_cur = self.inc("w2o", self.sy.dma_start(dst, src), 16)
                gelu_scale(0, t_h0)
                gelu_scale(1, t_h1)
                for fc in range(2, 8):
                    t_h = h_mms(fc, oct_)
                    gelu_scale(fc, t_h)
                w1_free = [self.cnt["pe"], self.cnt["pe"]]
                w2_tok_prev = w2_tok_cur
                scl_prev = list(scl_tok)
            w2_combos(first=False)
            self.barrier()
            if self.ck("y") and tg == 0:
                self.dump("y", self.BIGB[:, 0:4 * self.T])
                self.dump("wtok", self.WTOK[:])
                self.dump("t2d", self.HF[:])
                raise _DebugStop()
            self.finale(seq, tg, tgW)
            self.barrier()

    def wb_build(self, tgtile0, expert, gel_tok, scl_tok):
        pe, dv = self.pe, self.dv
        ntile = self.tgW // 128
        dv.drain()
        self.w(dv, "pe", self.cnt["pe"])
        for tt in range(ntile):
            ti = tgtile0 + tt
            ins = dv.tensor_scalar(self.AST[:, 2048 + tt * 128: 2048 + (tt + 1) * 128],
                                   self.ONES[:], self.WTOK[:, ti * 8 + expert: ti * 8 + expert + 1],
                                   None, op0=ALU.mult)
        t_stage = self.inc("dv", ins)
        self.w(pe, "dv", t_stage)
        self.w(pe, "ac", gel_tok[0])
        ptb = self.pt_bf16(0)
        for tt in range(ntile):
            mm = pe.transpose(ptb[:, tt * 128:(tt + 1) * 128],
                              self.AST[:, 2048 + tt * 128: 2048 + (tt + 1) * 128], self.IDB[:])
        t_tr = self.inc("pe", mm)
        self.w(dv, "pe", t_tr)
        ins = dv.tensor_copy(self.WBv, ptb[:, 0:self.tgW])
        return self.inc("dv", ins)

    def finale(self, seq, tg, tgW):
        pe, dv, sy = self.pe, self.dv, self.sy
        ntile = tgW // 128
        astf = self.AST[:].bitcast(F32)
        atmf = self.ATM[:].bitcast(F32)
        outslot = [astf[:, 1024:2048], atmf[:, 0:1024]]
        out_tok = [0, 0]
        f_cons = 0
        tr_free = 0
        for tt in range(ntile):
            ti = tg * ntile + tt
            par = tt % 2
            self.w(pe, "dv", tr_free)
            for ct in range(8):
                bank = ct // 4
                mm = pe.transpose(self.PS[:, bank * 512 + (ct % 4) * 128: bank * 512 + (ct % 4) * 128 + 128],
                                  self.BIGB[:, ct * tgW + tt * 128: ct * tgW + (tt + 1) * 128],
                                  self.IDF[:])
            t_tr = self.inc("pe", mm)
            self.w(dv, "pe", t_tr)
            self.w(dv, "dv", 0)
            ftmp = astf[:, 0:1024]
            ins = dv.scalar_tensor_tensor(ftmp, self.PS[:, 0:1024], 1.0,
                                          self.UU[:, ti * C:(ti + 1) * C], op0=ALU.mult, op1=ALU.add)
            t_f = self.inc("dv", ins)
            tr_free = t_f
            mu, rstd, t_rst = self.ln_stats(ftmp, par, src_dv_tok=t_f)
            self.w(dv, "ac", t_rst)
            self.w(dv, "qo%d" % par, out_tok[par])
            nrm = outslot[par]
            dv.tensor_scalar(nrm, ftmp, mu, rstd, op0=ALU.subtract, op1=ALU.mult)
            dv.drain()
            ins = dv.tensor_tensor(nrm, nrm, self.XR[:, ti * C:(ti + 1) * C], op=ALU.add)
            t_o = self.inc("dv", ins)
            self.ln_free[par] = t_o
            self.w(sy, "dv", t_o)
            dst = self.d_out[seq, ti * 128:(ti + 1) * 128, :]
            out_tok[par] = self.inc("qo%d" % par, sy.dma_start(dst, nrm), 16)

    def dump(self, name, ap):
        w = ap.shape[-1]
        npart = ap.shape[0]
        d = self.nc.dram_tensor("dbg_" + name, [npart, w], F32, kind="ExternalOutput").ap()
        self.barrier()
        for s in self.sem:
            if s != "qdbg" and self.cnt[s]:
                self.po.wait_ge(self.sem[s], self.cnt[s])
        self.inc("qdbg", self.po.dma_start(d, ap), 16)

    def ck(self, phase):
        if self.debug_phase is None:
            return False
        return phase == self.debug_phase

    def build(self, debug_phase=None):
        self.debug_phase = debug_phase

        def fin():
            for s in ("qo0", "qo1", "qdbg"):
                if self.cnt[s]:
                    self.po.wait_ge(self.sem[s], self.cnt[s])
                    self.sy.wait_ge(self.sem[s], self.cnt[s])
            return self.nc

        self.load_consts()
        try:
            self._main()
        except _DebugStop:
            pass
        return fin()

    def _main(self):
        for seq in range(2):
            if seq == 1:
                self.barrier()
            xt = self.load_x(seq)
            for eng in (self.dv, self.pe, self.ac):
                self.w(eng, "qx", xt)
            self.w(self.dv, "dv", self.const_tok)
            if self.ck("x"):
                self.dump("x", self.XR[:])
                raise _DebugStop()
            self.mha_slot(seq, 0, boot=(seq == 0))
            if self.ck("mha0"):
                self.dump("x1", self.XR[:])
                raise _DebugStop()
            self.mha_slot(seq, 1)
            if self.ck("mha1"):
                self.dump("x1", self.XR[:])
                self.dump("u", self.UU[:, 0:self.NT * C])
                raise _DebugStop()
            self.h_build(self.UU, gain_idx=1, cache_stats=False)
            self.barrier()
            if self.ck("t2"):
                self.dump("t2", self.HF[:])
                raise _DebugStop()
            self.router_phase()
            self.barrier()
            if self.ck("router"):
                self.dump("wtok", self.WTOK[:])
                raise _DebugStop()
            self.moe_phase(seq)
            if self.ck("y"):
                raise _DebugStop()


def build_program(T=2048):
    return KB(T).build()


def make_masks():
    kk = np.arange(128)[:, None]
    qq = np.arange(512)[None, :]
    return np.stack([np.where(qq >= 128 * r + kk, 0.0, -1e30) for r in range(4)]).astype(np.float32)


def kernel(L0_attn_x, L0_attn_W_attn, L0_attn_W_proj, L0_ffn_ln1_w, L0_ffn_c_attn_w,
           L0_ffn_c_proj_w, L0_ffn_ln2_w, L0_ffn_router_w, L0_ffn_w1, L0_ffn_w2):
    x = np.ascontiguousarray(np.asarray(L0_attn_x, dtype=np.float32))
    B, T, _ = x.shape
    n_cores = 8
    per = B // n_cores
    nc = build_program(T)
    consts = {
        "wqkv0": np.ascontiguousarray(L0_attn_W_attn, dtype=np.float32),
        "wqkv1": np.ascontiguousarray(L0_ffn_c_attn_w, dtype=np.float32),
        "wp0": np.ascontiguousarray(L0_attn_W_proj, dtype=np.float32),
        "wp1": np.ascontiguousarray(L0_ffn_c_proj_w, dtype=np.float32),
        "lnw": np.ascontiguousarray(
            np.stack([np.asarray(L0_ffn_ln1_w, dtype=np.float32).reshape(8, 128),
                      np.asarray(L0_ffn_ln2_w, dtype=np.float32).reshape(8, 128)])),
        "rwT": np.ascontiguousarray(np.asarray(L0_ffn_router_w, dtype=np.float32).T),
        "w1": np.ascontiguousarray(L0_ffn_w1, dtype=np.float32),
        "w2": np.ascontiguousarray(L0_ffn_w2, dtype=np.float32),
        "ident": np.eye(128, dtype=np.float32),
        "masks": make_masks(),
        "ones": np.ones((128, 128), dtype=np.float32),
    }
    in_maps = []
    for c in range(n_cores):
        m = dict(consts)
        m["x"] = np.ascontiguousarray(x[c * per:(c + 1) * per])
        in_maps.append(m)
    res = run_bass_kernel_spmd(nc, in_maps, core_ids=list(range(n_cores)))
    return np.concatenate([np.asarray(r["out"], dtype=np.float32) for r in res.results], axis=0)
